# version 02
# baseline (speedup 1.0000x reference)
import math
import os
import numpy as np
import ml_dtypes
from contextlib import ExitStack

import concourse.bass as bass
import concourse.mybir as mybir
from concourse.bass_utils import run_bass_kernel_spmd

F32 = mybir.dt.float32
BF16 = mybir.dt.bfloat16
AF = mybir.ActivationFunctionType
ALU = mybir.AluOpType
AX = mybir.AxisListType

P = 128
D = 1024
KC = 8
H = 8
DF = 256
TOK = 4096
NT = 8
TT = 512
NCH = 4
CHN = 128
DFF = 3072
NI = 24
RC = 12582912.0
LN16 = math.log(16.0)
QMAX = 32767.0

PH = int(os.environ.get("KPHASES", "99"))
KNOB = os.environ.get("KNOB", "")
ENGSEM = dict(tensor="T", vector="V", scalar="S", sync="YN", gpsimd="GN")
NONDMA = {"T", "V", "S", "YN", "GN"}


class Prog:
    def __init__(self):
        self.mode = "count"
        self.tags = {}
        self.cnt = {}
        self.engs = {}
        self.sems = {}
        self.water = {}

    def reset(self, mode):
        self.mode = mode
        self.cnt = {k: 0 for k in self.sems}
        self.water = {}

    def settag(self, tag, srctag):
        if self.mode == "count" and tag not in self.tags:
            self.tags[tag] = self.tags[srctag]

    def op(self, eng, emit, deps=(), inc=1, tag=None, sem=None):
        semname = sem or ENGSEM[eng]
        self.cnt[semname] = self.cnt.get(semname, 0) + inc
        val = self.cnt[semname]
        if self.mode == "count":
            if tag is not None:
                if tag in self.tags:
                    assert self.tags[tag] == (semname, val), f"tag drift {tag}"
                self.tags[tag] = (semname, val)
            return tag
        if self.mode != eng:
            return tag
        e = self.engs[eng]
        own = ENGSEM[eng]
        for d in deps:
            if d is None:
                continue
            s, v = self.tags[d]
            if s == own and s in NONDMA:
                assert v <= self.cnt[own], f"forward self-dep {d} on {eng}"
            if self.water.get(s, 0) >= v:
                continue
            e.wait_ge(self.sems[s], v)
            self.water[s] = v
        instr = emit(e)
        if inc:
            instr.then_inc(self.sems[semname], inc)
        return tag

    def barrier(self, name, sb):
        if self.mode == "count":
            self.tags[f"bar_{name}"] = {k: v for k, v in self.cnt.items()}
            return
        vals = self.tags[f"bar_{name}"]
        for eng in ENGSEM:
            if self.mode != eng:
                continue
            e = self.engs[eng]
            own = ENGSEM[eng]
            for sname, v in vals.items():
                if sname == own or v == 0:
                    continue
                if self.water.get(sname, 0) >= v:
                    continue
                e.wait_ge(self.sems[sname], v)
                self.water[sname] = v


def build_nc():
    nc = bass.Bass("TRN2", target_bir_lowering=False)

    xT = nc.dram_tensor("xT", [D, TOK], F32, kind="ExternalInput").ap()
    wqT = nc.dram_tensor("wqT", [D, D], F32, kind="ExternalInput").ap()
    wkT = nc.dram_tensor("wkT", [D, D], F32, kind="ExternalInput").ap()
    wvT = nc.dram_tensor("wvT", [D, D], F32, kind="ExternalInput").ap()
    woT = nc.dram_tensor("woT", [D, D], F32, kind="ExternalInput").ap()
    fmqT = nc.dram_tensor("fmqT", [P, P], F32, kind="ExternalInput").ap()
    fmkT = nc.dram_tensor("fmkT", [P, P], F32, kind="ExternalInput").ap()
    fmqb = nc.dram_tensor("fmqb", [P, 1], F32, kind="ExternalInput").ap()
    fmkb = nc.dram_tensor("fmkb", [P, 1], F32, kind="ExternalInput").ap()
    rmsw = nc.dram_tensor("rmsw", [P, 1], F32, kind="ExternalInput").ap()
    wup = nc.dram_tensor("wup", [DFF, D], F32, kind="ExternalInput").ap()
    wgate = nc.dram_tensor("wgate", [DFF, D], F32, kind="ExternalInput").ap()
    wdown = nc.dram_tensor("wdown", [D, DFF], F32, kind="ExternalInput").ap()
    wmod = nc.dram_tensor("wmod", [D, 6 * D], F32, kind="ExternalInput").ap()
    crow = nc.dram_tensor("crow", [1, D], F32, kind="ExternalInput").ap()
    smask = nc.dram_tensor("smask", [1, 8], F32, kind="ExternalInput").ap()
    out = nc.dram_tensor("out", [D, TOK], F32, kind="ExternalOutput").ap()

    hb_h = nc.dram_tensor("hb_h", [D, TOK], BF16, kind="Internal").ap()
    hb_kfm = nc.dram_tensor("hb_kfm", [H * DF, TOK], BF16, kind="Internal").ap()
    hb_ktm = nc.dram_tensor("hb_ktm", [TOK, H * DF], BF16, kind="Internal").ap()
    hb_v = nc.dram_tensor("hb_v", [TOK, D], BF16, kind="Internal").ap()
    x2h = nc.dram_tensor("x2h", [D, TOK], F32, kind="Internal").ap()
    h2h = nc.dram_tensor("h2h", [D, TOK], BF16, kind="Internal").ap()
    wupT_q = nc.dram_tensor("wupT_q", [D, DFF], BF16, kind="Internal").ap()
    wgT_q = nc.dram_tensor("wgT_q", [D, DFF], BF16, kind="Internal").ap()
    wdT_q = nc.dram_tensor("wdT_q", [DFF, D], BF16, kind="Internal").ap()
    Uh = nc.dram_tensor("Uh", [P, NT * NI * TT], BF16, kind="Internal").ap()
    Gh = nc.dram_tensor("Gh", [P, NT * NI * TT], BF16, kind="Internal").ap()
    g2h = nc.dram_tensor("g2h", [P, NT * NI * TT], BF16, kind="Internal").ap()
    prh = nc.dram_tensor("prh", [P, NT * NI * TT], BF16, kind="Internal").ap()
    sc_in = nc.dram_tensor("sc_in", [P, H * DF], F32, kind="Internal").ap()
    sc_out = nc.dram_tensor("sc_out", [8 * P, H * DF], F32, kind="Internal").ap()
    mx_in = nc.dram_tensor("mx_in", [P, 2], F32, kind="Internal").ap()
    mx_out = nc.dram_tensor("mx_out", [P, 2], F32, kind="Internal").ap()

    idt = nc.inline_tensor(np.eye(P, dtype=ml_dtypes.bfloat16), name="identbf")
    tri = nc.inline_tensor(np.triu(np.ones((P, P), dtype=ml_dtypes.bfloat16)), name="triubf")
    onc = nc.inline_tensor(np.ones((P, 1), dtype=ml_dtypes.bfloat16), name="onescolbf")
    onr = nc.inline_tensor(np.ones((1, P), dtype=np.float32), name="onesrowf")

    sb = {}
    total_sbuf = nc.sbuf_bytes_remaining

    def galloc(name, shape, dt):
        sb[name] = nc.alloc_sbuf_tensor("sb_" + name, shape, dt)
        return sb[name]

    for nm, shp, dt in [
        ("dmy", [1, 1], F32),
        ("ident", [P, P], BF16), ("triu", [P, P], BF16),
        ("onescol", [P, 1], BF16), ("onesrow", [1, P], F32),
        ("epscol", [P, 1], F32), ("rccol", [P, 1], F32), ("c2ecol", [P, 1], F32),
        ("fmqT", [P, P], BF16), ("fmkT", [P, P], BF16),
        ("fmqb", [P, 1], F32), ("fmkb", [P, 1], F32),
        ("bqp", [P, 1], F32), ("bqn", [P, 1], F32), ("bkn", [P, 1], F32),
        ("rmsw", [P, 1], F32),
        ("mcols", [P, 6, 8], F32), ("ca1", [P, 8], F32), ("cb1", [P, 8], F32),
        ("smrow", [1, 8], F32), ("sm128", [P, 8], F32),
        ("m0slots", [P, 64], F32),
        ("m1slots", [P, NT * NI], F32), ("m2slots", [P, NT * NI], F32),
        ("m3slots", [P, NT * NI], F32), ("m4slots", [P, NT * NI], F32),
        ("mxst", [P, 2], F32), ("mxrow", [1, 2 * P], F32), ("mxfold", [1, 2], F32),
        ("r0c", [P, 1], F32), ("s0c", [P, 1], F32), ("s4c", [P, 1], F32),
        ("r1c", [P, 1], F32), ("r2c", [P, 1], F32), ("s2c", [P, 1], F32),
        ("brc2c", [P, 1], F32), ("c32c", [P, 1], F32), ("c4c", [P, 1], F32),
        ("gbs4", [P, 8], F32),
        ("ccols", [P, 8], F32),
        ("row_a", [1, TT], F32), ("row_b", [1, TT], F32),
        ("row_m", [1, TT], F32), ("row_r", [1, TT], F32),
        ("srow", [1, 16], F32),
    ]:
        galloc(nm, shp, dt)

    probe = nc.alloc_sbuf_tensor("sb_gbase_probe", [1, 64], F32)
    GBASE = ((nc.lookup_mloc(probe).addr + 256 + 63) // 64) * 64
    LIMIT = GBASE + nc.sbuf_bytes_remaining - 512

    class Arena:
        def __init__(self, base):
            self.off = base

        def place(self, name, shape, dt):
            esz = 2 if dt == BF16 else 4
            n = esz
            for s_ in shape[1:]:
                n *= s_
            off = ((self.off + 63) // 64) * 64
            sb[name] = nc.alloc_sbuf_tensor_at("sb_" + name, shape, dt, offset=off)
            self.off = off + n
            assert self.off <= LIMIT, f"SBUF overflow at {name}: {self.off}"
            return sb[name]

    a0 = Arena(GBASE)
    a0.place("wmod0", [P, D], F32)
    a0.place("wmod1", [P, D], F32)
    a0.place("crow_sb", [1, D], F32)
    a0.place("modrow", [1, 6 * D], F32)
    a0.place("clamp_a", [1, D], F32)
    a0.place("clamp_b", [1, D], F32)

    aa = Arena(GBASE)
    aa.place("dqn0", [P, DFF], F32)
    aa.place("dsc", [P, DFF // 32], F32)
    aa.place("drc", [P, DFF // 32], F32)
    aa.place("dt1", [P, DFF], F32)
    aa.place("dqb", [P, DFF], BF16)
    aa.place("dst", [P, NI, P], BF16)
    aa.place("wk_sb", [P, KC, D], BF16)
    aa.place("wv_sb", [P, KC, D], BF16)
    aa.place("xs0", [P, KC, TT], F32)
    aa.place("xs1", [P, KC, TT], F32)
    aa.place("xbf", [P, KC, TT], BF16)
    aa.place("sq0", [P, TT], BF16)
    aa.place("sq1", [P, TT], BF16)
    aa.place("tmp1", [P, TT], BF16)
    aa.place("tmp2", [P, TT], BF16)
    aa.place("rsb", [P, TT], BF16)
    aa.place("mrsb", [P, TT], BF16)
    aa.place("h_sb", [P, KC, TT], BF16)
    aa.place("kq0", [P, TT], BF16)
    aa.place("kq1", [P, TT], BF16)
    aa.place("kfm_sb", [P, 2 * H, TT], BF16)
    aa.place("ktm_sb", [P, NCH, H * DF], BF16)
    aa.place("v_sb", [P, NCH, D], BF16)
    aa.place("Ssb", [P, H * DF], F32)

    ab = Arena(GBASE)
    ab.place("Sfin", [P, H * DF], F32)
    ab.place("wq_sb", [P, KC, D], BF16)
    ab.place("wo_sb", [P, KC, D], BF16)
    ab.place("cxs0", [P, KC, TT], F32)
    ab.place("cxs1", [P, KC, TT], F32)
    ab.place("h_r", [P, KC, TT], BF16)
    ab.place("kfm_r", [P, 2 * H, TT], BF16)
    ab.place("ktm_r", [P, NCH, H * DF], BF16)
    ab.place("v_r", [P, NCH, D], BF16)
    ab.place("qfm_sb", [P, 2 * H, TT], BF16)
    ab.place("qq0", [P, TT], BF16)
    ab.place("qq1", [P, TT], BF16)
    OSB_OFF = ((ab.off + 63) // 64) * 64
    ab.place("o_sb", [P, KC, TT], BF16)
    sb["gS0"] = nc.alloc_sbuf_tensor_at("sb_gS0", [P, H * DF], F32, offset=OSB_OFF)
    ab.place("Ssnap", [P, 2, DF], BF16)
    ab.place("at_sb", [P, 2, P], BF16)
    ab.place("o2_sb", [P, P], BF16)
    ab.place("rmsc", [P, 1], F32)
    ab.place("x2_sb", [P, KC, TT], F32)
    ab.place("ep_sb", [P, TT], F32)
    ab.place("h2_sb", [P, KC, TT], BF16)
    ab.place("cxbf", [P, KC, TT], BF16)
    ab.place("csq0", [P, TT], BF16)
    ab.place("csq1", [P, TT], BF16)
    ab.place("ctmp1", [P, TT], BF16)
    ab.place("ctmp2", [P, TT], BF16)
    ab.place("crsb", [P, TT], BF16)
    ab.place("cmrsb", [P, TT], BF16)

    ac = Arena(GBASE)
    ac.place("wup_sb", [P, KC, DFF], BF16)
    ac.place("wg_sb", [P, KC, DFF], BF16)
    ac.place("h2r0", [P, KC, TT], BF16)
    ac.place("h2r1", [P, KC, TT], BF16)
    ac.place("xqt0", [P, TT], F32)
    ac.place("xqt1", [P, TT], F32)
    ac.place("xq_sb", [P, KC, TT], BF16)
    ac.place("Ust", [P, NI, TT], BF16)
    ac.place("Gst", [P, NI, TT], BF16)

    ad = Arena(GBASE)
    ad.place("bufA0", [P, NI, TT], BF16)
    ad.place("bufA1", [P, NI, TT], BF16)
    ad.place("t1f0", [P, TT], F32)
    ad.place("t1f1", [P, TT], F32)
    Z1 = ad.off
    ad.place("bufB0", [P, NI, TT], BF16)
    ad.place("bufB1", [P, NI, TT], BF16)
    ad.place("sgf0", [P, TT], F32)
    ad.place("sgf1", [P, TT], F32)
    Z2 = ad.off
    ae1 = Arena(Z2)
    ae1.place("srbf", [P, TT], BF16)
    ae1.place("g2st", [P, NI, TT], BF16)
    ae2 = Arena(Z2)
    ae2.place("prst", [P, NI, TT], BF16)
    ae2.place("g2bf", [P, TT], BF16)
    ae2.place("upbf", [P, TT], BF16)
    awd = Arena(max(ae1.off, ae2.off))
    awd.place("wd_sb", [P, NI, D], BF16)
    af = Arena(Z1)
    af.place("obf", [P, NI, TT], BF16)
    af.place("x2r0", [P, KC, TT], F32)
    af.place("x2r1", [P, KC, TT], F32)
    af.place("outs0", [P, KC, TT], F32)
    af.place("ep20", [P, TT], F32)
    af.place("ep21", [P, TT], F32)

    ps = {}
    for i in range(4):
        ps[f"S{i}"] = nc.alloc_psum_tensor(f"psS{i}", [P, 512], F32)
    ps["big0"] = nc.alloc_psum_tensor("big0", [P, TT], F32)
    ps["big1"] = nc.alloc_psum_tensor("big1", [P, TT], F32)
    ps["pxa"] = nc.alloc_psum_tensor("pxa", [P, 1024], BF16)
    ps["pxb"] = nc.alloc_psum_tensor("pxb", [P, 1024], BF16)
    assert nc.psum_banks_remaining == 0

    def S_ap(h_, dc):
        sl = h_ * 2 + dc
        return ps[f"S{sl // 4}"][:, (sl % 4) * P:(sl % 4 + 1) * P]

    ctx = ExitStack()
    sems = {}
    for s in ["T", "V", "S", "YN", "GN", "LC", "MA", "MB", "DA", "DB", "LW",
              "LF", "SPA", "SPB", "SG", "SD", "X0", "X1", "H0", "H1",
              "K0", "K1", "U0", "U1", "SC1", "SC2", "SC3", "CC", "ST"]:
        sems[s] = ctx.enter_context(nc.semaphore(s))
    p = Prog()
    p.sems = sems

    xT_v = xT.rearrange("(kc q) n -> q kc n", q=P)
    hbh_v = hb_h.rearrange("(kc q) n -> q kc n", q=P)
    kfm_v = hb_kfm.rearrange("(r q) n -> q r n", q=P)
    ktm_v = hb_ktm.rearrange("(g q) c -> q g c", q=P)
    v_v = hb_v.rearrange("(g q) c -> q g c", q=P)
    x2h_v = x2h.rearrange("(kc q) n -> q kc n", q=P)
    h2h_v = h2h.rearrange("(kc q) n -> q kc n", q=P)
    out_v = out.rearrange("(kc q) n -> q kc n", q=P)

    def program():
        T = {}

        def dma(eng, dst, src, sem, tag=None, deps=()):
            return p.op(eng, lambda e: e.dma_start(out=dst, in_=src),
                        deps=deps, inc=16, tag=tag, sem=sem)

        dma("sync", sb["ident"][:, :], idt.ap(), "LC")
        dma("sync", sb["triu"][:, :], tri.ap(), "LC")
        dma("sync", sb["onescol"][:, :], onc.ap(), "LC")
        dma("sync", sb["onesrow"][:, :], onr.ap(), "LC")
        dma("sync", sb["fmqb"][:, :], fmqb, "LC")
        dma("sync", sb["fmkb"][:, :], fmkb, "LC")
        dma("sync", sb["rmsw"][:, :], rmsw, "LC")
        dma("sync", sb["smrow"][:, :], smask, "LC")
        dma("sync", sb["crow_sb"][:, :], crow, "LC", tag="ld_consts")
        p.op("gpsimd", lambda e: e.dma_start(out=sb["fmqT"][:, :], in_=fmqT),
             inc=16, sem="LW")
        p.op("gpsimd", lambda e: e.dma_start(out=sb["fmkT"][:, :], in_=fmkT),
             inc=16, sem="LW", tag="ld_fm")

        p.op("vector", lambda e: e.memset(sb["dmy"][:, :], 0.0))
        p.op("vector", lambda e: e.memset(sb["epscol"][:, :], 1e-5))
        p.op("vector", lambda e: e.memset(sb["rccol"][:, :], RC), tag="rc_set")
        p.op("vector", lambda e: e.memset(sb["c2ecol"][:, :], 128.0 * 1e-5 * (2.0 ** -48)),
             tag="c2e_set")
        p.op("vector", lambda e: e.tensor_scalar(sb["bqp"][:, :], sb["fmqb"][:, :], -LN16, None, op0=ALU.add),
             deps=["ld_consts"])
        p.op("vector", lambda e: e.tensor_scalar(sb["bqn"][:, :], sb["fmqb"][:, :], -1.0, -LN16, op0=ALU.mult, op1=ALU.add))
        p.op("vector", lambda e: e.tensor_scalar(sb["bkn"][:, :], sb["fmkb"][:, :], -1.0, None, op0=ALU.mult),
             tag="bias_cols")

        p.op("scalar", lambda e: e.activation(sb["crow_sb"][:, :], sb["crow_sb"][:, :], AF.Silu),
             deps=["ld_consts"], tag="silu_row")
        def mk_ccol(e):
            r = None
            for kc in range(KC):
                r = e.matmul(ps["big0"][:, kc:kc + 1],
                             sb["crow_sb"][0:1, kc * P:(kc + 1) * P],
                             sb["onesrow"][0:1, 0:1],
                             start=(kc == 0), stop=(kc == KC - 1))
            return r
        p.op("tensor", mk_ccol, deps=["silu_row", "ld_consts"], tag="ccol_mm")
        p.op("scalar", lambda e: e.activation(sb["ccols"][:, :], ps["big0"][:, 0:8], AF.Identity),
             deps=["ccol_mm"], tag="ccol_ev")

        wmod_sb = [sb["wmod0"], sb["wmod1"]]
        for g in range(6):
            for kc in range(KC):
                slot = (g * KC + kc) % 2
                prev = f"modmm_g{g}_k{kc}_m2" if (g * KC + kc) >= 2 else None
                gp, kp = divmod(g * KC + kc - 2, KC)
                dd = [f"modmm_g{gp}_k{kp}"] if (g * KC + kc) >= 2 else []
                dma("sync", wmod_sb[slot][:, :],
                    wmod[kc * P:(kc + 1) * P, g * D:(g + 1) * D],
                    f"M{'AB'[slot]}", tag=f"modld_g{g}_k{kc}", deps=dd)
            for kc in range(KC):
                slot = (g * KC + kc) % 2
                def mk_mod(g=g, kc=kc, slot=slot):
                    def f(e):
                        e.matmul(ps["big0"][0:1, :],
                                 sb["ccols"][:, kc:kc + 1],
                                 wmod_sb[slot][:, 0:TT],
                                 start=(kc == 0), stop=(kc == KC - 1))
                        return e.matmul(ps["big1"][0:1, :],
                                        sb["ccols"][:, kc:kc + 1],
                                        wmod_sb[slot][:, TT:D],
                                        start=(kc == 0), stop=(kc == KC - 1))
                    return f
                dd = [f"modld_g{g}_k{kc}", "ccol_ev"]
                if g >= 1:
                    dd.append(f"modev_g{g-1}")
                p.op("tensor", mk_mod(), deps=dd, tag=f"modmm_g{g}_k{kc}")
            def mk_ev(g=g):
                def f(e):
                    e.activation(sb["modrow"][:, g * D:g * D + TT], ps["big0"][0:1, :], AF.Identity)
                    return e.activation(sb["modrow"][:, g * D + TT:(g + 1) * D], ps["big1"][0:1, :], AF.Identity)
                return f
            p.op("scalar", mk_ev(), deps=[f"modmm_g{g}_k{KC-1}"], tag=f"modev_g{g}")

        for i, s_ in enumerate([2, 5]):
            sl_ = sb["modrow"][:, s_ * D:(s_ + 1) * D]
            p.op("scalar", lambda e, sl_=sl_: e.activation(sb["clamp_a"][:, :], sl_, AF.Abs),
                 deps=[f"modev_g{s_}", (f"cl_done{i-1}" if i else None)], tag=f"cl_abs{i}")
            p.op("scalar", lambda e, sl_=sl_: e.activation(sb["clamp_b"][:, :], sl_, AF.Sign),
                 deps=[(f"cl_done{i-1}" if i else None)], tag=f"cl_sgn{i}")
            p.op("vector", lambda e: e.tensor_scalar_max(sb["clamp_a"][:, :], sb["clamp_a"][:, :], 0.25),
                 deps=[f"cl_abs{i}"], tag=f"cl_max{i}")
            p.op("vector", lambda e, sl_=sl_: e.tensor_mul(sl_, sb["clamp_a"][:, :], sb["clamp_b"][:, :]),
                 deps=[f"cl_sgn{i}", f"cl_max{i}"], tag=f"cl_done{i}")

        def mk_scat(e):
            r = None
            for s_ in range(6):
                for kc in range(KC):
                    r = e.matmul(ps["big0"][:, s_ * 8 + kc:s_ * 8 + kc + 1],
                                 sb["modrow"][0:1, s_ * D + kc * P:s_ * D + (kc + 1) * P],
                                 sb["onesrow"][0:1, 0:1],
                                 start=(s_ == 0 and kc == 0), stop=(s_ == 5 and kc == KC - 1))
            return r
        p.op("tensor", mk_scat,
             deps=["cl_done0", "cl_done1", "modev_g0", "modev_g1",
                   "modev_g3", "modev_g4", "modev_g5", "ld_consts"],
             tag="mod_scat")
        p.op("scalar", lambda e: e.activation(sb["mcols"][:, :, :], ps["big0"][:, 0:48], AF.Identity),
             deps=["mod_scat"], tag="mcols_ev")
        p.op("vector", lambda e: e.tensor_scalar(sb["ca1"][:, :], sb["mcols"][:, 1, :], 1.0, None, op0=ALU.add),
             deps=["mcols_ev"])
        p.op("vector", lambda e: e.tensor_scalar(sb["cb1"][:, :], sb["mcols"][:, 4, :], 1.0, None, op0=ALU.add),
             tag="modcols_done")
        p.op("tensor", lambda e: e.matmul(ps["big1"][:, 0:8], sb["onesrow"][:, :],
                                          sb["smrow"][:, :], start=True, stop=True),
             deps=["ld_consts", f"modev_g5"], tag="sm_mm")
        p.op("scalar", lambda e: e.activation(sb["sm128"][:, :], ps["big1"][:, 0:8], AF.Identity),
             deps=["sm_mm"], tag="sm128_done")

        p.barrier("p0", sb)

        def finish_early():
            for t_ in range(NT):
                dma("sync", out_v[:, :, t_ * TT:(t_ + 1) * TT],
                    sb["xs0"][:, :, :], "ST", tag=f"fout{t_}")
            p.op("sync", lambda e: e.wait_ge(p.sems["ST"], 16 * NT), inc=0)
            p.op("gpsimd", lambda e: e.wait_ge(p.sems["ST"], 16 * NT), inc=0)

        if PH < 1:
            finish_early()
            return
        PSL = {"big0": "mcols_ev", "big1": "sm128_done"}
        dqi_t1 = {}
        dqi_st = {}
        dqn = [sb["dqn0"], sb["dqn0"]]
        dqi = 0

        def emit_dequant(wi, nat, natq, R, C):
            nonlocal dqi
            CB = C // P
            CG = C // 32
            natq_v = natq.rearrange("(cb q) r -> q cb r", q=P)
            for rt in range(R // P):
                i = dqi
                dqi += 1
                slot = i % 2
                dma("sync", dqn[slot][:, 0:C], nat[rt * P:(rt + 1) * P, :],
                    f"D{'AB'[i % 2]}", tag=f"dq_ld_{i}", deps=[dqi_t1.get(i - 1)])
                w3 = dqn[slot][:, 0:C].rearrange("q (g e) -> q g e", e=32)
                p.op("vector", lambda e, w3=w3, CG=CG: e.tensor_reduce(
                        sb["dsc"][:, 0:CG].unsqueeze(2), w3, axis=AX.X, op=ALU.max,
                        apply_absolute_value=True),
                     deps=[f"dq_ld_{i}"], tag=f"dq_sc_{i}")
                p.op("vector", lambda e, CG=CG: e.tensor_scalar(
                        sb["dsc"][:, 0:CG], sb["dsc"][:, 0:CG], 1.0 / 7.0, 1e-12,
                        op0=ALU.mult, op1=ALU.max),
                     deps=[f"dq_sc_{i}"], tag=f"dq_s7_{i}")
                p.op("vector", lambda e, CG=CG: e.reciprocal(sb["drc"][:, 0:CG], sb["dsc"][:, 0:CG]),
                     deps=[f"dq_s7_{i}"], tag=f"dq_rc_{i}")
                rcb = sb["drc"][:, 0:CG].unsqueeze(2).broadcast_to([P, CG, 32])
                scb = sb["dsc"][:, 0:CG].unsqueeze(2).broadcast_to([P, CG, 32])
                p.op("vector", lambda e, w3=w3, rcb=rcb, C=C: e.tensor_tensor(
                        sb["dt1"][:, 0:C].rearrange("q (g e) -> q g e", e=32), w3, rcb, op=ALU.mult),
                     deps=[f"dq_rc_{i}"], tag=f"dq_t1_{i}")
                dqi_t1[i] = f"dq_t1_{i}"
                p.op("scalar", lambda e, C=C: e.activation(
                        sb["dt1"][:, 0:C], sb["dt1"][:, 0:C], AF.Identity, bias=sb["rccol"][:, :]),
                     deps=[f"dq_t1_{i}", "rc_set"], tag=f"dq_t2_{i}")
                p.op("vector", lambda e, scb=scb, C=C: e.scalar_tensor_tensor(
                        sb["dqb"][:, 0:C].rearrange("q (g e) -> q g e", e=32),
                        sb["dt1"][:, 0:C].rearrange("q (g e) -> q g e", e=32),
                        -RC, scb, op0=ALU.add, op1=ALU.mult),
                     deps=[f"dq_t2_{i}"], tag=f"dq_q_{i}")
                for cb in range(CB):
                    p.op("tensor", lambda e, cb=cb: e.transpose(
                            ps["pxb"][:, 0:P], sb["dqb"][:, cb * P:(cb + 1) * P], sb["ident"][:, :]),
                         deps=[f"dq_q_{i}", PSL.get("pxb"), "ld_consts"],
                         tag=f"dq_tp_{i}_{cb}")
                    p.op("vector", lambda e, cb=cb: e.tensor_copy(sb["dst"][:, cb, :], ps["pxb"][:, 0:P]),
                         deps=[f"dq_tp_{i}_{cb}", dqi_st.get(i - 1)],
                         tag=f"dq_ev_{i}_{cb}")
                    PSL["pxb"] = f"dq_ev_{i}_{cb}"
                dma("gpsimd", natq_v[:, 0:CB, rt * P:(rt + 1) * P],
                    sb["dst"][:, 0:CB, :],
                    "SD", tag=f"dq_st_{i}", deps=[f"dq_ev_{i}_{CB-1}"])
                dqi_st[i] = f"dq_st_{i}"

        emit_dequant(0, wup, wupT_q, DFF, D)
        emit_dequant(1, wgate, wgT_q, DFF, D)
        emit_dequant(2, wdown, wdT_q, D, DFF)
        T["dq_done"] = f"dq_st_{2 * DFF // P + D // P - 1}"

        if PH < 2:
            finish_early()
            return
        def emit_ln(pre, src_tag, src_f32, xbf, sqp, tmp1, tmp2, rsb, mrsb,
                    hout, cacol, sacol, m0base, cast_deps, mod_deps):
            for kc in range(KC):
                p.op("scalar", lambda e, kc=kc: e.activation(
                        xbf[:, kc, :], src_f32[:, kc, :], AF.Identity),
                     deps=[src_tag] + (cast_deps if kc == 0 else []),
                     tag=f"{pre}cast{kc}")
                p.op("vector", lambda e, kc=kc: e.tensor_mul(
                        sqp[kc % 2][:, :], xbf[:, kc, :], xbf[:, kc, :]),
                     deps=[f"{pre}cast{kc}", T.get(f"{pre}stats{kc-2}")],
                     tag=f"{pre}sq{kc}")
                def mk_stats(e, kc=kc):
                    e.matmul(ps["big0"][0:1, :], sb["onescol"][:, :], xbf[:, kc, :],
                             start=(kc == 0), stop=(kc == KC - 1))
                    return e.matmul(ps["big1"][0:1, :], sb["onescol"][:, :], sqp[kc % 2][:, :],
                                    start=(kc == 0), stop=(kc == KC - 1))
                dd = [f"{pre}sq{kc}", "ld_consts"]
                if kc == 0:
                    dd += [PSL.get("big0"), PSL.get("big1")]
                p.op("tensor", mk_stats, deps=dd, tag=f"{pre}stats{kc}")
                T[f"{pre}stats{kc}"] = f"{pre}stats{kc}"
            p.settag(f"{pre}stats", f"{pre}stats{KC-1}")
            def mk_fin1(e):
                e.activation(sb["row_a"][:, :], ps["big0"][0:1, :], AF.Identity, scale=1.0 / D)
                return e.activation(sb["row_b"][:, :], ps["big1"][0:1, :], AF.Identity, scale=1.0 / D)
            p.op("scalar", mk_fin1, deps=[f"{pre}stats"], tag=f"{pre}fin1")
            p.op("vector", lambda e: e.tensor_mul(sb["row_m"][:, :], sb["row_a"][:, :], sb["row_a"][:, :]),
                 deps=[f"{pre}fin1"], tag=f"{pre}msq")
            p.op("vector", lambda e: e.tensor_sub(sb["row_b"][:, :], sb["row_b"][:, :], sb["row_m"][:, :]),
                 deps=[f"{pre}msq"], tag=f"{pre}var")
            p.op("scalar", lambda e: e.activation(sb["row_r"][:, :], sb["row_b"][:, :], AF.Ln, bias=sb["epscol"][0:1, :]),
                 deps=[f"{pre}var"], tag=f"{pre}rstd0")
            p.op("scalar", lambda e: e.activation(sb["row_r"][:, :], sb["row_r"][:, :], AF.Exp, scale=-0.5),
                 deps=[f"{pre}rstd0"], tag=f"{pre}rstd")
            p.op("vector", lambda e: e.tensor_mul(sb["row_m"][:, :], sb["row_a"][:, :], sb["row_r"][:, :]),
                 deps=[f"{pre}rstd"], tag=f"{pre}mr")
            def mk_bc(e):
                e.matmul(ps["big0"][:, :], sb["onesrow"][:, :], sb["row_r"][:, :], start=True, stop=True)
                return e.matmul(ps["big1"][:, :], sb["onesrow"][:, :], sb["row_m"][:, :], start=True, stop=True)
            p.op("tensor", mk_bc, deps=[f"{pre}mr", f"{pre}fin1"], tag=f"{pre}bc")
            def mk_bce(e):
                e.activation(rsb[:, :], ps["big0"][:, :], AF.Identity)
                return e.activation(mrsb[:, :], ps["big1"][:, :], AF.Identity)
            p.op("scalar", mk_bce, deps=[f"{pre}bc"], tag=f"{pre}bce")
            PSL["big0"] = f"{pre}bce"
            PSL["big1"] = f"{pre}bce"
            for kc in range(KC):
                dd = [f"{pre}bce"]
                if kc > 0:
                    dd.append(f"{pre}smod{kc-1}")
                elif mod_deps:
                    dd += mod_deps
                p.op("vector", lambda e, kc=kc: e.tensor_mul(tmp1[:, :], xbf[:, kc, :], rsb[:, :]),
                     deps=dd, tag=f"{pre}t1_{kc}")
                p.op("vector", lambda e: e.tensor_sub(tmp2[:, :], tmp1[:, :], mrsb[:, :]),
                     deps=[f"{pre}t1_{kc}"], tag=f"{pre}t2_{kc}")
                p.op("scalar", lambda e, kc=kc: e.activation(
                        hout[:, kc, :], tmp2[:, :], AF.Identity,
                        scale=cacol[:, kc:kc + 1], bias=sacol[:, kc:kc + 1]),
                     deps=[f"{pre}t2_{kc}", "modcols_done"], tag=f"{pre}smod{kc}")
                if m0base is not None:
                    p.op("vector", lambda e, kc=kc: e.tensor_reduce(
                            sb["m0slots"][:, m0base + kc:m0base + kc + 1], hout[:, kc, :],
                            axis=AX.X, op=ALU.max, apply_absolute_value=True),
                         deps=[f"{pre}smod{kc}"], tag=f"{pre}m0_{kc}")
            return f"{pre}smod{KC-1}"

        p.op("gpsimd", lambda e: e.dma_start(
                out=sb["wk_sb"][:, :, :], in_=wkT.rearrange("(kc q) f -> q kc f", q=P)),
             inc=16, sem="LW")
        p.op("gpsimd", lambda e: e.dma_start(
                out=sb["wv_sb"][:, :, :], in_=wvT.rearrange("(kc q) f -> q kc f", q=P)),
             inc=16, sem="LW", tag="ld_wkv")
        xs = [sb["xs0"], sb["xs1"]]
        KQLAST = [None, None]
        for t in range(NT):
            sl = t % 2
            dma("sync", xs[sl][:, :, :], xT_v[:, :, t * TT:(t + 1) * TT],
                f"X{sl}", tag=f"axld{t}", deps=[T.get(f"ax_free{t-2}")])
            lnt = emit_ln(
                f"aln{t}", f"axld{t}", xs[sl], sb["xbf"], [sb["sq0"], sb["sq1"]],
                sb["tmp1"], sb["tmp2"],
                sb["rsb"], sb["mrsb"], sb["h_sb"], sb["ca1"], sb["mcols"][:, 0, :],
                None,
                cast_deps=[T.get(f"aln_xbf_free{t-1}")],
                mod_deps=[T.get(f"vspill{t-1}"), T.get(f"h_used{t-1}")])
            T[f"ax_free{t}"] = f"aln{t}cast{KC-1}"
            T[f"aln_xbf_free{t}"] = f"aln{t}t1_{KC-1}"
            dma("gpsimd", hbh_v[:, :, t * TT:(t + 1) * TT], sb["h_sb"][:, :, :],
                "SPA", tag=f"hspill{t}", deps=[lnt])
            T[f"hspill{t}"] = f"hspill{t}"
            for tb in range(NCH):
                for vh in range(2):
                    bs = f"big{vh}"
                    def mk_v(e, tb=tb, vh=vh, bs=bs):
                        r = None
                        for kc in range(KC):
                            r = e.matmul(ps[bs][:, :],
                                         sb["h_sb"][:, kc, tb * CHN:(tb + 1) * CHN],
                                         sb["wv_sb"][:, kc, vh * TT:(vh + 1) * TT],
                                         start=(kc == 0), stop=(kc == KC - 1))
                        return r
                    p.op("tensor", mk_v, deps=[lnt, "ld_wkv", PSL.get(bs)],
                         tag=f"vmm{t}_{tb}_{vh}")
                    p.op("scalar", lambda e, tb=tb, vh=vh, bs=bs: e.activation(
                            sb["v_sb"][:, tb, vh * TT:(vh + 1) * TT], ps[bs][:, :], AF.Identity),
                         deps=[f"vmm{t}_{tb}_{vh}", T.get(f"hspill{t}"), T.get(f"v_used{t-1}")],
                         tag=f"vev{t}_{tb}_{vh}")
                    PSL[bs] = f"vev{t}_{tb}_{vh}"
            for h_ in range(H):
                ks = h_ % 2
                def mk_k(e, h_=h_, ks=ks):
                    r = None
                    for kc in range(KC):
                        r = e.matmul(ps[f"big{ks}"][:, :],
                                     sb["wk_sb"][:, kc, h_ * P:(h_ + 1) * P],
                                     sb["h_sb"][:, kc, :],
                                     start=(kc == 0), stop=(kc == KC - 1))
                    return r
                p.op("tensor", mk_k, deps=[lnt, "ld_wkv", PSL.get(f"big{ks}")],
                     tag=f"kmm{t}_{h_}")
                p.op("scalar", lambda e, ks=ks, bs=f"big{ks}": e.activation(
                        sb[f"kq{ks}"][:, :], ps[bs][:, :], AF.Identity),
                     deps=[f"kmm{t}_{h_}", KQLAST[ks]],
                     tag=f"kev{t}_{h_}")
                p.op("tensor", lambda e, ks=ks: e.matmul(
                        ps[f"big{ks}"][:, :], sb["fmkT"][:, :], sb[f"kq{ks}"][:, :],
                        start=True, stop=True),
                     deps=[f"kev{t}_{h_}", "ld_wkv"], tag=f"kymm{t}_{h_}")
                KQLAST[ks] = f"kymm{t}_{h_}"
                def mk_kexp(e, h_=h_, ks=ks):
                    e.activation(sb["kfm_sb"][:, 2 * h_, :], ps[f"big{ks}"][:, :], AF.Exp,
                                 bias=sb["fmkb"][:, :])
                    return e.activation(sb["kfm_sb"][:, 2 * h_ + 1, :], ps[f"big{ks}"][:, :], AF.Exp,
                                        bias=sb["bkn"][:, :], scale=-1.0)
                p.op("scalar", mk_kexp,
                     deps=[f"kymm{t}_{h_}", "bias_cols", T.get(f"hspill{t}"),
                           T.get(f"kfm_used{t-1}")],
                     tag=f"kexp{t}_{h_}")
                PSL[f"big{ks}"] = f"kexp{t}_{h_}"
                for dc in range(2):
                    for tb in range(NCH):
                        p.op("tensor", lambda e, h_=h_, dc=dc, tb=tb: e.transpose(
                                ps["pxa"][:, 0:P],
                                sb["kfm_sb"][:, 2 * h_ + dc, tb * CHN:(tb + 1) * CHN],
                                sb["ident"][:, :]),
                             deps=[f"kexp{t}_{h_}", T.get("pxa_ev"), "ld_consts"],
                             tag=f"ktp{t}_{h_}_{dc}_{tb}")
                        p.op("vector", lambda e, h_=h_, dc=dc, tb=tb: e.tensor_copy(
                                sb["ktm_sb"][:, tb, h_ * DF + dc * P:h_ * DF + (dc + 1) * P],
                                ps["pxa"][:, 0:P]),
                             deps=[f"ktp{t}_{h_}_{dc}_{tb}", T.get(f"hspill{t}"),
                                   T.get(f"ktm_used{t-1}")],
                             tag=f"ktev{t}_{h_}_{dc}_{tb}")
                        T["pxa_ev"] = f"ktev{t}_{h_}_{dc}_{tb}"
                def mk_sacc(e, t=t, h_=h_):
                    r = None
                    for ch in range(NCH):
                        for dc in range(2):
                            st_ = (t == 0 and ch == 0 and dc == 0 and h_ % 2 == 0)
                            r = e.matmul(S_ap(h_, dc),
                                         sb["ktm_sb"][:, ch, h_ * DF + dc * P:h_ * DF + (dc + 1) * P],
                                         sb["v_sb"][:, ch, h_ * P:(h_ + 1) * P],
                                         start=st_, stop=False, skip_group_check=True)
                    return r
                p.op("tensor", mk_sacc,
                     deps=[f"ktev{t}_{h_}_1_{NCH-1}", f"vev{t}_{NCH-1}_1"],
                     tag=f"sacc{t}_{h_}")
            T[f"h_used{t}"] = f"kmm{t}_{H-1}"
            T[f"v_used{t}"] = f"sacc{t}_{H-1}"
            T[f"ktm_used{t}"] = f"sacc{t}_{H-1}"
            T[f"kfm_used{t}"] = f"ktp{t}_{H-1}_1_{NCH-1}"
            dma("gpsimd", kfm_v[:, :, t * TT:(t + 1) * TT], sb["kfm_sb"][:, :, :],
                "SPA", tag=f"kfmspill{t}",
                deps=[f"kexp{t}_{H-1}", f"ktp{t}_{H-1}_1_{NCH-1}"])
            T[f"kfmspill{t}"] = f"kfmspill{t}"
            dma("gpsimd", ktm_v[:, t * NCH:(t + 1) * NCH, :], sb["ktm_sb"][:, :, :],
                "SPA", tag=f"ktmspill{t}", deps=[f"sacc{t}_{H-1}"])
            T[f"ktmspill{t}"] = f"ktmspill{t}"
            dma("gpsimd", v_v[:, t * NCH:(t + 1) * NCH, :], sb["v_sb"][:, :, :],
                "SPA", tag=f"vspill{t}", deps=[f"sacc{t}_{H-1}"])
            T[f"vspill{t}"] = f"vspill{t}"

        def mk_sev(e):
            r = None
            for h_ in range(H):
                for dc in range(2):
                    r = e.activation(sb["Ssb"][:, (h_ * 2 + dc) * P:(h_ * 2 + dc + 1) * P],
                                     S_ap(h_, dc), AF.Identity)
            return r
        p.op("scalar", mk_sev, deps=[f"sacc{NT-1}_{H-1}"], tag="s_ev")
        dma("gpsimd", sc_in, sb["Ssb"][:, :], "SPA", tag="s_stage", deps=["s_ev"])
        p.op("gpsimd", lambda e: e.collective_compute(
                "AllGather", ALU.bypass, replica_groups=[list(range(8))],
                ins=[sc_in.opt()], outs=[sc_out.opt()]),
             deps=["s_stage"], sem="CC", tag="cc_sg")

        p.barrier("p2a", sb)

        if PH < 3:
            finish_early()
            return
        gS = sb["gS0"]
        SFIN = sb["Sfin"]
        for r in range(8):
            dma("gpsimd", gS[:, :], sc_out[r * P:(r + 1) * P, :],
                "SG", tag=f"gsld{r}", deps=["cc_sg", T.get(f"sin{r-1}")])
            if r == 0:
                p.op("vector", lambda e: e.tensor_scalar(
                        SFIN[:, :], gS[:, :], sb["sm128"][:, 0:1], None, op0=ALU.mult),
                     deps=[f"gsld{r}", "sm128_done"], tag=f"sin{r}")
            else:
                p.op("vector", lambda e, r=r: e.scalar_tensor_tensor(
                        SFIN[:, :], gS[:, :], sb["sm128"][:, r:r + 1], SFIN[:, :],
                        op0=ALU.mult, op1=ALU.add),
                     deps=[f"gsld{r}", f"sin{r-1}"], tag=f"sin{r}")
            T[f"sin{r}"] = f"sin{r}"

        p.op("gpsimd", lambda e: e.dma_start(
                out=sb["wq_sb"][:, :, :], in_=wqT.rearrange("(kc q) f -> q kc f", q=P)),
             inc=16, sem="LW")
        p.op("gpsimd", lambda e: e.dma_start(
                out=sb["wo_sb"][:, :, :], in_=woT.rearrange("(kc q) f -> q kc f", q=P)),
             inc=16, sem="LW", tag="ld_wqo")
        def mk_worms(e):
            r = None
            for kc in range(KC):
                r = e.tensor_scalar(sb["wo_sb"][:, kc, :], sb["wo_sb"][:, kc, :],
                                    sb["rmsw"][:, :], None, op0=ALU.mult)
            return r
        p.op("vector", mk_worms, deps=["ld_wqo", "ld_consts"], tag="wo_rms")

        cxs = [sb["cxs0"], sb["cxs1"]]
        QQLAST = [None, None]
        psA_at = ps["big0"][:, 0:P]
        psA_o = ps["big0"][:, P:2 * P]
        for t in range(NT):
            sl = t % 2
            dma("sync", cxs[sl][:, :, :], xT_v[:, :, t * TT:(t + 1) * TT],
                f"X{sl}", tag=f"cxld{t}", deps=[T.get(f"cx_free{t-2}")])
            dma("sync", sb["h_r"][:, :, :], hbh_v[:, :, t * TT:(t + 1) * TT],
                f"H{sl}", tag=f"chld{t}",
                deps=["s_stage", T.get(f"hr_used{t-1}")])
            dma("gpsimd", sb["kfm_r"][:, :, :], kfm_v[:, :, t * TT:(t + 1) * TT],
                f"K{sl}", tag=f"ckfld{t}",
                deps=["s_stage", T.get(f"kfmr_used{t-1}")])
            dma("gpsimd", sb["ktm_r"][:, :, :], ktm_v[:, t * NCH:(t + 1) * NCH, :],
                f"K{sl}", tag=f"cktld{t}",
                deps=[T.get(f"ktmr_used{t-1}")])
            dma("gpsimd", sb["v_r"][:, :, :], v_v[:, t * NCH:(t + 1) * NCH, :],
                f"K{sl}", tag=f"cvld{t}",
                deps=[T.get(f"vr_used{t-1}")])
            for h_ in range(H):
                ks = h_ % 2
                def mk_q(e, h_=h_, ks=ks):
                    r = None
                    for kc in range(KC):
                        r = e.matmul(ps[f"big{ks}"][:, :],
                                     sb["wq_sb"][:, kc, h_ * P:(h_ + 1) * P],
                                     sb["h_r"][:, kc, :],
                                     start=(kc == 0), stop=(kc == KC - 1))
                    return r
                p.op("tensor", mk_q, deps=[f"chld{t}", "ld_wqo", PSL.get(f"big{ks}")],
                     tag=f"qmm{t}_{h_}")
                p.op("scalar", lambda e, ks=ks, bs=f"big{ks}": e.activation(
                        sb[f"qq{ks}"][:, :], ps[bs][:, :], AF.Identity),
                     deps=[f"qmm{t}_{h_}", QQLAST[ks]],
                     tag=f"qev{t}_{h_}")
                p.op("tensor", lambda e, ks=ks: e.matmul(
                        ps[f"big{ks}"][:, :], sb["fmqT"][:, :], sb[f"qq{ks}"][:, :],
                        start=True, stop=True),
                     deps=[f"qev{t}_{h_}", "ld_wqo"], tag=f"qymm{t}_{h_}")
                QQLAST[ks] = f"qymm{t}_{h_}"
                def mk_qexp(e, h_=h_, ks=ks):
                    e.activation(sb["qfm_sb"][:, 2 * h_, :], ps[f"big{ks}"][:, :], AF.Exp,
                                 bias=sb["bqp"][:, :])
                    return e.activation(sb["qfm_sb"][:, 2 * h_ + 1, :], ps[f"big{ks}"][:, :], AF.Exp,
                                        bias=sb["bqn"][:, :], scale=-1.0)
                p.op("scalar", mk_qexp,
                     deps=[f"qymm{t}_{h_}", "bias_cols", T.get(f"qfm_used{t-1}")],
                     tag=f"qexp{t}_{h_}")
                PSL[f"big{ks}"] = f"qexp{t}_{h_}"
            T[f"hr_used{t}"] = f"qmm{t}_{H-1}"
            for ch in range(NCH):
                for h_ in range(H):
                    hs = h_ % 2
                    cds = f"{t}_{ch}_{h_}"
                    if t == 0 and ch == 0:
                        def mk_snap(e, h_=h_, hs=hs):
                            r = None
                            for dc in range(2):
                                r = e.tensor_copy(sb["Ssnap"][:, hs, dc * P:(dc + 1) * P],
                                                  SFIN[:, (h_ * 2 + dc) * P:(h_ * 2 + dc + 1) * P])
                            return r
                        p.op("vector", mk_snap,
                             deps=["sin7", T.get(f"snapfree{hs}")],
                             tag=f"snap{cds}")
                    else:
                        def mk_snap(e, h_=h_, hs=hs):
                            r = None
                            for dc in range(2):
                                r = e.tensor_add(sb["Ssnap"][:, hs, dc * P:(dc + 1) * P],
                                                 S_ap(h_, dc),
                                                 SFIN[:, (h_ * 2 + dc) * P:(h_ * 2 + dc + 1) * P])
                            return r
                        p.op("vector", mk_snap,
                             deps=["sin7", T.get(f"supd_bank{h_ // 2}"),
                                   T.get(f"snapfree{hs}")],
                             tag=f"snap{cds}")
                    def mk_at(e, t=t, ch=ch, h_=h_):
                        r = None
                        for dc in range(2):
                            r = e.matmul(psA_at,
                                         sb["kfm_r"][:, 2 * h_ + dc, ch * CHN:(ch + 1) * CHN],
                                         sb["qfm_sb"][:, 2 * h_ + dc, ch * CHN:(ch + 1) * CHN],
                                         start=(dc == 0), stop=(dc == 1), skip_group_check=True)
                        return r
                    p.op("tensor", mk_at,
                         deps=[f"qexp{t}_{h_}", f"cvld{t}", T.get("psA_o_ev"),
                               T.get("psA_at_ev"), PSL.get("big0")],
                         tag=f"at{cds}")
                    PSL["big0"] = f"at{cds}"
                    p.op("vector", lambda e, hs=hs: e.tensor_tensor(
                            sb["at_sb"][:, hs, :], psA_at, sb["triu"][:, :], op=ALU.mult),
                         deps=[f"at{cds}", T.get(f"atfree{hs}")],
                         tag=f"atev{cds}")
                    T["psA_at_ev"] = f"atev{cds}"
                    def mk_o(e, t=t, ch=ch, h_=h_, hs=hs):
                        e.matmul(psA_o, sb["at_sb"][:, hs, :],
                                 sb["v_r"][:, ch, h_ * P:(h_ + 1) * P],
                                 start=False, stop=False, skip_group_check=True)
                        r = None
                        for dc in range(2):
                            r = e.matmul(psA_o,
                                         sb["qfm_sb"][:, 2 * h_ + dc, ch * CHN:(ch + 1) * CHN],
                                         sb["Ssnap"][:, hs, dc * P:(dc + 1) * P],
                                         start=False, stop=(dc == 1), skip_group_check=True)
                        return r
                    p.op("tensor", mk_o,
                         deps=[f"atev{cds}", f"snap{cds}", f"cvld{t}", T.get("psA_o_ev")],
                         tag=f"omm{cds}")
                    T[f"atfree{hs}"] = f"omm{cds}"
                    T[f"snapfree{hs}"] = f"omm{cds}"
                    def mk_su(e, t=t, ch=ch, h_=h_):
                        r = None
                        for dc in range(2):
                            st_ = (t == 0 and ch == 0 and dc == 0 and h_ % 2 == 0)
                            r = e.matmul(S_ap(h_, dc),
                                         sb["ktm_r"][:, ch, h_ * DF + dc * P:h_ * DF + (dc + 1) * P],
                                         sb["v_r"][:, ch, h_ * P:(h_ + 1) * P],
                                         start=st_, stop=False, skip_group_check=True)
                        return r
                    p.op("tensor", mk_su,
                         deps=[f"snap{cds}", f"cvld{t}"],
                         tag=f"supd{cds}")
                    T[f"supd_bank{h_ // 2}"] = f"supd{cds}"
                    p.op("scalar", lambda e: e.activation(sb["o2_sb"][:, :], psA_o, AF.Square,
                                                          scale=2.0 ** -24),
                         deps=[f"omm{cds}", T.get("o2_used")], tag=f"osq{cds}")
                    p.op("vector", lambda e: e.tensor_reduce(
                            sb["rmsc"][:, :], sb["o2_sb"][:, :], axis=AX.X, op=ALU.add),
                         deps=[f"osq{cds}"], tag=f"ossq{cds}")
                    p.op("scalar", lambda e: e.activation(sb["rmsc"][:, :], sb["rmsc"][:, :], AF.Sqrt,
                                                          bias=sb["c2ecol"][:, :]),
                         deps=[f"ossq{cds}", "c2e_set"], tag=f"orms0{cds}")
                    p.op("scalar", lambda e: e.activation(sb["rmsc"][:, :], sb["rmsc"][:, :], AF.Sqrt),
                         deps=[f"orms0{cds}"], tag=f"orms1{cds}")
                    p.op("vector", lambda e: e.reciprocal(sb["rmsc"][:, :], sb["rmsc"][:, :]),
                         deps=[f"orms1{cds}"], tag=f"orms2{cds}")
                    p.op("scalar", lambda e: e.activation(sb["rmsc"][:, :], sb["rmsc"][:, :], AF.Square,
                                                          scale=(128.0 ** 0.25) * (2.0 ** -12)),
                         deps=[f"orms2{cds}"], tag=f"orms{cds}")
                    p.op("vector", lambda e: e.tensor_scalar(
                            sb["o2_sb"][:, :], psA_o, sb["rmsc"][:, :], None, op0=ALU.mult),
                         deps=[f"orms{cds}"], tag=f"onorm{cds}")
                    T["psA_o_ev"] = f"onorm{cds}"
                    p.op("tensor", lambda e: e.transpose(
                            ps["pxa"][:, 0:P], sb["o2_sb"][:, :], sb["ident"][:, :]),
                         deps=[f"onorm{cds}", T.get("pxa_ev")],
                         tag=f"otp{cds}")
                    T["o2_used"] = f"otp{cds}"
                    p.op("vector", lambda e, h_=h_, ch=ch: e.tensor_copy(
                            sb["o_sb"][:, h_, ch * CHN:(ch + 1) * CHN], ps["pxa"][:, 0:P]),
                         deps=[f"otp{cds}", T.get(f"o_used{t-1}")],
                         tag=f"oev{cds}")
                    T["pxa_ev"] = f"oev{cds}"
            T[f"kfmr_used{t}"] = f"at{t}_{NCH-1}_{H-1}"
            T[f"ktmr_used{t}"] = f"supd{t}_{NCH-1}_{H-1}"
            T[f"vr_used{t}"] = f"supd{t}_{NCH-1}_{H-1}"
            T[f"qfm_used{t}"] = f"omm{t}_{NCH-1}_{H-1}"
            for fo in range(KC):
                bs = f"big{fo % 2}"
                def mk_wo(e, fo=fo, bs=bs):
                    r = None
                    for kc in range(KC):
                        r = e.matmul(ps[bs][:, :],
                                     sb["wo_sb"][:, kc, fo * P:(fo + 1) * P],
                                     sb["o_sb"][:, kc, :],
                                     start=(kc == 0), stop=(kc == KC - 1))
                    return r
                p.op("tensor", mk_wo,
                     deps=[f"oev{t}_{NCH-1}_{H-1}", "wo_rms", PSL.get(bs)],
                     tag=f"wo{t}_{fo}")
                p.op("scalar", lambda e, fo=fo, bs=bs: e.activation(
                        sb["ep_sb"][:, :], ps[bs][:, :], AF.Identity,
                        scale=sb["mcols"][:, 2, fo:fo + 1]),
                     deps=[f"wo{t}_{fo}", "modcols_done", T.get("ep_used")],
                     tag=f"ep1{t}_{fo}")
                PSL[bs] = f"ep1{t}_{fo}"
                p.op("vector", lambda e, fo=fo, sl=sl: e.tensor_add(
                        sb["x2_sb"][:, fo, :], cxs[sl][:, fo, :], sb["ep_sb"][:, :]),
                     deps=[f"ep1{t}_{fo}", f"cxld{t}", T.get(f"x2_free{t-1}")],
                     tag=f"x2{t}_{fo}")
                T["ep_used"] = f"x2{t}_{fo}"
            T[f"o_used{t}"] = f"wo{t}_{KC-1}"
            T[f"cx_free{t}"] = f"x2{t}_{KC-1}"
            dma("gpsimd", x2h_v[:, :, t * TT:(t + 1) * TT], sb["x2_sb"][:, :, :],
                "SPB", tag=f"x2sp{t}", deps=[f"x2{t}_{KC-1}"])
            ln2t = emit_ln(
                f"cln{t}", f"x2{t}_{KC-1}", sb["x2_sb"], sb["cxbf"],
                [sb["csq0"], sb["csq1"]], sb["ctmp1"],
                sb["ctmp2"], sb["crsb"], sb["cmrsb"], sb["h2_sb"], sb["cb1"],
                sb["mcols"][:, 3, :], t * KC,
                cast_deps=[T.get(f"cln_xbf_free{t-1}")],
                mod_deps=[f"x2sp{t}"])
            T[f"x2_free{t}"] = f"cln{t}cast{KC-1}"
            T[f"cln_xbf_free{t}"] = f"cln{t}t1_{KC-1}"
            dma("gpsimd", h2h_v[:, :, t * TT:(t + 1) * TT], sb["h2_sb"][:, :, :],
                "SPB", tag=f"h2sp{t}", deps=[ln2t])
            T[f"h2sp{t}"] = f"h2sp{t}"

        p.barrier("p2c", sb)

        MB = {"prev_stage": None, "prev_ld": None}

        def mbarrier(name, folds, dep):
            for j, (sn, lo, hi) in enumerate(folds):
                p.op("vector", lambda e, sn=sn, lo=lo, hi=hi, j=j: e.tensor_reduce(
                        sb["mxst"][:, j:j + 1], sb[sn][:, lo:hi], axis=AX.X, op=ALU.max),
                     deps=[dep, MB["prev_stage"]], tag=f"{name}_f{j}")
            if len(folds) == 1:
                p.op("vector", lambda e: e.memset(sb["mxst"][:, 1:2], 0.0),
                     deps=[MB["prev_stage"]], tag=f"{name}_f1")
            dma("gpsimd", mx_in, sb["mxst"][:, :], "SG", tag=f"{name}_stage",
                deps=[f"{name}_f0", f"{name}_f1"])
            MB["prev_stage"] = f"{name}_stage"
            p.op("gpsimd", lambda e: e.collective_compute(
                    "AllReduce", ALU.max, replica_groups=[list(range(8))],
                    ins=[mx_in.opt()], outs=[mx_out.opt()]),
                 deps=[f"{name}_stage"], sem="CC", tag=f"{name}_cc")
            dma("gpsimd", sb["mxrow"][:, :], mx_out, "SG", tag=f"{name}_ld",
                deps=[f"{name}_cc", MB["prev_ld"]])
            p.op("vector", lambda e: e.tensor_reduce(
                    sb["mxfold"][:, :].unsqueeze(2),
                    sb["mxrow"][:, :].rearrange("a (q c) -> a c q", c=2),
                    axis=AX.X, op=ALU.max),
                 deps=[f"{name}_ld"], tag=f"{name}_max")
            MB["prev_ld"] = f"{name}_max"

        def bcast_col(name, src_ap, dep, tg):
            p.op("tensor", lambda e: e.matmul(ps["big0"][:, 0:1], sb["onesrow"][:, :],
                                              src_ap, start=True, stop=True),
                 deps=[dep, PSL.get("big0"), "ld_consts"], tag=f"bcm_{tg}")
            p.op("scalar", lambda e: e.activation(sb[name][:, :], ps["big0"][:, 0:1], AF.Identity),
                 deps=[f"bcm_{tg}"], tag=f"bce_{tg}")
            PSL["big0"] = f"bce_{tg}"
            return f"bce_{tg}"

        if PH < 4:
            finish_early()
            return
        mbarrier("m0", [("m0slots", 0, 64)], f"cln{NT-1}m0_{KC-1}")
        p.op("vector", lambda e: e.tensor_scalar(sb["srow"][:, 0:1], sb["mxfold"][:, 0:1],
                                                 1.0 / QMAX, None, op0=ALU.mult),
             deps=["m0_max"], tag="s0row")
        p.op("vector", lambda e: e.reciprocal(sb["srow"][:, 1:2], sb["srow"][:, 0:1]),
             deps=["s0row"], tag="r0row")
        bcast_col("s0c", sb["srow"][0:1, 0:1], "s0row", "s0c")
        bcast_col("r0c", sb["srow"][0:1, 1:2], "r0row", "r0c")

        if PH < 5:
            finish_early()
            return
        dma("sync", sb["wup_sb"][:, :, :], wupT_q.rearrange("(kc q) f -> q kc f", q=P),
            "LF", tag="ld_wup", deps=[T["dq_done"]])
        dma("sync", sb["wg_sb"][:, :, :], wgT_q.rearrange("(kc q) f -> q kc f", q=P),
            "LF", tag="ld_wg", deps=[T["dq_done"]])
        h2r = [sb["h2r0"], sb["h2r1"]]
        xqt = [sb["xqt0"], sb["xqt1"]]
        for t in range(NT):
            sl = t % 2
            dma("sync", h2r[sl][:, :, :], h2h_v[:, :, t * TT:(t + 1) * TT],
                f"H{sl}", tag=f"bh2ld{t}",
                deps=[f"h2sp{NT-1}", T.get(f"bh2_free{t-2}")])
            for kc in range(KC):
                ts_ = kc % 2
                p.op("scalar", lambda e, sl=sl, kc=kc, ts_=ts_: e.activation(
                        xqt[ts_][:, :], h2r[sl][:, kc, :], AF.Identity,
                        scale=sb["r0c"][:, :], bias=sb["rccol"][:, :]),
                     deps=[f"bh2ld{t}", "bce_r0c",
                           (f"bxq{t}_{kc-2}" if kc >= 2 else
                            (f"bxq{t-1}_{KC-2+kc}" if t >= 1 else None))],
                     tag=f"bxqt{t}_{kc}")
                p.op("vector", lambda e, kc=kc, ts_=ts_: e.tensor_scalar(
                        sb["xq_sb"][:, kc, :], xqt[ts_][:, :], -RC, sb["s0c"][:, :],
                        op0=ALU.add, op1=ALU.mult),
                     deps=[f"bxqt{t}_{kc}", "bce_s0c", T.get(f"xq_used{t-1}")],
                     tag=f"bxq{t}_{kc}")
            T[f"bh2_free{t}"] = f"bxqt{t}_{KC-1}"
            for which, (wname, stname, slots, ldt, dsth) in enumerate(
                    [("wup_sb", "Ust", "m1slots", "ld_wup", Uh),
                     ("wg_sb", "Gst", "m2slots", "ld_wg", Gh)]):
                for ib in range(NI):
                    bs = f"big{ib % 2}"
                    def mk_u(e, wname=wname, ib=ib, bs=bs):
                        r = None
                        for kc in range(KC):
                            r = e.matmul(ps[bs][:, :], sb[wname][:, kc, ib * P:(ib + 1) * P],
                                         sb["xq_sb"][:, kc, :],
                                         start=(kc == 0), stop=(kc == KC - 1))
                        return r
                    p.op("tensor", mk_u,
                         deps=[f"bxq{t}_{KC-1}", "ld_wg", PSL.get(bs),
                               PSL.get(bs + "_mx")],
                         tag=f"bmm{t}_{which}_{ib}")
                    p.op("scalar", lambda e, stname=stname, ib=ib, bs=bs: e.activation(
                            sb[stname][:, ib, :], ps[bs][:, :], AF.Identity),
                         deps=[f"bmm{t}_{which}_{ib}", T.get(f"bst{t-1}_1")],
                         tag=f"bev{t}_{which}_{ib}")
                    p.op("vector", lambda e, slots=slots, stname=stname, ib=ib, t=t: e.tensor_reduce(
                            sb[slots][:, t * NI + ib:t * NI + ib + 1],
                            sb[stname][:, ib, :], axis=AX.X, op=ALU.max,
                            apply_absolute_value=True),
                         deps=[f"bev{t}_{which}_{ib}"],
                         tag=f"bmx{t}_{which}_{ib}")
                    PSL[bs] = f"bev{t}_{which}_{ib}"
                    PSL[bs + "_mx"] = f"bmx{t}_{which}_{ib}"
                dma("gpsimd", dsth[:, t * NI * TT:(t + 1) * NI * TT],
                    sb[stname][:, :, :], "SC1", tag=f"bst{t}_{which}",
                    deps=[f"bev{t}_{which}_{NI-1}", f"bmx{t}_{which}_{NI-1}"])
                T[f"bst{t}_{which}"] = f"bst{t}_{which}"
            T[f"xq_used{t}"] = f"bmm{t}_1_{NI-1}"

        if PH < 6:
            finish_early()
            return
        mbarrier("m12", [("m1slots", 0, NT * NI), ("m2slots", 0, NT * NI)],
                 f"bmx{NT-1}_1_{NI-1}")
        p.op("vector", lambda e: e.tensor_scalar(sb["srow"][:, 2:4], sb["mxfold"][:, 0:2],
                                                 1.0 / QMAX, None, op0=ALU.mult),
             deps=["m12_max"], tag="s12row")
        p.op("vector", lambda e: e.reciprocal(sb["srow"][:, 4:6], sb["srow"][:, 2:4]),
             deps=["s12row"], tag="r12row")
        bcast_col("r1c", sb["srow"][0:1, 4:5], "r12row", "r1c")
        bcast_col("r2c", sb["srow"][0:1, 5:6], "r12row", "r2c")
        bcast_col("s2c", sb["srow"][0:1, 3:4], "s12row", "s2c")
        p.op("vector", lambda e: e.tensor_scalar(sb["brc2c"][:, :], sb["s2c"][:, :],
                                                 -RC, None, op0=ALU.mult),
             deps=["bce_s2c"], tag="brc2_done")

        dma("sync", sb["wd_sb"][:, :, :], wdT_q.rearrange("(ic q) f -> q ic f", q=P),
            "LF", tag="ld_wd", deps=[T["dq_done"], f"bst{NT-1}_1"])
        gld = [sb["bufA0"], sb["bufA1"]]
        t1f = [sb["t1f0"], sb["t1f1"]]
        sgf = [sb["sgf0"], sb["sgf1"]]
        for t in range(NT):
            sl = t % 2
            dma("gpsimd", gld[sl][:, :, :], Gh[:, t * NI * TT:(t + 1) * NI * TT],
                f"U{sl}", tag=f"c1ld{t}",
                deps=[f"bst{NT-1}_1", T.get(f"c1_free{t-2}")])
            for ib in range(NI):
                ts_ = ib % 2
                pg = (f"c1gp{t}_{ib-2}" if ib >= 2 else
                      (f"c1gp{t-1}_{NI-2+ib}" if t >= 1 else None))
                p.op("scalar", lambda e, sl=sl, ib=ib, ts_=ts_: e.activation(
                        t1f[ts_][:, :], gld[sl][:, ib, :], AF.Identity,
                        scale=sb["r2c"][:, :], bias=sb["rccol"][:, :]),
                     deps=[f"c1ld{t}", "bce_r2c", pg],
                     tag=f"c1t1{t}_{ib}")
                p.op("scalar", lambda e, ts_=ts_: e.activation(
                        sgf[ts_][:, :], t1f[ts_][:, :], AF.Sigmoid,
                        scale=sb["s2c"][:, :], bias=sb["brc2c"][:, :]),
                     deps=["brc2_done", f"c1t1{t}_{ib}"], tag=f"c1sg{t}_{ib}")
                p.op("vector", lambda e, ts_=ts_: e.tensor_scalar(
                        sgf[ts_][:, :], sgf[ts_][:, :], 65536.0, RC,
                        op0=ALU.mult, op1=ALU.add),
                     deps=[f"c1sg{t}_{ib}"], tag=f"c1t3{t}_{ib}")
                p.op("vector", lambda e, ts_=ts_: e.tensor_scalar(
                        sb["srbf"][:, :], sgf[ts_][:, :], -RC, None, op0=ALU.add),
                     deps=[f"c1t3{t}_{ib}"], tag=f"c1sr{t}_{ib}")
                p.op("vector", lambda e, ib=ib, ts_=ts_: e.scalar_tensor_tensor(
                        sb["g2st"][:, ib, :], t1f[ts_][:, :], -RC, sb["srbf"][:, :],
                        op0=ALU.add, op1=ALU.mult),
                     deps=[f"c1sr{t}_{ib}", f"c1t1{t}_{ib}", T.get(f"c1st{t-1}")],
                     tag=f"c1gp{t}_{ib}")
                p.op("vector", lambda e, ib=ib, t=t: e.tensor_reduce(
                        sb["m3slots"][:, t * NI + ib:t * NI + ib + 1],
                        sb["g2st"][:, ib, :], axis=AX.X, op=ALU.max,
                        apply_absolute_value=True),
                     deps=[f"c1gp{t}_{ib}"], tag=f"c1mx{t}_{ib}")
            dma("gpsimd", g2h[:, t * NI * TT:(t + 1) * NI * TT], sb["g2st"][:, :, :],
                "SC2", tag=f"c1st{t}", deps=[f"c1gp{t}_{NI-1}", f"c1mx{t}_{NI-1}"])
            T[f"c1st{t}"] = f"c1st{t}"
            T[f"c1_free{t}"] = f"c1t1{t}_{NI-1}"

        if PH < 7:
            finish_early()
            return
        mbarrier("m3", [("m3slots", 0, NT * NI)], f"c1mx{NT-1}_{NI-1}")
        p.op("vector", lambda e: e.tensor_mul(sb["srow"][:, 6:7], sb["mxfold"][:, 0:1],
                                              sb["srow"][:, 3:4]),
             deps=["m3_max", "s12row"], tag="m3a")
        p.op("vector", lambda e: e.tensor_scalar(sb["srow"][:, 6:7], sb["srow"][:, 6:7],
                                                 1.0 / (65536.0 * QMAX), None, op0=ALU.mult),
             deps=["m3a"], tag="s3row")
        p.op("vector", lambda e: e.reciprocal(sb["srow"][:, 7:8], sb["srow"][:, 6:7]),
             deps=["s3row"], tag="r3row")
        p.op("vector", lambda e: e.tensor_mul(sb["srow"][:, 8:9], sb["srow"][:, 3:4],
                                              sb["srow"][:, 7:8]),
             deps=["r3row"], tag="c32a")
        p.op("vector", lambda e: e.tensor_scalar(sb["srow"][:, 8:9], sb["srow"][:, 8:9],
                                                 1.0 / 65536.0, None, op0=ALU.mult),
             deps=["c32a"], tag="c32row")
        bcast_col("c32c", sb["srow"][0:1, 8:9], "c32row", "c32c")

        uld = [sb["bufB0"], sb["bufB1"]]
        for t in range(NT):
            sl = t % 2
            dma("gpsimd", gld[sl][:, :, :], g2h[:, t * NI * TT:(t + 1) * NI * TT],
                f"U{sl}", tag=f"c2gld{t}",
                deps=[f"c1st{NT-1}", T.get(f"c2_free{t-2}")])
            dma("gpsimd", uld[sl][:, :, :], Uh[:, t * NI * TT:(t + 1) * NI * TT],
                f"K{sl}", tag=f"c2uld{t}",
                deps=[f"bst{NT-1}_1", T.get(f"c2_free{t-2}")])
            for ib in range(NI):
                ts_ = ib % 2
                pg = (f"c2pr{t}_{ib-2}" if ib >= 2 else
                      (f"c2pr{t-1}_{NI-2+ib}" if t >= 1 else None))
                p.op("scalar", lambda e, sl=sl, ib=ib, ts_=ts_: e.activation(
                        t1f[ts_][:, :], gld[sl][:, ib, :], AF.Identity,
                        scale=sb["c32c"][:, :], bias=sb["rccol"][:, :]),
                     deps=[f"c2gld{t}", "bce_c32c", pg], tag=f"c2t4{t}_{ib}")
                p.op("vector", lambda e, ts_=ts_: e.tensor_scalar(
                        sb["g2bf"][:, :], t1f[ts_][:, :], -RC, None, op0=ALU.add),
                     deps=[f"c2t4{t}_{ib}"], tag=f"c2g2{t}_{ib}")
                p.op("scalar", lambda e, sl=sl, ib=ib, ts_=ts_: e.activation(
                        sgf[ts_][:, :], uld[sl][:, ib, :], AF.Identity,
                        scale=sb["r1c"][:, :], bias=sb["rccol"][:, :]),
                     deps=[f"c2uld{t}", "bce_r1c"], tag=f"c2t5{t}_{ib}")
                p.op("vector", lambda e, ts_=ts_: e.tensor_scalar(
                        sb["upbf"][:, :], sgf[ts_][:, :], -RC, None, op0=ALU.add),
                     deps=[f"c2t5{t}_{ib}"], tag=f"c2up{t}_{ib}")
                p.op("vector", lambda e, ib=ib: e.tensor_mul(
                        sb["prst"][:, ib, :], sb["g2bf"][:, :], sb["upbf"][:, :]),
                     deps=[f"c2g2{t}_{ib}", f"c2up{t}_{ib}", T.get(f"c2st{t-1}")],
                     tag=f"c2pr{t}_{ib}")
                p.op("vector", lambda e, ib=ib, t=t: e.tensor_reduce(
                        sb["m4slots"][:, t * NI + ib:t * NI + ib + 1],
                        sb["prst"][:, ib, :], axis=AX.X, op=ALU.max,
                        apply_absolute_value=True),
                     deps=[f"c2pr{t}_{ib}"], tag=f"c2mx{t}_{ib}")
            dma("gpsimd", prh[:, t * NI * TT:(t + 1) * NI * TT], sb["prst"][:, :, :],
                "SC3", tag=f"c2st{t}", deps=[f"c2pr{t}_{NI-1}", f"c2mx{t}_{NI-1}"])
            T[f"c2st{t}"] = f"c2st{t}"
            T[f"c2_free{t}"] = f"c2pr{t}_{NI-1}"

        if PH < 8:
            finish_early()
            return
        mbarrier("m4", [("m4slots", 0, NT * NI)], f"c2mx{NT-1}_{NI-1}")
        p.op("vector", lambda e: e.tensor_mul(sb["srow"][:, 9:10], sb["srow"][:, 6:7],
                                              sb["srow"][:, 2:3]),
             deps=["s3row", "s12row", "m4_max"], tag="s31")
        p.op("vector", lambda e: e.tensor_mul(sb["srow"][:, 10:11], sb["mxfold"][:, 0:1],
                                              sb["srow"][:, 9:10]),
             deps=["s31"], tag="m4a")
        p.op("vector", lambda e: e.tensor_scalar(sb["srow"][:, 10:11], sb["srow"][:, 10:11],
                                                 1.0 / QMAX, None, op0=ALU.mult),
             deps=["m4a"], tag="s4row")
        p.op("vector", lambda e: e.reciprocal(sb["srow"][:, 11:12], sb["srow"][:, 10:11]),
             deps=["s4row"], tag="r4row")
        p.op("vector", lambda e: e.tensor_mul(sb["srow"][:, 11:12], sb["srow"][:, 11:12],
                                              sb["srow"][:, 9:10]),
             deps=["r4row"], tag="c4row")
        bcast_col("c4c", sb["srow"][0:1, 11:12], "c4row", "c4c")
        bcast_col("s4c", sb["srow"][0:1, 10:11], "s4row", "s4c")
        p.op("vector", lambda e: e.tensor_scalar(sb["gbs4"][:, :], sb["mcols"][:, 5, :],
                                                 sb["s4c"][:, :], None, op0=ALU.mult),
             deps=["bce_s4c", "modcols_done"], tag="gbs4_done")

        x2r = [sb["x2r0"], sb["x2r1"]]
        outs = [sb["outs0"], sb["outs0"]]
        ep2 = [sb["ep20"], sb["ep21"]]
        for t in range(NT):
            sl = t % 2
            dma("gpsimd", gld[sl][:, :, :], prh[:, t * NI * TT:(t + 1) * NI * TT],
                f"U{sl}", tag=f"c3pld{t}",
                deps=[f"c2st{NT-1}", T.get(f"c3_free{t-2}")])
            dma("sync", x2r[sl][:, :, :], x2h_v[:, :, t * TT:(t + 1) * TT],
                f"X{sl}", tag=f"c3xld{t}",
                deps=[f"h2sp{NT-1}", f"c2st{NT-1}", T.get(f"c3x2_free{t-2}")])
            for ib in range(NI):
                ts_ = ib % 2
                pg = (f"c3o{t}_{ib-2}" if ib >= 2 else
                      (f"c3o{t-1}_{NI-2+ib}" if t >= 1 else None))
                p.op("scalar", lambda e, sl=sl, ib=ib, ts_=ts_: e.activation(
                        t1f[ts_][:, :], gld[sl][:, ib, :], AF.Identity,
                        scale=sb["c4c"][:, :], bias=sb["rccol"][:, :]),
                     deps=[f"c3pld{t}", "bce_c4c", pg], tag=f"c3t{t}_{ib}")
                p.op("vector", lambda e, ib=ib, ts_=ts_: e.tensor_scalar(
                        sb["obf"][:, ib, :], t1f[ts_][:, :], -RC, None, op0=ALU.add),
                     deps=[f"c3t{t}_{ib}", T.get(f"c3mm_done{t-1}")],
                     tag=f"c3o{t}_{ib}")
            for fo in range(KC):
                bs = f"big{fo % 2}"
                es = fo % 2
                def mk_dn(e, fo=fo, bs=bs):
                    r = None
                    for ic in range(NI):
                        r = e.matmul(ps[bs][:, :], sb["wd_sb"][:, ic, fo * P:(fo + 1) * P],
                                     sb["obf"][:, ic, :],
                                     start=(ic == 0), stop=(ic == NI - 1))
                    return r
                p.op("tensor", mk_dn,
                     deps=[f"c3o{t}_{NI-1}", "ld_wd", PSL.get(bs)],
                     tag=f"c3mm{t}_{fo}")
                p.op("scalar", lambda e, fo=fo, bs=bs, es=es: e.activation(
                        ep2[es][:, :], ps[bs][:, :], AF.Identity,
                        scale=sb["gbs4"][:, fo:fo + 1]),
                     deps=[f"c3mm{t}_{fo}", "gbs4_done", T.get(f"c3ep_free{t}_{fo-2}"),
                           T.get(f"c3ep_freeprev{t}") if fo < 2 else None],
                     tag=f"c3ep{t}_{fo}")
                PSL[bs] = f"c3ep{t}_{fo}"
                p.op("vector", lambda e, fo=fo, sl=sl, es=es: e.tensor_add(
                        outs[sl][:, fo, :], x2r[sl][:, fo, :], ep2[es][:, :]),
                     deps=[f"c3ep{t}_{fo}", f"c3xld{t}", T.get(f"c3ost{t-1}")],
                     tag=f"c3out{t}_{fo}")
                T[f"c3ep_free{t}_{fo}"] = f"c3out{t}_{fo}"
            T[f"c3ep_freeprev{t+1}"] = f"c3out{t}_{KC-1}"
            T[f"c3mm_done{t}"] = f"c3mm{t}_{KC-1}"
            T[f"c3x2_free{t}"] = f"c3out{t}_{KC-1}"
            T[f"c3_free{t}"] = f"c3t{t}_{NI-1}"
            dma("sync", out_v[:, :, t * TT:(t + 1) * TT], outs[sl][:, :, :],
                "ST", tag=f"c3ost{t}", deps=[f"c3out{t}_{KC-1}"])
            T[f"c3ost{t}"] = f"c3ost{t}"

        p.op("sync", lambda e: e.wait_ge(p.sems["ST"], 16 * NT), inc=0)
        p.op("gpsimd", lambda e: e.wait_ge(p.sems["ST"], 16 * NT), inc=0)

    p.reset("count")
    program()
    snap = dict(p.tags)
    p.reset("count")
    program()
    assert p.tags == snap, "walk is not deterministic"

    with ctx:
        block = ctx.enter_context(nc.Block())

        def mk_section(eng):
            def f(engine):
                p.engs[eng] = engine
                p.reset(eng)
                program()
            return f

        block.sync(mk_section("sync"))
        block.gpsimd(mk_section("gpsimd"))
        block.tensor(mk_section("tensor"))
        block.scalar(mk_section("scalar"))
        block.vector(mk_section("vector"))

    return nc


def stage(inputs):
    x = np.asarray(inputs["L0_attn_x"], dtype=np.float32)
    shared = {
        "wqT": np.ascontiguousarray(np.asarray(inputs["L0_attn_Wq"], np.float32).T),
        "wkT": np.ascontiguousarray(np.asarray(inputs["L0_attn_Wk"], np.float32).T),
        "wvT": np.ascontiguousarray(np.asarray(inputs["L0_attn_Wv"], np.float32).T),
        "woT": np.ascontiguousarray(np.asarray(inputs["L0_attn_Wo"], np.float32).T),
        "fmqT": np.ascontiguousarray(np.asarray(inputs["L0_attn_fmq_w"], np.float32).T),
        "fmkT": np.ascontiguousarray(np.asarray(inputs["L0_attn_fmk_w"], np.float32).T),
        "fmqb": np.ascontiguousarray(np.asarray(inputs["L0_attn_fmq_b"], np.float32).reshape(P, 1)),
        "fmkb": np.ascontiguousarray(np.asarray(inputs["L0_attn_fmk_b"], np.float32).reshape(P, 1)),
        "rmsw": np.ascontiguousarray(np.asarray(inputs["L0_attn_rms_w"], np.float32).reshape(P, 1)),
        "wup": np.ascontiguousarray(np.asarray(inputs["L0_ffn_w_up"], np.float32)),
        "wgate": np.ascontiguousarray(np.asarray(inputs["L0_ffn_w_gate"], np.float32)),
        "wdown": np.ascontiguousarray(np.asarray(inputs["L0_ffn_w_down"], np.float32)),
        "wmod": np.ascontiguousarray(np.asarray(inputs["L0_wmod"], np.float32)),
    }
    c = np.asarray(inputs["L0_c"], dtype=np.float32)
    in_maps = []
    for core in range(8):
        b, hf = core // 2, core % 2
        m = dict(shared)
        m["xT"] = np.ascontiguousarray(x[b, hf * TOK:(hf + 1) * TOK, :].T)
        m["crow"] = np.ascontiguousarray(c[b:b + 1, :])
        sm = np.zeros((1, 8), dtype=np.float32)
        if hf == 1:
            sm[0, core - 1] = 1.0
        m["smask"] = sm
        in_maps.append(m)
    return in_maps


def kernel(**inputs):
    in_maps = stage(inputs)
    nc = build_nc()
    res = run_bass_kernel_spmd(nc, in_maps, core_ids=list(range(8)))
    out = np.empty((4, 2 * TOK, D), dtype=np.float32)
    for core in range(8):
        b, hf = core // 2, core % 2
        out[b, hf * TOK:(hf + 1) * TOK, :] = res.results[core]["out"].T
    return out


# version 04
# speedup vs baseline: 1.1090x; 1.1090x over previous
import math
import os
import numpy as np
import ml_dtypes
from contextlib import ExitStack

import concourse.bass as bass
import concourse.mybir as mybir
from concourse.bass_utils import run_bass_kernel_spmd

F32 = mybir.dt.float32
BF16 = mybir.dt.bfloat16
AF = mybir.ActivationFunctionType
ALU = mybir.AluOpType
AX = mybir.AxisListType

P = 128
D = 1024
KC = 8
H = 8
DF = 256
TOK = 4096
NT = 8
TT = 512
NCH = 4
CHN = 128
DFF = 3072
NI = 24
RC = 12582912.0
LN16 = math.log(16.0)
QMAX = 32767.0

PH = int(os.environ.get("KPHASES", "99"))
KNOB = os.environ.get("KNOB", "")
ENGSEM = dict(tensor="T", vector="V", scalar="S", sync="YN", gpsimd="GN")
NONDMA = {"T", "V", "S", "YN", "GN"}


class Prog:
    def __init__(self):
        self.mode = "count"
        self.tags = {}
        self.cnt = {}
        self.engs = {}
        self.sems = {}
        self.water = {}

    def reset(self, mode):
        self.mode = mode
        self.cnt = {k: 0 for k in self.sems}
        self.water = {}

    def settag(self, tag, srctag):
        if self.mode == "count" and tag not in self.tags:
            self.tags[tag] = self.tags[srctag]

    def op(self, eng, emit, deps=(), inc=1, tag=None, sem=None):
        semname = sem or ENGSEM[eng]
        self.cnt[semname] = self.cnt.get(semname, 0) + inc
        val = self.cnt[semname]
        if self.mode == "count":
            if tag is not None:
                if tag in self.tags:
                    assert self.tags[tag] == (semname, val), f"tag drift {tag}"
                self.tags[tag] = (semname, val)
            return tag
        if self.mode != eng:
            return tag
        e = self.engs[eng]
        own = ENGSEM[eng]
        for d in deps:
            if d is None:
                continue
            s, v = self.tags[d]
            if s == own and s in NONDMA:
                assert v <= self.cnt[own], f"forward self-dep {d} on {eng}"
            if self.water.get(s, 0) >= v:
                continue
            e.wait_ge(self.sems[s], v)
            self.water[s] = v
        instr = emit(e)
        if inc:
            instr.then_inc(self.sems[semname], inc)
        return tag

    def barrier(self, name, sb):
        if self.mode == "count":
            self.tags[f"bar_{name}"] = {k: v for k, v in self.cnt.items()}
            return
        vals = self.tags[f"bar_{name}"]
        for eng in ENGSEM:
            if self.mode != eng:
                continue
            e = self.engs[eng]
            own = ENGSEM[eng]
            for sname, v in vals.items():
                if sname == own or v == 0:
                    continue
                if self.water.get(sname, 0) >= v:
                    continue
                e.wait_ge(self.sems[sname], v)
                self.water[sname] = v


def build_nc():
    nc = bass.Bass("TRN2", target_bir_lowering=False)

    xT = nc.dram_tensor("xT", [D, TOK], F32, kind="ExternalInput").ap()
    wqT = nc.dram_tensor("wqT", [D, D], F32, kind="ExternalInput").ap()
    wkT = nc.dram_tensor("wkT", [D, D], F32, kind="ExternalInput").ap()
    wvT = nc.dram_tensor("wvT", [D, D], F32, kind="ExternalInput").ap()
    woT = nc.dram_tensor("woT", [D, D], F32, kind="ExternalInput").ap()
    fmqT = nc.dram_tensor("fmqT", [P, P], F32, kind="ExternalInput").ap()
    fmkT = nc.dram_tensor("fmkT", [P, P], F32, kind="ExternalInput").ap()
    fmqb = nc.dram_tensor("fmqb", [P, 1], F32, kind="ExternalInput").ap()
    fmkb = nc.dram_tensor("fmkb", [P, 1], F32, kind="ExternalInput").ap()
    rmsw = nc.dram_tensor("rmsw", [P, 1], F32, kind="ExternalInput").ap()
    wup = nc.dram_tensor("wup", [DFF, D], F32, kind="ExternalInput").ap()
    wgate = nc.dram_tensor("wgate", [DFF, D], F32, kind="ExternalInput").ap()
    wdown = nc.dram_tensor("wdown", [D, DFF], F32, kind="ExternalInput").ap()
    wmod = nc.dram_tensor("wmod", [D, 6 * D], F32, kind="ExternalInput").ap()
    crow = nc.dram_tensor("crow", [1, D], F32, kind="ExternalInput").ap()
    smask = nc.dram_tensor("smask", [1, 8], F32, kind="ExternalInput").ap()
    out = nc.dram_tensor("out", [D, TOK], F32, kind="ExternalOutput").ap()

    hb_h = nc.dram_tensor("hb_h", [D, TOK], BF16, kind="Internal").ap()
    hb_kfm = nc.dram_tensor("hb_kfm", [H * DF, TOK], BF16, kind="Internal").ap()
    hb_ktm = nc.dram_tensor("hb_ktm", [TOK, H * DF], BF16, kind="Internal").ap()
    hb_v = nc.dram_tensor("hb_v", [TOK, D], BF16, kind="Internal").ap()
    x2h = nc.dram_tensor("x2h", [D, TOK], F32, kind="Internal").ap()
    h2h = nc.dram_tensor("h2h", [D, TOK], BF16, kind="Internal").ap()
    wupT_q = nc.dram_tensor("wupT_q", [D, DFF], BF16, kind="Internal").ap()
    wgT_q = nc.dram_tensor("wgT_q", [D, DFF], BF16, kind="Internal").ap()
    wdT_q = nc.dram_tensor("wdT_q", [DFF, D], BF16, kind="Internal").ap()
    Uh = nc.dram_tensor("Uh", [P, NT * NI * TT], BF16, kind="Internal").ap()
    Gh = nc.dram_tensor("Gh", [P, NT * NI * TT], BF16, kind="Internal").ap()
    g2h = nc.dram_tensor("g2h", [P, NT * NI * TT], BF16, kind="Internal").ap()
    prh = nc.dram_tensor("prh", [P, NT * NI * TT], BF16, kind="Internal").ap()
    sc_in = nc.dram_tensor("sc_in", [P, H * DF], F32, kind="Internal").ap()
    sc_out = nc.dram_tensor("sc_out", [8 * P, H * DF], F32, kind="Internal").ap()
    mx_in = nc.dram_tensor("mx_in", [P, 2], F32, kind="Internal").ap()
    mx_out = nc.dram_tensor("mx_out", [P, 2], F32, kind="Internal").ap()

    idt = nc.inline_tensor(np.eye(P, dtype=ml_dtypes.bfloat16), name="identbf")
    tri = nc.inline_tensor(np.triu(np.ones((P, P), dtype=ml_dtypes.bfloat16)), name="triubf")
    onc = nc.inline_tensor(np.ones((P, 1), dtype=ml_dtypes.bfloat16), name="onescolbf")
    onr = nc.inline_tensor(np.ones((1, P), dtype=np.float32), name="onesrowf")

    sb = {}
    total_sbuf = nc.sbuf_bytes_remaining

    def galloc(name, shape, dt):
        sb[name] = nc.alloc_sbuf_tensor("sb_" + name, shape, dt)
        return sb[name]

    for nm, shp, dt in [
        ("dmy", [1, 1], F32),
        ("ident", [P, P], BF16), ("triu", [P, P], BF16),
        ("onescol", [P, 1], BF16), ("onesrow", [1, P], F32),
        ("epscol", [P, 1], F32), ("rccol", [P, 1], F32), ("c2ecol", [P, 1], F32),
        ("fmqT", [P, P], BF16), ("fmkT", [P, P], BF16),
        ("fmqb", [P, 1], F32), ("fmkb", [P, 1], F32),
        ("bqp", [P, 1], F32), ("bqn", [P, 1], F32), ("bkn", [P, 1], F32),
        ("rmsw", [P, 1], F32),
        ("mcols", [P, 6, 8], F32), ("ca1", [P, 8], F32), ("cb1", [P, 8], F32),
        ("smrow", [1, 8], F32), ("sm128", [P, 8], F32),
        ("m0slots", [P, 64], F32),
        ("m1slots", [P, NT * NI], F32), ("m2slots", [P, NT * NI], F32),
        ("m3slots", [P, NT * NI], F32), ("m4slots", [P, NT * NI], F32),
        ("mxst", [P, 2], F32), ("mxrow", [1, 2 * P], F32), ("mxfold", [1, 2], F32),
        ("r0c", [P, 1], F32), ("s0c", [P, 1], F32), ("s4c", [P, 1], F32),
        ("r1c", [P, 1], F32), ("r2c", [P, 1], F32), ("s2c", [P, 1], F32),
        ("brc2c", [P, 1], F32), ("c32c", [P, 1], F32), ("c4c", [P, 1], F32),
        ("gbs4", [P, 8], F32),
        ("ccols", [P, 8], F32),
        ("row_a", [1, TT], F32), ("row_b", [1, TT], F32),
        ("row_m", [1, TT], F32), ("row_r", [1, TT], F32),
        ("srow", [1, 16], F32),
    ]:
        galloc(nm, shp, dt)

    probe = nc.alloc_sbuf_tensor("sb_gbase_probe", [1, 64], F32)
    GBASE = ((nc.lookup_mloc(probe).addr + 256 + 63) // 64) * 64
    LIMIT = GBASE + nc.sbuf_bytes_remaining - 512

    class Arena:
        def __init__(self, base):
            self.off = base

        def place(self, name, shape, dt):
            esz = 2 if dt == BF16 else 4
            n = esz
            for s_ in shape[1:]:
                n *= s_
            off = ((self.off + 63) // 64) * 64
            sb[name] = nc.alloc_sbuf_tensor_at("sb_" + name, shape, dt, offset=off)
            self.off = off + n
            assert self.off <= LIMIT, f"SBUF overflow at {name}: {self.off}"
            return sb[name]

    a0 = Arena(GBASE)
    a0.place("wmod0", [P, D], F32)
    a0.place("wmod1", [P, D], F32)
    a0.place("crow_sb", [1, D], F32)
    a0.place("modrow", [1, 6 * D], F32)
    a0.place("clamp_a", [1, D], F32)
    a0.place("clamp_b", [1, D], F32)

    aa = Arena(GBASE)
    aa.place("dqn0", [P, DFF], F32)
    aa.place("dsc", [P, DFF // 32], F32)
    aa.place("drc", [P, DFF // 32], F32)
    aa.place("dt1", [P, DFF], F32)
    aa.place("dqb", [P, DFF], BF16)
    aa.place("dst", [P, NI, P], BF16)
    aa.place("wk_sb", [P, KC, D], BF16)
    aa.place("wv_sb", [P, KC, D], BF16)
    aa.place("xs0", [P, KC, TT], F32)
    aa.place("xs1", [P, KC, TT], F32)
    aa.place("xbf", [P, KC, TT], BF16)
    aa.place("sq0", [P, TT], BF16)
    aa.place("sq1", [P, TT], BF16)
    aa.place("tmp1", [P, TT], BF16)
    aa.place("tmp2", [P, TT], BF16)
    aa.place("rsb", [P, TT], BF16)
    aa.place("mrsb", [P, TT], BF16)
    aa.place("h_sb", [P, KC, TT], BF16)
    aa.place("kq0", [P, TT], BF16)
    aa.place("kq1", [P, TT], BF16)
    aa.place("kfm_sb", [P, 2 * H, TT], BF16)
    aa.place("ktm_sb", [P, NCH, H * DF], BF16)
    aa.place("v_sb", [P, NCH, D], BF16)
    aa.place("Ssb", [P, H * DF], F32)

    ab = Arena(GBASE)
    ab.place("Sfin", [P, H * DF], F32)
    ab.place("wq_sb", [P, KC, D], BF16)
    ab.place("wo_sb", [P, KC, D], BF16)
    ab.place("cxs0", [P, KC, TT], F32)
    ab.place("h_r", [P, KC, TT], BF16)
    ab.place("kfm_r", [P, 2 * H, TT], BF16)
    ab.place("ktm_r", [P, NCH, H * DF], BF16)
    ab.place("v_r", [P, NCH, D], BF16)
    ab.place("qfm_sb", [P, 2 * H, TT], BF16)
    ab.place("qq0", [P, TT], BF16)
    ab.place("qq1", [P, TT], BF16)
    OSB_OFF = ((ab.off + 63) // 64) * 64
    ab.place("o_sb", [P, KC, TT], BF16)
    sb["gS0"] = nc.alloc_sbuf_tensor_at("sb_gS0", [P, H * DF], F32, offset=OSB_OFF)
    ab.place("Ssnap", [P, 2, DF], BF16)
    ab.place("at_sb", [P, 2, P], BF16)
    ab.place("o2_sb", [P, P], BF16)
    ab.place("orow", [P, 32, P], BF16)
    ab.place("rms32", [P, 32], F32)
    ab.place("rmsc", [P, 1], F32)
    ab.place("x2_sb", [P, KC, TT], F32)
    ab.place("ep_sb", [P, TT], F32)
    ab.place("h2_sb", [P, KC, TT], BF16)
    ab.place("cxbf", [P, KC, TT], BF16)
    ab.place("csq0", [P, TT], BF16)
    ab.place("csq1", [P, TT], BF16)
    ab.place("ctmp1", [P, TT], BF16)
    ab.place("ctmp2", [P, TT], BF16)
    ab.place("crsb", [P, TT], BF16)
    ab.place("cmrsb", [P, TT], BF16)

    ac = Arena(GBASE)
    ac.place("wup_sb", [P, KC, DFF], BF16)
    ac.place("wg_sb", [P, KC, DFF], BF16)
    ac.place("h2r0", [P, KC, TT], BF16)
    ac.place("h2r1", [P, KC, TT], BF16)
    ac.place("xqt0", [P, TT], F32)
    ac.place("xqt1", [P, TT], F32)
    ac.place("xq_sb", [P, KC, TT], BF16)
    ac.place("Ust", [P, NI, TT], BF16)
    ac.place("Gst", [P, NI, TT], BF16)

    ad = Arena(GBASE)
    ad.place("bufA0", [P, NI, TT], BF16)
    ad.place("bufA1", [P, NI, TT], BF16)
    ad.place("t1f0", [P, TT], F32)
    ad.place("t1f1", [P, TT], F32)
    Z1 = ad.off
    ad.place("bufB0", [P, NI, TT], BF16)
    ad.place("bufB1", [P, NI, TT], BF16)
    ad.place("sgf0", [P, TT], F32)
    ad.place("sgf1", [P, TT], F32)
    Z2 = ad.off
    ae1 = Arena(Z2)
    ae1.place("srbf", [P, TT], BF16)
    ae1.place("g2st", [P, NI, TT], BF16)
    ae2 = Arena(Z2)
    ae2.place("prst", [P, NI, TT], BF16)
    ae2.place("g2bf", [P, TT], BF16)
    ae2.place("upbf", [P, TT], BF16)
    awd = Arena(max(ae1.off, ae2.off))
    awd.place("wd_sb", [P, NI, D], BF16)
    af = Arena(Z1)
    af.place("obf", [P, NI, TT], BF16)
    af.place("x2r0", [P, KC, TT], F32)
    af.place("x2r1", [P, KC, TT], F32)
    af.place("outs0", [P, KC, TT], F32)
    af.place("ep20", [P, TT], F32)
    af.place("ep21", [P, TT], F32)

    ps = {}
    for i in range(4):
        ps[f"S{i}"] = nc.alloc_psum_tensor(f"psS{i}", [P, 512], F32)
    ps["big0"] = nc.alloc_psum_tensor("big0", [P, TT], F32)
    ps["big1"] = nc.alloc_psum_tensor("big1", [P, TT], F32)
    ps["pxa"] = nc.alloc_psum_tensor("pxa", [P, 1024], BF16)
    ps["pxb"] = nc.alloc_psum_tensor("pxb", [P, 1024], BF16)
    assert nc.psum_banks_remaining == 0

    def S_ap(h_, dc):
        sl = h_ * 2 + dc
        return ps[f"S{sl // 4}"][:, (sl % 4) * P:(sl % 4 + 1) * P]

    ctx = ExitStack()
    sems = {}
    for s in ["T", "V", "S", "YN", "GN", "LC", "MA", "MB", "DA", "DB", "LW",
              "LF", "SPA", "SPB", "SG", "SD", "X0", "X1", "H0", "H1",
              "K0", "K1", "U0", "U1", "SC1", "SC2", "SC3", "CC", "ST"]:
        sems[s] = ctx.enter_context(nc.semaphore(s))
    p = Prog()
    p.sems = sems

    xT_v = xT.rearrange("(kc q) n -> q kc n", q=P)
    hbh_v = hb_h.rearrange("(kc q) n -> q kc n", q=P)
    kfm_v = hb_kfm.rearrange("(r q) n -> q r n", q=P)
    ktm_v = hb_ktm.rearrange("(g q) c -> q g c", q=P)
    v_v = hb_v.rearrange("(g q) c -> q g c", q=P)
    x2h_v = x2h.rearrange("(kc q) n -> q kc n", q=P)
    h2h_v = h2h.rearrange("(kc q) n -> q kc n", q=P)
    out_v = out.rearrange("(kc q) n -> q kc n", q=P)

    def program():
        T = {}

        def dma(eng, dst, src, sem, tag=None, deps=()):
            return p.op(eng, lambda e: e.dma_start(out=dst, in_=src),
                        deps=deps, inc=16, tag=tag, sem=sem)

        dma("sync", sb["ident"][:, :], idt.ap(), "LC")
        dma("sync", sb["triu"][:, :], tri.ap(), "LC")
        dma("sync", sb["onescol"][:, :], onc.ap(), "LC")
        dma("sync", sb["onesrow"][:, :], onr.ap(), "LC")
        dma("sync", sb["fmqb"][:, :], fmqb, "LC")
        dma("sync", sb["fmkb"][:, :], fmkb, "LC")
        dma("sync", sb["rmsw"][:, :], rmsw, "LC")
        dma("sync", sb["smrow"][:, :], smask, "LC")
        dma("sync", sb["crow_sb"][:, :], crow, "LC", tag="ld_consts")
        p.op("gpsimd", lambda e: e.dma_start(out=sb["fmqT"][:, :], in_=fmqT),
             inc=16, sem="LW")
        p.op("gpsimd", lambda e: e.dma_start(out=sb["fmkT"][:, :], in_=fmkT),
             inc=16, sem="LW", tag="ld_fm")

        p.op("vector", lambda e: e.memset(sb["dmy"][:, :], 0.0))
        p.op("vector", lambda e: e.memset(sb["epscol"][:, :], 1e-5))
        p.op("vector", lambda e: e.memset(sb["rccol"][:, :], RC), tag="rc_set")
        p.op("vector", lambda e: e.memset(sb["c2ecol"][:, :], 128.0 * 1e-5 * (2.0 ** -48)),
             tag="c2e_set")
        p.op("vector", lambda e: e.tensor_scalar(sb["bqp"][:, :], sb["fmqb"][:, :], -LN16, None, op0=ALU.add),
             deps=["ld_consts"])
        p.op("vector", lambda e: e.tensor_scalar(sb["bqn"][:, :], sb["fmqb"][:, :], -1.0, -LN16, op0=ALU.mult, op1=ALU.add))
        p.op("vector", lambda e: e.tensor_scalar(sb["bkn"][:, :], sb["fmkb"][:, :], -1.0, None, op0=ALU.mult),
             tag="bias_cols")

        p.op("scalar", lambda e: e.activation(sb["crow_sb"][:, :], sb["crow_sb"][:, :], AF.Silu),
             deps=["ld_consts"], tag="silu_row")
        def mk_ccol(e):
            r = None
            for kc in range(KC):
                r = e.matmul(ps["big0"][:, kc:kc + 1],
                             sb["crow_sb"][0:1, kc * P:(kc + 1) * P],
                             sb["onesrow"][0:1, 0:1],
                             start=(kc == 0), stop=(kc == KC - 1))
            return r
        p.op("tensor", mk_ccol, deps=["silu_row", "ld_consts"], tag="ccol_mm")
        p.op("scalar", lambda e: e.activation(sb["ccols"][:, :], ps["big0"][:, 0:8], AF.Identity),
             deps=["ccol_mm"], tag="ccol_ev")

        wmod_sb = [sb["wmod0"], sb["wmod1"]]
        for g in range(6):
            for kc in range(KC):
                slot = (g * KC + kc) % 2
                prev = f"modmm_g{g}_k{kc}_m2" if (g * KC + kc) >= 2 else None
                gp, kp = divmod(g * KC + kc - 2, KC)
                dd = [f"modmm_g{gp}_k{kp}"] if (g * KC + kc) >= 2 else []
                dma("sync", wmod_sb[slot][:, :],
                    wmod[kc * P:(kc + 1) * P, g * D:(g + 1) * D],
                    f"M{'AB'[slot]}", tag=f"modld_g{g}_k{kc}", deps=dd)
            for kc in range(KC):
                slot = (g * KC + kc) % 2
                def mk_mod(g=g, kc=kc, slot=slot):
                    def f(e):
                        e.matmul(ps["big0"][0:1, :],
                                 sb["ccols"][:, kc:kc + 1],
                                 wmod_sb[slot][:, 0:TT],
                                 start=(kc == 0), stop=(kc == KC - 1))
                        return e.matmul(ps["big1"][0:1, :],
                                        sb["ccols"][:, kc:kc + 1],
                                        wmod_sb[slot][:, TT:D],
                                        start=(kc == 0), stop=(kc == KC - 1))
                    return f
                dd = [f"modld_g{g}_k{kc}", "ccol_ev"]
                if g >= 1:
                    dd.append(f"modev_g{g-1}")
                p.op("tensor", mk_mod(), deps=dd, tag=f"modmm_g{g}_k{kc}")
            def mk_ev(g=g):
                def f(e):
                    e.activation(sb["modrow"][:, g * D:g * D + TT], ps["big0"][0:1, :], AF.Identity)
                    return e.activation(sb["modrow"][:, g * D + TT:(g + 1) * D], ps["big1"][0:1, :], AF.Identity)
                return f
            p.op("scalar", mk_ev(), deps=[f"modmm_g{g}_k{KC-1}"], tag=f"modev_g{g}")

        for i, s_ in enumerate([2, 5]):
            sl_ = sb["modrow"][:, s_ * D:(s_ + 1) * D]
            p.op("scalar", lambda e, sl_=sl_: e.activation(sb["clamp_a"][:, :], sl_, AF.Abs),
                 deps=[f"modev_g{s_}", (f"cl_done{i-1}" if i else None)], tag=f"cl_abs{i}")
            p.op("scalar", lambda e, sl_=sl_: e.activation(sb["clamp_b"][:, :], sl_, AF.Sign),
                 deps=[(f"cl_done{i-1}" if i else None)], tag=f"cl_sgn{i}")
            p.op("vector", lambda e: e.tensor_scalar_max(sb["clamp_a"][:, :], sb["clamp_a"][:, :], 0.25),
                 deps=[f"cl_abs{i}"], tag=f"cl_max{i}")
            p.op("vector", lambda e, sl_=sl_: e.tensor_mul(sl_, sb["clamp_a"][:, :], sb["clamp_b"][:, :]),
                 deps=[f"cl_sgn{i}", f"cl_max{i}"], tag=f"cl_done{i}")

        def mk_scat(e):
            r = None
            for s_ in range(6):
                for kc in range(KC):
                    r = e.matmul(ps["big0"][:, s_ * 8 + kc:s_ * 8 + kc + 1],
                                 sb["modrow"][0:1, s_ * D + kc * P:s_ * D + (kc + 1) * P],
                                 sb["onesrow"][0:1, 0:1],
                                 start=(s_ == 0 and kc == 0), stop=(s_ == 5 and kc == KC - 1))
            return r
        p.op("tensor", mk_scat,
             deps=["cl_done0", "cl_done1", "modev_g0", "modev_g1",
                   "modev_g3", "modev_g4", "modev_g5", "ld_consts"],
             tag="mod_scat")
        p.op("scalar", lambda e: e.activation(sb["mcols"][:, :, :], ps["big0"][:, 0:48], AF.Identity),
             deps=["mod_scat"], tag="mcols_ev")
        p.op("vector", lambda e: e.tensor_scalar(sb["ca1"][:, :], sb["mcols"][:, 1, :], 1.0, None, op0=ALU.add),
             deps=["mcols_ev"])
        p.op("vector", lambda e: e.tensor_scalar(sb["cb1"][:, :], sb["mcols"][:, 4, :], 1.0, None, op0=ALU.add),
             tag="modcols_done")
        p.op("tensor", lambda e: e.matmul(ps["big1"][:, 0:8], sb["onesrow"][:, :],
                                          sb["smrow"][:, :], start=True, stop=True),
             deps=["ld_consts", f"modev_g5"], tag="sm_mm")
        p.op("scalar", lambda e: e.activation(sb["sm128"][:, :], ps["big1"][:, 0:8], AF.Identity),
             deps=["sm_mm"], tag="sm128_done")

        p.barrier("p0", sb)

        def finish_early():
            for t_ in range(NT):
                dma("sync", out_v[:, :, t_ * TT:(t_ + 1) * TT],
                    sb["xs0"][:, :, :], "ST", tag=f"fout{t_}")
            p.op("sync", lambda e: e.wait_ge(p.sems["ST"], 16 * NT), inc=0)
            p.op("gpsimd", lambda e: e.wait_ge(p.sems["ST"], 16 * NT), inc=0)

        if PH < 1:
            finish_early()
            return
        PSL = {"big0": "mcols_ev", "big1": "sm128_done"}
        dqi_t1 = {}
        dqi_st = {}
        dqi_q = {}
        dqi_tp = {}
        dqn = [sb["dqn0"], sb["dqn0"]]
        dqi = 0

        DQJ = []

        def emit_dequant(wi, nat, natq, R, C):
            CB = C // P
            CG = C // 32
            natq_v = natq.rearrange("(cb q) r -> q cb r", q=P)
            for rt in range(R // P):
                DQJ.append((nat, natq_v, rt, C, CB, CG))

        def run_dq_jobs(n):
            nonlocal dqi
            for _ in range(n):
                if not DQJ:
                    return
                nat, natq_v, rt, C, CB, CG = DQJ.pop(0)
                i = dqi
                dqi += 1
                slot = i % 2
                dma("sync", dqn[slot][:, 0:C], nat[rt * P:(rt + 1) * P, :],
                    f"D{'AB'[i % 2]}", tag=f"dq_ld_{i}", deps=[dqi_t1.get(i - 1)])
                w3 = dqn[slot][:, 0:C].rearrange("q (g e) -> q g e", e=32)
                p.op("vector", lambda e, w3=w3, CG=CG: e.tensor_reduce(
                        sb["dsc"][:, 0:CG].unsqueeze(2), w3, axis=AX.X, op=ALU.max,
                        apply_absolute_value=True),
                     deps=[f"dq_ld_{i}", dqi_q.get(i - 1), dqi_tp.get(i - 1)],
                     tag=f"dq_sc_{i}")
                p.op("vector", lambda e, CG=CG: e.tensor_scalar(
                        sb["dsc"][:, 0:CG], sb["dsc"][:, 0:CG], 1.0 / 7.0, 1e-12,
                        op0=ALU.mult, op1=ALU.max),
                     deps=[f"dq_sc_{i}"], tag=f"dq_s7_{i}")
                p.op("vector", lambda e, CG=CG: e.reciprocal(sb["drc"][:, 0:CG], sb["dsc"][:, 0:CG]),
                     deps=[f"dq_s7_{i}"], tag=f"dq_rc_{i}")
                rcb = sb["drc"][:, 0:CG].unsqueeze(2).broadcast_to([P, CG, 32])
                scb = sb["dsc"][:, 0:CG].unsqueeze(2).broadcast_to([P, CG, 32])
                p.op("vector", lambda e, w3=w3, rcb=rcb, C=C: e.tensor_tensor(
                        sb["dt1"][:, 0:C].rearrange("q (g e) -> q g e", e=32), w3, rcb, op=ALU.mult),
                     deps=[f"dq_rc_{i}"], tag=f"dq_t1_{i}")
                dqi_t1[i] = f"dq_t1_{i}"
                p.op("scalar", lambda e, C=C: e.activation(
                        sb["dt1"][:, 0:C], sb["dt1"][:, 0:C], AF.Identity, bias=sb["rccol"][:, :]),
                     deps=[f"dq_t1_{i}", "rc_set"], tag=f"dq_t2_{i}")
                p.op("vector", lambda e, scb=scb, C=C: e.scalar_tensor_tensor(
                        sb["dqb"][:, 0:C].rearrange("q (g e) -> q g e", e=32),
                        sb["dt1"][:, 0:C].rearrange("q (g e) -> q g e", e=32),
                        -RC, scb, op0=ALU.add, op1=ALU.mult),
                     deps=[f"dq_t2_{i}"], tag=f"dq_q_{i}")
                dqi_q[i] = f"dq_q_{i}"
                NG = (CB + 3) // 4
                for g in range(NG):
                    gb = min(4, CB - g * 4)
                    def mk_dqtp(e, g=g, gb=gb):
                        r = None
                        for j in range(gb):
                            cb = g * 4 + j
                            r = e.transpose(ps["pxb"][:, j * P:(j + 1) * P],
                                            sb["dqb"][:, cb * P:(cb + 1) * P],
                                            sb["ident"][:, :])
                        return r
                    p.op("tensor", mk_dqtp,
                         deps=[f"dq_q_{i}", PSL.get("pxb"), "ld_consts"],
                         tag=f"dq_tp_{i}_{g}")
                    p.op("scalar", lambda e, g=g, gb=gb: e.activation(
                            sb["dst"][:, g * 4:g * 4 + gb, :],
                            ps["pxb"][:, 0:gb * P].rearrange("q (cb c) -> q cb c", c=P),
                            AF.Identity),
                         deps=[f"dq_tp_{i}_{g}", dqi_st.get(i - 1)],
                         tag=f"dq_ev_{i}_{g}")
                    PSL["pxb"] = f"dq_ev_{i}_{g}"
                dqi_tp[i] = f"dq_tp_{i}_{NG-1}"
                dma("gpsimd", natq_v[:, 0:CB, rt * P:(rt + 1) * P],
                    sb["dst"][:, 0:CB, :],
                    "SD", tag=f"dq_st_{i}", deps=[f"dq_ev_{i}_{NG-1}"])
                dqi_st[i] = f"dq_st_{i}"

        emit_dequant(0, wup, wupT_q, DFF, D)
        emit_dequant(1, wgate, wgT_q, DFF, D)
        emit_dequant(2, wdown, wdT_q, D, DFF)

        if PH < 2:
            finish_early()
            return
        def emit_ln(pre, src_tag, src_f32, xbf, sqp, tmp1, tmp2, rsb, mrsb,
                    hout, cacol, sacol, m0base, cast_deps, mod_deps):
            for kc in range(KC):
                p.op("scalar", lambda e, kc=kc: e.activation(
                        xbf[:, kc, :], src_f32[:, kc, :], AF.Identity),
                     deps=[src_tag] + (cast_deps if kc == 0 else []),
                     tag=f"{pre}cast{kc}")
                p.op("vector", lambda e, kc=kc: e.tensor_mul(
                        sqp[kc % 2][:, :], xbf[:, kc, :], xbf[:, kc, :]),
                     deps=[f"{pre}cast{kc}", T.get(f"{pre}stats{kc-2}")],
                     tag=f"{pre}sq{kc}")
                def mk_stats(e, kc=kc):
                    e.matmul(ps["big0"][0:1, :], sb["onescol"][:, :], xbf[:, kc, :],
                             start=(kc == 0), stop=(kc == KC - 1))
                    return e.matmul(ps["big1"][0:1, :], sb["onescol"][:, :], sqp[kc % 2][:, :],
                                    start=(kc == 0), stop=(kc == KC - 1))
                dd = [f"{pre}sq{kc}", "ld_consts"]
                if kc == 0:
                    dd += [PSL.get("big0"), PSL.get("big1")]
                p.op("tensor", mk_stats, deps=dd, tag=f"{pre}stats{kc}")
                T[f"{pre}stats{kc}"] = f"{pre}stats{kc}"
            p.settag(f"{pre}stats", f"{pre}stats{KC-1}")
            def mk_fin1(e):
                e.activation(sb["row_a"][:, :], ps["big0"][0:1, :], AF.Identity, scale=1.0 / D)
                return e.activation(sb["row_b"][:, :], ps["big1"][0:1, :], AF.Identity, scale=1.0 / D)
            p.op("scalar", mk_fin1, deps=[f"{pre}stats"], tag=f"{pre}fin1")
            p.op("vector", lambda e: e.tensor_mul(sb["row_m"][:, :], sb["row_a"][:, :], sb["row_a"][:, :]),
                 deps=[f"{pre}fin1"], tag=f"{pre}msq")
            p.op("vector", lambda e: e.tensor_sub(sb["row_b"][:, :], sb["row_b"][:, :], sb["row_m"][:, :]),
                 deps=[f"{pre}msq"], tag=f"{pre}var")
            p.op("scalar", lambda e: e.activation(sb["row_r"][:, :], sb["row_b"][:, :], AF.Ln, bias=sb["epscol"][0:1, :]),
                 deps=[f"{pre}var"], tag=f"{pre}rstd0")
            p.op("scalar", lambda e: e.activation(sb["row_r"][:, :], sb["row_r"][:, :], AF.Exp, scale=-0.5),
                 deps=[f"{pre}rstd0"], tag=f"{pre}rstd")
            p.op("vector", lambda e: e.tensor_mul(sb["row_m"][:, :], sb["row_a"][:, :], sb["row_r"][:, :]),
                 deps=[f"{pre}rstd"], tag=f"{pre}mr")
            def mk_bc(e):
                e.matmul(ps["big0"][:, :], sb["onesrow"][:, :], sb["row_r"][:, :], start=True, stop=True)
                return e.matmul(ps["big1"][:, :], sb["onesrow"][:, :], sb["row_m"][:, :], start=True, stop=True)
            p.op("tensor", mk_bc, deps=[f"{pre}mr", f"{pre}fin1"], tag=f"{pre}bc")
            def mk_bce(e):
                e.activation(rsb[:, :], ps["big0"][:, :], AF.Identity)
                return e.activation(mrsb[:, :], ps["big1"][:, :], AF.Identity)
            p.op("scalar", mk_bce, deps=[f"{pre}bc"], tag=f"{pre}bce")
            PSL["big0"] = f"{pre}bce"
            PSL["big1"] = f"{pre}bce"
            for kc in range(KC):
                dd = [f"{pre}bce"]
                if kc > 0:
                    dd.append(f"{pre}smod{kc-1}")
                elif mod_deps:
                    dd += mod_deps
                p.op("vector", lambda e, kc=kc: e.tensor_mul(tmp1[:, :], xbf[:, kc, :], rsb[:, :]),
                     deps=dd, tag=f"{pre}t1_{kc}")
                p.op("vector", lambda e: e.tensor_sub(tmp2[:, :], tmp1[:, :], mrsb[:, :]),
                     deps=[f"{pre}t1_{kc}"], tag=f"{pre}t2_{kc}")
                p.op("scalar", lambda e, kc=kc: e.activation(
                        hout[:, kc, :], tmp2[:, :], AF.Identity,
                        scale=cacol[:, kc:kc + 1], bias=sacol[:, kc:kc + 1]),
                     deps=[f"{pre}t2_{kc}", "modcols_done"], tag=f"{pre}smod{kc}")
                if m0base is not None:
                    p.op("vector", lambda e, kc=kc: e.tensor_reduce(
                            sb["m0slots"][:, m0base + kc:m0base + kc + 1], hout[:, kc, :],
                            axis=AX.X, op=ALU.max, apply_absolute_value=True),
                         deps=[f"{pre}smod{kc}"], tag=f"{pre}m0_{kc}")
            return f"{pre}smod{KC-1}"

        p.op("gpsimd", lambda e: e.dma_start(
                out=sb["wk_sb"][:, :, :], in_=wkT.rearrange("(kc q) f -> q kc f", q=P)),
             inc=16, sem="LW")
        p.op("gpsimd", lambda e: e.dma_start(
                out=sb["wv_sb"][:, :, :], in_=wvT.rearrange("(kc q) f -> q kc f", q=P)),
             inc=16, sem="LW", tag="ld_wkv")
        xs = [sb["xs0"], sb["xs1"]]
        KQLAST = [None, None]
        for t in range(NT):
            sl = t % 2
            dma("sync", xs[sl][:, :, :], xT_v[:, :, t * TT:(t + 1) * TT],
                f"X{sl}", tag=f"axld{t}", deps=[T.get(f"ax_free{t-2}")])
            lnt = emit_ln(
                f"aln{t}", f"axld{t}", xs[sl], sb["xbf"], [sb["sq0"], sb["sq1"]],
                sb["tmp1"], sb["tmp2"],
                sb["rsb"], sb["mrsb"], sb["h_sb"], sb["ca1"], sb["mcols"][:, 0, :],
                None,
                cast_deps=[T.get(f"aln_xbf_free{t-1}")],
                mod_deps=[T.get(f"vspill{t-1}"), T.get(f"h_used{t-1}")])
            T[f"ax_free{t}"] = f"aln{t}cast{KC-1}"
            T[f"aln_xbf_free{t}"] = f"aln{t}t1_{KC-1}"
            dma("gpsimd", hbh_v[:, :, t * TT:(t + 1) * TT], sb["h_sb"][:, :, :],
                "SPA", tag=f"hspill{t}", deps=[lnt])
            T[f"hspill{t}"] = f"hspill{t}"
            for tb in range(NCH):
                for vh in range(2):
                    bs = f"big{vh}"
                    def mk_v(e, tb=tb, vh=vh, bs=bs):
                        r = None
                        for kc in range(KC):
                            r = e.matmul(ps[bs][:, :],
                                         sb["h_sb"][:, kc, tb * CHN:(tb + 1) * CHN],
                                         sb["wv_sb"][:, kc, vh * TT:(vh + 1) * TT],
                                         start=(kc == 0), stop=(kc == KC - 1))
                        return r
                    p.op("tensor", mk_v, deps=[lnt, "ld_wkv", PSL.get(bs)],
                         tag=f"vmm{t}_{tb}_{vh}")
                    p.op("scalar", lambda e, tb=tb, vh=vh, bs=bs: e.activation(
                            sb["v_sb"][:, tb, vh * TT:(vh + 1) * TT], ps[bs][:, :], AF.Identity),
                         deps=[f"vmm{t}_{tb}_{vh}", T.get(f"hspill{t}"), T.get(f"v_used{t-1}")],
                         tag=f"vev{t}_{tb}_{vh}")
                    PSL[bs] = f"vev{t}_{tb}_{vh}"
            for h_ in range(H):
                ks = h_ % 2
                def mk_k(e, h_=h_, ks=ks):
                    r = None
                    for kc in range(KC):
                        r = e.matmul(ps[f"big{ks}"][:, :],
                                     sb["wk_sb"][:, kc, h_ * P:(h_ + 1) * P],
                                     sb["h_sb"][:, kc, :],
                                     start=(kc == 0), stop=(kc == KC - 1))
                    return r
                p.op("tensor", mk_k, deps=[lnt, "ld_wkv", PSL.get(f"big{ks}")],
                     tag=f"kmm{t}_{h_}")
                p.op("scalar", lambda e, ks=ks, bs=f"big{ks}": e.activation(
                        sb[f"kq{ks}"][:, :], ps[bs][:, :], AF.Identity),
                     deps=[f"kmm{t}_{h_}", KQLAST[ks]],
                     tag=f"kev{t}_{h_}")
                p.op("tensor", lambda e, ks=ks: e.matmul(
                        ps[f"big{ks}"][:, :], sb["fmkT"][:, :], sb[f"kq{ks}"][:, :],
                        start=True, stop=True),
                     deps=[f"kev{t}_{h_}", "ld_wkv"], tag=f"kymm{t}_{h_}")
                KQLAST[ks] = f"kymm{t}_{h_}"
                def mk_kexp(e, h_=h_, ks=ks):
                    e.activation(sb["kfm_sb"][:, 2 * h_, :], ps[f"big{ks}"][:, :], AF.Exp,
                                 bias=sb["fmkb"][:, :])
                    return e.activation(sb["kfm_sb"][:, 2 * h_ + 1, :], ps[f"big{ks}"][:, :], AF.Exp,
                                        bias=sb["bkn"][:, :], scale=-1.0)
                p.op("scalar", mk_kexp,
                     deps=[f"kymm{t}_{h_}", "bias_cols", T.get(f"hspill{t}"),
                           T.get(f"kfm_used{t-1}")],
                     tag=f"kexp{t}_{h_}")
                PSL[f"big{ks}"] = f"kexp{t}_{h_}"
                for dc in range(2):
                    def mk_ktp(e, h_=h_, dc=dc):
                        r = None
                        for tb in range(NCH):
                            r = e.transpose(
                                ps["pxa"][:, tb * P:(tb + 1) * P],
                                sb["kfm_sb"][:, 2 * h_ + dc, tb * CHN:(tb + 1) * CHN],
                                sb["ident"][:, :])
                        return r
                    p.op("tensor", mk_ktp,
                         deps=[f"kexp{t}_{h_}", T.get("pxa_ev"), "ld_consts"],
                         tag=f"ktp{t}_{h_}_{dc}")
                    p.op("vector", lambda e, h_=h_, dc=dc: e.tensor_copy(
                            sb["ktm_sb"][:, 0:NCH, h_ * DF + dc * P:h_ * DF + (dc + 1) * P],
                            ps["pxa"][:, 0:TT].rearrange("q (tb c) -> q tb c", c=P)),
                         deps=[f"ktp{t}_{h_}_{dc}", T.get(f"hspill{t}"),
                               T.get(f"ktm_used{t-1}")],
                         tag=f"ktev{t}_{h_}_{dc}")
                    T["pxa_ev"] = f"ktev{t}_{h_}_{dc}"
                def mk_sacc(e, t=t, h_=h_):
                    r = None
                    for ch in range(NCH):
                        for dc in range(2):
                            st_ = (t == 0 and ch == 0 and dc == 0 and h_ % 2 == 0)
                            r = e.matmul(S_ap(h_, dc),
                                         sb["ktm_sb"][:, ch, h_ * DF + dc * P:h_ * DF + (dc + 1) * P],
                                         sb["v_sb"][:, ch, h_ * P:(h_ + 1) * P],
                                         start=st_, stop=False, skip_group_check=True)
                    return r
                p.op("tensor", mk_sacc,
                     deps=[f"ktev{t}_{h_}_1", f"vev{t}_{NCH-1}_1"],
                     tag=f"sacc{t}_{h_}")
            T[f"h_used{t}"] = f"kmm{t}_{H-1}"
            T[f"v_used{t}"] = f"sacc{t}_{H-1}"
            T[f"ktm_used{t}"] = f"sacc{t}_{H-1}"
            T[f"kfm_used{t}"] = f"ktp{t}_{H-1}_1"
            dma("gpsimd", kfm_v[:, :, t * TT:(t + 1) * TT], sb["kfm_sb"][:, :, :],
                "SPA", tag=f"kfmspill{t}",
                deps=[f"kexp{t}_{H-1}", f"ktp{t}_{H-1}_1"])
            T[f"kfmspill{t}"] = f"kfmspill{t}"
            dma("gpsimd", ktm_v[:, t * NCH:(t + 1) * NCH, :], sb["ktm_sb"][:, :, :],
                "SPA", tag=f"ktmspill{t}", deps=[f"sacc{t}_{H-1}"])
            T[f"ktmspill{t}"] = f"ktmspill{t}"
            dma("gpsimd", v_v[:, t * NCH:(t + 1) * NCH, :], sb["v_sb"][:, :, :],
                "SPA", tag=f"vspill{t}", deps=[f"sacc{t}_{H-1}"])
            T[f"vspill{t}"] = f"vspill{t}"
            run_dq_jobs(7)

        run_dq_jobs(10 ** 9)
        T["dq_done"] = f"dq_st_{dqi - 1}"

        def mk_sev(e):
            r = None
            for h_ in range(H):
                for dc in range(2):
                    r = e.activation(sb["Ssb"][:, (h_ * 2 + dc) * P:(h_ * 2 + dc + 1) * P],
                                     S_ap(h_, dc), AF.Identity)
            return r
        p.op("scalar", mk_sev, deps=[f"sacc{NT-1}_{H-1}"], tag="s_ev")
        dma("gpsimd", sc_in, sb["Ssb"][:, :], "SPA", tag="s_stage", deps=["s_ev"])
        p.op("gpsimd", lambda e: e.collective_compute(
                "AllGather", ALU.bypass, replica_groups=[list(range(8))],
                ins=[sc_in.opt()], outs=[sc_out.opt()]),
             deps=["s_stage"], sem="CC", tag="cc_sg")

        p.barrier("p2a", sb)

        if PH < 3:
            finish_early()
            return
        gS = sb["gS0"]
        SFIN = sb["Sfin"]
        for r in range(8):
            dma("gpsimd", gS[:, :], sc_out[r * P:(r + 1) * P, :],
                "SG", tag=f"gsld{r}", deps=["cc_sg", T.get(f"sin{r-1}")])
            if r == 0:
                p.op("vector", lambda e: e.tensor_scalar(
                        SFIN[:, :], gS[:, :], sb["sm128"][:, 0:1], None, op0=ALU.mult),
                     deps=[f"gsld{r}", "sm128_done"], tag=f"sin{r}")
            else:
                p.op("vector", lambda e, r=r: e.scalar_tensor_tensor(
                        SFIN[:, :], gS[:, :], sb["sm128"][:, r:r + 1], SFIN[:, :],
                        op0=ALU.mult, op1=ALU.add),
                     deps=[f"gsld{r}", f"sin{r-1}"], tag=f"sin{r}")
            T[f"sin{r}"] = f"sin{r}"

        p.op("gpsimd", lambda e: e.dma_start(
                out=sb["wq_sb"][:, :, :], in_=wqT.rearrange("(kc q) f -> q kc f", q=P)),
             inc=16, sem="LW")
        p.op("gpsimd", lambda e: e.dma_start(
                out=sb["wo_sb"][:, :, :], in_=woT.rearrange("(kc q) f -> q kc f", q=P)),
             inc=16, sem="LW", tag="ld_wqo")
        def mk_worms(e):
            r = None
            for kc in range(KC):
                r = e.tensor_scalar(sb["wo_sb"][:, kc, :], sb["wo_sb"][:, kc, :],
                                    sb["rmsw"][:, :], None, op0=ALU.mult)
            return r
        p.op("vector", mk_worms, deps=["ld_wqo", "ld_consts"], tag="wo_rms")

        cxs = [sb["cxs0"], sb["cxs0"]]
        QQLAST = [None, None]
        psA_at = ps["big0"][:, 0:P]
        psA_o = ps["big0"][:, P:2 * P]
        for t in range(NT):
            sl = t % 2
            dma("sync", cxs[sl][:, :, :], xT_v[:, :, t * TT:(t + 1) * TT],
                f"X{sl}", tag=f"cxld{t}", deps=[T.get(f"cx_free{t-1}")])
            dma("sync", sb["h_r"][:, :, :], hbh_v[:, :, t * TT:(t + 1) * TT],
                f"H{sl}", tag=f"chld{t}",
                deps=["s_stage", T.get(f"hr_used{t-1}")])
            dma("gpsimd", sb["kfm_r"][:, :, :], kfm_v[:, :, t * TT:(t + 1) * TT],
                f"K{sl}", tag=f"ckfld{t}",
                deps=["s_stage", T.get(f"kfmr_used{t-1}")])
            dma("gpsimd", sb["ktm_r"][:, :, :], ktm_v[:, t * NCH:(t + 1) * NCH, :],
                f"K{sl}", tag=f"cktld{t}",
                deps=[T.get(f"ktmr_used{t-1}")])
            dma("gpsimd", sb["v_r"][:, :, :], v_v[:, t * NCH:(t + 1) * NCH, :],
                f"K{sl}", tag=f"cvld{t}",
                deps=[T.get(f"vr_used{t-1}")])
            for h_ in range(H):
                ks = h_ % 2
                def mk_q(e, h_=h_, ks=ks):
                    r = None
                    for kc in range(KC):
                        r = e.matmul(ps[f"big{ks}"][:, :],
                                     sb["wq_sb"][:, kc, h_ * P:(h_ + 1) * P],
                                     sb["h_r"][:, kc, :],
                                     start=(kc == 0), stop=(kc == KC - 1))
                    return r
                p.op("tensor", mk_q, deps=[f"chld{t}", "ld_wqo", PSL.get(f"big{ks}")],
                     tag=f"qmm{t}_{h_}")
                p.op("scalar", lambda e, ks=ks, bs=f"big{ks}": e.activation(
                        sb[f"qq{ks}"][:, :], ps[bs][:, :], AF.Identity),
                     deps=[f"qmm{t}_{h_}", QQLAST[ks]],
                     tag=f"qev{t}_{h_}")
                p.op("tensor", lambda e, ks=ks: e.matmul(
                        ps[f"big{ks}"][:, :], sb["fmqT"][:, :], sb[f"qq{ks}"][:, :],
                        start=True, stop=True),
                     deps=[f"qev{t}_{h_}", "ld_wqo"], tag=f"qymm{t}_{h_}")
                QQLAST[ks] = f"qymm{t}_{h_}"
                def mk_qexp(e, h_=h_, ks=ks):
                    e.activation(sb["qfm_sb"][:, 2 * h_, :], ps[f"big{ks}"][:, :], AF.Exp,
                                 bias=sb["bqp"][:, :])
                    return e.activation(sb["qfm_sb"][:, 2 * h_ + 1, :], ps[f"big{ks}"][:, :], AF.Exp,
                                        bias=sb["bqn"][:, :], scale=-1.0)
                p.op("scalar", mk_qexp,
                     deps=[f"qymm{t}_{h_}", "bias_cols", T.get(f"qfm_used{t-1}")],
                     tag=f"qexp{t}_{h_}")
                PSL[f"big{ks}"] = f"qexp{t}_{h_}"
            T[f"hr_used{t}"] = f"qmm{t}_{H-1}"
            psA = [ps["big0"], ps["big1"]]
            for ch in range(NCH):
                for h_ in range(H):
                    idx = ch * H + h_
                    sl2 = idx % 2
                    hs = h_ % 2
                    pat = psA[sl2][:, 0:P]
                    po = psA[sl2][:, P:2 * P]
                    cds = f"{t}_{ch}_{h_}"
                    if t == 0 and ch == 0:
                        def mk_snap(e, h_=h_, hs=hs):
                            r = None
                            for dc in range(2):
                                r = e.tensor_copy(sb["Ssnap"][:, hs, dc * P:(dc + 1) * P],
                                                  SFIN[:, (h_ * 2 + dc) * P:(h_ * 2 + dc + 1) * P])
                            return r
                        p.op("vector", mk_snap,
                             deps=["sin7", T.get(f"snapfree{hs}")],
                             tag=f"snap{cds}")
                    else:
                        def mk_snap(e, h_=h_, hs=hs):
                            r = None
                            for dc in range(2):
                                r = e.tensor_add(sb["Ssnap"][:, hs, dc * P:(dc + 1) * P],
                                                 S_ap(h_, dc),
                                                 SFIN[:, (h_ * 2 + dc) * P:(h_ * 2 + dc + 1) * P])
                            return r
                        p.op("vector", mk_snap,
                             deps=["sin7", T.get(f"supd_bank{h_ // 2}"),
                                   T.get(f"snapfree{hs}")],
                             tag=f"snap{cds}")
                    def mk_at(e, t=t, ch=ch, h_=h_, pat=pat):
                        r = None
                        for dc in range(2):
                            r = e.matmul(pat,
                                         sb["kfm_r"][:, 2 * h_ + dc, ch * CHN:(ch + 1) * CHN],
                                         sb["qfm_sb"][:, 2 * h_ + dc, ch * CHN:(ch + 1) * CHN],
                                         start=(dc == 0), stop=(dc == 1), skip_group_check=True)
                        return r
                    p.op("tensor", mk_at,
                         deps=[f"qexp{t}_{h_}", f"cvld{t}", T.get(f"psA_park{sl2}"),
                               T.get(f"psA_atev{sl2}"), PSL.get(f"big{sl2}")],
                         tag=f"at{cds}")
                    PSL[f"big{sl2}"] = f"at{cds}"
                    p.op("vector", lambda e, hs=hs, pat=pat: e.tensor_tensor(
                            sb["at_sb"][:, hs, :], pat, sb["triu"][:, :], op=ALU.mult),
                         deps=[f"at{cds}", T.get(f"atfree{hs}")],
                         tag=f"atev{cds}")
                    T[f"psA_atev{sl2}"] = f"atev{cds}"
                    def mk_o(e, t=t, ch=ch, h_=h_, hs=hs, po=po):
                        e.matmul(po, sb["at_sb"][:, hs, :],
                                 sb["v_r"][:, ch, h_ * P:(h_ + 1) * P],
                                 start=False, stop=False, skip_group_check=True)
                        r = None
                        for dc in range(2):
                            r = e.matmul(po,
                                         sb["qfm_sb"][:, 2 * h_ + dc, ch * CHN:(ch + 1) * CHN],
                                         sb["Ssnap"][:, hs, dc * P:(dc + 1) * P],
                                         start=False, stop=(dc == 1), skip_group_check=True)
                        return r
                    p.op("tensor", mk_o,
                         deps=[f"atev{cds}", f"snap{cds}", f"cvld{t}",
                               T.get(f"psA_park{sl2}")],
                         tag=f"omm{cds}")
                    T[f"atfree{hs}"] = f"omm{cds}"
                    T[f"snapfree{hs}"] = f"omm{cds}"
                    p.op("scalar", lambda e, idx=idx, po=po: e.activation(
                            sb["orow"][:, idx, :], po, AF.Identity),
                         deps=[f"omm{cds}", T.get(f"orow_used{t-1}_{idx}")],
                         tag=f"opark{cds}")
                    T[f"psA_park{sl2}"] = f"opark{cds}"
                    p.op("scalar", lambda e, idx=idx: e.activation(
                            sb["o2_sb"][:, :], sb["orow"][:, idx, :], AF.Square,
                            scale=2.0 ** -24),
                         deps=[f"opark{cds}", T.get("osq_prev")], tag=f"osqa{cds}")
                    p.op("vector", lambda e, idx=idx: e.tensor_reduce(
                            sb["rms32"][:, idx:idx + 1], sb["o2_sb"][:, :],
                            axis=AX.X, op=ALU.add),
                         deps=[f"osqa{cds}"], tag=f"osq{cds}")
                    T["osq_prev"] = f"osq{cds}"
                    def mk_su(e, t=t, ch=ch, h_=h_):
                        r = None
                        for dc in range(2):
                            st_ = (t == 0 and ch == 0 and dc == 0 and h_ % 2 == 0)
                            r = e.matmul(S_ap(h_, dc),
                                         sb["ktm_r"][:, ch, h_ * DF + dc * P:h_ * DF + (dc + 1) * P],
                                         sb["v_r"][:, ch, h_ * P:(h_ + 1) * P],
                                         start=st_, stop=False, skip_group_check=True)
                        return r
                    p.op("tensor", mk_su,
                         deps=[f"snap{cds}", f"cvld{t}"],
                         tag=f"supd{cds}")
                    T[f"supd_bank{h_ // 2}"] = f"supd{cds}"
            p.op("scalar", lambda e: e.activation(sb["rms32"][:, :], sb["rms32"][:, :], AF.Sqrt,
                                                  bias=sb["c2ecol"][:, :]),
                 deps=[f"osq{t}_{NCH-1}_{H-1}", "c2e_set", T.get(f"rms32_used{t-1}")],
                 tag=f"rchain1_{t}")
            p.op("scalar", lambda e: e.activation(sb["rms32"][:, :], sb["rms32"][:, :], AF.Sqrt),
                 deps=[f"rchain1_{t}"], tag=f"rchain2_{t}")
            p.op("vector", lambda e: e.reciprocal(sb["rms32"][:, :], sb["rms32"][:, :]),
                 deps=[f"rchain2_{t}"], tag=f"rchain3_{t}")
            p.op("scalar", lambda e: e.activation(sb["rms32"][:, :], sb["rms32"][:, :], AF.Square,
                                                  scale=(128.0 ** 0.25) * (2.0 ** -12)),
                 deps=[f"rchain3_{t}"], tag=f"rchain_{t}")
            pxs = [ps["pxa"], ps["pxb"]]
            for idx in range(32):
                ch, h_ = divmod(idx, H)
                cds = f"{t}_{ch}_{h_}"
                p.op("vector", lambda e, idx=idx: e.tensor_scalar(
                        sb["orow"][:, idx, :], sb["orow"][:, idx, :],
                        sb["rms32"][:, idx:idx + 1], None, op0=ALU.mult),
                     deps=[f"rchain_{t}"], tag=f"onorm{cds}")
                px = pxs[idx % 2]
                p.op("tensor", lambda e, idx=idx, px=px: e.transpose(
                        px[:, 0:P], sb["orow"][:, idx, :], sb["ident"][:, :]),
                     deps=[f"onorm{cds}", T.get(f"px_ev{idx % 2}")],
                     tag=f"otp{cds}")
                p.op("vector", lambda e, h_=h_, ch=ch, px=px: e.tensor_copy(
                        sb["o_sb"][:, h_, ch * CHN:(ch + 1) * CHN], px[:, 0:P]),
                     deps=[f"otp{cds}", T.get(f"o_used{t-1}")],
                     tag=f"oev{cds}")
                T[f"px_ev{idx % 2}"] = f"oev{cds}"
                T[f"orow_used{t}_{idx}"] = f"otp{cds}"
            T[f"rms32_used{t}"] = f"onorm{t}_{NCH-1}_{H-1}"
            for fo in range(KC):
                bs = f"big{fo % 2}"
                def mk_wo(e, fo=fo, bs=bs):
                    r = None
                    for kc in range(KC):
                        r = e.matmul(ps[bs][:, :],
                                     sb["wo_sb"][:, kc, fo * P:(fo + 1) * P],
                                     sb["o_sb"][:, kc, :],
                                     start=(kc == 0), stop=(kc == KC - 1))
                    return r
                p.op("tensor", mk_wo,
                     deps=[f"oev{t}_{NCH-1}_{H-1}", "wo_rms", PSL.get(bs),
                           T.get(f"psA_park{fo % 2}"), T.get(f"psA_atev{fo % 2}")],
                     tag=f"wo{t}_{fo}")
                p.op("scalar", lambda e, fo=fo, bs=bs: e.activation(
                        sb["ep_sb"][:, :], ps[bs][:, :], AF.Identity,
                        scale=sb["mcols"][:, 2, fo:fo + 1]),
                     deps=[f"wo{t}_{fo}", "modcols_done", T.get("ep_used")],
                     tag=f"ep1{t}_{fo}")
                PSL[bs] = f"ep1{t}_{fo}"
                p.op("vector", lambda e, fo=fo, sl=sl: e.tensor_add(
                        sb["x2_sb"][:, fo, :], cxs[sl][:, fo, :], sb["ep_sb"][:, :]),
                     deps=[f"ep1{t}_{fo}", f"cxld{t}", T.get(f"x2_free{t-1}")],
                     tag=f"x2{t}_{fo}")
                T["ep_used"] = f"x2{t}_{fo}"
            T[f"o_used{t}"] = f"wo{t}_{KC-1}"
            T[f"cx_free{t}"] = f"x2{t}_{KC-1}"
            dma("gpsimd", x2h_v[:, :, t * TT:(t + 1) * TT], sb["x2_sb"][:, :, :],
                "SPB", tag=f"x2sp{t}", deps=[f"x2{t}_{KC-1}"])
            ln2t = emit_ln(
                f"cln{t}", f"x2{t}_{KC-1}", sb["x2_sb"], sb["cxbf"],
                [sb["csq0"], sb["csq1"]], sb["ctmp1"],
                sb["ctmp2"], sb["crsb"], sb["cmrsb"], sb["h2_sb"], sb["cb1"],
                sb["mcols"][:, 3, :], t * KC,
                cast_deps=[T.get(f"cln_xbf_free{t-1}")],
                mod_deps=[f"x2sp{t}"])
            T[f"x2_free{t}"] = f"cln{t}cast{KC-1}"
            T[f"cln_xbf_free{t}"] = f"cln{t}t1_{KC-1}"
            dma("gpsimd", h2h_v[:, :, t * TT:(t + 1) * TT], sb["h2_sb"][:, :, :],
                "SPB", tag=f"h2sp{t}", deps=[ln2t])
            T[f"h2sp{t}"] = f"h2sp{t}"

        p.barrier("p2c", sb)

        MB = {"prev_stage": None, "prev_ld": None}

        def mbarrier(name, folds, dep):
            for j, (sn, lo, hi) in enumerate(folds):
                p.op("vector", lambda e, sn=sn, lo=lo, hi=hi, j=j: e.tensor_reduce(
                        sb["mxst"][:, j:j + 1], sb[sn][:, lo:hi], axis=AX.X, op=ALU.max),
                     deps=[dep, MB["prev_stage"]], tag=f"{name}_f{j}")
            if len(folds) == 1:
                p.op("vector", lambda e: e.memset(sb["mxst"][:, 1:2], 0.0),
                     deps=[MB["prev_stage"]], tag=f"{name}_f1")
            dma("gpsimd", mx_in, sb["mxst"][:, :], "SG", tag=f"{name}_stage",
                deps=[f"{name}_f0", f"{name}_f1"])
            MB["prev_stage"] = f"{name}_stage"
            p.op("gpsimd", lambda e: e.collective_compute(
                    "AllReduce", ALU.max, replica_groups=[list(range(8))],
                    ins=[mx_in.opt()], outs=[mx_out.opt()]),
                 deps=[f"{name}_stage"], sem="CC", tag=f"{name}_cc")
            dma("gpsimd", sb["mxrow"][:, :], mx_out, "SG", tag=f"{name}_ld",
                deps=[f"{name}_cc", MB["prev_ld"]])
            p.op("vector", lambda e: e.tensor_reduce(
                    sb["mxfold"][:, :].unsqueeze(2),
                    sb["mxrow"][:, :].rearrange("a (q c) -> a c q", c=2),
                    axis=AX.X, op=ALU.max),
                 deps=[f"{name}_ld"], tag=f"{name}_max")
            MB["prev_ld"] = f"{name}_max"

        def bcast_col(name, src_ap, dep, tg):
            p.op("tensor", lambda e: e.matmul(ps["big0"][:, 0:1], sb["onesrow"][:, :],
                                              src_ap, start=True, stop=True),
                 deps=[dep, PSL.get("big0"), "ld_consts"], tag=f"bcm_{tg}")
            p.op("scalar", lambda e: e.activation(sb[name][:, :], ps["big0"][:, 0:1], AF.Identity),
                 deps=[f"bcm_{tg}"], tag=f"bce_{tg}")
            PSL["big0"] = f"bce_{tg}"
            return f"bce_{tg}"

        if PH < 4:
            finish_early()
            return
        mbarrier("m0", [("m0slots", 0, 64)], f"cln{NT-1}m0_{KC-1}")
        p.op("vector", lambda e: e.tensor_scalar(sb["srow"][:, 0:1], sb["mxfold"][:, 0:1],
                                                 1.0 / QMAX, None, op0=ALU.mult),
             deps=["m0_max"], tag="s0row")
        p.op("vector", lambda e: e.reciprocal(sb["srow"][:, 1:2], sb["srow"][:, 0:1]),
             deps=["s0row"], tag="r0row")
        bcast_col("s0c", sb["srow"][0:1, 0:1], "s0row", "s0c")
        bcast_col("r0c", sb["srow"][0:1, 1:2], "r0row", "r0c")

        if PH < 5:
            finish_early()
            return
        dma("sync", sb["wup_sb"][:, :, :], wupT_q.rearrange("(kc q) f -> q kc f", q=P),
            "LF", tag="ld_wup", deps=[T["dq_done"]])
        dma("sync", sb["wg_sb"][:, :, :], wgT_q.rearrange("(kc q) f -> q kc f", q=P),
            "LF", tag="ld_wg", deps=[T["dq_done"]])
        h2r = [sb["h2r0"], sb["h2r1"]]
        xqt = [sb["xqt0"], sb["xqt1"]]
        for t in range(NT):
            sl = t % 2
            dma("sync", h2r[sl][:, :, :], h2h_v[:, :, t * TT:(t + 1) * TT],
                f"H{sl}", tag=f"bh2ld{t}",
                deps=[f"h2sp{NT-1}", T.get(f"bh2_free{t-2}")])
            for kc in range(KC):
                ts_ = kc % 2
                p.op("scalar", lambda e, sl=sl, kc=kc, ts_=ts_: e.activation(
                        xqt[ts_][:, :], h2r[sl][:, kc, :], AF.Identity,
                        scale=sb["r0c"][:, :], bias=sb["rccol"][:, :]),
                     deps=[f"bh2ld{t}", "bce_r0c",
                           (f"bxq{t}_{kc-2}" if kc >= 2 else
                            (f"bxq{t-1}_{KC-2+kc}" if t >= 1 else None))],
                     tag=f"bxqt{t}_{kc}")
                p.op("vector", lambda e, kc=kc, ts_=ts_: e.tensor_scalar(
                        sb["xq_sb"][:, kc, :], xqt[ts_][:, :], -RC, sb["s0c"][:, :],
                        op0=ALU.add, op1=ALU.mult),
                     deps=[f"bxqt{t}_{kc}", "bce_s0c", T.get(f"xq_used{t-1}")],
                     tag=f"bxq{t}_{kc}")
            T[f"bh2_free{t}"] = f"bxqt{t}_{KC-1}"
            for which, (wname, stname, slots, ldt, dsth) in enumerate(
                    [("wup_sb", "Ust", "m1slots", "ld_wup", Uh),
                     ("wg_sb", "Gst", "m2slots", "ld_wg", Gh)]):
                for ib in range(NI):
                    bs = f"big{ib % 2}"
                    def mk_u(e, wname=wname, ib=ib, bs=bs):
                        r = None
                        for kc in range(KC):
                            r = e.matmul(ps[bs][:, :], sb[wname][:, kc, ib * P:(ib + 1) * P],
                                         sb["xq_sb"][:, kc, :],
                                         start=(kc == 0), stop=(kc == KC - 1))
                        return r
                    p.op("tensor", mk_u,
                         deps=[f"bxq{t}_{KC-1}", "ld_wg", PSL.get(bs),
                               PSL.get(bs + "_mx")],
                         tag=f"bmm{t}_{which}_{ib}")
                    p.op("scalar", lambda e, stname=stname, ib=ib, bs=bs: e.activation(
                            sb[stname][:, ib, :], ps[bs][:, :], AF.Identity),
                         deps=[f"bmm{t}_{which}_{ib}", T.get(f"bst{t-1}_1")],
                         tag=f"bev{t}_{which}_{ib}")
                    p.op("vector", lambda e, slots=slots, stname=stname, ib=ib, t=t: e.tensor_reduce(
                            sb[slots][:, t * NI + ib:t * NI + ib + 1],
                            sb[stname][:, ib, :], axis=AX.X, op=ALU.max,
                            apply_absolute_value=True),
                         deps=[f"bev{t}_{which}_{ib}"],
                         tag=f"bmx{t}_{which}_{ib}")
                    PSL[bs] = f"bev{t}_{which}_{ib}"
                    PSL[bs + "_mx"] = f"bmx{t}_{which}_{ib}"
                dma("gpsimd", dsth[:, t * NI * TT:(t + 1) * NI * TT],
                    sb[stname][:, :, :], "SC1", tag=f"bst{t}_{which}",
                    deps=[f"bev{t}_{which}_{NI-1}", f"bmx{t}_{which}_{NI-1}"])
                T[f"bst{t}_{which}"] = f"bst{t}_{which}"
            T[f"xq_used{t}"] = f"bmm{t}_1_{NI-1}"

        if PH < 6:
            finish_early()
            return
        mbarrier("m12", [("m1slots", 0, NT * NI), ("m2slots", 0, NT * NI)],
                 f"bmx{NT-1}_1_{NI-1}")
        p.op("vector", lambda e: e.tensor_scalar(sb["srow"][:, 2:4], sb["mxfold"][:, 0:2],
                                                 1.0 / QMAX, None, op0=ALU.mult),
             deps=["m12_max"], tag="s12row")
        p.op("vector", lambda e: e.reciprocal(sb["srow"][:, 4:6], sb["srow"][:, 2:4]),
             deps=["s12row"], tag="r12row")
        bcast_col("r1c", sb["srow"][0:1, 4:5], "r12row", "r1c")
        bcast_col("r2c", sb["srow"][0:1, 5:6], "r12row", "r2c")
        bcast_col("s2c", sb["srow"][0:1, 3:4], "s12row", "s2c")
        p.op("vector", lambda e: e.tensor_scalar(sb["brc2c"][:, :], sb["s2c"][:, :],
                                                 -RC, None, op0=ALU.mult),
             deps=["bce_s2c"], tag="brc2_done")

        dma("sync", sb["wd_sb"][:, :, :], wdT_q.rearrange("(ic q) f -> q ic f", q=P),
            "LF", tag="ld_wd", deps=[T["dq_done"], f"bst{NT-1}_1"])
        gld = [sb["bufA0"], sb["bufA1"]]
        t1f = [sb["t1f0"], sb["t1f1"]]
        sgf = [sb["sgf0"], sb["sgf1"]]
        for t in range(NT):
            sl = t % 2
            dma("gpsimd", gld[sl][:, :, :], Gh[:, t * NI * TT:(t + 1) * NI * TT],
                f"U{sl}", tag=f"c1ld{t}",
                deps=[f"bst{NT-1}_1", T.get(f"c1_free{t-2}")])
            for ib in range(NI):
                ts_ = ib % 2
                pg = (f"c1gp{t}_{ib-2}" if ib >= 2 else
                      (f"c1gp{t-1}_{NI-2+ib}" if t >= 1 else None))
                p.op("scalar", lambda e, sl=sl, ib=ib, ts_=ts_: e.activation(
                        t1f[ts_][:, :], gld[sl][:, ib, :], AF.Identity,
                        scale=sb["r2c"][:, :], bias=sb["rccol"][:, :]),
                     deps=[f"c1ld{t}", "bce_r2c", pg],
                     tag=f"c1t1{t}_{ib}")
                p.op("scalar", lambda e, ts_=ts_: e.activation(
                        sgf[ts_][:, :], t1f[ts_][:, :], AF.Sigmoid,
                        scale=sb["s2c"][:, :], bias=sb["brc2c"][:, :]),
                     deps=["brc2_done", f"c1t1{t}_{ib}"], tag=f"c1sg{t}_{ib}")
                p.op("vector", lambda e, ts_=ts_: e.tensor_scalar(
                        sgf[ts_][:, :], sgf[ts_][:, :], 65536.0, RC,
                        op0=ALU.mult, op1=ALU.add),
                     deps=[f"c1sg{t}_{ib}"], tag=f"c1t3{t}_{ib}")
                p.op("vector", lambda e, ts_=ts_: e.tensor_scalar(
                        sb["srbf"][:, :], sgf[ts_][:, :], -RC, None, op0=ALU.add),
                     deps=[f"c1t3{t}_{ib}"], tag=f"c1sr{t}_{ib}")
                p.op("vector", lambda e, ib=ib, ts_=ts_: e.scalar_tensor_tensor(
                        sb["g2st"][:, ib, :], t1f[ts_][:, :], -RC, sb["srbf"][:, :],
                        op0=ALU.add, op1=ALU.mult),
                     deps=[f"c1sr{t}_{ib}", f"c1t1{t}_{ib}", T.get(f"c1st{t-1}")],
                     tag=f"c1gp{t}_{ib}")
                p.op("vector", lambda e, ib=ib, t=t: e.tensor_reduce(
                        sb["m3slots"][:, t * NI + ib:t * NI + ib + 1],
                        sb["g2st"][:, ib, :], axis=AX.X, op=ALU.max,
                        apply_absolute_value=True),
                     deps=[f"c1gp{t}_{ib}"], tag=f"c1mx{t}_{ib}")
            dma("gpsimd", g2h[:, t * NI * TT:(t + 1) * NI * TT], sb["g2st"][:, :, :],
                "SC2", tag=f"c1st{t}", deps=[f"c1gp{t}_{NI-1}", f"c1mx{t}_{NI-1}"])
            T[f"c1st{t}"] = f"c1st{t}"
            T[f"c1_free{t}"] = f"c1t1{t}_{NI-1}"

        if PH < 7:
            finish_early()
            return
        mbarrier("m3", [("m3slots", 0, NT * NI)], f"c1mx{NT-1}_{NI-1}")
        p.op("vector", lambda e: e.tensor_mul(sb["srow"][:, 6:7], sb["mxfold"][:, 0:1],
                                              sb["srow"][:, 3:4]),
             deps=["m3_max", "s12row"], tag="m3a")
        p.op("vector", lambda e: e.tensor_scalar(sb["srow"][:, 6:7], sb["srow"][:, 6:7],
                                                 1.0 / (65536.0 * QMAX), None, op0=ALU.mult),
             deps=["m3a"], tag="s3row")
        p.op("vector", lambda e: e.reciprocal(sb["srow"][:, 7:8], sb["srow"][:, 6:7]),
             deps=["s3row"], tag="r3row")
        p.op("vector", lambda e: e.tensor_mul(sb["srow"][:, 8:9], sb["srow"][:, 3:4],
                                              sb["srow"][:, 7:8]),
             deps=["r3row"], tag="c32a")
        p.op("vector", lambda e: e.tensor_scalar(sb["srow"][:, 8:9], sb["srow"][:, 8:9],
                                                 1.0 / 65536.0, None, op0=ALU.mult),
             deps=["c32a"], tag="c32row")
        bcast_col("c32c", sb["srow"][0:1, 8:9], "c32row", "c32c")

        uld = [sb["bufB0"], sb["bufB1"]]
        for t in range(NT):
            sl = t % 2
            dma("gpsimd", gld[sl][:, :, :], g2h[:, t * NI * TT:(t + 1) * NI * TT],
                f"U{sl}", tag=f"c2gld{t}",
                deps=[f"c1st{NT-1}", T.get(f"c2_free{t-2}")])
            dma("gpsimd", uld[sl][:, :, :], Uh[:, t * NI * TT:(t + 1) * NI * TT],
                f"K{sl}", tag=f"c2uld{t}",
                deps=[f"bst{NT-1}_1", T.get(f"c2_free{t-2}")])
            for ib in range(NI):
                ts_ = ib % 2
                pg = (f"c2pr{t}_{ib-2}" if ib >= 2 else
                      (f"c2pr{t-1}_{NI-2+ib}" if t >= 1 else None))
                p.op("scalar", lambda e, sl=sl, ib=ib, ts_=ts_: e.activation(
                        t1f[ts_][:, :], gld[sl][:, ib, :], AF.Identity,
                        scale=sb["c32c"][:, :], bias=sb["rccol"][:, :]),
                     deps=[f"c2gld{t}", "bce_c32c", pg], tag=f"c2t4{t}_{ib}")
                p.op("vector", lambda e, ts_=ts_: e.tensor_scalar(
                        sb["g2bf"][:, :], t1f[ts_][:, :], -RC, None, op0=ALU.add),
                     deps=[f"c2t4{t}_{ib}"], tag=f"c2g2{t}_{ib}")
                p.op("scalar", lambda e, sl=sl, ib=ib, ts_=ts_: e.activation(
                        sgf[ts_][:, :], uld[sl][:, ib, :], AF.Identity,
                        scale=sb["r1c"][:, :], bias=sb["rccol"][:, :]),
                     deps=[f"c2uld{t}", "bce_r1c"], tag=f"c2t5{t}_{ib}")
                p.op("vector", lambda e, ts_=ts_: e.tensor_scalar(
                        sb["upbf"][:, :], sgf[ts_][:, :], -RC, None, op0=ALU.add),
                     deps=[f"c2t5{t}_{ib}"], tag=f"c2up{t}_{ib}")
                p.op("vector", lambda e, ib=ib: e.tensor_mul(
                        sb["prst"][:, ib, :], sb["g2bf"][:, :], sb["upbf"][:, :]),
                     deps=[f"c2g2{t}_{ib}", f"c2up{t}_{ib}", T.get(f"c2st{t-1}")],
                     tag=f"c2pr{t}_{ib}")
                p.op("vector", lambda e, ib=ib, t=t: e.tensor_reduce(
                        sb["m4slots"][:, t * NI + ib:t * NI + ib + 1],
                        sb["prst"][:, ib, :], axis=AX.X, op=ALU.max,
                        apply_absolute_value=True),
                     deps=[f"c2pr{t}_{ib}"], tag=f"c2mx{t}_{ib}")
            dma("gpsimd", prh[:, t * NI * TT:(t + 1) * NI * TT], sb["prst"][:, :, :],
                "SC3", tag=f"c2st{t}", deps=[f"c2pr{t}_{NI-1}", f"c2mx{t}_{NI-1}"])
            T[f"c2st{t}"] = f"c2st{t}"
            T[f"c2_free{t}"] = f"c2pr{t}_{NI-1}"

        if PH < 8:
            finish_early()
            return
        mbarrier("m4", [("m4slots", 0, NT * NI)], f"c2mx{NT-1}_{NI-1}")
        p.op("vector", lambda e: e.tensor_mul(sb["srow"][:, 9:10], sb["srow"][:, 6:7],
                                              sb["srow"][:, 2:3]),
             deps=["s3row", "s12row", "m4_max"], tag="s31")
        p.op("vector", lambda e: e.tensor_mul(sb["srow"][:, 10:11], sb["mxfold"][:, 0:1],
                                              sb["srow"][:, 9:10]),
             deps=["s31"], tag="m4a")
        p.op("vector", lambda e: e.tensor_scalar(sb["srow"][:, 10:11], sb["srow"][:, 10:11],
                                                 1.0 / QMAX, None, op0=ALU.mult),
             deps=["m4a"], tag="s4row")
        p.op("vector", lambda e: e.reciprocal(sb["srow"][:, 11:12], sb["srow"][:, 10:11]),
             deps=["s4row"], tag="r4row")
        p.op("vector", lambda e: e.tensor_mul(sb["srow"][:, 11:12], sb["srow"][:, 11:12],
                                              sb["srow"][:, 9:10]),
             deps=["r4row"], tag="c4row")
        bcast_col("c4c", sb["srow"][0:1, 11:12], "c4row", "c4c")
        bcast_col("s4c", sb["srow"][0:1, 10:11], "s4row", "s4c")
        p.op("vector", lambda e: e.tensor_scalar(sb["gbs4"][:, :], sb["mcols"][:, 5, :],
                                                 sb["s4c"][:, :], None, op0=ALU.mult),
             deps=["bce_s4c", "modcols_done"], tag="gbs4_done")

        x2r = [sb["x2r0"], sb["x2r1"]]
        outs = [sb["outs0"], sb["outs0"]]
        ep2 = [sb["ep20"], sb["ep21"]]
        for t in range(NT):
            sl = t % 2
            dma("gpsimd", gld[sl][:, :, :], prh[:, t * NI * TT:(t + 1) * NI * TT],
                f"U{sl}", tag=f"c3pld{t}",
                deps=[f"c2st{NT-1}", T.get(f"c3_free{t-2}")])
            dma("sync", x2r[sl][:, :, :], x2h_v[:, :, t * TT:(t + 1) * TT],
                f"X{sl}", tag=f"c3xld{t}",
                deps=[f"h2sp{NT-1}", f"c2st{NT-1}", T.get(f"c3x2_free{t-2}")])
            for ib in range(NI):
                ts_ = ib % 2
                pg = (f"c3o{t}_{ib-2}" if ib >= 2 else
                      (f"c3o{t-1}_{NI-2+ib}" if t >= 1 else None))
                p.op("scalar", lambda e, sl=sl, ib=ib, ts_=ts_: e.activation(
                        t1f[ts_][:, :], gld[sl][:, ib, :], AF.Identity,
                        scale=sb["c4c"][:, :], bias=sb["rccol"][:, :]),
                     deps=[f"c3pld{t}", "bce_c4c", pg], tag=f"c3t{t}_{ib}")
                p.op("vector", lambda e, ib=ib, ts_=ts_: e.tensor_scalar(
                        sb["obf"][:, ib, :], t1f[ts_][:, :], -RC, None, op0=ALU.add),
                     deps=[f"c3t{t}_{ib}", T.get(f"c3mm_done{t-1}")],
                     tag=f"c3o{t}_{ib}")
            for fo in range(KC):
                bs = f"big{fo % 2}"
                es = fo % 2
                def mk_dn(e, fo=fo, bs=bs):
                    r = None
                    for ic in range(NI):
                        r = e.matmul(ps[bs][:, :], sb["wd_sb"][:, ic, fo * P:(fo + 1) * P],
                                     sb["obf"][:, ic, :],
                                     start=(ic == 0), stop=(ic == NI - 1))
                    return r
                p.op("tensor", mk_dn,
                     deps=[f"c3o{t}_{NI-1}", "ld_wd", PSL.get(bs)],
                     tag=f"c3mm{t}_{fo}")
                p.op("scalar", lambda e, fo=fo, bs=bs, es=es: e.activation(
                        ep2[es][:, :], ps[bs][:, :], AF.Identity,
                        scale=sb["gbs4"][:, fo:fo + 1]),
                     deps=[f"c3mm{t}_{fo}", "gbs4_done", T.get(f"c3ep_free{t}_{fo-2}"),
                           T.get(f"c3ep_freeprev{t}") if fo < 2 else None],
                     tag=f"c3ep{t}_{fo}")
                PSL[bs] = f"c3ep{t}_{fo}"
                p.op("vector", lambda e, fo=fo, sl=sl, es=es: e.tensor_add(
                        outs[sl][:, fo, :], x2r[sl][:, fo, :], ep2[es][:, :]),
                     deps=[f"c3ep{t}_{fo}", f"c3xld{t}", T.get(f"c3ost{t-1}")],
                     tag=f"c3out{t}_{fo}")
                T[f"c3ep_free{t}_{fo}"] = f"c3out{t}_{fo}"
            T[f"c3ep_freeprev{t+1}"] = f"c3out{t}_{KC-1}"
            T[f"c3mm_done{t}"] = f"c3mm{t}_{KC-1}"
            T[f"c3x2_free{t}"] = f"c3out{t}_{KC-1}"
            T[f"c3_free{t}"] = f"c3t{t}_{NI-1}"
            dma("sync", out_v[:, :, t * TT:(t + 1) * TT], outs[sl][:, :, :],
                "ST", tag=f"c3ost{t}", deps=[f"c3out{t}_{KC-1}"])
            T[f"c3ost{t}"] = f"c3ost{t}"

        p.op("sync", lambda e: e.wait_ge(p.sems["ST"], 16 * NT), inc=0)
        p.op("gpsimd", lambda e: e.wait_ge(p.sems["ST"], 16 * NT), inc=0)

    p.reset("count")
    program()
    snap = dict(p.tags)
    p.reset("count")
    program()
    assert p.tags == snap, "walk is not deterministic"

    with ctx:
        block = ctx.enter_context(nc.Block())

        def mk_section(eng):
            def f(engine):
                p.engs[eng] = engine
                p.reset(eng)
                program()
            return f

        block.sync(mk_section("sync"))
        block.gpsimd(mk_section("gpsimd"))
        block.tensor(mk_section("tensor"))
        block.scalar(mk_section("scalar"))
        block.vector(mk_section("vector"))

    return nc


def stage(inputs):
    x = np.asarray(inputs["L0_attn_x"], dtype=np.float32)
    shared = {
        "wqT": np.ascontiguousarray(np.asarray(inputs["L0_attn_Wq"], np.float32).T),
        "wkT": np.ascontiguousarray(np.asarray(inputs["L0_attn_Wk"], np.float32).T),
        "wvT": np.ascontiguousarray(np.asarray(inputs["L0_attn_Wv"], np.float32).T),
        "woT": np.ascontiguousarray(np.asarray(inputs["L0_attn_Wo"], np.float32).T),
        "fmqT": np.ascontiguousarray(np.asarray(inputs["L0_attn_fmq_w"], np.float32).T),
        "fmkT": np.ascontiguousarray(np.asarray(inputs["L0_attn_fmk_w"], np.float32).T),
        "fmqb": np.ascontiguousarray(np.asarray(inputs["L0_attn_fmq_b"], np.float32).reshape(P, 1)),
        "fmkb": np.ascontiguousarray(np.asarray(inputs["L0_attn_fmk_b"], np.float32).reshape(P, 1)),
        "rmsw": np.ascontiguousarray(np.asarray(inputs["L0_attn_rms_w"], np.float32).reshape(P, 1)),
        "wup": np.ascontiguousarray(np.asarray(inputs["L0_ffn_w_up"], np.float32)),
        "wgate": np.ascontiguousarray(np.asarray(inputs["L0_ffn_w_gate"], np.float32)),
        "wdown": np.ascontiguousarray(np.asarray(inputs["L0_ffn_w_down"], np.float32)),
        "wmod": np.ascontiguousarray(np.asarray(inputs["L0_wmod"], np.float32)),
    }
    c = np.asarray(inputs["L0_c"], dtype=np.float32)
    in_maps = []
    for core in range(8):
        b, hf = core // 2, core % 2
        m = dict(shared)
        m["xT"] = np.ascontiguousarray(x[b, hf * TOK:(hf + 1) * TOK, :].T)
        m["crow"] = np.ascontiguousarray(c[b:b + 1, :])
        sm = np.zeros((1, 8), dtype=np.float32)
        if hf == 1:
            sm[0, core - 1] = 1.0
        m["smask"] = sm
        in_maps.append(m)
    return in_maps


def kernel(**inputs):
    in_maps = stage(inputs)
    nc = build_nc()
    res = run_bass_kernel_spmd(nc, in_maps, core_ids=list(range(8)))
    out = np.empty((4, 2 * TOK, D), dtype=np.float32)
    for core in range(8):
        b, hf = core // 2, core % 2
        out[b, hf * TOK:(hf + 1) * TOK, :] = res.results[core]["out"].T
    return out


# version 05
# speedup vs baseline: 1.1981x; 1.0804x over previous
import math
import os
import numpy as np
import ml_dtypes
from contextlib import ExitStack

import concourse.bass as bass
import concourse.mybir as mybir
from concourse.bass_utils import run_bass_kernel_spmd

F32 = mybir.dt.float32
BF16 = mybir.dt.bfloat16
AF = mybir.ActivationFunctionType
ALU = mybir.AluOpType
AX = mybir.AxisListType

P = 128
D = 1024
KC = 8
H = 8
DF = 256
TOK = 4096
NT = 8
TT = 512
NCH = 4
CHN = 128
DFF = 3072
NI = 24
RC = 12582912.0
LN16 = math.log(16.0)
QMAX = 32767.0

PH = int(os.environ.get("KPHASES", "99"))
KNOB = os.environ.get("KNOB", "")
ENGSEM = dict(tensor="T", vector="V", scalar="S", sync="YN", gpsimd="GN")
NONDMA = {"T", "V", "S", "YN", "GN"}


class Prog:
    def __init__(self):
        self.mode = "count"
        self.tags = {}
        self.cnt = {}
        self.engs = {}
        self.sems = {}
        self.water = {}

    def reset(self, mode):
        self.mode = mode
        self.cnt = {k: 0 for k in self.sems}
        self.water = {}

    def settag(self, tag, srctag):
        if self.mode == "count" and tag not in self.tags:
            self.tags[tag] = self.tags[srctag]

    def op(self, eng, emit, deps=(), inc=1, tag=None, sem=None):
        semname = sem or ENGSEM[eng]
        self.cnt[semname] = self.cnt.get(semname, 0) + inc
        val = self.cnt[semname]
        if self.mode == "count":
            if tag is not None:
                if tag in self.tags:
                    assert self.tags[tag] == (semname, val), f"tag drift {tag}"
                self.tags[tag] = (semname, val)
            return tag
        if self.mode != eng:
            return tag
        e = self.engs[eng]
        own = ENGSEM[eng]
        for d in deps:
            if d is None:
                continue
            s, v = self.tags[d]
            if s == own and s in NONDMA:
                assert v <= self.cnt[own], f"forward self-dep {d} on {eng}"
            if self.water.get(s, 0) >= v:
                continue
            e.wait_ge(self.sems[s], v)
            self.water[s] = v
        instr = emit(e)
        if inc:
            instr.then_inc(self.sems[semname], inc)
        return tag

    def barrier(self, name, sb):
        if self.mode == "count":
            self.tags[f"bar_{name}"] = {k: v for k, v in self.cnt.items()}
            return
        vals = self.tags[f"bar_{name}"]
        for eng in ENGSEM:
            if self.mode != eng:
                continue
            e = self.engs[eng]
            own = ENGSEM[eng]
            for sname, v in vals.items():
                if sname == own or v == 0:
                    continue
                if self.water.get(sname, 0) >= v:
                    continue
                e.wait_ge(self.sems[sname], v)
                self.water[sname] = v


def build_nc():
    nc = bass.Bass("TRN2", target_bir_lowering=False)

    xT = nc.dram_tensor("xT", [D, TOK], F32, kind="ExternalInput").ap()
    wqT = nc.dram_tensor("wqT", [D, D], F32, kind="ExternalInput").ap()
    wkT = nc.dram_tensor("wkT", [D, D], F32, kind="ExternalInput").ap()
    wvT = nc.dram_tensor("wvT", [D, D], F32, kind="ExternalInput").ap()
    woT = nc.dram_tensor("woT", [D, D], F32, kind="ExternalInput").ap()
    fmqT = nc.dram_tensor("fmqT", [P, P], F32, kind="ExternalInput").ap()
    fmkT = nc.dram_tensor("fmkT", [P, P], F32, kind="ExternalInput").ap()
    fmqb = nc.dram_tensor("fmqb", [P, 1], F32, kind="ExternalInput").ap()
    fmkb = nc.dram_tensor("fmkb", [P, 1], F32, kind="ExternalInput").ap()
    rmsw = nc.dram_tensor("rmsw", [P, 1], F32, kind="ExternalInput").ap()
    wup = nc.dram_tensor("wup", [DFF, D], F32, kind="ExternalInput").ap()
    wgate = nc.dram_tensor("wgate", [DFF, D], F32, kind="ExternalInput").ap()
    wdown = nc.dram_tensor("wdown", [D, DFF], F32, kind="ExternalInput").ap()
    wmod = nc.dram_tensor("wmod", [D, 6 * D], F32, kind="ExternalInput").ap()
    crow = nc.dram_tensor("crow", [1, D], F32, kind="ExternalInput").ap()
    smask = nc.dram_tensor("smask", [1, 8], F32, kind="ExternalInput").ap()
    out = nc.dram_tensor("out", [D, TOK], F32, kind="ExternalOutput").ap()

    hb_h = nc.dram_tensor("hb_h", [D, TOK], BF16, kind="Internal").ap()
    hb_kfm = nc.dram_tensor("hb_kfm", [H * DF, TOK], BF16, kind="Internal").ap()
    hb_ktm = nc.dram_tensor("hb_ktm", [TOK, H * DF], BF16, kind="Internal").ap()
    hb_v = nc.dram_tensor("hb_v", [TOK, D], BF16, kind="Internal").ap()
    x2h = nc.dram_tensor("x2h", [D, TOK], F32, kind="Internal").ap()
    h2h = nc.dram_tensor("h2h", [D, TOK], BF16, kind="Internal").ap()
    wupT_q = nc.dram_tensor("wupT_q", [D, DFF], BF16, kind="Internal").ap()
    wgT_q = nc.dram_tensor("wgT_q", [D, DFF], BF16, kind="Internal").ap()
    wdT_q = nc.dram_tensor("wdT_q", [DFF, D], BF16, kind="Internal").ap()
    Uh = nc.dram_tensor("Uh", [P, NT * NI * TT], BF16, kind="Internal").ap()
    Gh = nc.dram_tensor("Gh", [P, NT * NI * TT], BF16, kind="Internal").ap()
    g2h = nc.dram_tensor("g2h", [P, NT * NI * TT], BF16, kind="Internal").ap()
    prh = nc.dram_tensor("prh", [P, NT * NI * TT], BF16, kind="Internal").ap()
    sc_in = nc.dram_tensor("sc_in", [P, H * DF], F32, kind="Internal").ap()
    sc_out = nc.dram_tensor("sc_out", [8 * P, H * DF], F32, kind="Internal").ap()
    mx_in = nc.dram_tensor("mx_in", [P, 2], F32, kind="Internal").ap()
    mx_out = nc.dram_tensor("mx_out", [P, 2], F32, kind="Internal").ap()

    idt = nc.inline_tensor(np.eye(P, dtype=ml_dtypes.bfloat16), name="identbf")
    tri = nc.inline_tensor(np.triu(np.ones((P, P), dtype=ml_dtypes.bfloat16)), name="triubf")
    onc = nc.inline_tensor(np.ones((P, 1), dtype=ml_dtypes.bfloat16), name="onescolbf")
    onr = nc.inline_tensor(np.ones((1, P), dtype=np.float32), name="onesrowf")

    sb = {}
    total_sbuf = nc.sbuf_bytes_remaining

    def galloc(name, shape, dt):
        sb[name] = nc.alloc_sbuf_tensor("sb_" + name, shape, dt)
        return sb[name]

    for nm, shp, dt in [
        ("dmy", [1, 1], F32),
        ("ident", [P, P], BF16), ("triu", [P, P], BF16),
        ("onescol", [P, 1], BF16), ("onesrow", [1, P], F32),
        ("epscol", [P, 1], F32), ("rccol", [P, 1], F32), ("c2ecol", [P, 1], F32),
        ("fmqT", [P, P], BF16), ("fmkT", [P, P], BF16),
        ("fmqb", [P, 1], F32), ("fmkb", [P, 1], F32),
        ("bqp", [P, 1], F32), ("bqn", [P, 1], F32), ("bkn", [P, 1], F32),
        ("rmsw", [P, 1], F32),
        ("mcols", [P, 6, 8], F32), ("ca1", [P, 8], F32), ("cb1", [P, 8], F32),
        ("smrow", [1, 8], F32), ("sm128", [P, 8], F32),
        ("m0slots", [P, 64], F32),
        ("m1slots", [P, NT * NI], F32), ("m2slots", [P, NT * NI], F32),
        ("m3slots", [P, NT * NI], F32), ("m4slots", [P, NT * NI], F32),
        ("mxst", [P, 2], F32), ("mxrow", [1, 2 * P], F32), ("mxfold", [1, 2], F32),
        ("r0c", [P, 1], F32), ("s0c", [P, 1], F32), ("s4c", [P, 1], F32),
        ("r1c", [P, 1], F32), ("r2c", [P, 1], F32), ("s2c", [P, 1], F32),
        ("brc2c", [P, 1], F32), ("c32c", [P, 1], F32), ("c4c", [P, 1], F32),
        ("gbs4", [P, 8], F32), ("mrccol", [P, 1], F32),
        ("ccols", [P, 8], F32),
        ("row_a", [1, TT], F32), ("row_b", [1, TT], F32),
        ("row_m", [1, TT], F32), ("row_r", [1, TT], F32),
        ("srow", [1, 16], F32),
    ]:
        galloc(nm, shp, dt)

    probe = nc.alloc_sbuf_tensor("sb_gbase_probe", [1, 64], F32)
    GBASE = ((nc.lookup_mloc(probe).addr + 256 + 63) // 64) * 64
    LIMIT = GBASE + nc.sbuf_bytes_remaining - 512

    class Arena:
        def __init__(self, base):
            self.off = base

        def place(self, name, shape, dt):
            esz = 2 if dt == BF16 else 4
            n = esz
            for s_ in shape[1:]:
                n *= s_
            off = ((self.off + 63) // 64) * 64
            sb[name] = nc.alloc_sbuf_tensor_at("sb_" + name, shape, dt, offset=off)
            self.off = off + n
            assert self.off <= LIMIT, f"SBUF overflow at {name}: {self.off}"
            return sb[name]

    a0 = Arena(GBASE)
    a0.place("wmod0", [P, D], F32)
    a0.place("wmod1", [P, D], F32)
    a0.place("crow_sb", [1, D], F32)
    a0.place("modrow", [1, 6 * D], F32)
    a0.place("clamp_a", [1, D], F32)
    a0.place("clamp_b", [1, D], F32)

    aa = Arena(GBASE)
    aa.place("dqn0", [P, DFF], F32)
    aa.place("dsc", [P, DFF // 32], F32)
    aa.place("drc", [P, DFF // 32], F32)
    aa.place("dt1", [P, DFF], F32)
    aa.place("dqb", [P, DFF], BF16)
    aa.place("dst", [P, NI, P], BF16)
    aa.place("wk_sb", [P, KC, D], BF16)
    aa.place("wv_sb", [P, KC, D], BF16)
    aa.place("xs0", [P, KC, TT], F32)
    aa.place("xs1", [P, KC, TT], F32)
    aa.place("xbf", [P, KC, TT], BF16)
    aa.place("sq0", [P, TT], BF16)
    aa.place("sq1", [P, TT], BF16)
    aa.place("tmp1", [P, TT], BF16)
    aa.place("tmp2", [P, TT], BF16)
    aa.place("rsb", [P, TT], BF16)
    aa.place("mrsb", [P, TT], BF16)
    aa.place("h_sb", [P, KC, TT], BF16)
    aa.place("kq0", [P, TT], BF16)
    aa.place("kq1", [P, TT], BF16)
    aa.place("kfm_sb", [P, 2 * H, TT], BF16)
    aa.place("ktm_sb", [P, NCH, H * DF], BF16)
    aa.place("v_sb", [P, NCH, D], BF16)
    aa.place("Ssb", [P, H * DF], F32)

    ab = Arena(GBASE)
    ab.place("Sfin", [P, H * DF], F32)
    ab.place("wq_sb", [P, KC, D], BF16)
    ab.place("wo_sb", [P, KC, D], BF16)
    ab.place("cxs0", [P, KC, TT], F32)
    ab.place("h_r", [P, KC, TT], BF16)
    ab.place("kfm_r", [P, 2 * H, TT], BF16)
    ab.place("ktm_r", [P, NCH, H * DF], BF16)
    ab.place("v_r", [P, NCH, D], BF16)
    ab.place("qfm_sb", [P, 2 * H, TT], BF16)
    ab.place("qq0", [P, TT], BF16)
    ab.place("qq1", [P, TT], BF16)
    OSB_OFF = ((ab.off + 63) // 64) * 64
    ab.place("o_sb", [P, KC, TT], BF16)
    sb["gS0"] = nc.alloc_sbuf_tensor_at("sb_gS0", [P, H * DF], F32, offset=OSB_OFF)
    ab.place("Ssnap", [P, 2, DF], BF16)
    ab.place("at_sb", [P, 2, P], BF16)
    ab.place("o2w", [P, 2, 8 * P], BF16)
    ab.place("orow", [P, 32, P], BF16)
    ab.place("rms32", [P, 32], F32)
    ab.place("rmsc", [P, 1], F32)
    ab.place("x2_sb", [P, KC, TT], F32)
    ab.place("ep_sb", [P, TT], F32)
    ab.place("h2_sb", [P, KC, TT], BF16)
    ab.place("cxbf", [P, KC, TT], BF16)
    ab.place("csq0", [P, TT], BF16)
    ab.place("csq1", [P, TT], BF16)
    ab.place("ctmp1", [P, TT], BF16)
    ab.place("ctmp2", [P, TT], BF16)
    ab.place("crsb", [P, TT], BF16)
    ab.place("cmrsb", [P, TT], BF16)

    ac = Arena(GBASE)
    ac.place("wup_sb", [P, KC, DFF], BF16)
    ac.place("wg_sb", [P, KC, DFF], BF16)
    ac.place("h2r0", [P, KC, TT], BF16)
    ac.place("h2r1", [P, KC, TT], BF16)
    ac.place("xqt0", [P, TT], F32)
    ac.place("xqt1", [P, TT], F32)
    ac.place("xq_sb", [P, KC, TT], BF16)
    ac.place("Ust", [P, NI, TT], BF16)
    ac.place("Gst", [P, NI, TT], BF16)

    ad = Arena(GBASE)
    ad.place("bufA0", [P, NI, TT], BF16)
    ad.place("bufA1", [P, NI, TT], BF16)
    ad.place("t1f0", [P, TT], F32)
    ad.place("t1f1", [P, TT], F32)
    ad.place("t1g0", [P, TT], F32)
    ad.place("t1g1", [P, TT], F32)
    Z1 = ad.off
    ad.place("bufB0", [P, NI, TT], BF16)
    ad.place("bufB1", [P, NI, TT], BF16)
    ad.place("sgf0", [P, TT], F32)
    ad.place("sgf1", [P, TT], F32)
    Z2 = ad.off
    ae1 = Arena(Z2)
    ae1.place("srbf", [P, TT], BF16)
    ae1.place("tgbf", [P, 2, TT], BF16)
    ae1.place("g2st", [P, NI, TT], BF16)
    ae2 = Arena(Z2)
    ae2.place("prst", [P, NI, TT], BF16)
    ae2.place("g2bf", [P, TT], BF16)
    ae2.place("upbf", [P, TT], BF16)
    awd = Arena(max(ae1.off, ae2.off))
    awd.place("wd_sb", [P, NI, D], BF16)
    af = Arena(Z1)
    af.place("obf", [P, NI, TT], BF16)
    af.place("x2r0", [P, KC, TT], F32)
    af.place("x2r1", [P, KC, TT], F32)
    af.place("outs0", [P, KC, TT], F32)
    af.place("ep20", [P, TT], F32)
    af.place("ep21", [P, TT], F32)

    ps = {}
    for i in range(4):
        ps[f"S{i}"] = nc.alloc_psum_tensor(f"psS{i}", [P, 512], F32)
    ps["big0"] = nc.alloc_psum_tensor("big0", [P, TT], F32)
    ps["big1"] = nc.alloc_psum_tensor("big1", [P, TT], F32)
    ps["pxa"] = nc.alloc_psum_tensor("pxa", [P, 1024], BF16)
    ps["pxb"] = nc.alloc_psum_tensor("pxb", [P, 1024], BF16)
    assert nc.psum_banks_remaining == 0

    def S_ap(h_, dc):
        sl = h_ * 2 + dc
        return ps[f"S{sl // 4}"][:, (sl % 4) * P:(sl % 4 + 1) * P]

    ctx = ExitStack()
    sems = {}
    for s in ["T", "V", "S", "YN", "GN", "LC", "MA", "MB", "DA", "DB", "LW",
              "LF", "SPA", "SPB", "SG", "SD", "X0", "X1", "H0", "H1",
              "K0", "K1", "U0", "U1", "SC1", "SC2", "SC3", "SC2B", "SC3B", "CC", "ST"]:
        sems[s] = ctx.enter_context(nc.semaphore(s))
    p = Prog()
    p.sems = sems

    xT_v = xT.rearrange("(kc q) n -> q kc n", q=P)
    hbh_v = hb_h.rearrange("(kc q) n -> q kc n", q=P)
    kfm_v = hb_kfm.rearrange("(r q) n -> q r n", q=P)
    ktm_v = hb_ktm.rearrange("(g q) c -> q g c", q=P)
    v_v = hb_v.rearrange("(g q) c -> q g c", q=P)
    x2h_v = x2h.rearrange("(kc q) n -> q kc n", q=P)
    h2h_v = h2h.rearrange("(kc q) n -> q kc n", q=P)
    out_v = out.rearrange("(kc q) n -> q kc n", q=P)

    def program():
        T = {}

        def dma(eng, dst, src, sem, tag=None, deps=()):
            return p.op(eng, lambda e: e.dma_start(out=dst, in_=src),
                        deps=deps, inc=16, tag=tag, sem=sem)

        dma("sync", sb["ident"][:, :], idt.ap(), "LC")
        dma("sync", sb["triu"][:, :], tri.ap(), "LC")
        dma("sync", sb["onescol"][:, :], onc.ap(), "LC")
        dma("sync", sb["onesrow"][:, :], onr.ap(), "LC")
        dma("sync", sb["fmqb"][:, :], fmqb, "LC")
        dma("sync", sb["fmkb"][:, :], fmkb, "LC")
        dma("sync", sb["rmsw"][:, :], rmsw, "LC")
        dma("sync", sb["smrow"][:, :], smask, "LC")
        dma("sync", sb["crow_sb"][:, :], crow, "LC", tag="ld_consts")
        p.op("gpsimd", lambda e: e.dma_start(out=sb["fmqT"][:, :], in_=fmqT),
             inc=16, sem="LW")
        p.op("gpsimd", lambda e: e.dma_start(out=sb["fmkT"][:, :], in_=fmkT),
             inc=16, sem="LW", tag="ld_fm")

        p.op("vector", lambda e: e.memset(sb["dmy"][:, :], 0.0))
        p.op("vector", lambda e: e.memset(sb["epscol"][:, :], 1e-5))
        p.op("vector", lambda e: e.memset(sb["rccol"][:, :], RC), tag="rc_set")
        p.op("vector", lambda e: e.memset(sb["mrccol"][:, :], -RC), tag="mrc_set")
        p.op("vector", lambda e: e.memset(sb["c2ecol"][:, :], 128.0 * 1e-5 * (2.0 ** -48)),
             tag="c2e_set")
        p.op("vector", lambda e: e.tensor_scalar(sb["bqp"][:, :], sb["fmqb"][:, :], -LN16, None, op0=ALU.add),
             deps=["ld_consts"])
        p.op("vector", lambda e: e.tensor_scalar(sb["bqn"][:, :], sb["fmqb"][:, :], -1.0, -LN16, op0=ALU.mult, op1=ALU.add))
        p.op("vector", lambda e: e.tensor_scalar(sb["bkn"][:, :], sb["fmkb"][:, :], -1.0, None, op0=ALU.mult),
             tag="bias_cols")

        p.op("scalar", lambda e: e.activation(sb["crow_sb"][:, :], sb["crow_sb"][:, :], AF.Silu),
             deps=["ld_consts"], tag="silu_row")
        def mk_ccol(e):
            r = None
            for kc in range(KC):
                r = e.matmul(ps["big0"][:, kc:kc + 1],
                             sb["crow_sb"][0:1, kc * P:(kc + 1) * P],
                             sb["onesrow"][0:1, 0:1],
                             start=(kc == 0), stop=(kc == KC - 1))
            return r
        p.op("tensor", mk_ccol, deps=["silu_row", "ld_consts"], tag="ccol_mm")
        p.op("scalar", lambda e: e.activation(sb["ccols"][:, :], ps["big0"][:, 0:8], AF.Identity),
             deps=["ccol_mm"], tag="ccol_ev")

        wmod_sb = [sb["wmod0"], sb["wmod1"]]
        for g in range(6):
            for kc in range(KC):
                slot = (g * KC + kc) % 2
                prev = f"modmm_g{g}_k{kc}_m2" if (g * KC + kc) >= 2 else None
                gp, kp = divmod(g * KC + kc - 2, KC)
                dd = [f"modmm_g{gp}_k{kp}"] if (g * KC + kc) >= 2 else []
                dma("sync", wmod_sb[slot][:, :],
                    wmod[kc * P:(kc + 1) * P, g * D:(g + 1) * D],
                    f"M{'AB'[slot]}", tag=f"modld_g{g}_k{kc}", deps=dd)
            for kc in range(KC):
                slot = (g * KC + kc) % 2
                def mk_mod(g=g, kc=kc, slot=slot):
                    def f(e):
                        e.matmul(ps["big0"][0:1, :],
                                 sb["ccols"][:, kc:kc + 1],
                                 wmod_sb[slot][:, 0:TT],
                                 start=(kc == 0), stop=(kc == KC - 1))
                        return e.matmul(ps["big1"][0:1, :],
                                        sb["ccols"][:, kc:kc + 1],
                                        wmod_sb[slot][:, TT:D],
                                        start=(kc == 0), stop=(kc == KC - 1))
                    return f
                dd = [f"modld_g{g}_k{kc}", "ccol_ev"]
                if g >= 1:
                    dd.append(f"modev_g{g-1}")
                p.op("tensor", mk_mod(), deps=dd, tag=f"modmm_g{g}_k{kc}")
            def mk_ev(g=g):
                def f(e):
                    e.activation(sb["modrow"][:, g * D:g * D + TT], ps["big0"][0:1, :], AF.Identity)
                    return e.activation(sb["modrow"][:, g * D + TT:(g + 1) * D], ps["big1"][0:1, :], AF.Identity)
                return f
            p.op("scalar", mk_ev(), deps=[f"modmm_g{g}_k{KC-1}"], tag=f"modev_g{g}")

        for i, s_ in enumerate([2, 5]):
            sl_ = sb["modrow"][:, s_ * D:(s_ + 1) * D]
            p.op("scalar", lambda e, sl_=sl_: e.activation(sb["clamp_a"][:, :], sl_, AF.Abs),
                 deps=[f"modev_g{s_}", (f"cl_done{i-1}" if i else None)], tag=f"cl_abs{i}")
            p.op("scalar", lambda e, sl_=sl_: e.activation(sb["clamp_b"][:, :], sl_, AF.Sign),
                 deps=[(f"cl_done{i-1}" if i else None)], tag=f"cl_sgn{i}")
            p.op("vector", lambda e: e.tensor_scalar_max(sb["clamp_a"][:, :], sb["clamp_a"][:, :], 0.25),
                 deps=[f"cl_abs{i}"], tag=f"cl_max{i}")
            p.op("vector", lambda e, sl_=sl_: e.tensor_mul(sl_, sb["clamp_a"][:, :], sb["clamp_b"][:, :]),
                 deps=[f"cl_sgn{i}", f"cl_max{i}"], tag=f"cl_done{i}")

        def mk_scat(e):
            r = None
            for s_ in range(6):
                for kc in range(KC):
                    r = e.matmul(ps["big0"][:, s_ * 8 + kc:s_ * 8 + kc + 1],
                                 sb["modrow"][0:1, s_ * D + kc * P:s_ * D + (kc + 1) * P],
                                 sb["onesrow"][0:1, 0:1],
                                 start=(s_ == 0 and kc == 0), stop=(s_ == 5 and kc == KC - 1))
            return r
        p.op("tensor", mk_scat,
             deps=["cl_done0", "cl_done1", "modev_g0", "modev_g1",
                   "modev_g3", "modev_g4", "modev_g5", "ld_consts"],
             tag="mod_scat")
        p.op("scalar", lambda e: e.activation(sb["mcols"][:, :, :], ps["big0"][:, 0:48], AF.Identity),
             deps=["mod_scat"], tag="mcols_ev")
        p.op("vector", lambda e: e.tensor_scalar(sb["ca1"][:, :], sb["mcols"][:, 1, :], 1.0, None, op0=ALU.add),
             deps=["mcols_ev"])
        p.op("vector", lambda e: e.tensor_scalar(sb["cb1"][:, :], sb["mcols"][:, 4, :], 1.0, None, op0=ALU.add),
             tag="modcols_done")
        p.op("tensor", lambda e: e.matmul(ps["big1"][:, 0:8], sb["onesrow"][:, :],
                                          sb["smrow"][:, :], start=True, stop=True),
             deps=["ld_consts", f"modev_g5"], tag="sm_mm")
        p.op("scalar", lambda e: e.activation(sb["sm128"][:, :], ps["big1"][:, 0:8], AF.Identity),
             deps=["sm_mm"], tag="sm128_done")

        p.barrier("p0", sb)

        def finish_early():
            for t_ in range(NT):
                dma("sync", out_v[:, :, t_ * TT:(t_ + 1) * TT],
                    sb["xs0"][:, :, :], "ST", tag=f"fout{t_}")
            p.op("sync", lambda e: e.wait_ge(p.sems["ST"], 16 * NT), inc=0)
            p.op("gpsimd", lambda e: e.wait_ge(p.sems["ST"], 16 * NT), inc=0)

        if PH < 1:
            finish_early()
            return
        PSL = {"big0": "mcols_ev", "big1": "sm128_done"}
        dqi_t1 = {}
        dqi_st = {}
        dqi_q = {}
        dqi_tp = {}
        dqn = [sb["dqn0"], sb["dqn0"]]
        dqi = 0

        DQJ = []

        def emit_dequant(wi, nat, natq, R, C):
            CB = C // P
            CG = C // 32
            natq_v = natq.rearrange("(cb q) r -> q cb r", q=P)
            for rt in range(R // P):
                DQJ.append((nat, natq_v, rt, C, CB, CG))

        def run_dq_jobs(n):
            nonlocal dqi
            for _ in range(n):
                if not DQJ:
                    return
                nat, natq_v, rt, C, CB, CG = DQJ.pop(0)
                i = dqi
                dqi += 1
                slot = i % 2
                dma("sync", dqn[slot][:, 0:C], nat[rt * P:(rt + 1) * P, :],
                    f"D{'AB'[i % 2]}", tag=f"dq_ld_{i}", deps=[dqi_t1.get(i - 1)])
                w3 = dqn[slot][:, 0:C].rearrange("q (g e) -> q g e", e=32)
                p.op("vector", lambda e, w3=w3, CG=CG: e.tensor_reduce(
                        sb["dsc"][:, 0:CG].unsqueeze(2), w3, axis=AX.X, op=ALU.max,
                        apply_absolute_value=True),
                     deps=[f"dq_ld_{i}", dqi_q.get(i - 1), dqi_tp.get(i - 1)],
                     tag=f"dq_sc_{i}")
                p.op("vector", lambda e, CG=CG: e.tensor_scalar(
                        sb["dsc"][:, 0:CG], sb["dsc"][:, 0:CG], 1.0 / 7.0, 1e-12,
                        op0=ALU.mult, op1=ALU.max),
                     deps=[f"dq_sc_{i}"], tag=f"dq_s7_{i}")
                p.op("vector", lambda e, CG=CG: e.reciprocal(sb["drc"][:, 0:CG], sb["dsc"][:, 0:CG]),
                     deps=[f"dq_s7_{i}"], tag=f"dq_rc_{i}")
                rcb = sb["drc"][:, 0:CG].unsqueeze(2).broadcast_to([P, CG, 32])
                scb = sb["dsc"][:, 0:CG].unsqueeze(2).broadcast_to([P, CG, 32])
                p.op("vector", lambda e, w3=w3, rcb=rcb, C=C: e.tensor_tensor(
                        sb["dt1"][:, 0:C].rearrange("q (g e) -> q g e", e=32), w3, rcb, op=ALU.mult),
                     deps=[f"dq_rc_{i}"], tag=f"dq_t1_{i}")
                dqi_t1[i] = f"dq_t1_{i}"
                p.op("scalar", lambda e, C=C: e.activation(
                        sb["dt1"][:, 0:C], sb["dt1"][:, 0:C], AF.Identity, bias=sb["rccol"][:, :]),
                     deps=[f"dq_t1_{i}", "rc_set"], tag=f"dq_t2_{i}")
                p.op("vector", lambda e, scb=scb, C=C: e.scalar_tensor_tensor(
                        sb["dqb"][:, 0:C].rearrange("q (g e) -> q g e", e=32),
                        sb["dt1"][:, 0:C].rearrange("q (g e) -> q g e", e=32),
                        -RC, scb, op0=ALU.add, op1=ALU.mult),
                     deps=[f"dq_t2_{i}"], tag=f"dq_q_{i}")
                dqi_q[i] = f"dq_q_{i}"
                NG = (CB + 3) // 4
                for g in range(NG):
                    gb = min(4, CB - g * 4)
                    def mk_dqtp(e, g=g, gb=gb):
                        r = None
                        for j in range(gb):
                            cb = g * 4 + j
                            r = e.transpose(ps["pxb"][:, j * P:(j + 1) * P],
                                            sb["dqb"][:, cb * P:(cb + 1) * P],
                                            sb["ident"][:, :])
                        return r
                    p.op("tensor", mk_dqtp,
                         deps=[f"dq_q_{i}", PSL.get("pxb"), "ld_consts"],
                         tag=f"dq_tp_{i}_{g}")
                    p.op("scalar", lambda e, g=g, gb=gb: e.activation(
                            sb["dst"][:, g * 4:g * 4 + gb, :],
                            ps["pxb"][:, 0:gb * P].rearrange("q (cb c) -> q cb c", c=P),
                            AF.Identity),
                         deps=[f"dq_tp_{i}_{g}", dqi_st.get(i - 1)],
                         tag=f"dq_ev_{i}_{g}")
                    PSL["pxb"] = f"dq_ev_{i}_{g}"
                dqi_tp[i] = f"dq_tp_{i}_{NG-1}"
                dma("gpsimd", natq_v[:, 0:CB, rt * P:(rt + 1) * P],
                    sb["dst"][:, 0:CB, :],
                    "SD", tag=f"dq_st_{i}", deps=[f"dq_ev_{i}_{NG-1}"])
                dqi_st[i] = f"dq_st_{i}"

        emit_dequant(0, wup, wupT_q, DFF, D)
        emit_dequant(1, wgate, wgT_q, DFF, D)
        emit_dequant(2, wdown, wdT_q, D, DFF)

        if PH < 2:
            finish_early()
            return
        def emit_ln(pre, src_tag, src_f32, xbf, sqp, tmp1, tmp2, rsb, mrsb,
                    hout, cacol, sacol, m0base, cast_deps, mod_deps):
            for kc in range(KC):
                p.op("scalar", lambda e, kc=kc: e.activation(
                        xbf[:, kc, :], src_f32[:, kc, :], AF.Identity),
                     deps=[src_tag] + (cast_deps if kc == 0 else []),
                     tag=f"{pre}cast{kc}")
                p.op("vector", lambda e, kc=kc: e.tensor_mul(
                        sqp[kc % 2][:, :], xbf[:, kc, :], xbf[:, kc, :]),
                     deps=[f"{pre}cast{kc}", T.get(f"{pre}stats{kc-2}")],
                     tag=f"{pre}sq{kc}")
                def mk_stats(e, kc=kc):
                    e.matmul(ps["big0"][0:1, :], sb["onescol"][:, :], xbf[:, kc, :],
                             start=(kc == 0), stop=(kc == KC - 1))
                    return e.matmul(ps["big1"][0:1, :], sb["onescol"][:, :], sqp[kc % 2][:, :],
                                    start=(kc == 0), stop=(kc == KC - 1))
                dd = [f"{pre}sq{kc}", "ld_consts"]
                if kc == 0:
                    dd += [PSL.get("big0"), PSL.get("big1")]
                p.op("tensor", mk_stats, deps=dd, tag=f"{pre}stats{kc}")
                T[f"{pre}stats{kc}"] = f"{pre}stats{kc}"
            p.settag(f"{pre}stats", f"{pre}stats{KC-1}")
            def mk_fin1(e):
                e.activation(sb["row_a"][:, :], ps["big0"][0:1, :], AF.Identity, scale=1.0 / D)
                return e.activation(sb["row_b"][:, :], ps["big1"][0:1, :], AF.Identity, scale=1.0 / D)
            p.op("scalar", mk_fin1, deps=[f"{pre}stats"], tag=f"{pre}fin1")
            p.op("vector", lambda e: e.tensor_mul(sb["row_m"][:, :], sb["row_a"][:, :], sb["row_a"][:, :]),
                 deps=[f"{pre}fin1"], tag=f"{pre}msq")
            p.op("vector", lambda e: e.tensor_sub(sb["row_b"][:, :], sb["row_b"][:, :], sb["row_m"][:, :]),
                 deps=[f"{pre}msq"], tag=f"{pre}var")
            p.op("scalar", lambda e: e.activation(sb["row_r"][:, :], sb["row_b"][:, :], AF.Ln, bias=sb["epscol"][0:1, :]),
                 deps=[f"{pre}var"], tag=f"{pre}rstd0")
            p.op("scalar", lambda e: e.activation(sb["row_r"][:, :], sb["row_r"][:, :], AF.Exp, scale=-0.5),
                 deps=[f"{pre}rstd0"], tag=f"{pre}rstd")
            p.op("vector", lambda e: e.tensor_mul(sb["row_m"][:, :], sb["row_a"][:, :], sb["row_r"][:, :]),
                 deps=[f"{pre}rstd"], tag=f"{pre}mr")
            def mk_bc(e):
                e.matmul(ps["big0"][:, :], sb["onesrow"][:, :], sb["row_r"][:, :], start=True, stop=True)
                return e.matmul(ps["big1"][:, :], sb["onesrow"][:, :], sb["row_m"][:, :], start=True, stop=True)
            p.op("tensor", mk_bc, deps=[f"{pre}mr", f"{pre}fin1"], tag=f"{pre}bc")
            def mk_bce(e):
                e.activation(rsb[:, :], ps["big0"][:, :], AF.Identity)
                return e.activation(mrsb[:, :], ps["big1"][:, :], AF.Identity)
            p.op("scalar", mk_bce, deps=[f"{pre}bc"], tag=f"{pre}bce")
            PSL["big0"] = f"{pre}bce"
            PSL["big1"] = f"{pre}bce"
            for kc in range(KC):
                dd = [f"{pre}bce"]
                if kc > 0:
                    dd.append(f"{pre}smod{kc-1}")
                elif mod_deps:
                    dd += mod_deps
                p.op("vector", lambda e, kc=kc: e.tensor_mul(tmp1[:, :], xbf[:, kc, :], rsb[:, :]),
                     deps=dd, tag=f"{pre}t1_{kc}")
                p.op("vector", lambda e: e.tensor_sub(tmp2[:, :], tmp1[:, :], mrsb[:, :]),
                     deps=[f"{pre}t1_{kc}"], tag=f"{pre}t2_{kc}")
                p.op("scalar", lambda e, kc=kc: e.activation(
                        hout[:, kc, :], tmp2[:, :], AF.Identity,
                        scale=cacol[:, kc:kc + 1], bias=sacol[:, kc:kc + 1]),
                     deps=[f"{pre}t2_{kc}", "modcols_done"], tag=f"{pre}smod{kc}")
                if m0base is not None:
                    p.op("vector", lambda e, kc=kc: e.tensor_reduce(
                            sb["m0slots"][:, m0base + kc:m0base + kc + 1], hout[:, kc, :],
                            axis=AX.X, op=ALU.max, apply_absolute_value=True),
                         deps=[f"{pre}smod{kc}"], tag=f"{pre}m0_{kc}")
            return f"{pre}smod{KC-1}"

        p.op("gpsimd", lambda e: e.dma_start(
                out=sb["wk_sb"][:, :, :], in_=wkT.rearrange("(kc q) f -> q kc f", q=P)),
             inc=16, sem="LW")
        p.op("gpsimd", lambda e: e.dma_start(
                out=sb["wv_sb"][:, :, :], in_=wvT.rearrange("(kc q) f -> q kc f", q=P)),
             inc=16, sem="LW", tag="ld_wkv")
        xs = [sb["xs0"], sb["xs1"]]
        KQLAST = [None, None]
        for t in range(NT):
            sl = t % 2
            dma("sync", xs[sl][:, :, :], xT_v[:, :, t * TT:(t + 1) * TT],
                f"X{sl}", tag=f"axld{t}", deps=[T.get(f"ax_free{t-2}")])
            lnt = emit_ln(
                f"aln{t}", f"axld{t}", xs[sl], sb["xbf"], [sb["sq0"], sb["sq1"]],
                sb["tmp1"], sb["tmp2"],
                sb["rsb"], sb["mrsb"], sb["h_sb"], sb["ca1"], sb["mcols"][:, 0, :],
                None,
                cast_deps=[T.get(f"aln_xbf_free{t-1}")],
                mod_deps=[T.get(f"vspill{t-1}"), T.get(f"h_used{t-1}")])
            T[f"ax_free{t}"] = f"aln{t}cast{KC-1}"
            T[f"aln_xbf_free{t}"] = f"aln{t}t1_{KC-1}"
            dma("gpsimd", hbh_v[:, :, t * TT:(t + 1) * TT], sb["h_sb"][:, :, :],
                "SPA", tag=f"hspill{t}", deps=[lnt])
            T[f"hspill{t}"] = f"hspill{t}"
            for tb in range(NCH):
                for vh in range(2):
                    bs = f"big{vh}"
                    def mk_v(e, tb=tb, vh=vh, bs=bs):
                        r = None
                        for kc in range(KC):
                            r = e.matmul(ps[bs][:, :],
                                         sb["h_sb"][:, kc, tb * CHN:(tb + 1) * CHN],
                                         sb["wv_sb"][:, kc, vh * TT:(vh + 1) * TT],
                                         start=(kc == 0), stop=(kc == KC - 1))
                        return r
                    p.op("tensor", mk_v, deps=[lnt, "ld_wkv", PSL.get(bs)],
                         tag=f"vmm{t}_{tb}_{vh}")
                    p.op("scalar", lambda e, tb=tb, vh=vh, bs=bs: e.activation(
                            sb["v_sb"][:, tb, vh * TT:(vh + 1) * TT], ps[bs][:, :], AF.Identity),
                         deps=[f"vmm{t}_{tb}_{vh}", T.get(f"hspill{t}"), T.get(f"v_used{t-1}")],
                         tag=f"vev{t}_{tb}_{vh}")
                    PSL[bs] = f"vev{t}_{tb}_{vh}"
            for h_ in range(H):
                ks = h_ % 2
                def mk_k(e, h_=h_, ks=ks):
                    r = None
                    for kc in range(KC):
                        r = e.matmul(ps[f"big{ks}"][:, :],
                                     sb["wk_sb"][:, kc, h_ * P:(h_ + 1) * P],
                                     sb["h_sb"][:, kc, :],
                                     start=(kc == 0), stop=(kc == KC - 1))
                    return r
                p.op("tensor", mk_k, deps=[lnt, "ld_wkv", PSL.get(f"big{ks}")],
                     tag=f"kmm{t}_{h_}")
                p.op("scalar", lambda e, ks=ks, bs=f"big{ks}": e.activation(
                        sb[f"kq{ks}"][:, :], ps[bs][:, :], AF.Identity),
                     deps=[f"kmm{t}_{h_}", KQLAST[ks]],
                     tag=f"kev{t}_{h_}")
                p.op("tensor", lambda e, ks=ks: e.matmul(
                        ps[f"big{ks}"][:, :], sb["fmkT"][:, :], sb[f"kq{ks}"][:, :],
                        start=True, stop=True),
                     deps=[f"kev{t}_{h_}", "ld_wkv"], tag=f"kymm{t}_{h_}")
                KQLAST[ks] = f"kymm{t}_{h_}"
                def mk_kexp(e, h_=h_, ks=ks):
                    e.activation(sb["kfm_sb"][:, 2 * h_, :], ps[f"big{ks}"][:, :], AF.Exp,
                                 bias=sb["fmkb"][:, :])
                    return e.activation(sb["kfm_sb"][:, 2 * h_ + 1, :], ps[f"big{ks}"][:, :], AF.Exp,
                                        bias=sb["bkn"][:, :], scale=-1.0)
                p.op("scalar", mk_kexp,
                     deps=[f"kymm{t}_{h_}", "bias_cols", T.get(f"hspill{t}"),
                           T.get(f"kfm_used{t-1}")],
                     tag=f"kexp{t}_{h_}")
                PSL[f"big{ks}"] = f"kexp{t}_{h_}"
                for dc in range(2):
                    def mk_ktp(e, h_=h_, dc=dc):
                        r = None
                        for tb in range(NCH):
                            r = e.transpose(
                                ps["pxa"][:, tb * P:(tb + 1) * P],
                                sb["kfm_sb"][:, 2 * h_ + dc, tb * CHN:(tb + 1) * CHN],
                                sb["ident"][:, :])
                        return r
                    p.op("tensor", mk_ktp,
                         deps=[f"kexp{t}_{h_}", T.get("pxa_ev"), "ld_consts"],
                         tag=f"ktp{t}_{h_}_{dc}")
                    p.op("vector", lambda e, h_=h_, dc=dc: e.tensor_copy(
                            sb["ktm_sb"][:, 0:NCH, h_ * DF + dc * P:h_ * DF + (dc + 1) * P],
                            ps["pxa"][:, 0:TT].rearrange("q (tb c) -> q tb c", c=P)),
                         deps=[f"ktp{t}_{h_}_{dc}", T.get(f"hspill{t}"),
                               T.get(f"ktm_used{t-1}")],
                         tag=f"ktev{t}_{h_}_{dc}")
                    T["pxa_ev"] = f"ktev{t}_{h_}_{dc}"
                def mk_sacc(e, t=t, h_=h_):
                    r = None
                    for ch in range(NCH):
                        for dc in range(2):
                            st_ = (t == 0 and ch == 0 and dc == 0 and h_ % 2 == 0)
                            r = e.matmul(S_ap(h_, dc),
                                         sb["ktm_sb"][:, ch, h_ * DF + dc * P:h_ * DF + (dc + 1) * P],
                                         sb["v_sb"][:, ch, h_ * P:(h_ + 1) * P],
                                         start=st_, stop=False, skip_group_check=True)
                    return r
                p.op("tensor", mk_sacc,
                     deps=[f"ktev{t}_{h_}_1", f"vev{t}_{NCH-1}_1"],
                     tag=f"sacc{t}_{h_}")
            T[f"h_used{t}"] = f"kmm{t}_{H-1}"
            T[f"v_used{t}"] = f"sacc{t}_{H-1}"
            T[f"ktm_used{t}"] = f"sacc{t}_{H-1}"
            T[f"kfm_used{t}"] = f"ktp{t}_{H-1}_1"
            dma("gpsimd", kfm_v[:, :, t * TT:(t + 1) * TT], sb["kfm_sb"][:, :, :],
                "SPA", tag=f"kfmspill{t}",
                deps=[f"kexp{t}_{H-1}", f"ktp{t}_{H-1}_1"])
            T[f"kfmspill{t}"] = f"kfmspill{t}"
            dma("gpsimd", ktm_v[:, t * NCH:(t + 1) * NCH, :], sb["ktm_sb"][:, :, :],
                "SPA", tag=f"ktmspill{t}", deps=[f"sacc{t}_{H-1}"])
            T[f"ktmspill{t}"] = f"ktmspill{t}"
            dma("gpsimd", v_v[:, t * NCH:(t + 1) * NCH, :], sb["v_sb"][:, :, :],
                "SPA", tag=f"vspill{t}", deps=[f"sacc{t}_{H-1}"])
            T[f"vspill{t}"] = f"vspill{t}"
            run_dq_jobs(7)

        run_dq_jobs(10 ** 9)
        T["dq_done"] = f"dq_st_{dqi - 1}"

        def mk_sev(e):
            r = None
            for h_ in range(H):
                for dc in range(2):
                    r = e.activation(sb["Ssb"][:, (h_ * 2 + dc) * P:(h_ * 2 + dc + 1) * P],
                                     S_ap(h_, dc), AF.Identity)
            return r
        p.op("scalar", mk_sev, deps=[f"sacc{NT-1}_{H-1}"], tag="s_ev")
        dma("gpsimd", sc_in, sb["Ssb"][:, :], "SPA", tag="s_stage", deps=["s_ev"])
        p.op("gpsimd", lambda e: e.collective_compute(
                "AllGather", ALU.bypass, replica_groups=[list(range(8))],
                ins=[sc_in.opt()], outs=[sc_out.opt()]),
             deps=["s_stage"], sem="CC", tag="cc_sg")

        p.barrier("p2a", sb)

        if PH < 3:
            finish_early()
            return
        gS = sb["gS0"]
        SFIN = sb["Sfin"]
        for r in range(8):
            dma("gpsimd", gS[:, :], sc_out[r * P:(r + 1) * P, :],
                "SG", tag=f"gsld{r}", deps=["cc_sg", T.get(f"sin{r-1}")])
            if r == 0:
                p.op("vector", lambda e: e.tensor_scalar(
                        SFIN[:, :], gS[:, :], sb["sm128"][:, 0:1], None, op0=ALU.mult),
                     deps=[f"gsld{r}", "sm128_done"], tag=f"sin{r}")
            else:
                p.op("vector", lambda e, r=r: e.scalar_tensor_tensor(
                        SFIN[:, :], gS[:, :], sb["sm128"][:, r:r + 1], SFIN[:, :],
                        op0=ALU.mult, op1=ALU.add),
                     deps=[f"gsld{r}", f"sin{r-1}"], tag=f"sin{r}")
            T[f"sin{r}"] = f"sin{r}"

        p.op("gpsimd", lambda e: e.dma_start(
                out=sb["wq_sb"][:, :, :], in_=wqT.rearrange("(kc q) f -> q kc f", q=P)),
             inc=16, sem="LW")
        p.op("gpsimd", lambda e: e.dma_start(
                out=sb["wo_sb"][:, :, :], in_=woT.rearrange("(kc q) f -> q kc f", q=P)),
             inc=16, sem="LW", tag="ld_wqo")
        def mk_worms(e):
            r = None
            for kc in range(KC):
                r = e.tensor_scalar(sb["wo_sb"][:, kc, :], sb["wo_sb"][:, kc, :],
                                    sb["rmsw"][:, :], None, op0=ALU.mult)
            return r
        p.op("vector", mk_worms, deps=["ld_wqo", "ld_consts"], tag="wo_rms")

        cxs = [sb["cxs0"], sb["cxs0"]]
        QQLAST = [None, None]
        psA_at = ps["big0"][:, 0:P]
        psA_o = ps["big0"][:, P:2 * P]
        for t in range(NT):
            sl = t % 2
            dma("sync", cxs[sl][:, :, :], xT_v[:, :, t * TT:(t + 1) * TT],
                f"X{sl}", tag=f"cxld{t}", deps=[T.get(f"cx_free{t-1}")])
            dma("sync", sb["h_r"][:, :, :], hbh_v[:, :, t * TT:(t + 1) * TT],
                f"H{sl}", tag=f"chld{t}",
                deps=["s_stage", T.get(f"hr_used{t-1}")])
            dma("gpsimd", sb["kfm_r"][:, :, :], kfm_v[:, :, t * TT:(t + 1) * TT],
                f"K{sl}", tag=f"ckfld{t}",
                deps=["s_stage", T.get(f"kfmr_used{t-1}")])
            dma("gpsimd", sb["ktm_r"][:, :, :], ktm_v[:, t * NCH:(t + 1) * NCH, :],
                f"K{sl}", tag=f"cktld{t}",
                deps=[T.get(f"ktmr_used{t-1}")])
            dma("gpsimd", sb["v_r"][:, :, :], v_v[:, t * NCH:(t + 1) * NCH, :],
                f"K{sl}", tag=f"cvld{t}",
                deps=[T.get(f"vr_used{t-1}")])
            for h_ in range(H):
                ks = h_ % 2
                def mk_q(e, h_=h_, ks=ks):
                    r = None
                    for kc in range(KC):
                        r = e.matmul(ps[f"big{ks}"][:, :],
                                     sb["wq_sb"][:, kc, h_ * P:(h_ + 1) * P],
                                     sb["h_r"][:, kc, :],
                                     start=(kc == 0), stop=(kc == KC - 1))
                    return r
                p.op("tensor", mk_q, deps=[f"chld{t}", "ld_wqo", PSL.get(f"big{ks}")],
                     tag=f"qmm{t}_{h_}")
                p.op("scalar", lambda e, ks=ks, bs=f"big{ks}": e.activation(
                        sb[f"qq{ks}"][:, :], ps[bs][:, :], AF.Identity),
                     deps=[f"qmm{t}_{h_}", QQLAST[ks]],
                     tag=f"qev{t}_{h_}")
                p.op("tensor", lambda e, ks=ks: e.matmul(
                        ps[f"big{ks}"][:, :], sb["fmqT"][:, :], sb[f"qq{ks}"][:, :],
                        start=True, stop=True),
                     deps=[f"qev{t}_{h_}", "ld_wqo"], tag=f"qymm{t}_{h_}")
                QQLAST[ks] = f"qymm{t}_{h_}"
                def mk_qexp(e, h_=h_, ks=ks):
                    e.activation(sb["qfm_sb"][:, 2 * h_, :], ps[f"big{ks}"][:, :], AF.Exp,
                                 bias=sb["bqp"][:, :])
                    return e.activation(sb["qfm_sb"][:, 2 * h_ + 1, :], ps[f"big{ks}"][:, :], AF.Exp,
                                        bias=sb["bqn"][:, :], scale=-1.0)
                p.op("scalar", mk_qexp,
                     deps=[f"qymm{t}_{h_}", "bias_cols", T.get(f"qfm_used{t-1}")],
                     tag=f"qexp{t}_{h_}")
                PSL[f"big{ks}"] = f"qexp{t}_{h_}"
            T[f"hr_used{t}"] = f"qmm{t}_{H-1}"
            psA = [ps["big0"], ps["big1"]]
            HORD = [0, 2, 4, 6, 1, 3, 5, 7]
            def S2_ap(h_):
                return ps[f"S{h_ // 2}"][:, (2 * h_ % 4) * P:(2 * h_ % 4) * P + DF]
            for k in range(33):
                if k < 32:
                    ch, pos = divmod(k, H)
                    h_ = HORD[pos]
                    sl2 = k % 2
                    pat = psA[sl2][:, 0:P]
                    cds = f"{t}_{k}"
                    if t == 0 and ch == 0:
                        p.op("vector", lambda e, h_=h_, sl2=sl2: e.tensor_copy(
                                sb["Ssnap"][:, sl2, :], SFIN[:, h_ * DF:(h_ + 1) * DF]),
                             deps=["sin7", T.get(f"snapfree{sl2}")], tag=f"snap{cds}")
                    else:
                        p.op("vector", lambda e, h_=h_, sl2=sl2: e.tensor_add(
                                sb["Ssnap"][:, sl2, :], S2_ap(h_),
                                SFIN[:, h_ * DF:(h_ + 1) * DF]),
                             deps=["sin7", T.get(f"supd_bank{h_ // 2}"),
                                   T.get(f"snapfree{sl2}")], tag=f"snap{cds}")
                    def mk_at(e, h_=h_, ch=ch, pat=pat):
                        r = None
                        for dc in range(2):
                            r = e.matmul(pat,
                                         sb["kfm_r"][:, 2 * h_ + dc, ch * CHN:(ch + 1) * CHN],
                                         sb["qfm_sb"][:, 2 * h_ + dc, ch * CHN:(ch + 1) * CHN],
                                         start=(dc == 0), stop=(dc == 1), skip_group_check=True)
                        return r
                    p.op("tensor", mk_at,
                         deps=[f"qexp{t}_{h_}", f"cvld{t}", T.get(f"park_f{sl2}"),
                               T.get(f"atev_f{sl2}"), PSL.get(f"big{sl2}")],
                         tag=f"at{cds}")
                    PSL[f"big{sl2}"] = f"at{cds}"
                    p.op("vector", lambda e, sl2=sl2, pat=pat: e.tensor_tensor(
                            sb["at_sb"][:, sl2, :], pat, sb["triu"][:, :], op=ALU.mult),
                         deps=[f"at{cds}", T.get(f"atsb_f{sl2}")],
                         tag=f"atev{cds}")
                    T[f"atev_f{sl2}"] = f"atev{cds}"
                if k >= 1:
                    km = k - 1
                    ch, pos = divmod(km, H)
                    h_ = HORD[pos]
                    sl2 = km % 2
                    po = psA[sl2][:, P:2 * P]
                    cds = f"{t}_{km}"
                    def mk_o(e, h_=h_, ch=ch, sl2=sl2, po=po):
                        e.matmul(po, sb["at_sb"][:, sl2, :],
                                 sb["v_r"][:, ch, h_ * P:(h_ + 1) * P],
                                 start=False, stop=False, skip_group_check=True)
                        r = None
                        for dc in range(2):
                            r = e.matmul(po,
                                         sb["qfm_sb"][:, 2 * h_ + dc, ch * CHN:(ch + 1) * CHN],
                                         sb["Ssnap"][:, sl2, dc * P:(dc + 1) * P],
                                         start=False, stop=(dc == 1), skip_group_check=True)
                        return r
                    p.op("tensor", mk_o,
                         deps=[f"atev{cds}", f"snap{cds}", f"cvld{t}",
                               T.get(f"park_f{sl2}")],
                         tag=f"omm{cds}")
                    T[f"atsb_f{sl2}"] = f"omm{cds}"
                    T[f"snapfree{sl2}"] = f"omm{cds}"
                    def mk_su(e, h_=h_, ch=ch, t=t):
                        r = None
                        for dc in range(2):
                            st_ = (t == 0 and ch == 0 and dc == 0 and h_ % 2 == 0)
                            r = e.matmul(S_ap(h_, dc),
                                         sb["ktm_r"][:, ch, h_ * DF + dc * P:h_ * DF + (dc + 1) * P],
                                         sb["v_r"][:, ch, h_ * P:(h_ + 1) * P],
                                         start=st_, stop=False, skip_group_check=True)
                        return r
                    p.op("tensor", mk_su,
                         deps=[f"snap{cds}", f"cvld{t}"],
                         tag=f"supd{cds}")
                    T[f"supd_bank{h_ // 2}"] = f"supd{cds}"
                    p.op("vector", lambda e, km=km, po=po: e.tensor_copy(
                            sb["orow"][:, km, :], po),
                         deps=[f"omm{cds}", T.get(f"orow_used{t-1}_{km}")],
                         tag=f"vpark{cds}")
                    T[f"park_f{sl2}"] = f"vpark{cds}"
            T[f"kfmr_used{t}"] = f"at{t}_31"
            T[f"ktmr_used{t}"] = f"supd{t}_31"
            T[f"vr_used{t}"] = f"supd{t}_31"
            T[f"qfm_used{t}"] = f"omm{t}_31"
            for g in range(NCH):
                dh = g % 2
                p.op("scalar", lambda e, g=g, dh=dh: e.activation(
                        sb["o2w"][:, dh, :].rearrange("q (i c) -> q i c", c=P),
                        sb["orow"][:, g * 8:(g + 1) * 8, :], AF.Square, scale=2.0 ** -24),
                     deps=[f"vpark{t}_{g*8+7}", T.get(f"o2w_rd{dh}")],
                     tag=f"osqa{t}_{g}")
                p.op("vector", lambda e, g=g, dh=dh: e.tensor_reduce(
                        sb["rms32"][:, g * 8:(g + 1) * 8].unsqueeze(2),
                        sb["o2w"][:, dh, :].rearrange("q (i c) -> q i c", c=P),
                        axis=AX.X, op=ALU.add),
                     deps=[f"osqa{t}_{g}"], tag=f"osqr{t}_{g}")
                T[f"o2w_rd{dh}"] = f"osqr{t}_{g}"
            p.op("scalar", lambda e: e.activation(sb["rms32"][:, :], sb["rms32"][:, :], AF.Sqrt,
                                                  bias=sb["c2ecol"][:, :]),
                 deps=[f"osqr{t}_{NCH-1}", "c2e_set", T.get(f"rms32_used{t-1}")],
                 tag=f"rchain1_{t}")
            p.op("scalar", lambda e: e.activation(sb["rms32"][:, :], sb["rms32"][:, :], AF.Sqrt),
                 deps=[f"rchain1_{t}"], tag=f"rchain2_{t}")
            p.op("vector", lambda e: e.reciprocal(sb["rms32"][:, :], sb["rms32"][:, :]),
                 deps=[f"rchain2_{t}"], tag=f"rchain3_{t}")
            p.op("scalar", lambda e: e.activation(sb["rms32"][:, :], sb["rms32"][:, :], AF.Square,
                                                  scale=(128.0 ** 0.25) * (2.0 ** -12)),
                 deps=[f"rchain3_{t}"], tag=f"rchain_{t}")
            for g in range(NCH):
                p.op("vector", lambda e, g=g: e.tensor_tensor(
                        sb["orow"][:, g * 8:(g + 1) * 8, :],
                        sb["orow"][:, g * 8:(g + 1) * 8, :],
                        sb["rms32"][:, g * 8:(g + 1) * 8].unsqueeze(2).broadcast_to([P, 8, P]),
                        op=ALU.mult),
                     deps=[f"rchain_{t}"], tag=f"ong{t}_{g}")
            T[f"rms32_used{t}"] = f"ong{t}_{NCH-1}"
            pxs = [ps["pxa"], ps["pxb"]]
            POS = {h2_: i2_ for i2_, h2_ in enumerate(HORD)}
            for ch in range(NCH):
                for half in range(2):
                    px = pxs[(ch * 2 + half) % 2]
                    ks = [ch * 8 + POS[half * 4 + j] for j in range(4)]
                    def mk_otp(e, ks=ks, px=px):
                        r = None
                        for j, k_ in enumerate(ks):
                            r = e.transpose(px[:, j * P:(j + 1) * P],
                                            sb["orow"][:, k_, :], sb["ident"][:, :])
                        return r
                    p.op("tensor", mk_otp,
                         deps=[f"ong{t}_{ch}", T.get(f"px_ev{(ch*2+half)%2}"), "ld_consts"],
                         tag=f"otpg{t}_{ch}_{half}")
                    for k_ in ks:
                        T[f"orow_used{t}_{k_}"] = f"otpg{t}_{ch}_{half}"
                    p.op("vector", lambda e, ch=ch, half=half, px=px: e.tensor_copy(
                            sb["o_sb"][:, half * 4:(half + 1) * 4, ch * CHN:(ch + 1) * CHN],
                            px[:, 0:TT].rearrange("q (j c) -> q j c", c=P)),
                         deps=[f"otpg{t}_{ch}_{half}", T.get(f"o_used{t-1}")],
                         tag=f"oevg{t}_{ch}_{half}")
                    T[f"px_ev{(ch*2+half)%2}"] = f"oevg{t}_{ch}_{half}"
            for fo in range(KC):
                bs = f"big{fo % 2}"
                def mk_wo(e, fo=fo, bs=bs):
                    r = None
                    for kc in range(KC):
                        r = e.matmul(ps[bs][:, :],
                                     sb["wo_sb"][:, kc, fo * P:(fo + 1) * P],
                                     sb["o_sb"][:, kc, :],
                                     start=(kc == 0), stop=(kc == KC - 1))
                    return r
                p.op("tensor", mk_wo,
                     deps=[f"oevg{t}_{NCH-1}_1", "wo_rms", PSL.get(bs),
                           T.get(f"park_f{fo % 2}"), T.get(f"atev_f{fo % 2}")],
                     tag=f"wo{t}_{fo}")
                p.op("scalar", lambda e, fo=fo, bs=bs: e.activation(
                        sb["ep_sb"][:, :], ps[bs][:, :], AF.Identity,
                        scale=sb["mcols"][:, 2, fo:fo + 1]),
                     deps=[f"wo{t}_{fo}", "modcols_done", T.get("ep_used")],
                     tag=f"ep1{t}_{fo}")
                PSL[bs] = f"ep1{t}_{fo}"
                p.op("vector", lambda e, fo=fo, sl=sl: e.tensor_add(
                        sb["x2_sb"][:, fo, :], cxs[sl][:, fo, :], sb["ep_sb"][:, :]),
                     deps=[f"ep1{t}_{fo}", f"cxld{t}", T.get(f"x2_free{t-1}")],
                     tag=f"x2{t}_{fo}")
                T["ep_used"] = f"x2{t}_{fo}"
            T[f"o_used{t}"] = f"wo{t}_{KC-1}"
            T[f"cx_free{t}"] = f"x2{t}_{KC-1}"
            dma("gpsimd", x2h_v[:, :, t * TT:(t + 1) * TT], sb["x2_sb"][:, :, :],
                "SPB", tag=f"x2sp{t}", deps=[f"x2{t}_{KC-1}"])
            ln2t = emit_ln(
                f"cln{t}", f"x2{t}_{KC-1}", sb["x2_sb"], sb["cxbf"],
                [sb["csq0"], sb["csq1"]], sb["ctmp1"],
                sb["ctmp2"], sb["crsb"], sb["cmrsb"], sb["h2_sb"], sb["cb1"],
                sb["mcols"][:, 3, :], t * KC,
                cast_deps=[T.get(f"cln_xbf_free{t-1}")],
                mod_deps=[f"x2sp{t}"])
            T[f"x2_free{t}"] = f"cln{t}cast{KC-1}"
            T[f"cln_xbf_free{t}"] = f"cln{t}t1_{KC-1}"
            dma("gpsimd", h2h_v[:, :, t * TT:(t + 1) * TT], sb["h2_sb"][:, :, :],
                "SPB", tag=f"h2sp{t}", deps=[ln2t])
            T[f"h2sp{t}"] = f"h2sp{t}"

        p.barrier("p2c", sb)

        MB = {"prev_stage": None, "prev_ld": None}

        def mbarrier(name, folds, dep):
            for j, (sn, lo, hi) in enumerate(folds):
                p.op("vector", lambda e, sn=sn, lo=lo, hi=hi, j=j: e.tensor_reduce(
                        sb["mxst"][:, j:j + 1], sb[sn][:, lo:hi], axis=AX.X, op=ALU.max),
                     deps=[dep, MB["prev_stage"]], tag=f"{name}_f{j}")
            if len(folds) == 1:
                p.op("vector", lambda e: e.memset(sb["mxst"][:, 1:2], 0.0),
                     deps=[MB["prev_stage"]], tag=f"{name}_f1")
            dma("gpsimd", mx_in, sb["mxst"][:, :], "SG", tag=f"{name}_stage",
                deps=[f"{name}_f0", f"{name}_f1"])
            MB["prev_stage"] = f"{name}_stage"
            p.op("gpsimd", lambda e: e.collective_compute(
                    "AllReduce", ALU.max, replica_groups=[list(range(8))],
                    ins=[mx_in.opt()], outs=[mx_out.opt()]),
                 deps=[f"{name}_stage"], sem="CC", tag=f"{name}_cc")
            dma("gpsimd", sb["mxrow"][:, :], mx_out, "SG", tag=f"{name}_ld",
                deps=[f"{name}_cc", MB["prev_ld"]])
            p.op("vector", lambda e: e.tensor_reduce(
                    sb["mxfold"][:, :].unsqueeze(2),
                    sb["mxrow"][:, :].rearrange("a (q c) -> a c q", c=2),
                    axis=AX.X, op=ALU.max),
                 deps=[f"{name}_ld"], tag=f"{name}_max")
            MB["prev_ld"] = f"{name}_max"

        def bcast_col(name, src_ap, dep, tg):
            p.op("tensor", lambda e: e.matmul(ps["big0"][:, 0:1], sb["onesrow"][:, :],
                                              src_ap, start=True, stop=True),
                 deps=[dep, PSL.get("big0"), "ld_consts"], tag=f"bcm_{tg}")
            p.op("scalar", lambda e: e.activation(sb[name][:, :], ps["big0"][:, 0:1], AF.Identity),
                 deps=[f"bcm_{tg}"], tag=f"bce_{tg}")
            PSL["big0"] = f"bce_{tg}"
            return f"bce_{tg}"

        if PH < 4:
            finish_early()
            return
        mbarrier("m0", [("m0slots", 0, 64)], f"cln{NT-1}m0_{KC-1}")
        p.op("vector", lambda e: e.tensor_scalar(sb["srow"][:, 0:1], sb["mxfold"][:, 0:1],
                                                 1.0 / QMAX, None, op0=ALU.mult),
             deps=["m0_max"], tag="s0row")
        p.op("vector", lambda e: e.reciprocal(sb["srow"][:, 1:2], sb["srow"][:, 0:1]),
             deps=["s0row"], tag="r0row")
        bcast_col("s0c", sb["srow"][0:1, 0:1], "s0row", "s0c")
        bcast_col("r0c", sb["srow"][0:1, 1:2], "r0row", "r0c")

        if PH < 5:
            finish_early()
            return
        dma("sync", sb["wup_sb"][:, :, :], wupT_q.rearrange("(kc q) f -> q kc f", q=P),
            "LF", tag="ld_wup", deps=[T["dq_done"]])
        dma("sync", sb["wg_sb"][:, :, :], wgT_q.rearrange("(kc q) f -> q kc f", q=P),
            "LF", tag="ld_wg", deps=[T["dq_done"]])
        h2r = [sb["h2r0"], sb["h2r1"]]
        xqt = [sb["xqt0"], sb["xqt1"]]
        for t in range(NT):
            sl = t % 2
            dma("sync", h2r[sl][:, :, :], h2h_v[:, :, t * TT:(t + 1) * TT],
                f"H{sl}", tag=f"bh2ld{t}",
                deps=[f"h2sp{NT-1}", T.get(f"bh2_free{t-2}")])
            for kc in range(KC):
                ts_ = kc % 2
                p.op("scalar", lambda e, sl=sl, kc=kc, ts_=ts_: e.activation(
                        xqt[ts_][:, :], h2r[sl][:, kc, :], AF.Identity,
                        scale=sb["r0c"][:, :], bias=sb["rccol"][:, :]),
                     deps=[f"bh2ld{t}", "bce_r0c",
                           (f"bxq{t}_{kc-2}" if kc >= 2 else
                            (f"bxq{t-1}_{KC-2+kc}" if t >= 1 else None))],
                     tag=f"bxqt{t}_{kc}")
                p.op("vector", lambda e, kc=kc, ts_=ts_: e.tensor_scalar(
                        sb["xq_sb"][:, kc, :], xqt[ts_][:, :], -RC, sb["s0c"][:, :],
                        op0=ALU.add, op1=ALU.mult),
                     deps=[f"bxqt{t}_{kc}", "bce_s0c", T.get(f"xq_used{t-1}")],
                     tag=f"bxq{t}_{kc}")
            T[f"bh2_free{t}"] = f"bxqt{t}_{KC-1}"
            for which, (wname, stname, slots, ldt, dsth) in enumerate(
                    [("wup_sb", "Ust", "m1slots", "ld_wup", Uh),
                     ("wg_sb", "Gst", "m2slots", "ld_wg", Gh)]):
                for ib in range(NI):
                    bs = f"big{ib % 2}"
                    def mk_u(e, wname=wname, ib=ib, bs=bs):
                        r = None
                        for kc in range(KC):
                            r = e.matmul(ps[bs][:, :], sb[wname][:, kc, ib * P:(ib + 1) * P],
                                         sb["xq_sb"][:, kc, :],
                                         start=(kc == 0), stop=(kc == KC - 1))
                        return r
                    p.op("tensor", mk_u,
                         deps=[f"bxq{t}_{KC-1}", "ld_wg", PSL.get(bs),
                               PSL.get(bs + "_mx")],
                         tag=f"bmm{t}_{which}_{ib}")
                    p.op("scalar", lambda e, stname=stname, ib=ib, bs=bs: e.activation(
                            sb[stname][:, ib, :], ps[bs][:, :], AF.Identity),
                         deps=[f"bmm{t}_{which}_{ib}", T.get(f"bst{t-1}_1")],
                         tag=f"bev{t}_{which}_{ib}")
                    p.op("vector", lambda e, slots=slots, stname=stname, ib=ib, t=t: e.tensor_reduce(
                            sb[slots][:, t * NI + ib:t * NI + ib + 1],
                            sb[stname][:, ib, :], axis=AX.X, op=ALU.max,
                            apply_absolute_value=True),
                         deps=[f"bev{t}_{which}_{ib}"],
                         tag=f"bmx{t}_{which}_{ib}")
                    PSL[bs] = f"bev{t}_{which}_{ib}"
                    PSL[bs + "_mx"] = f"bmx{t}_{which}_{ib}"
                dma("gpsimd", dsth[:, t * NI * TT:(t + 1) * NI * TT],
                    sb[stname][:, :, :], "SC1", tag=f"bst{t}_{which}",
                    deps=[f"bev{t}_{which}_{NI-1}", f"bmx{t}_{which}_{NI-1}"])
                T[f"bst{t}_{which}"] = f"bst{t}_{which}"
            T[f"xq_used{t}"] = f"bmm{t}_1_{NI-1}"

        if PH < 6:
            finish_early()
            return
        mbarrier("m12", [("m1slots", 0, NT * NI), ("m2slots", 0, NT * NI)],
                 f"bmx{NT-1}_1_{NI-1}")
        p.op("vector", lambda e: e.tensor_scalar(sb["srow"][:, 2:4], sb["mxfold"][:, 0:2],
                                                 1.0 / QMAX, None, op0=ALU.mult),
             deps=["m12_max"], tag="s12row")
        p.op("vector", lambda e: e.reciprocal(sb["srow"][:, 4:6], sb["srow"][:, 2:4]),
             deps=["s12row"], tag="r12row")
        bcast_col("r1c", sb["srow"][0:1, 4:5], "r12row", "r1c")
        bcast_col("r2c", sb["srow"][0:1, 5:6], "r12row", "r2c")
        bcast_col("s2c", sb["srow"][0:1, 3:4], "s12row", "s2c")
        p.op("vector", lambda e: e.tensor_scalar(sb["brc2c"][:, :], sb["s2c"][:, :],
                                                 -RC, None, op0=ALU.mult),
             deps=["bce_s2c"], tag="brc2_done")

        dma("sync", sb["wd_sb"][:, :, :], wdT_q.rearrange("(ic q) f -> q ic f", q=P),
            "LF", tag="ld_wd", deps=[T["dq_done"], f"bst{NT-1}_1"])
        gld = [sb["bufA0"], sb["bufA1"]]
        t1f = [sb["t1f0"], sb["t1f1"]]
        sgf = [sb["sgf0"], sb["sgf1"]]
        NH = NI // 2
        for t in range(NT):
            sl = t % 2
            dma("gpsimd", gld[sl][:, :, :], Gh[:, t * NI * TT:(t + 1) * NI * TT],
                f"U{sl}", tag=f"c1ld{t}",
                deps=[f"bst{NT-1}_1", T.get(f"c1_free{t-2}")])
            for ib in range(NI):
                ts_ = ib % 2
                pg = (f"c1gp{t}_{ib-2}" if ib >= 2 else
                      (f"c1gp{t-1}_{NI-2+ib}" if t >= 1 else None))
                p.op("scalar", lambda e, sl=sl, ib=ib, ts_=ts_: e.activation(
                        t1f[ts_][:, :], gld[sl][:, ib, :], AF.Identity,
                        scale=sb["r2c"][:, :], bias=sb["rccol"][:, :]),
                     deps=[f"c1ld{t}", "bce_r2c", pg],
                     tag=f"c1t1{t}_{ib}")
                p.op("scalar", lambda e, ts_=ts_: e.activation(
                        sgf[ts_][:, :], t1f[ts_][:, :], AF.Sigmoid,
                        scale=sb["s2c"][:, :], bias=sb["brc2c"][:, :]),
                     deps=["brc2_done", f"c1t1{t}_{ib}"], tag=f"c1sg{t}_{ib}")
                p.op("scalar", lambda e, ts_=ts_: e.activation(
                        sb["tgbf"][:, ts_, :], t1f[ts_][:, :], AF.Identity,
                        bias=sb["mrccol"][:, :]),
                     deps=[f"c1sg{t}_{ib}", "mrc_set", pg], tag=f"c1tg{t}_{ib}")
                p.op("vector", lambda e, ts_=ts_: e.tensor_scalar(
                        sgf[ts_][:, :], sgf[ts_][:, :], 65536.0, RC,
                        op0=ALU.mult, op1=ALU.add),
                     deps=[f"c1sg{t}_{ib}"], tag=f"c1t3{t}_{ib}")
                p.op("vector", lambda e, ts_=ts_: e.tensor_scalar(
                        sb["srbf"][:, :], sgf[ts_][:, :], -RC, 65535.0,
                        op0=ALU.add, op1=ALU.min),
                     deps=[f"c1t3{t}_{ib}"], tag=f"c1sr{t}_{ib}")
                p.op("vector", lambda e, ib=ib, ts_=ts_: e.tensor_mul(
                        sb["g2st"][:, ib, :], sb["tgbf"][:, ts_, :], sb["srbf"][:, :]),
                     deps=[f"c1sr{t}_{ib}", f"c1tg{t}_{ib}",
                           T.get(f"c1st{t-1}_{ib // NH}")],
                     tag=f"c1gp{t}_{ib}")
                p.op("vector", lambda e, ib=ib, t=t: e.tensor_reduce(
                        sb["m3slots"][:, t * NI + ib:t * NI + ib + 1],
                        sb["g2st"][:, ib, :], axis=AX.X, op=ALU.max,
                        apply_absolute_value=True),
                     deps=[f"c1gp{t}_{ib}"], tag=f"c1mx{t}_{ib}")
                if ib == NH - 1:
                    dma("gpsimd", g2h[:, (t * NI) * TT:(t * NI + NH) * TT],
                        sb["g2st"][:, 0:NH, :],
                        "SC2", tag=f"c1st{t}_0", deps=[f"c1gp{t}_{NH-1}", f"c1mx{t}_{NH-1}"])
                    T[f"c1st{t}_0"] = f"c1st{t}_0"
            dma("gpsimd", g2h[:, (t * NI + NH) * TT:(t + 1) * NI * TT],
                sb["g2st"][:, NH:NI, :],
                "SC2B", tag=f"c1st{t}_1", deps=[f"c1gp{t}_{NI-1}", f"c1mx{t}_{NI-1}"])
            T[f"c1st{t}_1"] = f"c1st{t}_1"
            T[f"c1_free{t}"] = f"c1t1{t}_{NI-1}"

        if PH < 7:
            finish_early()
            return
        mbarrier("m3", [("m3slots", 0, NT * NI)], f"c1mx{NT-1}_{NI-1}")
        p.op("vector", lambda e: e.tensor_mul(sb["srow"][:, 6:7], sb["mxfold"][:, 0:1],
                                              sb["srow"][:, 3:4]),
             deps=["m3_max", "s12row"], tag="m3a")
        p.op("vector", lambda e: e.tensor_scalar(sb["srow"][:, 6:7], sb["srow"][:, 6:7],
                                                 1.0 / (65536.0 * QMAX), None, op0=ALU.mult),
             deps=["m3a"], tag="s3row")
        p.op("vector", lambda e: e.reciprocal(sb["srow"][:, 7:8], sb["srow"][:, 6:7]),
             deps=["s3row"], tag="r3row")
        p.op("vector", lambda e: e.tensor_mul(sb["srow"][:, 8:9], sb["srow"][:, 3:4],
                                              sb["srow"][:, 7:8]),
             deps=["r3row"], tag="c32a")
        p.op("vector", lambda e: e.tensor_scalar(sb["srow"][:, 8:9], sb["srow"][:, 8:9],
                                                 1.0 / 65536.0, None, op0=ALU.mult),
             deps=["c32a"], tag="c32row")
        bcast_col("c32c", sb["srow"][0:1, 8:9], "c32row", "c32c")

        uld = [sb["bufB0"], sb["bufB1"]]
        for t in range(NT):
            sl = t % 2
            dma("gpsimd", gld[sl][:, :, :], g2h[:, t * NI * TT:(t + 1) * NI * TT],
                f"U{sl}", tag=f"c2gld{t}",
                deps=[f"c1st{NT-1}_0", f"c1st{NT-1}_1", T.get(f"c2_free{t-2}")])
            dma("gpsimd", uld[sl][:, :, :], Uh[:, t * NI * TT:(t + 1) * NI * TT],
                f"K{sl}", tag=f"c2uld{t}",
                deps=[f"bst{NT-1}_1", T.get(f"c2_free{t-2}")])
            for ib in range(NI):
                ts_ = ib % 2
                pg = (f"c2pr{t}_{ib-2}" if ib >= 2 else
                      (f"c2pr{t-1}_{NI-2+ib}" if t >= 1 else None))
                p.op("scalar", lambda e, sl=sl, ib=ib, ts_=ts_: e.activation(
                        t1f[ts_][:, :], gld[sl][:, ib, :], AF.Identity,
                        scale=sb["c32c"][:, :], bias=sb["rccol"][:, :]),
                     deps=[f"c2gld{t}", "bce_c32c", pg], tag=f"c2t4{t}_{ib}")
                p.op("vector", lambda e, ts_=ts_: e.tensor_scalar(
                        sb["g2bf"][:, :], t1f[ts_][:, :], -RC, None, op0=ALU.add),
                     deps=[f"c2t4{t}_{ib}"], tag=f"c2g2{t}_{ib}")
                p.op("scalar", lambda e, sl=sl, ib=ib, ts_=ts_: e.activation(
                        sgf[ts_][:, :], uld[sl][:, ib, :], AF.Identity,
                        scale=sb["r1c"][:, :], bias=sb["rccol"][:, :]),
                     deps=[f"c2uld{t}", "bce_r1c"], tag=f"c2t5{t}_{ib}")
                p.op("vector", lambda e, ts_=ts_: e.tensor_scalar(
                        sb["upbf"][:, :], sgf[ts_][:, :], -RC, None, op0=ALU.add),
                     deps=[f"c2t5{t}_{ib}"], tag=f"c2up{t}_{ib}")
                p.op("vector", lambda e, ib=ib: e.tensor_mul(
                        sb["prst"][:, ib, :], sb["g2bf"][:, :], sb["upbf"][:, :]),
                     deps=[f"c2g2{t}_{ib}", f"c2up{t}_{ib}",
                           T.get(f"c2st{t-1}_{ib // NH}")],
                     tag=f"c2pr{t}_{ib}")
                if ib == NH - 1:
                    dma("gpsimd", prh[:, (t * NI) * TT:(t * NI + NH) * TT],
                        sb["prst"][:, 0:NH, :],
                        "SC3", tag=f"c2st{t}_0", deps=[f"c2pr{t}_{NH-1}", f"c2mx{t}_{NH-1}"])
                    T[f"c2st{t}_0"] = f"c2st{t}_0"
                p.op("vector", lambda e, ib=ib, t=t: e.tensor_reduce(
                        sb["m4slots"][:, t * NI + ib:t * NI + ib + 1],
                        sb["prst"][:, ib, :], axis=AX.X, op=ALU.max,
                        apply_absolute_value=True),
                     deps=[f"c2pr{t}_{ib}"], tag=f"c2mx{t}_{ib}")
            dma("gpsimd", prh[:, (t * NI + NH) * TT:(t + 1) * NI * TT],
                sb["prst"][:, NH:NI, :],
                "SC3B", tag=f"c2st{t}_1", deps=[f"c2pr{t}_{NI-1}", f"c2mx{t}_{NI-1}"])
            T[f"c2st{t}_1"] = f"c2st{t}_1"
            T[f"c2_free{t}"] = f"c2pr{t}_{NI-1}"

        if PH < 8:
            finish_early()
            return
        mbarrier("m4", [("m4slots", 0, NT * NI)], f"c2mx{NT-1}_{NI-1}")
        p.op("vector", lambda e: e.tensor_mul(sb["srow"][:, 9:10], sb["srow"][:, 6:7],
                                              sb["srow"][:, 2:3]),
             deps=["s3row", "s12row", "m4_max"], tag="s31")
        p.op("vector", lambda e: e.tensor_mul(sb["srow"][:, 10:11], sb["mxfold"][:, 0:1],
                                              sb["srow"][:, 9:10]),
             deps=["s31"], tag="m4a")
        p.op("vector", lambda e: e.tensor_scalar(sb["srow"][:, 10:11], sb["srow"][:, 10:11],
                                                 1.0 / QMAX, None, op0=ALU.mult),
             deps=["m4a"], tag="s4row")
        p.op("vector", lambda e: e.reciprocal(sb["srow"][:, 11:12], sb["srow"][:, 10:11]),
             deps=["s4row"], tag="r4row")
        p.op("vector", lambda e: e.tensor_mul(sb["srow"][:, 11:12], sb["srow"][:, 11:12],
                                              sb["srow"][:, 9:10]),
             deps=["r4row"], tag="c4row")
        bcast_col("c4c", sb["srow"][0:1, 11:12], "c4row", "c4c")
        bcast_col("s4c", sb["srow"][0:1, 10:11], "s4row", "s4c")
        p.op("vector", lambda e: e.tensor_scalar(sb["gbs4"][:, :], sb["mcols"][:, 5, :],
                                                 sb["s4c"][:, :], None, op0=ALU.mult),
             deps=["bce_s4c", "modcols_done"], tag="gbs4_done")

        x2r = [sb["x2r0"], sb["x2r1"]]
        t1g = [sb["t1g0"], sb["t1g1"]]
        outs = [sb["outs0"], sb["outs0"]]
        ep2 = [sb["ep20"], sb["ep21"]]
        for t in range(NT):
            sl = t % 2
            dma("gpsimd", gld[sl][:, :, :], prh[:, t * NI * TT:(t + 1) * NI * TT],
                f"U{sl}", tag=f"c3pld{t}",
                deps=[f"c2st{NT-1}_0", f"c2st{NT-1}_1", T.get(f"c3_free{t-2}")])
            dma("sync", x2r[sl][:, :, :], x2h_v[:, :, t * TT:(t + 1) * TT],
                f"X{sl}", tag=f"c3xld{t}",
                deps=[f"h2sp{NT-1}", f"c2st{NT-1}_0", f"c2st{NT-1}_1", T.get(f"c3x2_free{t-2}")])
            for ib2 in range(NI // 2):
                ts_ = ib2 % 2
                pg = (f"c3o{t}_{ib2-1}" if ib2 >= 1 else
                      (f"c3o{t-1}_{NI//2-1}" if t >= 1 else None))
                def mk_c3t(e, sl=sl, ib2=ib2, ts_=ts_):
                    r = None
                    for j in range(2):
                        r = e.activation(t1f[ts_][:, :] if j == 0 else t1g[ts_][:, :],
                                         gld[sl][:, 2 * ib2 + j, :], AF.Identity,
                                         scale=sb["c4c"][:, :], bias=sb["rccol"][:, :])
                    return r
                p.op("scalar", mk_c3t,
                     deps=[f"c3pld{t}", "bce_c4c", pg], tag=f"c3t{t}_{ib2}")
                def mk_c3o(e, ib2=ib2, ts_=ts_):
                    e.tensor_scalar(sb["obf"][:, 2 * ib2, :], t1f[ts_][:, :],
                                    -RC, None, op0=ALU.add)
                    return e.tensor_scalar(sb["obf"][:, 2 * ib2 + 1, :], t1g[ts_][:, :],
                                           -RC, None, op0=ALU.add)
                p.op("vector", mk_c3o,
                     deps=[f"c3t{t}_{ib2}", T.get(f"c3mm_done{t-1}")],
                     tag=f"c3o{t}_{ib2}")
            for fo in range(KC):
                bs = f"big{fo % 2}"
                es = fo % 2
                def mk_dn(e, fo=fo, bs=bs):
                    r = None
                    for ic in range(NI):
                        r = e.matmul(ps[bs][:, :], sb["wd_sb"][:, ic, fo * P:(fo + 1) * P],
                                     sb["obf"][:, ic, :],
                                     start=(ic == 0), stop=(ic == NI - 1))
                    return r
                p.op("tensor", mk_dn,
                     deps=[f"c3o{t}_{NI//2-1}", "ld_wd", PSL.get(bs)],
                     tag=f"c3mm{t}_{fo}")
                p.op("scalar", lambda e, fo=fo, bs=bs, es=es: e.activation(
                        ep2[es][:, :], ps[bs][:, :], AF.Identity,
                        scale=sb["gbs4"][:, fo:fo + 1]),
                     deps=[f"c3mm{t}_{fo}", "gbs4_done", T.get(f"c3ep_free{t}_{fo-2}"),
                           T.get(f"c3ep_freeprev{t}") if fo < 2 else None],
                     tag=f"c3ep{t}_{fo}")
                PSL[bs] = f"c3ep{t}_{fo}"
                p.op("vector", lambda e, fo=fo, sl=sl, es=es: e.tensor_add(
                        outs[sl][:, fo, :], x2r[sl][:, fo, :], ep2[es][:, :]),
                     deps=[f"c3ep{t}_{fo}", f"c3xld{t}", T.get(f"c3ost{t-1}")],
                     tag=f"c3out{t}_{fo}")
                T[f"c3ep_free{t}_{fo}"] = f"c3out{t}_{fo}"
            T[f"c3ep_freeprev{t+1}"] = f"c3out{t}_{KC-1}"
            T[f"c3mm_done{t}"] = f"c3mm{t}_{KC-1}"
            T[f"c3x2_free{t}"] = f"c3out{t}_{KC-1}"
            T[f"c3_free{t}"] = f"c3t{t}_{NI//2-1}"
            dma("sync", out_v[:, :, t * TT:(t + 1) * TT], outs[sl][:, :, :],
                "ST", tag=f"c3ost{t}", deps=[f"c3out{t}_{KC-1}"])
            T[f"c3ost{t}"] = f"c3ost{t}"

        p.op("sync", lambda e: e.wait_ge(p.sems["ST"], 16 * NT), inc=0)
        p.op("gpsimd", lambda e: e.wait_ge(p.sems["ST"], 16 * NT), inc=0)

    p.reset("count")
    program()
    snap = dict(p.tags)
    p.reset("count")
    program()
    assert p.tags == snap, "walk is not deterministic"

    with ctx:
        block = ctx.enter_context(nc.Block())

        def mk_section(eng):
            def f(engine):
                p.engs[eng] = engine
                p.reset(eng)
                program()
            return f

        block.sync(mk_section("sync"))
        block.gpsimd(mk_section("gpsimd"))
        block.tensor(mk_section("tensor"))
        block.scalar(mk_section("scalar"))
        block.vector(mk_section("vector"))

    return nc


def stage(inputs):
    x = np.asarray(inputs["L0_attn_x"], dtype=np.float32)
    shared = {
        "wqT": np.ascontiguousarray(np.asarray(inputs["L0_attn_Wq"], np.float32).T),
        "wkT": np.ascontiguousarray(np.asarray(inputs["L0_attn_Wk"], np.float32).T),
        "wvT": np.ascontiguousarray(np.asarray(inputs["L0_attn_Wv"], np.float32).T),
        "woT": np.ascontiguousarray(np.asarray(inputs["L0_attn_Wo"], np.float32).T),
        "fmqT": np.ascontiguousarray(np.asarray(inputs["L0_attn_fmq_w"], np.float32).T),
        "fmkT": np.ascontiguousarray(np.asarray(inputs["L0_attn_fmk_w"], np.float32).T),
        "fmqb": np.ascontiguousarray(np.asarray(inputs["L0_attn_fmq_b"], np.float32).reshape(P, 1)),
        "fmkb": np.ascontiguousarray(np.asarray(inputs["L0_attn_fmk_b"], np.float32).reshape(P, 1)),
        "rmsw": np.ascontiguousarray(np.asarray(inputs["L0_attn_rms_w"], np.float32).reshape(P, 1)),
        "wup": np.ascontiguousarray(np.asarray(inputs["L0_ffn_w_up"], np.float32)),
        "wgate": np.ascontiguousarray(np.asarray(inputs["L0_ffn_w_gate"], np.float32)),
        "wdown": np.ascontiguousarray(np.asarray(inputs["L0_ffn_w_down"], np.float32)),
        "wmod": np.ascontiguousarray(np.asarray(inputs["L0_wmod"], np.float32)),
    }
    c = np.asarray(inputs["L0_c"], dtype=np.float32)
    in_maps = []
    for core in range(8):
        b, hf = core // 2, core % 2
        m = dict(shared)
        m["xT"] = np.ascontiguousarray(x[b, hf * TOK:(hf + 1) * TOK, :].T)
        m["crow"] = np.ascontiguousarray(c[b:b + 1, :])
        sm = np.zeros((1, 8), dtype=np.float32)
        if hf == 1:
            sm[0, core - 1] = 1.0
        m["smask"] = sm
        in_maps.append(m)
    return in_maps


def kernel(**inputs):
    in_maps = stage(inputs)
    nc = build_nc()
    res = run_bass_kernel_spmd(nc, in_maps, core_ids=list(range(8)))
    out = np.empty((4, 2 * TOK, D), dtype=np.float32)
    for core in range(8):
        b, hf = core // 2, core % 2
        out[b, hf * TOK:(hf + 1) * TOK, :] = res.results[core]["out"].T
    return out


# version 06
# speedup vs baseline: 1.2333x; 1.0293x over previous
import math
import os
import numpy as np
import ml_dtypes
from contextlib import ExitStack

import concourse.bass as bass
import concourse.mybir as mybir
from concourse.bass_utils import run_bass_kernel_spmd

F32 = mybir.dt.float32
BF16 = mybir.dt.bfloat16
AF = mybir.ActivationFunctionType
ALU = mybir.AluOpType
AX = mybir.AxisListType

P = 128
D = 1024
KC = 8
H = 8
DF = 256
TOK = 4096
NT = 8
TT = 512
NCH = 4
CHN = 128
DFF = 3072
NI = 24
RC = 12582912.0
LN16 = math.log(16.0)
QMAX = 32767.0

PH = int(os.environ.get("KPHASES", "99"))
KNOB = os.environ.get("KNOB", "")
ENGSEM = dict(tensor="T", vector="V", scalar="S", sync="YN", gpsimd="GN")
NONDMA = {"T", "V", "S", "YN", "GN"}


class Prog:
    def __init__(self):
        self.mode = "count"
        self.tags = {}
        self.cnt = {}
        self.engs = {}
        self.sems = {}
        self.water = {}

    def reset(self, mode):
        self.mode = mode
        self.cnt = {k: 0 for k in self.sems}
        self.water = {}

    def settag(self, tag, srctag):
        if self.mode == "count" and tag not in self.tags:
            self.tags[tag] = self.tags[srctag]

    def op(self, eng, emit, deps=(), inc=1, tag=None, sem=None):
        semname = sem or ENGSEM[eng]
        self.cnt[semname] = self.cnt.get(semname, 0) + inc
        val = self.cnt[semname]
        if self.mode == "count":
            if tag is not None:
                if tag in self.tags:
                    assert self.tags[tag] == (semname, val), f"tag drift {tag}"
                self.tags[tag] = (semname, val)
            return tag
        if self.mode != eng:
            return tag
        e = self.engs[eng]
        own = ENGSEM[eng]
        for d in deps:
            if d is None:
                continue
            s, v = self.tags[d]
            if s == own and s in NONDMA:
                assert v <= self.cnt[own], f"forward self-dep {d} on {eng}"
            if self.water.get(s, 0) >= v:
                continue
            e.wait_ge(self.sems[s], v)
            self.water[s] = v
        instr = emit(e)
        if inc:
            instr.then_inc(self.sems[semname], inc)
        return tag

    def barrier(self, name, sb):
        if self.mode == "count":
            self.tags[f"bar_{name}"] = {k: v for k, v in self.cnt.items()}
            return
        vals = self.tags[f"bar_{name}"]
        for eng in ENGSEM:
            if self.mode != eng:
                continue
            e = self.engs[eng]
            own = ENGSEM[eng]
            for sname, v in vals.items():
                if sname == own or v == 0:
                    continue
                if self.water.get(sname, 0) >= v:
                    continue
                e.wait_ge(self.sems[sname], v)
                self.water[sname] = v


def build_nc():
    nc = bass.Bass("TRN2", target_bir_lowering=False)

    xT = nc.dram_tensor("xT", [D, TOK], F32, kind="ExternalInput").ap()
    wqT = nc.dram_tensor("wqT", [D, D], F32, kind="ExternalInput").ap()
    wkT = nc.dram_tensor("wkT", [D, D], F32, kind="ExternalInput").ap()
    wvT = nc.dram_tensor("wvT", [D, D], F32, kind="ExternalInput").ap()
    woT = nc.dram_tensor("woT", [D, D], F32, kind="ExternalInput").ap()
    fmqT = nc.dram_tensor("fmqT", [P, P], F32, kind="ExternalInput").ap()
    fmkT = nc.dram_tensor("fmkT", [P, P], F32, kind="ExternalInput").ap()
    fmqb = nc.dram_tensor("fmqb", [P, 1], F32, kind="ExternalInput").ap()
    fmkb = nc.dram_tensor("fmkb", [P, 1], F32, kind="ExternalInput").ap()
    rmsw = nc.dram_tensor("rmsw", [P, 1], F32, kind="ExternalInput").ap()
    wup = nc.dram_tensor("wup", [DFF, D], F32, kind="ExternalInput").ap()
    wgate = nc.dram_tensor("wgate", [DFF, D], F32, kind="ExternalInput").ap()
    wdown = nc.dram_tensor("wdown", [D, DFF], F32, kind="ExternalInput").ap()
    wmod = nc.dram_tensor("wmod", [D, 6 * D], F32, kind="ExternalInput").ap()
    crow = nc.dram_tensor("crow", [1, D], F32, kind="ExternalInput").ap()
    smask = nc.dram_tensor("smask", [1, 8], F32, kind="ExternalInput").ap()
    out = nc.dram_tensor("out", [D, TOK], F32, kind="ExternalOutput").ap()

    hb_h = nc.dram_tensor("hb_h", [D, TOK], BF16, kind="Internal").ap()
    hb_kfm = nc.dram_tensor("hb_kfm", [H * DF, TOK], BF16, kind="Internal").ap()
    hb_ktm = nc.dram_tensor("hb_ktm", [TOK, H * DF], BF16, kind="Internal").ap()
    hb_v = nc.dram_tensor("hb_v", [TOK, D], BF16, kind="Internal").ap()
    x2h = nc.dram_tensor("x2h", [D, TOK], F32, kind="Internal").ap()
    h2h = nc.dram_tensor("h2h", [D, TOK], BF16, kind="Internal").ap()
    wupT_q = nc.dram_tensor("wupT_q", [D, DFF], BF16, kind="Internal").ap()
    wgT_q = nc.dram_tensor("wgT_q", [D, DFF], BF16, kind="Internal").ap()
    wdT_q = nc.dram_tensor("wdT_q", [DFF, D], BF16, kind="Internal").ap()
    Uh = nc.dram_tensor("Uh", [P, NT * NI * TT], BF16, kind="Internal").ap()
    Gh = nc.dram_tensor("Gh", [P, NT * NI * TT], BF16, kind="Internal").ap()
    g2h = nc.dram_tensor("g2h", [P, NT * NI * TT], BF16, kind="Internal").ap()
    prh = nc.dram_tensor("prh", [P, NT * NI * TT], BF16, kind="Internal").ap()
    sc_in = nc.dram_tensor("sc_in", [P, H * DF], F32, kind="Internal").ap()
    sc_out = nc.dram_tensor("sc_out", [2 * P, H * DF], F32, kind="Internal").ap()
    mx_in = nc.dram_tensor("mx_in", [P, 2], F32, kind="Internal").ap()
    mx_out = nc.dram_tensor("mx_out", [P, 2], F32, kind="Internal").ap()

    idt = nc.inline_tensor(np.eye(P, dtype=ml_dtypes.bfloat16), name="identbf")
    tri = nc.inline_tensor(np.triu(np.ones((P, P), dtype=ml_dtypes.bfloat16)), name="triubf")
    onc = nc.inline_tensor(np.ones((P, 1), dtype=ml_dtypes.bfloat16), name="onescolbf")
    onr = nc.inline_tensor(np.ones((1, P), dtype=np.float32), name="onesrowf")

    sb = {}
    total_sbuf = nc.sbuf_bytes_remaining

    def galloc(name, shape, dt):
        sb[name] = nc.alloc_sbuf_tensor("sb_" + name, shape, dt)
        return sb[name]

    for nm, shp, dt in [
        ("dmy", [1, 1], F32),
        ("ident", [P, P], BF16), ("triu", [P, P], BF16),
        ("onescol", [P, 1], BF16), ("onesrow", [1, P], F32),
        ("epscol", [P, 1], F32), ("rccol", [P, 1], F32), ("c2ecol", [P, 1], F32),
        ("fmqT", [P, P], BF16), ("fmkT", [P, P], BF16),
        ("fmqb", [P, 1], F32), ("fmkb", [P, 1], F32),
        ("bqp", [P, 1], F32), ("bqn", [P, 1], F32), ("bkn", [P, 1], F32),
        ("rmsw", [P, 1], F32),
        ("mcols", [P, 6, 8], F32), ("ca1", [P, 8], F32), ("cb1", [P, 8], F32),
        ("smrow", [1, 8], F32), ("sm128", [P, 8], F32),
        ("m0slots", [P, 64], F32),
        ("m1slots", [P, NT * NI], F32), ("m2slots", [P, NT * NI], F32),
        ("m3slots", [P, NT * NI], F32), ("m4slots", [P, NT * NI], F32),
        ("mxst", [P, 2], F32), ("mxrow", [1, 2 * P], F32), ("mxfold", [1, 2], F32),
        ("r0c", [P, 1], F32), ("s0c", [P, 1], F32), ("s4c", [P, 1], F32),
        ("r1c", [P, 1], F32), ("r2c", [P, 1], F32), ("s2c", [P, 1], F32),
        ("brc2c", [P, 1], F32), ("c32c", [P, 1], F32), ("c4c", [P, 1], F32),
        ("gbs4", [P, 8], F32), ("mrccol", [P, 1], F32),
        ("ccols", [P, 8], F32),
        ("row_a", [1, TT], F32), ("row_b", [1, TT], F32),
        ("row_m", [1, TT], F32), ("row_r", [1, TT], F32),
        ("srow", [1, 16], F32),
    ]:
        galloc(nm, shp, dt)

    probe = nc.alloc_sbuf_tensor("sb_gbase_probe", [1, 64], F32)
    GBASE = ((nc.lookup_mloc(probe).addr + 256 + 63) // 64) * 64
    LIMIT = GBASE + nc.sbuf_bytes_remaining - 512

    class Arena:
        def __init__(self, base):
            self.off = base

        def place(self, name, shape, dt):
            esz = 2 if dt == BF16 else 4
            n = esz
            for s_ in shape[1:]:
                n *= s_
            off = ((self.off + 63) // 64) * 64
            sb[name] = nc.alloc_sbuf_tensor_at("sb_" + name, shape, dt, offset=off)
            self.off = off + n
            assert self.off <= LIMIT, f"SBUF overflow at {name}: {self.off}"
            return sb[name]

    a0 = Arena(GBASE)
    a0.place("wmod0", [P, D], F32)
    a0.place("wmod1", [P, D], F32)
    a0.place("crow_sb", [1, D], F32)
    a0.place("modrow", [1, 6 * D], F32)
    a0.place("clamp_a", [1, D], F32)
    a0.place("clamp_b", [1, D], F32)

    aa = Arena(GBASE)
    aa.place("dqn0", [P, DFF // 2], F32)
    aa.place("dqn1", [P, DFF // 2], F32)
    aa.place("dsc", [P, DFF // 64], F32)
    aa.place("drc", [P, DFF // 64], F32)
    aa.place("dt1", [P, DFF // 2], F32)
    aa.place("dqb", [P, DFF // 2], BF16)
    aa.place("dst", [P, NI, P], BF16)
    aa.place("wk_sb", [P, KC, D], BF16)
    aa.place("wv_sb", [P, KC, D], BF16)
    aa.place("xs0", [P, KC, TT], F32)
    aa.place("xs1", [P, KC, TT], F32)
    aa.place("xbf", [P, KC, TT], BF16)
    aa.place("sq0", [P, TT], BF16)
    aa.place("sq1", [P, TT], BF16)
    aa.place("tmp1", [P, TT], BF16)
    aa.place("tmp2", [P, TT], BF16)
    aa.place("rsb", [P, TT], BF16)
    aa.place("mrsb", [P, TT], BF16)
    aa.place("h_sb", [P, KC, TT], BF16)
    aa.place("kq0", [P, TT], BF16)
    aa.place("kq1", [P, TT], BF16)
    aa.place("kfm_sb", [P, 2 * H, TT], BF16)
    aa.place("ktm_sb", [P, NCH, H * DF], BF16)
    aa.place("v_sb", [P, NCH, D], BF16)
    aa.place("Ssb", [P, H * DF], F32)

    ab = Arena(GBASE)
    ab.place("Sfin", [P, H * DF], F32)
    ab.place("wq_sb", [P, KC, D], BF16)
    ab.place("wo_sb", [P, KC, D], BF16)
    ab.place("cxs0", [P, KC, TT], F32)
    ab.place("h_r", [P, KC, TT], BF16)
    ab.place("kfm_r", [P, 2 * H, TT], BF16)
    ab.place("ktm_r", [P, NCH, H * DF], BF16)
    ab.place("v_r", [P, NCH, D], BF16)
    ab.place("qfm_sb", [P, 2 * H, TT], BF16)
    ab.place("qq0", [P, TT], BF16)
    ab.place("qq1", [P, TT], BF16)
    OSB_OFF = ((ab.off + 63) // 64) * 64
    ab.place("o_sb", [P, KC, TT], BF16)
    sb["gS0"] = nc.alloc_sbuf_tensor_at("sb_gS0", [P, H * DF], F32, offset=OSB_OFF)
    ab.place("Ssnap", [P, 2, DF], BF16)
    ab.place("at_sb", [P, 2, P], BF16)
    ab.place("o2w", [P, 2, 8 * P], BF16)
    ab.place("orow", [P, 32, P], BF16)
    ab.place("rms32", [P, 32], F32)
    ab.place("rmsc", [P, 1], F32)
    ab.place("x2_sb", [P, KC, TT], F32)
    ab.place("ep_sb", [P, TT], F32)
    ab.place("h2_sb", [P, KC, TT], BF16)
    ab.place("cxbf", [P, KC, TT], BF16)
    ab.place("csq0", [P, TT], BF16)
    ab.place("csq1", [P, TT], BF16)
    ab.place("ctmp1", [P, TT], BF16)
    ab.place("ctmp2", [P, TT], BF16)
    ab.place("crsb", [P, TT], BF16)
    ab.place("cmrsb", [P, TT], BF16)

    ac = Arena(GBASE)
    ac.place("wup_sb", [P, KC, DFF], BF16)
    ac.place("wg_sb", [P, KC, DFF], BF16)
    ac.place("h2r0", [P, KC, TT], BF16)
    ac.place("h2r1", [P, KC, TT], BF16)
    ac.place("xqt0", [P, TT], F32)
    ac.place("xqt1", [P, TT], F32)
    ac.place("xq_sb", [P, KC, TT], BF16)
    ac.place("Ust", [P, NI, TT], BF16)
    ac.place("Gst", [P, NI, TT], BF16)

    ad = Arena(GBASE)
    ad.place("bufA0", [P, NI, TT], BF16)
    ad.place("bufA1", [P, NI, TT], BF16)
    ad.place("t1f0", [P, TT], F32)
    ad.place("t1f1", [P, TT], F32)
    ad.place("t1g0", [P, TT], F32)
    ad.place("t1g1", [P, TT], F32)
    Z1 = ad.off
    ad.place("bufB0", [P, NI, TT], BF16)
    ad.place("bufB1", [P, NI, TT], BF16)
    ad.place("sgf0", [P, TT], F32)
    ad.place("sgf1", [P, TT], F32)
    Z2 = ad.off
    ae1 = Arena(Z2)
    ae1.place("srbf", [P, TT], BF16)
    ae1.place("tgbf", [P, 2, TT], BF16)
    ae1.place("g2st", [P, NI, TT], BF16)
    ae2 = Arena(Z2)
    ae2.place("prst", [P, NI, TT], BF16)
    ae2.place("g2bf", [P, TT], BF16)
    ae2.place("upbf", [P, TT], BF16)
    awd = Arena(max(ae1.off, ae2.off))
    awd.place("wd_sb", [P, NI, D], BF16)
    af = Arena(Z1)
    af.place("obf", [P, NI, TT], BF16)
    af.place("x2r0", [P, KC, TT], F32)
    af.place("x2r1", [P, KC, TT], F32)
    af.place("outs0", [P, KC, TT], F32)
    af.place("ep20", [P, TT], F32)
    af.place("ep21", [P, TT], F32)

    ps = {}
    for i in range(4):
        ps[f"S{i}"] = nc.alloc_psum_tensor(f"psS{i}", [P, 512], F32)
    ps["big0"] = nc.alloc_psum_tensor("big0", [P, TT], F32)
    ps["big1"] = nc.alloc_psum_tensor("big1", [P, TT], F32)
    ps["pxa"] = nc.alloc_psum_tensor("pxa", [P, 1024], BF16)
    ps["pxb"] = nc.alloc_psum_tensor("pxb", [P, 1024], BF16)
    assert nc.psum_banks_remaining == 0

    def S_ap(h_, dc):
        sl = h_ * 2 + dc
        return ps[f"S{sl // 4}"][:, (sl % 4) * P:(sl % 4 + 1) * P]

    ctx = ExitStack()
    sems = {}
    for s in ["T", "V", "S", "YN", "GN", "LC", "MA", "MB", "DA", "DB", "LW",
              "LF", "SPA", "SPB", "SG", "SD", "X0", "X1", "H0", "H1",
              "K0", "K1", "U0", "U1", "SC1", "SC2", "SC3", "SC2B", "SC3B", "CC", "ST"]:
        sems[s] = ctx.enter_context(nc.semaphore(s))
    p = Prog()
    p.sems = sems

    xT_v = xT.rearrange("(kc q) n -> q kc n", q=P)
    hbh_v = hb_h.rearrange("(kc q) n -> q kc n", q=P)
    kfm_v = hb_kfm.rearrange("(r q) n -> q r n", q=P)
    ktm_v = hb_ktm.rearrange("(g q) c -> q g c", q=P)
    v_v = hb_v.rearrange("(g q) c -> q g c", q=P)
    x2h_v = x2h.rearrange("(kc q) n -> q kc n", q=P)
    h2h_v = h2h.rearrange("(kc q) n -> q kc n", q=P)
    out_v = out.rearrange("(kc q) n -> q kc n", q=P)

    def program():
        T = {}

        def dma(eng, dst, src, sem, tag=None, deps=()):
            return p.op(eng, lambda e: e.dma_start(out=dst, in_=src),
                        deps=deps, inc=16, tag=tag, sem=sem)

        dma("sync", sb["ident"][:, :], idt.ap(), "LC")
        dma("sync", sb["triu"][:, :], tri.ap(), "LC")
        dma("sync", sb["onescol"][:, :], onc.ap(), "LC")
        dma("sync", sb["onesrow"][:, :], onr.ap(), "LC")
        dma("sync", sb["fmqb"][:, :], fmqb, "LC")
        dma("sync", sb["fmkb"][:, :], fmkb, "LC")
        dma("sync", sb["rmsw"][:, :], rmsw, "LC")
        dma("sync", sb["smrow"][:, :], smask, "LC")
        dma("sync", sb["crow_sb"][:, :], crow, "LC", tag="ld_consts")
        p.op("gpsimd", lambda e: e.dma_start(out=sb["fmqT"][:, :], in_=fmqT),
             inc=16, sem="LW")
        p.op("gpsimd", lambda e: e.dma_start(out=sb["fmkT"][:, :], in_=fmkT),
             inc=16, sem="LW", tag="ld_fm")

        p.op("vector", lambda e: e.memset(sb["dmy"][:, :], 0.0))
        p.op("vector", lambda e: e.memset(sb["epscol"][:, :], 1e-5))
        p.op("vector", lambda e: e.memset(sb["rccol"][:, :], RC), tag="rc_set")
        p.op("vector", lambda e: e.memset(sb["mrccol"][:, :], -RC), tag="mrc_set")
        p.op("vector", lambda e: e.memset(sb["c2ecol"][:, :], 128.0 * 1e-5 * (2.0 ** -48)),
             tag="c2e_set")
        p.op("vector", lambda e: e.tensor_scalar(sb["bqp"][:, :], sb["fmqb"][:, :], -LN16, None, op0=ALU.add),
             deps=["ld_consts"])
        p.op("vector", lambda e: e.tensor_scalar(sb["bqn"][:, :], sb["fmqb"][:, :], -1.0, -LN16, op0=ALU.mult, op1=ALU.add))
        p.op("vector", lambda e: e.tensor_scalar(sb["bkn"][:, :], sb["fmkb"][:, :], -1.0, None, op0=ALU.mult),
             tag="bias_cols")

        p.op("scalar", lambda e: e.activation(sb["crow_sb"][:, :], sb["crow_sb"][:, :], AF.Silu),
             deps=["ld_consts"], tag="silu_row")
        def mk_ccol(e):
            r = None
            for kc in range(KC):
                r = e.matmul(ps["big0"][:, kc:kc + 1],
                             sb["crow_sb"][0:1, kc * P:(kc + 1) * P],
                             sb["onesrow"][0:1, 0:1],
                             start=(kc == 0), stop=(kc == KC - 1))
            return r
        p.op("tensor", mk_ccol, deps=["silu_row", "ld_consts"], tag="ccol_mm")
        p.op("scalar", lambda e: e.activation(sb["ccols"][:, :], ps["big0"][:, 0:8], AF.Identity),
             deps=["ccol_mm"], tag="ccol_ev")

        wmod_sb = [sb["wmod0"], sb["wmod1"]]
        for g in range(6):
            for kc in range(KC):
                slot = (g * KC + kc) % 2
                prev = f"modmm_g{g}_k{kc}_m2" if (g * KC + kc) >= 2 else None
                gp, kp = divmod(g * KC + kc - 2, KC)
                dd = [f"modmm_g{gp}_k{kp}"] if (g * KC + kc) >= 2 else []
                dma("sync", wmod_sb[slot][:, :],
                    wmod[kc * P:(kc + 1) * P, g * D:(g + 1) * D],
                    f"M{'AB'[slot]}", tag=f"modld_g{g}_k{kc}", deps=dd)
            for kc in range(KC):
                slot = (g * KC + kc) % 2
                def mk_mod(g=g, kc=kc, slot=slot):
                    def f(e):
                        e.matmul(ps["big0"][0:1, :],
                                 sb["ccols"][:, kc:kc + 1],
                                 wmod_sb[slot][:, 0:TT],
                                 start=(kc == 0), stop=(kc == KC - 1))
                        return e.matmul(ps["big1"][0:1, :],
                                        sb["ccols"][:, kc:kc + 1],
                                        wmod_sb[slot][:, TT:D],
                                        start=(kc == 0), stop=(kc == KC - 1))
                    return f
                dd = [f"modld_g{g}_k{kc}", "ccol_ev"]
                if g >= 1:
                    dd.append(f"modev_g{g-1}")
                p.op("tensor", mk_mod(), deps=dd, tag=f"modmm_g{g}_k{kc}")
            def mk_ev(g=g):
                def f(e):
                    e.activation(sb["modrow"][:, g * D:g * D + TT], ps["big0"][0:1, :], AF.Identity)
                    return e.activation(sb["modrow"][:, g * D + TT:(g + 1) * D], ps["big1"][0:1, :], AF.Identity)
                return f
            p.op("scalar", mk_ev(), deps=[f"modmm_g{g}_k{KC-1}"], tag=f"modev_g{g}")

        for i, s_ in enumerate([2, 5]):
            sl_ = sb["modrow"][:, s_ * D:(s_ + 1) * D]
            p.op("scalar", lambda e, sl_=sl_: e.activation(sb["clamp_a"][:, :], sl_, AF.Abs),
                 deps=[f"modev_g{s_}", (f"cl_done{i-1}" if i else None)], tag=f"cl_abs{i}")
            p.op("scalar", lambda e, sl_=sl_: e.activation(sb["clamp_b"][:, :], sl_, AF.Sign),
                 deps=[(f"cl_done{i-1}" if i else None)], tag=f"cl_sgn{i}")
            p.op("vector", lambda e: e.tensor_scalar_max(sb["clamp_a"][:, :], sb["clamp_a"][:, :], 0.25),
                 deps=[f"cl_abs{i}"], tag=f"cl_max{i}")
            p.op("vector", lambda e, sl_=sl_: e.tensor_mul(sl_, sb["clamp_a"][:, :], sb["clamp_b"][:, :]),
                 deps=[f"cl_sgn{i}", f"cl_max{i}"], tag=f"cl_done{i}")

        def mk_scat(e):
            r = None
            for s_ in range(6):
                for kc in range(KC):
                    r = e.matmul(ps["big0"][:, s_ * 8 + kc:s_ * 8 + kc + 1],
                                 sb["modrow"][0:1, s_ * D + kc * P:s_ * D + (kc + 1) * P],
                                 sb["onesrow"][0:1, 0:1],
                                 start=(s_ == 0 and kc == 0), stop=(s_ == 5 and kc == KC - 1))
            return r
        p.op("tensor", mk_scat,
             deps=["cl_done0", "cl_done1", "modev_g0", "modev_g1",
                   "modev_g3", "modev_g4", "modev_g5", "ld_consts"],
             tag="mod_scat")
        p.op("scalar", lambda e: e.activation(sb["mcols"][:, :, :], ps["big0"][:, 0:48], AF.Identity),
             deps=["mod_scat"], tag="mcols_ev")
        p.op("vector", lambda e: e.tensor_scalar(sb["ca1"][:, :], sb["mcols"][:, 1, :], 1.0, None, op0=ALU.add),
             deps=["mcols_ev"])
        p.op("vector", lambda e: e.tensor_scalar(sb["cb1"][:, :], sb["mcols"][:, 4, :], 1.0, None, op0=ALU.add),
             tag="modcols_done")
        p.op("tensor", lambda e: e.matmul(ps["big1"][:, 0:8], sb["onesrow"][:, :],
                                          sb["smrow"][:, :], start=True, stop=True),
             deps=["ld_consts", f"modev_g5"], tag="sm_mm")
        p.op("scalar", lambda e: e.activation(sb["sm128"][:, :], ps["big1"][:, 0:8], AF.Identity),
             deps=["sm_mm"], tag="sm128_done")

        p.barrier("p0", sb)

        def finish_early():
            for t_ in range(NT):
                dma("sync", out_v[:, :, t_ * TT:(t_ + 1) * TT],
                    sb["xs0"][:, :, :], "ST", tag=f"fout{t_}")
            p.op("sync", lambda e: e.wait_ge(p.sems["ST"], 16 * NT), inc=0)
            p.op("gpsimd", lambda e: e.wait_ge(p.sems["ST"], 16 * NT), inc=0)

        if PH < 1:
            finish_early()
            return
        PSL = {"big0": "mcols_ev", "big1": "sm128_done"}
        dqi_t1 = {}
        dqi_st = {}
        dqi_q = {}
        dqi_tp = {}
        dqn = [sb["dqn0"], sb["dqn1"]]
        dqi = 0

        DQJ = []

        def emit_dequant(wi, nat, natq, R, C):
            natq_v = natq.rearrange("(cb q) r -> q cb r", q=P)
            NSP = 2 if C > 1536 else 1
            Cs = C // NSP
            for rt in range(R // P):
                for sp in range(NSP):
                    DQJ.append((nat, natq_v, rt, sp * Cs, Cs, Cs // P, Cs // 32))

        def run_dq_jobs(n):
            nonlocal dqi
            for _ in range(n):
                if not DQJ:
                    return
                nat, natq_v, rt, coff, C, CB, CG = DQJ.pop(0)
                i = dqi
                dqi += 1
                slot = i % 2
                dma("sync", dqn[slot][:, 0:C],
                    nat[rt * P:(rt + 1) * P, coff:coff + C],
                    f"D{'AB'[i % 2]}", tag=f"dq_ld_{i}", deps=[dqi_t1.get(i - 2)])
                w3 = dqn[slot][:, 0:C].rearrange("q (g e) -> q g e", e=32)
                p.op("vector", lambda e, w3=w3, CG=CG: e.tensor_reduce(
                        sb["dsc"][:, 0:CG].unsqueeze(2), w3, axis=AX.X, op=ALU.max,
                        apply_absolute_value=True),
                     deps=[f"dq_ld_{i}", dqi_q.get(i - 1), dqi_tp.get(i - 1)],
                     tag=f"dq_sc_{i}")
                p.op("vector", lambda e, CG=CG: e.tensor_scalar(
                        sb["dsc"][:, 0:CG], sb["dsc"][:, 0:CG], 1.0 / 7.0, 1e-12,
                        op0=ALU.mult, op1=ALU.max),
                     deps=[f"dq_sc_{i}"], tag=f"dq_s7_{i}")
                p.op("vector", lambda e, CG=CG: e.reciprocal(sb["drc"][:, 0:CG], sb["dsc"][:, 0:CG]),
                     deps=[f"dq_s7_{i}"], tag=f"dq_rc_{i}")
                rcb = sb["drc"][:, 0:CG].unsqueeze(2).broadcast_to([P, CG, 32])
                scb = sb["dsc"][:, 0:CG].unsqueeze(2).broadcast_to([P, CG, 32])
                p.op("vector", lambda e, w3=w3, rcb=rcb, C=C: e.tensor_tensor(
                        sb["dt1"][:, 0:C].rearrange("q (g e) -> q g e", e=32), w3, rcb, op=ALU.mult),
                     deps=[f"dq_rc_{i}"], tag=f"dq_t1_{i}")
                dqi_t1[i] = f"dq_t1_{i}"
                p.op("scalar", lambda e, C=C: e.activation(
                        sb["dt1"][:, 0:C], sb["dt1"][:, 0:C], AF.Identity, bias=sb["rccol"][:, :]),
                     deps=[f"dq_t1_{i}", "rc_set"], tag=f"dq_t2_{i}")
                p.op("vector", lambda e, scb=scb, C=C: e.scalar_tensor_tensor(
                        sb["dqb"][:, 0:C].rearrange("q (g e) -> q g e", e=32),
                        sb["dt1"][:, 0:C].rearrange("q (g e) -> q g e", e=32),
                        -RC, scb, op0=ALU.add, op1=ALU.mult),
                     deps=[f"dq_t2_{i}"], tag=f"dq_q_{i}")
                dqi_q[i] = f"dq_q_{i}"
                NG = (CB + 3) // 4
                for g in range(NG):
                    gb = min(4, CB - g * 4)
                    def mk_dqtp(e, g=g, gb=gb):
                        r = None
                        for j in range(gb):
                            cb = g * 4 + j
                            r = e.transpose(ps["pxb"][:, j * P:(j + 1) * P],
                                            sb["dqb"][:, cb * P:(cb + 1) * P],
                                            sb["ident"][:, :])
                        return r
                    p.op("tensor", mk_dqtp,
                         deps=[f"dq_q_{i}", PSL.get("pxb"), "ld_consts"],
                         tag=f"dq_tp_{i}_{g}")
                    p.op("scalar", lambda e, g=g, gb=gb: e.activation(
                            sb["dst"][:, g * 4:g * 4 + gb, :],
                            ps["pxb"][:, 0:gb * P].rearrange("q (cb c) -> q cb c", c=P),
                            AF.Identity),
                         deps=[f"dq_tp_{i}_{g}", dqi_st.get(i - 1)],
                         tag=f"dq_ev_{i}_{g}")
                    PSL["pxb"] = f"dq_ev_{i}_{g}"
                dqi_tp[i] = f"dq_tp_{i}_{NG-1}"
                dma("gpsimd", natq_v[:, coff // P:coff // P + CB, rt * P:(rt + 1) * P],
                    sb["dst"][:, 0:CB, :],
                    "SD", tag=f"dq_st_{i}", deps=[f"dq_ev_{i}_{NG-1}"])
                dqi_st[i] = f"dq_st_{i}"

        emit_dequant(0, wup, wupT_q, DFF, D)
        emit_dequant(1, wgate, wgT_q, DFF, D)
        emit_dequant(2, wdown, wdT_q, D, DFF)

        if PH < 2:
            finish_early()
            return
        def emit_ln(pre, src_tag, src_f32, xbf, sqp, tmp1, tmp2, rsb, mrsb,
                    hout, cacol, sacol, m0base, cast_deps, mod_deps):
            for kc in range(KC):
                p.op("scalar", lambda e, kc=kc: e.activation(
                        xbf[:, kc, :], src_f32[:, kc, :], AF.Identity),
                     deps=[src_tag] + (cast_deps if kc == 0 else []),
                     tag=f"{pre}cast{kc}")
                p.op("vector", lambda e, kc=kc: e.tensor_mul(
                        sqp[kc % 2][:, :], xbf[:, kc, :], xbf[:, kc, :]),
                     deps=[f"{pre}cast{kc}", T.get(f"{pre}stats{kc-2}")],
                     tag=f"{pre}sq{kc}")
                def mk_stats(e, kc=kc):
                    e.matmul(ps["big0"][0:1, :], sb["onescol"][:, :], xbf[:, kc, :],
                             start=(kc == 0), stop=(kc == KC - 1))
                    return e.matmul(ps["big1"][0:1, :], sb["onescol"][:, :], sqp[kc % 2][:, :],
                                    start=(kc == 0), stop=(kc == KC - 1))
                dd = [f"{pre}sq{kc}", "ld_consts"]
                if kc == 0:
                    dd += [PSL.get("big0"), PSL.get("big1")]
                p.op("tensor", mk_stats, deps=dd, tag=f"{pre}stats{kc}")
                T[f"{pre}stats{kc}"] = f"{pre}stats{kc}"
            p.settag(f"{pre}stats", f"{pre}stats{KC-1}")
            def mk_fin1(e):
                e.activation(sb["row_a"][:, :], ps["big0"][0:1, :], AF.Identity, scale=1.0 / D)
                return e.activation(sb["row_b"][:, :], ps["big1"][0:1, :], AF.Identity, scale=1.0 / D)
            p.op("scalar", mk_fin1, deps=[f"{pre}stats"], tag=f"{pre}fin1")
            p.op("vector", lambda e: e.tensor_mul(sb["row_m"][:, :], sb["row_a"][:, :], sb["row_a"][:, :]),
                 deps=[f"{pre}fin1"], tag=f"{pre}msq")
            p.op("vector", lambda e: e.tensor_sub(sb["row_b"][:, :], sb["row_b"][:, :], sb["row_m"][:, :]),
                 deps=[f"{pre}msq"], tag=f"{pre}var")
            p.op("scalar", lambda e: e.activation(sb["row_r"][:, :], sb["row_b"][:, :], AF.Ln, bias=sb["epscol"][0:1, :]),
                 deps=[f"{pre}var"], tag=f"{pre}rstd0")
            p.op("scalar", lambda e: e.activation(sb["row_r"][:, :], sb["row_r"][:, :], AF.Exp, scale=-0.5),
                 deps=[f"{pre}rstd0"], tag=f"{pre}rstd")
            p.op("vector", lambda e: e.tensor_mul(sb["row_m"][:, :], sb["row_a"][:, :], sb["row_r"][:, :]),
                 deps=[f"{pre}rstd"], tag=f"{pre}mr")
            def mk_bc(e):
                e.matmul(ps["big0"][:, :], sb["onesrow"][:, :], sb["row_r"][:, :], start=True, stop=True)
                return e.matmul(ps["big1"][:, :], sb["onesrow"][:, :], sb["row_m"][:, :], start=True, stop=True)
            p.op("tensor", mk_bc, deps=[f"{pre}mr", f"{pre}fin1"], tag=f"{pre}bc")
            def mk_bce(e):
                e.activation(rsb[:, :], ps["big0"][:, :], AF.Identity)
                return e.activation(mrsb[:, :], ps["big1"][:, :], AF.Identity)
            p.op("scalar", mk_bce, deps=[f"{pre}bc"], tag=f"{pre}bce")
            PSL["big0"] = f"{pre}bce"
            PSL["big1"] = f"{pre}bce"
            p.op("vector", lambda e: e.tensor_tensor(
                    xbf[:, :, :], xbf[:, :, :],
                    rsb[:, :].unsqueeze(1).broadcast_to([P, KC, TT]), op=ALU.mult),
                 deps=[f"{pre}bce"] + (mod_deps or []), tag=f"{pre}t1")
            p.op("vector", lambda e: e.tensor_tensor(
                    xbf[:, :, :], xbf[:, :, :],
                    mrsb[:, :].unsqueeze(1).broadcast_to([P, KC, TT]), op=ALU.subtract),
                 deps=[f"{pre}t1"], tag=f"{pre}t2")
            for kc in range(KC):
                p.op("scalar", lambda e, kc=kc: e.activation(
                        hout[:, kc, :], xbf[:, kc, :], AF.Identity,
                        scale=cacol[:, kc:kc + 1], bias=sacol[:, kc:kc + 1]),
                     deps=[f"{pre}t2", "modcols_done"], tag=f"{pre}smod{kc}")
                if m0base is not None:
                    p.op("vector", lambda e, kc=kc: e.tensor_reduce(
                            sb["m0slots"][:, m0base + kc:m0base + kc + 1], hout[:, kc, :],
                            axis=AX.X, op=ALU.max, apply_absolute_value=True),
                         deps=[f"{pre}smod{kc}"], tag=f"{pre}m0_{kc}")
            return f"{pre}smod{KC-1}"

        p.op("gpsimd", lambda e: e.dma_start(
                out=sb["wk_sb"][:, :, :], in_=wkT.rearrange("(kc q) f -> q kc f", q=P)),
             inc=16, sem="LW")
        p.op("gpsimd", lambda e: e.dma_start(
                out=sb["wv_sb"][:, :, :], in_=wvT.rearrange("(kc q) f -> q kc f", q=P)),
             inc=16, sem="LW", tag="ld_wkv")
        xs = [sb["xs0"], sb["xs1"]]
        KQLAST = [None, None]
        for t in range(NT):
            sl = t % 2
            dma("sync", xs[sl][:, :, :], xT_v[:, :, t * TT:(t + 1) * TT],
                f"X{sl}", tag=f"axld{t}", deps=[T.get(f"ax_free{t-2}")])
            lnt = emit_ln(
                f"aln{t}", f"axld{t}", xs[sl], sb["xbf"], [sb["sq0"], sb["sq1"]],
                sb["tmp1"], sb["tmp2"],
                sb["rsb"], sb["mrsb"], sb["h_sb"], sb["ca1"], sb["mcols"][:, 0, :],
                None,
                cast_deps=[T.get(f"aln_xbf_free{t-1}")],
                mod_deps=[T.get(f"vspill{t-1}"), T.get(f"h_used{t-1}")])
            T[f"ax_free{t}"] = f"aln{t}cast{KC-1}"
            T[f"aln_xbf_free{t}"] = f"aln{t}smod{KC-1}"
            dma("gpsimd", hbh_v[:, :, t * TT:(t + 1) * TT], sb["h_sb"][:, :, :],
                "SPA", tag=f"hspill{t}", deps=[lnt])
            T[f"hspill{t}"] = f"hspill{t}"
            for tb in range(NCH):
                for vh in range(2):
                    bs = f"big{vh}"
                    def mk_v(e, tb=tb, vh=vh, bs=bs):
                        r = None
                        for kc in range(KC):
                            r = e.matmul(ps[bs][:, :],
                                         sb["h_sb"][:, kc, tb * CHN:(tb + 1) * CHN],
                                         sb["wv_sb"][:, kc, vh * TT:(vh + 1) * TT],
                                         start=(kc == 0), stop=(kc == KC - 1))
                        return r
                    p.op("tensor", mk_v, deps=[lnt, "ld_wkv", PSL.get(bs)],
                         tag=f"vmm{t}_{tb}_{vh}")
                    p.op("scalar", lambda e, tb=tb, vh=vh, bs=bs: e.activation(
                            sb["v_sb"][:, tb, vh * TT:(vh + 1) * TT], ps[bs][:, :], AF.Identity),
                         deps=[f"vmm{t}_{tb}_{vh}", T.get(f"hspill{t}"), T.get(f"v_used{t-1}")],
                         tag=f"vev{t}_{tb}_{vh}")
                    PSL[bs] = f"vev{t}_{tb}_{vh}"
            for h_ in range(H):
                ks = h_ % 2
                def mk_k(e, h_=h_, ks=ks):
                    r = None
                    for kc in range(KC):
                        r = e.matmul(ps[f"big{ks}"][:, :],
                                     sb["wk_sb"][:, kc, h_ * P:(h_ + 1) * P],
                                     sb["h_sb"][:, kc, :],
                                     start=(kc == 0), stop=(kc == KC - 1))
                    return r
                p.op("tensor", mk_k, deps=[lnt, "ld_wkv", PSL.get(f"big{ks}")],
                     tag=f"kmm{t}_{h_}")
                p.op("scalar", lambda e, ks=ks, bs=f"big{ks}": e.activation(
                        sb[f"kq{ks}"][:, :], ps[bs][:, :], AF.Identity),
                     deps=[f"kmm{t}_{h_}", KQLAST[ks]],
                     tag=f"kev{t}_{h_}")
                p.op("tensor", lambda e, ks=ks: e.matmul(
                        ps[f"big{ks}"][:, :], sb["fmkT"][:, :], sb[f"kq{ks}"][:, :],
                        start=True, stop=True),
                     deps=[f"kev{t}_{h_}", "ld_wkv"], tag=f"kymm{t}_{h_}")
                KQLAST[ks] = f"kymm{t}_{h_}"
                def mk_kexp(e, h_=h_, ks=ks):
                    e.activation(sb["kfm_sb"][:, 2 * h_, :], ps[f"big{ks}"][:, :], AF.Exp,
                                 bias=sb["fmkb"][:, :])
                    return e.activation(sb["kfm_sb"][:, 2 * h_ + 1, :], ps[f"big{ks}"][:, :], AF.Exp,
                                        bias=sb["bkn"][:, :], scale=-1.0)
                p.op("scalar", mk_kexp,
                     deps=[f"kymm{t}_{h_}", "bias_cols", T.get(f"hspill{t}"),
                           T.get(f"kfm_used{t-1}")],
                     tag=f"kexp{t}_{h_}")
                PSL[f"big{ks}"] = f"kexp{t}_{h_}"
                for dc in range(2):
                    def mk_ktp(e, h_=h_, dc=dc):
                        r = None
                        for tb in range(NCH):
                            r = e.transpose(
                                ps["pxa"][:, tb * P:(tb + 1) * P],
                                sb["kfm_sb"][:, 2 * h_ + dc, tb * CHN:(tb + 1) * CHN],
                                sb["ident"][:, :])
                        return r
                    p.op("tensor", mk_ktp,
                         deps=[f"kexp{t}_{h_}", T.get("pxa_ev"), "ld_consts"],
                         tag=f"ktp{t}_{h_}_{dc}")
                    p.op("vector", lambda e, h_=h_, dc=dc: e.tensor_copy(
                            sb["ktm_sb"][:, 0:NCH, h_ * DF + dc * P:h_ * DF + (dc + 1) * P],
                            ps["pxa"][:, 0:TT].rearrange("q (tb c) -> q tb c", c=P)),
                         deps=[f"ktp{t}_{h_}_{dc}", T.get(f"hspill{t}"),
                               T.get(f"ktm_used{t-1}")],
                         tag=f"ktev{t}_{h_}_{dc}")
                    T["pxa_ev"] = f"ktev{t}_{h_}_{dc}"
                def mk_sacc(e, t=t, h_=h_):
                    r = None
                    for ch in range(NCH):
                        for dc in range(2):
                            st_ = (t == 0 and ch == 0 and dc == 0 and h_ % 2 == 0)
                            r = e.matmul(S_ap(h_, dc),
                                         sb["ktm_sb"][:, ch, h_ * DF + dc * P:h_ * DF + (dc + 1) * P],
                                         sb["v_sb"][:, ch, h_ * P:(h_ + 1) * P],
                                         start=st_, stop=False, skip_group_check=True)
                    return r
                p.op("tensor", mk_sacc,
                     deps=[f"ktev{t}_{h_}_1", f"vev{t}_{NCH-1}_1"],
                     tag=f"sacc{t}_{h_}")
            T[f"h_used{t}"] = f"kmm{t}_{H-1}"
            T[f"v_used{t}"] = f"sacc{t}_{H-1}"
            T[f"ktm_used{t}"] = f"sacc{t}_{H-1}"
            T[f"kfm_used{t}"] = f"ktp{t}_{H-1}_1"
            dma("gpsimd", kfm_v[:, :, t * TT:(t + 1) * TT], sb["kfm_sb"][:, :, :],
                "SPA", tag=f"kfmspill{t}",
                deps=[f"kexp{t}_{H-1}", f"ktp{t}_{H-1}_1"])
            T[f"kfmspill{t}"] = f"kfmspill{t}"
            dma("gpsimd", ktm_v[:, t * NCH:(t + 1) * NCH, :], sb["ktm_sb"][:, :, :],
                "SPA", tag=f"ktmspill{t}", deps=[f"sacc{t}_{H-1}"])
            T[f"ktmspill{t}"] = f"ktmspill{t}"
            dma("gpsimd", v_v[:, t * NCH:(t + 1) * NCH, :], sb["v_sb"][:, :, :],
                "SPA", tag=f"vspill{t}", deps=[f"sacc{t}_{H-1}"])
            T[f"vspill{t}"] = f"vspill{t}"
            run_dq_jobs(8)

        run_dq_jobs(10 ** 9)
        T["dq_done"] = f"dq_st_{dqi - 1}"

        def mk_sev(e):
            r = None
            for h_ in range(H):
                for dc in range(2):
                    r = e.activation(sb["Ssb"][:, (h_ * 2 + dc) * P:(h_ * 2 + dc + 1) * P],
                                     S_ap(h_, dc), AF.Identity)
            return r
        p.op("scalar", mk_sev, deps=[f"sacc{NT-1}_{H-1}"], tag="s_ev")
        dma("gpsimd", sc_in, sb["Ssb"][:, :], "SPA", tag="s_stage", deps=["s_ev"])
        p.op("gpsimd", lambda e: e.collective_compute(
                "AllGather", ALU.bypass,
                replica_groups=[[0, 1], [2, 3], [4, 5], [6, 7]],
                ins=[sc_in.opt()], outs=[sc_out.opt()]),
             deps=["s_stage"], sem="CC", tag="cc_sg")

        p.barrier("p2a", sb)

        if PH < 3:
            finish_early()
            return
        gS = sb["gS0"]
        SFIN = sb["Sfin"]
        for r in range(2):
            dma("gpsimd", gS[:, :], sc_out[r * P:(r + 1) * P, :],
                "SG", tag=f"gsld{r}", deps=["cc_sg", T.get(f"sin{r-1}")])
            if r == 0:
                p.op("vector", lambda e: e.tensor_scalar(
                        SFIN[:, :], gS[:, :], sb["sm128"][:, 0:1], None, op0=ALU.mult),
                     deps=[f"gsld{r}", "sm128_done"], tag=f"sin{r}")
            else:
                p.op("vector", lambda e, r=r: e.scalar_tensor_tensor(
                        SFIN[:, :], gS[:, :], sb["sm128"][:, r:r + 1], SFIN[:, :],
                        op0=ALU.mult, op1=ALU.add),
                     deps=[f"gsld{r}", f"sin{r-1}"], tag=f"sin{r}")
            T[f"sin{r}"] = f"sin{r}"
        p.settag("sin7", "sin1")

        p.op("gpsimd", lambda e: e.dma_start(
                out=sb["wq_sb"][:, :, :], in_=wqT.rearrange("(kc q) f -> q kc f", q=P)),
             inc=16, sem="LW")
        p.op("gpsimd", lambda e: e.dma_start(
                out=sb["wo_sb"][:, :, :], in_=woT.rearrange("(kc q) f -> q kc f", q=P)),
             inc=16, sem="LW", tag="ld_wqo")
        def mk_worms(e):
            r = None
            for kc in range(KC):
                r = e.tensor_scalar(sb["wo_sb"][:, kc, :], sb["wo_sb"][:, kc, :],
                                    sb["rmsw"][:, :], None, op0=ALU.mult)
            return r
        p.op("vector", mk_worms, deps=["ld_wqo", "ld_consts"], tag="wo_rms")

        cxs = [sb["cxs0"], sb["cxs0"]]
        QQLAST = [None, None]
        psA_at = ps["big0"][:, 0:P]
        psA_o = ps["big0"][:, P:2 * P]
        for t in range(NT):
            sl = t % 2
            dma("sync", cxs[sl][:, :, :], xT_v[:, :, t * TT:(t + 1) * TT],
                f"X{sl}", tag=f"cxld{t}", deps=[T.get(f"cx_free{t-1}")])
            dma("sync", sb["h_r"][:, :, :], hbh_v[:, :, t * TT:(t + 1) * TT],
                f"H{sl}", tag=f"chld{t}",
                deps=["s_stage", T.get(f"hr_used{t-1}")])
            dma("gpsimd", sb["kfm_r"][:, :, :], kfm_v[:, :, t * TT:(t + 1) * TT],
                f"K{sl}", tag=f"ckfld{t}",
                deps=["s_stage", T.get(f"kfmr_used{t-1}")])
            dma("gpsimd", sb["ktm_r"][:, :, :], ktm_v[:, t * NCH:(t + 1) * NCH, :],
                f"K{sl}", tag=f"cktld{t}",
                deps=[T.get(f"ktmr_used{t-1}")])
            dma("gpsimd", sb["v_r"][:, :, :], v_v[:, t * NCH:(t + 1) * NCH, :],
                f"K{sl}", tag=f"cvld{t}",
                deps=[T.get(f"vr_used{t-1}")])
            for h_ in range(H):
                ks = h_ % 2
                def mk_q(e, h_=h_, ks=ks):
                    r = None
                    for kc in range(KC):
                        r = e.matmul(ps[f"big{ks}"][:, :],
                                     sb["wq_sb"][:, kc, h_ * P:(h_ + 1) * P],
                                     sb["h_r"][:, kc, :],
                                     start=(kc == 0), stop=(kc == KC - 1))
                    return r
                p.op("tensor", mk_q, deps=[f"chld{t}", "ld_wqo", PSL.get(f"big{ks}")],
                     tag=f"qmm{t}_{h_}")
                p.op("scalar", lambda e, ks=ks, bs=f"big{ks}": e.activation(
                        sb[f"qq{ks}"][:, :], ps[bs][:, :], AF.Identity),
                     deps=[f"qmm{t}_{h_}", QQLAST[ks]],
                     tag=f"qev{t}_{h_}")
                p.op("tensor", lambda e, ks=ks: e.matmul(
                        ps[f"big{ks}"][:, :], sb["fmqT"][:, :], sb[f"qq{ks}"][:, :],
                        start=True, stop=True),
                     deps=[f"qev{t}_{h_}", "ld_wqo"], tag=f"qymm{t}_{h_}")
                QQLAST[ks] = f"qymm{t}_{h_}"
                def mk_qexp(e, h_=h_, ks=ks):
                    e.activation(sb["qfm_sb"][:, 2 * h_, :], ps[f"big{ks}"][:, :], AF.Exp,
                                 bias=sb["bqp"][:, :])
                    return e.activation(sb["qfm_sb"][:, 2 * h_ + 1, :], ps[f"big{ks}"][:, :], AF.Exp,
                                        bias=sb["bqn"][:, :], scale=-1.0)
                p.op("scalar", mk_qexp,
                     deps=[f"qymm{t}_{h_}", "bias_cols", T.get(f"qfm_used{t-1}")],
                     tag=f"qexp{t}_{h_}")
                PSL[f"big{ks}"] = f"qexp{t}_{h_}"
            T[f"hr_used{t}"] = f"qmm{t}_{H-1}"
            psA = [ps["big0"], ps["big1"]]
            HORD = [0, 2, 4, 6, 1, 3, 5, 7]
            def S2_ap(h_):
                return ps[f"S{h_ // 2}"][:, (2 * h_ % 4) * P:(2 * h_ % 4) * P + DF]
            for k in range(33):
                if k < 32:
                    ch, pos = divmod(k, H)
                    h_ = HORD[pos]
                    sl2 = k % 2
                    pat = psA[sl2][:, 0:P]
                    cds = f"{t}_{k}"
                    if t == 0 and ch == 0:
                        p.op("vector", lambda e, h_=h_, sl2=sl2: e.tensor_copy(
                                sb["Ssnap"][:, sl2, :], SFIN[:, h_ * DF:(h_ + 1) * DF]),
                             deps=["sin7", T.get(f"snapfree{sl2}")], tag=f"snap{cds}")
                    else:
                        p.op("vector", lambda e, h_=h_, sl2=sl2: e.tensor_add(
                                sb["Ssnap"][:, sl2, :], S2_ap(h_),
                                SFIN[:, h_ * DF:(h_ + 1) * DF]),
                             deps=["sin7", T.get(f"supd_bank{h_ // 2}"),
                                   T.get(f"snapfree{sl2}")], tag=f"snap{cds}")
                    def mk_at(e, h_=h_, ch=ch, pat=pat):
                        r = None
                        for dc in range(2):
                            r = e.matmul(pat,
                                         sb["kfm_r"][:, 2 * h_ + dc, ch * CHN:(ch + 1) * CHN],
                                         sb["qfm_sb"][:, 2 * h_ + dc, ch * CHN:(ch + 1) * CHN],
                                         start=(dc == 0), stop=(dc == 1), skip_group_check=True)
                        return r
                    p.op("tensor", mk_at,
                         deps=[f"qexp{t}_{h_}", f"cvld{t}", T.get(f"park_f{sl2}"),
                               T.get(f"atev_f{sl2}"), PSL.get(f"big{sl2}")],
                         tag=f"at{cds}")
                    PSL[f"big{sl2}"] = f"at{cds}"
                    p.op("vector", lambda e, sl2=sl2, pat=pat: e.tensor_tensor(
                            sb["at_sb"][:, sl2, :], pat, sb["triu"][:, :], op=ALU.mult),
                         deps=[f"at{cds}", T.get(f"atsb_f{sl2}")],
                         tag=f"atev{cds}")
                    T[f"atev_f{sl2}"] = f"atev{cds}"
                if k >= 1:
                    km = k - 1
                    ch, pos = divmod(km, H)
                    h_ = HORD[pos]
                    sl2 = km % 2
                    po = psA[sl2][:, P:2 * P]
                    cds = f"{t}_{km}"
                    def mk_o(e, h_=h_, ch=ch, sl2=sl2, po=po):
                        e.matmul(po, sb["at_sb"][:, sl2, :],
                                 sb["v_r"][:, ch, h_ * P:(h_ + 1) * P],
                                 start=False, stop=False, skip_group_check=True)
                        r = None
                        for dc in range(2):
                            r = e.matmul(po,
                                         sb["qfm_sb"][:, 2 * h_ + dc, ch * CHN:(ch + 1) * CHN],
                                         sb["Ssnap"][:, sl2, dc * P:(dc + 1) * P],
                                         start=False, stop=(dc == 1), skip_group_check=True)
                        return r
                    p.op("tensor", mk_o,
                         deps=[f"atev{cds}", f"snap{cds}", f"cvld{t}",
                               T.get(f"park_f{sl2}")],
                         tag=f"omm{cds}")
                    T[f"atsb_f{sl2}"] = f"omm{cds}"
                    T[f"snapfree{sl2}"] = f"omm{cds}"
                    def mk_su(e, h_=h_, ch=ch, t=t):
                        r = None
                        for dc in range(2):
                            st_ = (t == 0 and ch == 0 and dc == 0 and h_ % 2 == 0)
                            r = e.matmul(S_ap(h_, dc),
                                         sb["ktm_r"][:, ch, h_ * DF + dc * P:h_ * DF + (dc + 1) * P],
                                         sb["v_r"][:, ch, h_ * P:(h_ + 1) * P],
                                         start=st_, stop=False, skip_group_check=True)
                        return r
                    p.op("tensor", mk_su,
                         deps=[f"snap{cds}", f"cvld{t}"],
                         tag=f"supd{cds}")
                    T[f"supd_bank{h_ // 2}"] = f"supd{cds}"
                    p.op("vector", lambda e, km=km, po=po: e.tensor_copy(
                            sb["orow"][:, km, :], po),
                         deps=[f"omm{cds}", T.get(f"orow_used{t-1}_{km}")],
                         tag=f"vpark{cds}")
                    T[f"park_f{sl2}"] = f"vpark{cds}"
            T[f"kfmr_used{t}"] = f"at{t}_31"
            T[f"ktmr_used{t}"] = f"supd{t}_31"
            T[f"vr_used{t}"] = f"supd{t}_31"
            T[f"qfm_used{t}"] = f"omm{t}_31"
            for g in range(NCH):
                dh = g % 2
                p.op("scalar", lambda e, g=g, dh=dh: e.activation(
                        sb["o2w"][:, dh, :].rearrange("q (i c) -> q i c", c=P),
                        sb["orow"][:, g * 8:(g + 1) * 8, :], AF.Square, scale=2.0 ** -24),
                     deps=[f"vpark{t}_{g*8+7}", T.get(f"o2w_rd{dh}")],
                     tag=f"osqa{t}_{g}")
                p.op("vector", lambda e, g=g, dh=dh: e.tensor_reduce(
                        sb["rms32"][:, g * 8:(g + 1) * 8].unsqueeze(2),
                        sb["o2w"][:, dh, :].rearrange("q (i c) -> q i c", c=P),
                        axis=AX.X, op=ALU.add),
                     deps=[f"osqa{t}_{g}"], tag=f"osqr{t}_{g}")
                T[f"o2w_rd{dh}"] = f"osqr{t}_{g}"
            p.op("scalar", lambda e: e.activation(sb["rms32"][:, :], sb["rms32"][:, :], AF.Sqrt,
                                                  bias=sb["c2ecol"][:, :]),
                 deps=[f"osqr{t}_{NCH-1}", "c2e_set", T.get(f"rms32_used{t-1}")],
                 tag=f"rchain1_{t}")
            p.op("scalar", lambda e: e.activation(sb["rms32"][:, :], sb["rms32"][:, :], AF.Sqrt),
                 deps=[f"rchain1_{t}"], tag=f"rchain2_{t}")
            p.op("vector", lambda e: e.reciprocal(sb["rms32"][:, :], sb["rms32"][:, :]),
                 deps=[f"rchain2_{t}"], tag=f"rchain3_{t}")
            p.op("scalar", lambda e: e.activation(sb["rms32"][:, :], sb["rms32"][:, :], AF.Square,
                                                  scale=(128.0 ** 0.25) * (2.0 ** -12)),
                 deps=[f"rchain3_{t}"], tag=f"rchain_{t}")
            for g in range(NCH):
                p.op("vector", lambda e, g=g: e.tensor_tensor(
                        sb["orow"][:, g * 8:(g + 1) * 8, :],
                        sb["orow"][:, g * 8:(g + 1) * 8, :],
                        sb["rms32"][:, g * 8:(g + 1) * 8].unsqueeze(2).broadcast_to([P, 8, P]),
                        op=ALU.mult),
                     deps=[f"rchain_{t}"], tag=f"ong{t}_{g}")
            T[f"rms32_used{t}"] = f"ong{t}_{NCH-1}"
            pxs = [ps["pxa"], ps["pxb"]]
            POS = {h2_: i2_ for i2_, h2_ in enumerate(HORD)}
            for ch in range(NCH):
                for half in range(2):
                    px = pxs[(ch * 2 + half) % 2]
                    ks = [ch * 8 + POS[half * 4 + j] for j in range(4)]
                    def mk_otp(e, ks=ks, px=px):
                        r = None
                        for j, k_ in enumerate(ks):
                            r = e.transpose(px[:, j * P:(j + 1) * P],
                                            sb["orow"][:, k_, :], sb["ident"][:, :])
                        return r
                    p.op("tensor", mk_otp,
                         deps=[f"ong{t}_{ch}", T.get(f"px_ev{(ch*2+half)%2}"), "ld_consts"],
                         tag=f"otpg{t}_{ch}_{half}")
                    for k_ in ks:
                        T[f"orow_used{t}_{k_}"] = f"otpg{t}_{ch}_{half}"
                    p.op("vector", lambda e, ch=ch, half=half, px=px: e.tensor_copy(
                            sb["o_sb"][:, half * 4:(half + 1) * 4, ch * CHN:(ch + 1) * CHN],
                            px[:, 0:TT].rearrange("q (j c) -> q j c", c=P)),
                         deps=[f"otpg{t}_{ch}_{half}", T.get(f"o_used{t-1}")],
                         tag=f"oevg{t}_{ch}_{half}")
                    T[f"px_ev{(ch*2+half)%2}"] = f"oevg{t}_{ch}_{half}"
            for fo in range(KC):
                bs = f"big{fo % 2}"
                def mk_wo(e, fo=fo, bs=bs):
                    r = None
                    for kc in range(KC):
                        r = e.matmul(ps[bs][:, :],
                                     sb["wo_sb"][:, kc, fo * P:(fo + 1) * P],
                                     sb["o_sb"][:, kc, :],
                                     start=(kc == 0), stop=(kc == KC - 1))
                    return r
                p.op("tensor", mk_wo,
                     deps=[f"oevg{t}_{NCH-1}_1", "wo_rms", PSL.get(bs),
                           T.get(f"park_f{fo % 2}"), T.get(f"atev_f{fo % 2}")],
                     tag=f"wo{t}_{fo}")
                p.op("scalar", lambda e, fo=fo, bs=bs: e.activation(
                        sb["ep_sb"][:, :], ps[bs][:, :], AF.Identity,
                        scale=sb["mcols"][:, 2, fo:fo + 1]),
                     deps=[f"wo{t}_{fo}", "modcols_done", T.get("ep_used")],
                     tag=f"ep1{t}_{fo}")
                PSL[bs] = f"ep1{t}_{fo}"
                p.op("vector", lambda e, fo=fo, sl=sl: e.tensor_add(
                        sb["x2_sb"][:, fo, :], cxs[sl][:, fo, :], sb["ep_sb"][:, :]),
                     deps=[f"ep1{t}_{fo}", f"cxld{t}", T.get(f"x2_free{t-1}")],
                     tag=f"x2{t}_{fo}")
                T["ep_used"] = f"x2{t}_{fo}"
            T[f"o_used{t}"] = f"wo{t}_{KC-1}"
            T[f"cx_free{t}"] = f"x2{t}_{KC-1}"
            dma("gpsimd", x2h_v[:, :, t * TT:(t + 1) * TT], sb["x2_sb"][:, :, :],
                "SPB", tag=f"x2sp{t}", deps=[f"x2{t}_{KC-1}"])
            ln2t = emit_ln(
                f"cln{t}", f"x2{t}_{KC-1}", sb["x2_sb"], sb["cxbf"],
                [sb["csq0"], sb["csq1"]], sb["ctmp1"],
                sb["ctmp2"], sb["crsb"], sb["cmrsb"], sb["h2_sb"], sb["cb1"],
                sb["mcols"][:, 3, :], t * KC,
                cast_deps=[T.get(f"cln_xbf_free{t-1}")],
                mod_deps=[f"x2sp{t}"])
            T[f"x2_free{t}"] = f"cln{t}cast{KC-1}"
            T[f"cln_xbf_free{t}"] = f"cln{t}smod{KC-1}"
            dma("gpsimd", h2h_v[:, :, t * TT:(t + 1) * TT], sb["h2_sb"][:, :, :],
                "SPB", tag=f"h2sp{t}", deps=[ln2t])
            T[f"h2sp{t}"] = f"h2sp{t}"

        p.barrier("p2c", sb)

        MB = {"prev_stage": None, "prev_ld": None}

        def mbarrier(name, folds, dep):
            for j, (sn, lo, hi) in enumerate(folds):
                p.op("vector", lambda e, sn=sn, lo=lo, hi=hi, j=j: e.tensor_reduce(
                        sb["mxst"][:, j:j + 1], sb[sn][:, lo:hi], axis=AX.X, op=ALU.max),
                     deps=[dep, MB["prev_stage"]], tag=f"{name}_f{j}")
            if len(folds) == 1:
                p.op("vector", lambda e: e.memset(sb["mxst"][:, 1:2], 0.0),
                     deps=[MB["prev_stage"]], tag=f"{name}_f1")
            dma("gpsimd", mx_in, sb["mxst"][:, :], "SG", tag=f"{name}_stage",
                deps=[f"{name}_f0", f"{name}_f1"])
            MB["prev_stage"] = f"{name}_stage"
            p.op("gpsimd", lambda e: e.collective_compute(
                    "AllReduce", ALU.max, replica_groups=[list(range(8))],
                    ins=[mx_in.opt()], outs=[mx_out.opt()]),
                 deps=[f"{name}_stage"], sem="CC", tag=f"{name}_cc")
            dma("gpsimd", sb["mxrow"][:, :], mx_out, "SG", tag=f"{name}_ld",
                deps=[f"{name}_cc", MB["prev_ld"]])
            p.op("vector", lambda e: e.tensor_reduce(
                    sb["mxfold"][:, :].unsqueeze(2),
                    sb["mxrow"][:, :].rearrange("a (q c) -> a c q", c=2),
                    axis=AX.X, op=ALU.max),
                 deps=[f"{name}_ld"], tag=f"{name}_max")
            MB["prev_ld"] = f"{name}_max"

        def bcast_col(name, src_ap, dep, tg):
            p.op("tensor", lambda e: e.matmul(ps["big0"][:, 0:1], sb["onesrow"][:, :],
                                              src_ap, start=True, stop=True),
                 deps=[dep, PSL.get("big0"), "ld_consts"], tag=f"bcm_{tg}")
            p.op("scalar", lambda e: e.activation(sb[name][:, :], ps["big0"][:, 0:1], AF.Identity),
                 deps=[f"bcm_{tg}"], tag=f"bce_{tg}")
            PSL["big0"] = f"bce_{tg}"
            return f"bce_{tg}"

        if PH < 4:
            finish_early()
            return
        mbarrier("m0", [("m0slots", 0, 64)], f"cln{NT-1}m0_{KC-1}")
        p.op("vector", lambda e: e.tensor_scalar(sb["srow"][:, 0:1], sb["mxfold"][:, 0:1],
                                                 1.0 / QMAX, None, op0=ALU.mult),
             deps=["m0_max"], tag="s0row")
        p.op("vector", lambda e: e.reciprocal(sb["srow"][:, 1:2], sb["srow"][:, 0:1]),
             deps=["s0row"], tag="r0row")
        bcast_col("s0c", sb["srow"][0:1, 0:1], "s0row", "s0c")
        bcast_col("r0c", sb["srow"][0:1, 1:2], "r0row", "r0c")

        if PH < 5:
            finish_early()
            return
        dma("sync", sb["wup_sb"][:, :, :], wupT_q.rearrange("(kc q) f -> q kc f", q=P),
            "LF", tag="ld_wup", deps=[T["dq_done"]])
        dma("sync", sb["wg_sb"][:, :, :], wgT_q.rearrange("(kc q) f -> q kc f", q=P),
            "LF", tag="ld_wg", deps=[T["dq_done"]])
        h2r = [sb["h2r0"], sb["h2r1"]]
        xqt = [sb["xqt0"], sb["xqt1"]]
        for t in range(NT):
            sl = t % 2
            dma("sync", h2r[sl][:, :, :], h2h_v[:, :, t * TT:(t + 1) * TT],
                f"H{sl}", tag=f"bh2ld{t}",
                deps=[f"h2sp{NT-1}", T.get(f"bh2_free{t-2}")])
            for kc in range(KC):
                ts_ = kc % 2
                p.op("scalar", lambda e, sl=sl, kc=kc, ts_=ts_: e.activation(
                        xqt[ts_][:, :], h2r[sl][:, kc, :], AF.Identity,
                        scale=sb["r0c"][:, :], bias=sb["rccol"][:, :]),
                     deps=[f"bh2ld{t}", "bce_r0c",
                           (f"bxq{t}_{kc-2}" if kc >= 2 else
                            (f"bxq{t-1}_{KC-2+kc}" if t >= 1 else None))],
                     tag=f"bxqt{t}_{kc}")
                p.op("vector", lambda e, kc=kc, ts_=ts_: e.tensor_scalar(
                        sb["xq_sb"][:, kc, :], xqt[ts_][:, :], -RC, sb["s0c"][:, :],
                        op0=ALU.add, op1=ALU.mult),
                     deps=[f"bxqt{t}_{kc}", "bce_s0c", T.get(f"xq_used{t-1}")],
                     tag=f"bxq{t}_{kc}")
            T[f"bh2_free{t}"] = f"bxqt{t}_{KC-1}"
            for which, (wname, stname, slots, ldt, dsth) in enumerate(
                    [("wup_sb", "Ust", "m1slots", "ld_wup", Uh),
                     ("wg_sb", "Gst", "m2slots", "ld_wg", Gh)]):
                for ib in range(NI):
                    bs = f"big{ib % 2}"
                    def mk_u(e, wname=wname, ib=ib, bs=bs):
                        r = None
                        for kc in range(KC):
                            r = e.matmul(ps[bs][:, :], sb[wname][:, kc, ib * P:(ib + 1) * P],
                                         sb["xq_sb"][:, kc, :],
                                         start=(kc == 0), stop=(kc == KC - 1))
                        return r
                    p.op("tensor", mk_u,
                         deps=[f"bxq{t}_{KC-1}", "ld_wg", PSL.get(bs),
                               PSL.get(bs + "_mx")],
                         tag=f"bmm{t}_{which}_{ib}")
                    p.op("scalar", lambda e, stname=stname, ib=ib, bs=bs: e.activation(
                            sb[stname][:, ib, :], ps[bs][:, :], AF.Identity),
                         deps=[f"bmm{t}_{which}_{ib}", T.get(f"bst{t-1}_1")],
                         tag=f"bev{t}_{which}_{ib}")
                    p.op("vector", lambda e, slots=slots, stname=stname, ib=ib, t=t: e.tensor_reduce(
                            sb[slots][:, t * NI + ib:t * NI + ib + 1],
                            sb[stname][:, ib, :], axis=AX.X, op=ALU.max,
                            apply_absolute_value=True),
                         deps=[f"bev{t}_{which}_{ib}"],
                         tag=f"bmx{t}_{which}_{ib}")
                    PSL[bs] = f"bev{t}_{which}_{ib}"
                    PSL[bs + "_mx"] = f"bmx{t}_{which}_{ib}"
                dma("gpsimd", dsth[:, t * NI * TT:(t + 1) * NI * TT],
                    sb[stname][:, :, :], "SC1", tag=f"bst{t}_{which}",
                    deps=[f"bev{t}_{which}_{NI-1}", f"bmx{t}_{which}_{NI-1}"])
                T[f"bst{t}_{which}"] = f"bst{t}_{which}"
            T[f"xq_used{t}"] = f"bmm{t}_1_{NI-1}"

        if PH < 6:
            finish_early()
            return
        mbarrier("m12", [("m1slots", 0, NT * NI), ("m2slots", 0, NT * NI)],
                 f"bmx{NT-1}_1_{NI-1}")
        p.op("vector", lambda e: e.tensor_scalar(sb["srow"][:, 2:4], sb["mxfold"][:, 0:2],
                                                 1.0 / QMAX, None, op0=ALU.mult),
             deps=["m12_max"], tag="s12row")
        p.op("vector", lambda e: e.reciprocal(sb["srow"][:, 4:6], sb["srow"][:, 2:4]),
             deps=["s12row"], tag="r12row")
        bcast_col("r1c", sb["srow"][0:1, 4:5], "r12row", "r1c")
        bcast_col("r2c", sb["srow"][0:1, 5:6], "r12row", "r2c")
        bcast_col("s2c", sb["srow"][0:1, 3:4], "s12row", "s2c")
        p.op("vector", lambda e: e.tensor_scalar(sb["brc2c"][:, :], sb["s2c"][:, :],
                                                 -RC, None, op0=ALU.mult),
             deps=["bce_s2c"], tag="brc2_done")

        dma("sync", sb["wd_sb"][:, :, :], wdT_q.rearrange("(ic q) f -> q ic f", q=P),
            "LF", tag="ld_wd", deps=[T["dq_done"], f"bst{NT-1}_1"])
        gld = [sb["bufA0"], sb["bufA1"]]
        t1f = [sb["t1f0"], sb["t1f1"]]
        sgf = [sb["sgf0"], sb["sgf1"]]
        NH = NI // 2
        for t in range(NT):
            sl = t % 2
            dma("gpsimd", gld[sl][:, :, :], Gh[:, t * NI * TT:(t + 1) * NI * TT],
                f"U{sl}", tag=f"c1ld{t}",
                deps=[f"bst{NT-1}_1", T.get(f"c1_free{t-2}")])
            for ib in range(NI):
                ts_ = ib % 2
                pg = (f"c1gp{t}_{ib-2}" if ib >= 2 else
                      (f"c1gp{t-1}_{NI-2+ib}" if t >= 1 else None))
                p.op("scalar", lambda e, sl=sl, ib=ib, ts_=ts_: e.activation(
                        t1f[ts_][:, :], gld[sl][:, ib, :], AF.Identity,
                        scale=sb["r2c"][:, :], bias=sb["rccol"][:, :]),
                     deps=[f"c1ld{t}", "bce_r2c", pg],
                     tag=f"c1t1{t}_{ib}")
                p.op("scalar", lambda e, ts_=ts_: e.activation(
                        sgf[ts_][:, :], t1f[ts_][:, :], AF.Sigmoid,
                        scale=sb["s2c"][:, :], bias=sb["brc2c"][:, :]),
                     deps=["brc2_done", f"c1t1{t}_{ib}"], tag=f"c1sg{t}_{ib}")
                p.op("scalar", lambda e, ts_=ts_: e.activation(
                        sb["tgbf"][:, ts_, :], t1f[ts_][:, :], AF.Identity,
                        bias=sb["mrccol"][:, :]),
                     deps=[f"c1sg{t}_{ib}", "mrc_set", pg], tag=f"c1tg{t}_{ib}")
                p.op("vector", lambda e, ts_=ts_: e.tensor_scalar(
                        sgf[ts_][:, :], sgf[ts_][:, :], 65536.0, RC,
                        op0=ALU.mult, op1=ALU.add),
                     deps=[f"c1sg{t}_{ib}"], tag=f"c1t3{t}_{ib}")
                p.op("vector", lambda e, ts_=ts_: e.tensor_scalar(
                        sb["srbf"][:, :], sgf[ts_][:, :], -RC, 65535.0,
                        op0=ALU.add, op1=ALU.min),
                     deps=[f"c1t3{t}_{ib}"], tag=f"c1sr{t}_{ib}")
                p.op("vector", lambda e, ib=ib, ts_=ts_: e.tensor_mul(
                        sb["g2st"][:, ib, :], sb["tgbf"][:, ts_, :], sb["srbf"][:, :]),
                     deps=[f"c1sr{t}_{ib}", f"c1tg{t}_{ib}",
                           T.get(f"c1st{t-1}_{ib // NH}")],
                     tag=f"c1gp{t}_{ib}")
                p.op("vector", lambda e, ib=ib, t=t: e.tensor_reduce(
                        sb["m3slots"][:, t * NI + ib:t * NI + ib + 1],
                        sb["g2st"][:, ib, :], axis=AX.X, op=ALU.max,
                        apply_absolute_value=True),
                     deps=[f"c1gp{t}_{ib}"], tag=f"c1mx{t}_{ib}")
                if ib == NH - 1:
                    dma("gpsimd", g2h[:, (t * NI) * TT:(t * NI + NH) * TT],
                        sb["g2st"][:, 0:NH, :],
                        "SC2", tag=f"c1st{t}_0", deps=[f"c1gp{t}_{NH-1}", f"c1mx{t}_{NH-1}"])
                    T[f"c1st{t}_0"] = f"c1st{t}_0"
            dma("gpsimd", g2h[:, (t * NI + NH) * TT:(t + 1) * NI * TT],
                sb["g2st"][:, NH:NI, :],
                "SC2B", tag=f"c1st{t}_1", deps=[f"c1gp{t}_{NI-1}", f"c1mx{t}_{NI-1}"])
            T[f"c1st{t}_1"] = f"c1st{t}_1"
            T[f"c1_free{t}"] = f"c1t1{t}_{NI-1}"

        if PH < 7:
            finish_early()
            return
        mbarrier("m3", [("m3slots", 0, NT * NI)], f"c1mx{NT-1}_{NI-1}")
        p.op("vector", lambda e: e.tensor_mul(sb["srow"][:, 6:7], sb["mxfold"][:, 0:1],
                                              sb["srow"][:, 3:4]),
             deps=["m3_max", "s12row"], tag="m3a")
        p.op("vector", lambda e: e.tensor_scalar(sb["srow"][:, 6:7], sb["srow"][:, 6:7],
                                                 1.0 / (65536.0 * QMAX), None, op0=ALU.mult),
             deps=["m3a"], tag="s3row")
        p.op("vector", lambda e: e.reciprocal(sb["srow"][:, 7:8], sb["srow"][:, 6:7]),
             deps=["s3row"], tag="r3row")
        p.op("vector", lambda e: e.tensor_mul(sb["srow"][:, 8:9], sb["srow"][:, 3:4],
                                              sb["srow"][:, 7:8]),
             deps=["r3row"], tag="c32a")
        p.op("vector", lambda e: e.tensor_scalar(sb["srow"][:, 8:9], sb["srow"][:, 8:9],
                                                 1.0 / 65536.0, None, op0=ALU.mult),
             deps=["c32a"], tag="c32row")
        bcast_col("c32c", sb["srow"][0:1, 8:9], "c32row", "c32c")

        uld = [sb["bufB0"], sb["bufB1"]]
        for t in range(NT):
            sl = t % 2
            dma("gpsimd", gld[sl][:, :, :], g2h[:, t * NI * TT:(t + 1) * NI * TT],
                f"U{sl}", tag=f"c2gld{t}",
                deps=[f"c1st{NT-1}_0", f"c1st{NT-1}_1", T.get(f"c2_free{t-2}")])
            dma("gpsimd", uld[sl][:, :, :], Uh[:, t * NI * TT:(t + 1) * NI * TT],
                f"K{sl}", tag=f"c2uld{t}",
                deps=[f"bst{NT-1}_1", T.get(f"c2_free{t-2}")])
            for ib in range(NI):
                ts_ = ib % 2
                pg = (f"c2pr{t}_{ib-2}" if ib >= 2 else
                      (f"c2pr{t-1}_{NI-2+ib}" if t >= 1 else None))
                p.op("scalar", lambda e, sl=sl, ib=ib, ts_=ts_: e.activation(
                        t1f[ts_][:, :], gld[sl][:, ib, :], AF.Identity,
                        scale=sb["c32c"][:, :], bias=sb["rccol"][:, :]),
                     deps=[f"c2gld{t}", "bce_c32c", pg], tag=f"c2t4{t}_{ib}")
                p.op("vector", lambda e, ts_=ts_: e.tensor_scalar(
                        sb["g2bf"][:, :], t1f[ts_][:, :], -RC, None, op0=ALU.add),
                     deps=[f"c2t4{t}_{ib}"], tag=f"c2g2{t}_{ib}")
                p.op("scalar", lambda e, sl=sl, ib=ib, ts_=ts_: e.activation(
                        sgf[ts_][:, :], uld[sl][:, ib, :], AF.Identity,
                        scale=sb["r1c"][:, :], bias=sb["rccol"][:, :]),
                     deps=[f"c2uld{t}", "bce_r1c"], tag=f"c2t5{t}_{ib}")
                p.op("vector", lambda e, ts_=ts_: e.tensor_scalar(
                        sb["upbf"][:, :], sgf[ts_][:, :], -RC, None, op0=ALU.add),
                     deps=[f"c2t5{t}_{ib}"], tag=f"c2up{t}_{ib}")
                p.op("vector", lambda e, ib=ib: e.tensor_mul(
                        sb["prst"][:, ib, :], sb["g2bf"][:, :], sb["upbf"][:, :]),
                     deps=[f"c2g2{t}_{ib}", f"c2up{t}_{ib}",
                           T.get(f"c2st{t-1}_{ib // NH}")],
                     tag=f"c2pr{t}_{ib}")
                if ib == NH - 1:
                    dma("gpsimd", prh[:, (t * NI) * TT:(t * NI + NH) * TT],
                        sb["prst"][:, 0:NH, :],
                        "SC3", tag=f"c2st{t}_0", deps=[f"c2pr{t}_{NH-1}", f"c2mx{t}_{NH-1}"])
                    T[f"c2st{t}_0"] = f"c2st{t}_0"
                p.op("vector", lambda e, ib=ib, t=t: e.tensor_reduce(
                        sb["m4slots"][:, t * NI + ib:t * NI + ib + 1],
                        sb["prst"][:, ib, :], axis=AX.X, op=ALU.max,
                        apply_absolute_value=True),
                     deps=[f"c2pr{t}_{ib}"], tag=f"c2mx{t}_{ib}")
            dma("gpsimd", prh[:, (t * NI + NH) * TT:(t + 1) * NI * TT],
                sb["prst"][:, NH:NI, :],
                "SC3B", tag=f"c2st{t}_1", deps=[f"c2pr{t}_{NI-1}", f"c2mx{t}_{NI-1}"])
            T[f"c2st{t}_1"] = f"c2st{t}_1"
            T[f"c2_free{t}"] = f"c2pr{t}_{NI-1}"

        if PH < 8:
            finish_early()
            return
        mbarrier("m4", [("m4slots", 0, NT * NI)], f"c2mx{NT-1}_{NI-1}")
        p.op("vector", lambda e: e.tensor_mul(sb["srow"][:, 9:10], sb["srow"][:, 6:7],
                                              sb["srow"][:, 2:3]),
             deps=["s3row", "s12row", "m4_max"], tag="s31")
        p.op("vector", lambda e: e.tensor_mul(sb["srow"][:, 10:11], sb["mxfold"][:, 0:1],
                                              sb["srow"][:, 9:10]),
             deps=["s31"], tag="m4a")
        p.op("vector", lambda e: e.tensor_scalar(sb["srow"][:, 10:11], sb["srow"][:, 10:11],
                                                 1.0 / QMAX, None, op0=ALU.mult),
             deps=["m4a"], tag="s4row")
        p.op("vector", lambda e: e.reciprocal(sb["srow"][:, 11:12], sb["srow"][:, 10:11]),
             deps=["s4row"], tag="r4row")
        p.op("vector", lambda e: e.tensor_mul(sb["srow"][:, 11:12], sb["srow"][:, 11:12],
                                              sb["srow"][:, 9:10]),
             deps=["r4row"], tag="c4row")
        bcast_col("c4c", sb["srow"][0:1, 11:12], "c4row", "c4c")
        bcast_col("s4c", sb["srow"][0:1, 10:11], "s4row", "s4c")
        p.op("vector", lambda e: e.tensor_scalar(sb["gbs4"][:, :], sb["mcols"][:, 5, :],
                                                 sb["s4c"][:, :], None, op0=ALU.mult),
             deps=["bce_s4c", "modcols_done"], tag="gbs4_done")

        x2r = [sb["x2r0"], sb["x2r1"]]
        t1g = [sb["t1g0"], sb["t1g1"]]
        outs = [sb["outs0"], sb["outs0"]]
        ep2 = [sb["ep20"], sb["ep21"]]
        for t in range(NT):
            sl = t % 2
            dma("gpsimd", gld[sl][:, :, :], prh[:, t * NI * TT:(t + 1) * NI * TT],
                f"U{sl}", tag=f"c3pld{t}",
                deps=[f"c2st{NT-1}_0", f"c2st{NT-1}_1", T.get(f"c3_free{t-2}")])
            dma("sync", x2r[sl][:, :, :], x2h_v[:, :, t * TT:(t + 1) * TT],
                f"X{sl}", tag=f"c3xld{t}",
                deps=[f"h2sp{NT-1}", f"c2st{NT-1}_0", f"c2st{NT-1}_1", T.get(f"c3x2_free{t-2}")])
            for ib2 in range(NI // 2):
                ts_ = ib2 % 2
                pg = (f"c3o{t}_{ib2-1}" if ib2 >= 1 else
                      (f"c3o{t-1}_{NI//2-1}" if t >= 1 else None))
                def mk_c3t(e, sl=sl, ib2=ib2, ts_=ts_):
                    r = None
                    for j in range(2):
                        r = e.activation(t1f[ts_][:, :] if j == 0 else t1g[ts_][:, :],
                                         gld[sl][:, 2 * ib2 + j, :], AF.Identity,
                                         scale=sb["c4c"][:, :], bias=sb["rccol"][:, :])
                    return r
                p.op("scalar", mk_c3t,
                     deps=[f"c3pld{t}", "bce_c4c", pg], tag=f"c3t{t}_{ib2}")
                def mk_c3o(e, ib2=ib2, ts_=ts_):
                    e.tensor_scalar(sb["obf"][:, 2 * ib2, :], t1f[ts_][:, :],
                                    -RC, None, op0=ALU.add)
                    return e.tensor_scalar(sb["obf"][:, 2 * ib2 + 1, :], t1g[ts_][:, :],
                                           -RC, None, op0=ALU.add)
                p.op("vector", mk_c3o,
                     deps=[f"c3t{t}_{ib2}", T.get(f"c3mm_done{t-1}")],
                     tag=f"c3o{t}_{ib2}")
            for fo in range(KC):
                bs = f"big{fo % 2}"
                es = fo % 2
                def mk_dn(e, fo=fo, bs=bs):
                    r = None
                    for ic in range(NI):
                        r = e.matmul(ps[bs][:, :], sb["wd_sb"][:, ic, fo * P:(fo + 1) * P],
                                     sb["obf"][:, ic, :],
                                     start=(ic == 0), stop=(ic == NI - 1))
                    return r
                p.op("tensor", mk_dn,
                     deps=[f"c3o{t}_{NI//2-1}", "ld_wd", PSL.get(bs)],
                     tag=f"c3mm{t}_{fo}")
                p.op("scalar", lambda e, fo=fo, bs=bs, es=es: e.activation(
                        ep2[es][:, :], ps[bs][:, :], AF.Identity,
                        scale=sb["gbs4"][:, fo:fo + 1]),
                     deps=[f"c3mm{t}_{fo}", "gbs4_done", T.get(f"c3ep_free{t}_{fo-2}"),
                           T.get(f"c3ep_freeprev{t}") if fo < 2 else None],
                     tag=f"c3ep{t}_{fo}")
                PSL[bs] = f"c3ep{t}_{fo}"
                p.op("vector", lambda e, fo=fo, sl=sl, es=es: e.tensor_add(
                        outs[sl][:, fo, :], x2r[sl][:, fo, :], ep2[es][:, :]),
                     deps=[f"c3ep{t}_{fo}", f"c3xld{t}", T.get(f"c3ost{t-1}")],
                     tag=f"c3out{t}_{fo}")
                T[f"c3ep_free{t}_{fo}"] = f"c3out{t}_{fo}"
            T[f"c3ep_freeprev{t+1}"] = f"c3out{t}_{KC-1}"
            T[f"c3mm_done{t}"] = f"c3mm{t}_{KC-1}"
            T[f"c3x2_free{t}"] = f"c3out{t}_{KC-1}"
            T[f"c3_free{t}"] = f"c3t{t}_{NI//2-1}"
            dma("sync", out_v[:, :, t * TT:(t + 1) * TT], outs[sl][:, :, :],
                "ST", tag=f"c3ost{t}", deps=[f"c3out{t}_{KC-1}"])
            T[f"c3ost{t}"] = f"c3ost{t}"

        p.op("sync", lambda e: e.wait_ge(p.sems["ST"], 16 * NT), inc=0)
        p.op("gpsimd", lambda e: e.wait_ge(p.sems["ST"], 16 * NT), inc=0)

    p.reset("count")
    program()
    snap = dict(p.tags)
    p.reset("count")
    program()
    assert p.tags == snap, "walk is not deterministic"

    with ctx:
        block = ctx.enter_context(nc.Block())

        def mk_section(eng):
            def f(engine):
                p.engs[eng] = engine
                p.reset(eng)
                program()
            return f

        block.sync(mk_section("sync"))
        block.gpsimd(mk_section("gpsimd"))
        block.tensor(mk_section("tensor"))
        block.scalar(mk_section("scalar"))
        block.vector(mk_section("vector"))

    return nc


def stage(inputs):
    x = np.asarray(inputs["L0_attn_x"], dtype=np.float32)
    shared = {
        "wqT": np.ascontiguousarray(np.asarray(inputs["L0_attn_Wq"], np.float32).T),
        "wkT": np.ascontiguousarray(np.asarray(inputs["L0_attn_Wk"], np.float32).T),
        "wvT": np.ascontiguousarray(np.asarray(inputs["L0_attn_Wv"], np.float32).T),
        "woT": np.ascontiguousarray(np.asarray(inputs["L0_attn_Wo"], np.float32).T),
        "fmqT": np.ascontiguousarray(np.asarray(inputs["L0_attn_fmq_w"], np.float32).T),
        "fmkT": np.ascontiguousarray(np.asarray(inputs["L0_attn_fmk_w"], np.float32).T),
        "fmqb": np.ascontiguousarray(np.asarray(inputs["L0_attn_fmq_b"], np.float32).reshape(P, 1)),
        "fmkb": np.ascontiguousarray(np.asarray(inputs["L0_attn_fmk_b"], np.float32).reshape(P, 1)),
        "rmsw": np.ascontiguousarray(np.asarray(inputs["L0_attn_rms_w"], np.float32).reshape(P, 1)),
        "wup": np.ascontiguousarray(np.asarray(inputs["L0_ffn_w_up"], np.float32)),
        "wgate": np.ascontiguousarray(np.asarray(inputs["L0_ffn_w_gate"], np.float32)),
        "wdown": np.ascontiguousarray(np.asarray(inputs["L0_ffn_w_down"], np.float32)),
        "wmod": np.ascontiguousarray(np.asarray(inputs["L0_wmod"], np.float32)),
    }
    c = np.asarray(inputs["L0_c"], dtype=np.float32)
    in_maps = []
    for core in range(8):
        b, hf = core // 2, core % 2
        m = dict(shared)
        m["xT"] = np.ascontiguousarray(x[b, hf * TOK:(hf + 1) * TOK, :].T)
        m["crow"] = np.ascontiguousarray(c[b:b + 1, :])
        sm = np.zeros((1, 8), dtype=np.float32)
        if hf == 1:
            sm[0, 0] = 1.0
        m["smask"] = sm
        in_maps.append(m)
    return in_maps


def kernel(**inputs):
    in_maps = stage(inputs)
    nc = build_nc()
    res = run_bass_kernel_spmd(nc, in_maps, core_ids=list(range(8)))
    out = np.empty((4, 2 * TOK, D), dtype=np.float32)
    for core in range(8):
        b, hf = core // 2, core % 2
        out[b, hf * TOK:(hf + 1) * TOK, :] = res.results[core]["out"].T
    return out


# version 07
# speedup vs baseline: 1.3251x; 1.0745x over previous
import math
import os
import numpy as np
import ml_dtypes
from contextlib import ExitStack

import concourse.bass as bass
import concourse.mybir as mybir
from concourse.bass_utils import run_bass_kernel_spmd

F32 = mybir.dt.float32
BF16 = mybir.dt.bfloat16
AF = mybir.ActivationFunctionType
ALU = mybir.AluOpType
AX = mybir.AxisListType

P = 128
D = 1024
KC = 8
H = 8
DF = 256
TOK = 4096
NT = 8
TT = 512
NCH = 4
CHN = 128
DFF = 3072
NI = 24
RC = 12582912.0
LN16 = math.log(16.0)
QMAX = 32767.0

PH = int(os.environ.get("KPHASES", "99"))
KNOB = os.environ.get("KNOB", "")
ENGSEM = dict(tensor="T", vector="V", scalar="S", sync="YN", gpsimd="GN")
NONDMA = {"T", "V", "S", "YN", "GN"}


class Prog:
    def __init__(self):
        self.mode = "count"
        self.tags = {}
        self.cnt = {}
        self.engs = {}
        self.sems = {}
        self.water = {}

    def reset(self, mode):
        self.mode = mode
        self.cnt = {k: 0 for k in self.sems}
        self.water = {}

    def settag(self, tag, srctag):
        if self.mode == "count" and tag not in self.tags:
            self.tags[tag] = self.tags[srctag]

    def op(self, eng, emit, deps=(), inc=1, tag=None, sem=None):
        semname = sem or ENGSEM[eng]
        self.cnt[semname] = self.cnt.get(semname, 0) + inc
        val = self.cnt[semname]
        if self.mode == "count":
            if tag is not None:
                if tag in self.tags:
                    assert self.tags[tag] == (semname, val), f"tag drift {tag}"
                self.tags[tag] = (semname, val)
            return tag
        if self.mode != eng:
            return tag
        e = self.engs[eng]
        own = ENGSEM[eng]
        for d in deps:
            if d is None:
                continue
            s, v = self.tags[d]
            if s == own and s in NONDMA:
                assert v <= self.cnt[own], f"forward self-dep {d} on {eng}"
            if self.water.get(s, 0) >= v:
                continue
            e.wait_ge(self.sems[s], v)
            self.water[s] = v
        instr = emit(e)
        if inc:
            instr.then_inc(self.sems[semname], inc)
        return tag

    def barrier(self, name, sb):
        if self.mode == "count":
            self.tags[f"bar_{name}"] = {k: v for k, v in self.cnt.items()}
            return
        vals = self.tags[f"bar_{name}"]
        for eng in ENGSEM:
            if self.mode != eng:
                continue
            e = self.engs[eng]
            own = ENGSEM[eng]
            for sname, v in vals.items():
                if sname == own or v == 0:
                    continue
                if self.water.get(sname, 0) >= v:
                    continue
                e.wait_ge(self.sems[sname], v)
                self.water[sname] = v


def build_nc():
    nc = bass.Bass("TRN2", target_bir_lowering=False)

    xT = nc.dram_tensor("xT", [D, TOK], F32, kind="ExternalInput").ap()
    wqT = nc.dram_tensor("wqT", [D, D], F32, kind="ExternalInput").ap()
    wkT = nc.dram_tensor("wkT", [D, D], F32, kind="ExternalInput").ap()
    wvT = nc.dram_tensor("wvT", [D, D], F32, kind="ExternalInput").ap()
    woT = nc.dram_tensor("woT", [D, D], F32, kind="ExternalInput").ap()
    fmqT = nc.dram_tensor("fmqT", [P, P], F32, kind="ExternalInput").ap()
    fmkT = nc.dram_tensor("fmkT", [P, P], F32, kind="ExternalInput").ap()
    fmqb = nc.dram_tensor("fmqb", [P, 1], F32, kind="ExternalInput").ap()
    fmkb = nc.dram_tensor("fmkb", [P, 1], F32, kind="ExternalInput").ap()
    rmsw = nc.dram_tensor("rmsw", [P, 1], F32, kind="ExternalInput").ap()
    wup = nc.dram_tensor("wup", [DFF, D], F32, kind="ExternalInput").ap()
    wgate = nc.dram_tensor("wgate", [DFF, D], F32, kind="ExternalInput").ap()
    wdown = nc.dram_tensor("wdown", [D, DFF], F32, kind="ExternalInput").ap()
    wmod = nc.dram_tensor("wmod", [D, 6 * D], F32, kind="ExternalInput").ap()
    crow = nc.dram_tensor("crow", [1, D], F32, kind="ExternalInput").ap()
    smask = nc.dram_tensor("smask", [1, 8], F32, kind="ExternalInput").ap()
    out = nc.dram_tensor("out", [D, TOK], F32, kind="ExternalOutput").ap()

    hb_h = nc.dram_tensor("hb_h", [D, TOK], BF16, kind="Internal").ap()
    hb_kfm = nc.dram_tensor("hb_kfm", [H * DF, TOK], BF16, kind="Internal").ap()
    hb_ktm = nc.dram_tensor("hb_ktm", [TOK, H * DF], BF16, kind="Internal").ap()
    hb_v = nc.dram_tensor("hb_v", [TOK, D], BF16, kind="Internal").ap()
    x2h = nc.dram_tensor("x2h", [D, TOK], F32, kind="Internal").ap()
    h2h = nc.dram_tensor("h2h", [D, TOK], BF16, kind="Internal").ap()
    wupT_q = nc.dram_tensor("wupT_q", [D, DFF], BF16, kind="Internal").ap()
    wgT_q = nc.dram_tensor("wgT_q", [D, DFF], BF16, kind="Internal").ap()
    wdT_q = nc.dram_tensor("wdT_q", [DFF, D], BF16, kind="Internal").ap()
    Uh = nc.dram_tensor("Uh", [P, NT * NI * TT], BF16, kind="Internal").ap()
    Gh = nc.dram_tensor("Gh", [P, NT * NI * TT], BF16, kind="Internal").ap()
    g2h = nc.dram_tensor("g2h", [P, NT * NI * TT], BF16, kind="Internal").ap()
    prh = nc.dram_tensor("prh", [P, NT * NI * TT], BF16, kind="Internal").ap()
    sc_in = nc.dram_tensor("sc_in", [P, H * DF], F32, kind="Internal").ap()
    sc_out = nc.dram_tensor("sc_out", [2 * P, H * DF], F32, kind="Internal").ap()
    mx_in = nc.dram_tensor("mx_in", [P, 2], F32, kind="Internal").ap()
    mx_out = nc.dram_tensor("mx_out", [P, 2], F32, kind="Internal").ap()

    idt = nc.inline_tensor(np.eye(P, dtype=ml_dtypes.bfloat16), name="identbf")
    tri = nc.inline_tensor(np.triu(np.ones((P, P), dtype=ml_dtypes.bfloat16)), name="triubf")
    onc = nc.inline_tensor(np.ones((P, 1), dtype=ml_dtypes.bfloat16), name="onescolbf")
    onr = nc.inline_tensor(np.ones((1, P), dtype=np.float32), name="onesrowf")

    sb = {}
    total_sbuf = nc.sbuf_bytes_remaining

    def galloc(name, shape, dt):
        sb[name] = nc.alloc_sbuf_tensor("sb_" + name, shape, dt)
        return sb[name]

    for nm, shp, dt in [
        ("dmy", [1, 1], F32),
        ("ident", [P, P], BF16), ("triu", [P, P], BF16),
        ("onescol", [P, 1], BF16), ("onesrow", [1, P], F32),
        ("epscol", [P, 1], F32), ("rccol", [P, 1], F32), ("c2ecol", [P, 1], F32),
        ("fmqT", [P, P], BF16), ("fmkT", [P, P], BF16),
        ("fmqb", [P, 1], F32), ("fmkb", [P, 1], F32),
        ("bqp", [P, 1], F32), ("bqn", [P, 1], F32), ("bkn", [P, 1], F32),
        ("rmsw", [P, 1], F32),
        ("mcols", [P, 6, 8], F32), ("ca1", [P, 8], F32), ("cb1", [P, 8], F32),
        ("smrow", [1, 8], F32), ("sm128", [P, 8], F32),
        ("m0slots", [P, 64], F32),
        ("m1slots", [P, NT * NI], F32), ("m2slots", [P, NT * NI], F32),
        ("m3slots", [P, NT * NI], F32), ("m4slots", [P, NT * NI], F32),
        ("mxst", [P, 2], F32), ("mxrow", [1, 2 * P], F32), ("mxfold", [1, 2], F32),
        ("r0c", [P, 1], F32), ("s0c", [P, 1], F32), ("s4c", [P, 1], F32),
        ("r1c", [P, 1], F32), ("r2c", [P, 1], F32), ("s2c", [P, 1], F32),
        ("brc2c", [P, 1], F32), ("c32c", [P, 1], F32), ("c4c", [P, 1], F32),
        ("gbs4", [P, 8], F32), ("mrccol", [P, 1], F32),
        ("ccols", [P, 8], F32),
        ("row_a", [1, TT], F32), ("row_b", [1, TT], F32),
        ("row_m", [1, TT], F32), ("row_r", [1, TT], F32),
        ("srow", [1, 16], F32),
    ]:
        galloc(nm, shp, dt)

    probe = nc.alloc_sbuf_tensor("sb_gbase_probe", [1, 64], F32)
    GBASE = ((nc.lookup_mloc(probe).addr + 256 + 63) // 64) * 64
    LIMIT = GBASE + nc.sbuf_bytes_remaining - 512

    class Arena:
        def __init__(self, base):
            self.off = base

        def place(self, name, shape, dt):
            esz = 2 if dt == BF16 else 4
            n = esz
            for s_ in shape[1:]:
                n *= s_
            off = ((self.off + 63) // 64) * 64
            sb[name] = nc.alloc_sbuf_tensor_at("sb_" + name, shape, dt, offset=off)
            self.off = off + n
            assert self.off <= LIMIT, f"SBUF overflow at {name}: {self.off}"
            return sb[name]

    a0 = Arena(GBASE)
    a0.place("wmod0", [P, D], F32)
    a0.place("wmod1", [P, D], F32)
    a0.place("crow_sb", [1, D], F32)
    a0.place("modrow", [1, 6 * D], F32)
    a0.place("clamp_a", [1, D], F32)
    a0.place("clamp_b", [1, D], F32)

    aa = Arena(GBASE)
    aa.place("dqn0", [P, DFF // 2], F32)
    aa.place("dqn1", [P, DFF // 2], F32)
    aa.place("dsc", [P, DFF // 64], F32)
    aa.place("drc", [P, DFF // 64], F32)
    aa.place("dt1", [P, DFF // 2], F32)
    aa.place("dqb", [P, DFF // 2], BF16)
    aa.place("dst", [P, NI, P], BF16)
    aa.place("wk_sb", [P, KC, D], BF16)
    aa.place("wv_sb", [P, KC, D], BF16)
    aa.place("xs0", [P, KC, TT], F32)
    aa.place("xs1", [P, KC, TT], F32)
    aa.place("xbf", [P, KC, TT], BF16)
    aa.place("sq0", [P, TT], BF16)
    aa.place("sq1", [P, TT], BF16)
    aa.place("tmp1", [P, TT], BF16)
    aa.place("tmp2", [P, TT], BF16)
    aa.place("rsb", [P, TT], BF16)
    aa.place("mrsb", [P, TT], BF16)
    aa.place("h_sb", [P, KC, TT], BF16)
    aa.place("kq0", [P, TT], BF16)
    aa.place("kq1", [P, TT], BF16)
    aa.place("kfm_sb", [P, 2 * H, TT], BF16)
    aa.place("ktm_sb", [P, NCH, H * DF], BF16)
    aa.place("v_sb", [P, NCH, D], BF16)
    aa.place("Ssb", [P, H * DF], F32)

    ab = Arena(GBASE)
    ab.place("Sfin", [P, H * DF], F32)
    ab.place("wq_sb", [P, KC, D], BF16)
    ab.place("wo_sb", [P, KC, D], BF16)
    ab.place("cxs0", [P, KC, TT], F32)
    ab.place("h_r", [P, KC, TT], BF16)
    ab.place("kfm_r", [P, 2 * H, TT], BF16)
    ab.place("ktm_r", [P, NCH, H * DF], BF16)
    ab.place("v_r", [P, NCH, D], BF16)
    ab.place("qfm_sb", [P, 2 * H, TT], BF16)
    ab.place("qq0", [P, TT], BF16)
    ab.place("qq1", [P, TT], BF16)
    OSB_OFF = ((ab.off + 63) // 64) * 64
    ab.place("o_sb", [P, KC, TT], BF16)
    sb["gS0"] = nc.alloc_sbuf_tensor_at("sb_gS0", [P, H * DF], F32, offset=OSB_OFF)
    ab.place("Ssnap", [P, 2, DF], BF16)
    ab.place("at_sb", [P, 2, P], BF16)
    ab.place("o2w", [P, 2, 8 * P], BF16)
    ab.place("orow", [P, 32, P], BF16)
    ab.place("rms32", [P, 32], F32)
    ab.place("rmsc", [P, 1], F32)
    ab.place("x2_sb", [P, KC, TT], F32)
    ab.place("ep_sb", [P, TT], F32)
    ab.place("h2_sb", [P, KC, TT], BF16)
    ab.place("cxbf", [P, KC, TT], BF16)
    ab.place("csq0", [P, TT], BF16)
    ab.place("csq1", [P, TT], BF16)
    ab.place("ctmp1", [P, TT], BF16)
    ab.place("ctmp2", [P, TT], BF16)
    ab.place("crsb", [P, TT], BF16)
    ab.place("cmrsb", [P, TT], BF16)

    ac = Arena(GBASE)
    ac.place("wup_sb", [P, KC, DFF], BF16)
    ac.place("wg_sb", [P, KC, DFF], BF16)
    ac.place("h2r0", [P, KC, TT], BF16)
    ac.place("h2r1", [P, KC, TT], BF16)
    ac.place("xqt0", [P, TT], F32)
    ac.place("xqt1", [P, TT], F32)
    ac.place("xq_sb", [P, KC, TT], BF16)
    ac.place("Ust", [P, NI, TT], BF16)
    ac.place("Gst", [P, NI, TT], BF16)

    ad = Arena(GBASE)
    ad.place("bufA0", [P, NI, TT], BF16)
    ad.place("bufA1", [P, NI, TT], BF16)
    ad.place("t1f0", [P, TT], F32)
    ad.place("t1f1", [P, TT], F32)
    ad.place("t1g0", [P, TT], F32)
    ad.place("t1g1", [P, TT], F32)
    Z1 = ad.off
    ad.place("bufB0", [P, NI, TT], BF16)
    ad.place("bufB1", [P, NI, TT], BF16)
    ad.place("sgf0", [P, TT], F32)
    ad.place("sgf1", [P, TT], F32)
    ad.place("sgf2", [P, TT], F32)
    ad.place("sgf3", [P, TT], F32)
    Z2 = ad.off
    ae1 = Arena(Z2)
    ae1.place("srbf", [P, TT], BF16)
    ae1.place("tgbf", [P, 4, TT], BF16)
    ae1.place("g2st", [P, NI, TT], BF16)
    ae2 = Arena(Z2)
    ae2.place("prst", [P, NI, TT], BF16)
    ae2.place("g2bf", [P, TT], BF16)
    ae2.place("upbf", [P, TT], BF16)
    awd = Arena(max(ae1.off, ae2.off))
    awd.place("wd_sb", [P, NI, D], BF16)
    af = Arena(Z1)
    af.place("obf", [P, NI, TT], BF16)
    af.place("x2r0", [P, KC, TT], F32)
    af.place("x2r1", [P, KC, TT], F32)
    af.place("outs0", [P, KC, TT], F32)
    af.place("ep20", [P, TT], F32)
    af.place("ep21", [P, TT], F32)

    ps = {}
    for i in range(4):
        ps[f"S{i}"] = nc.alloc_psum_tensor(f"psS{i}", [P, 512], F32)
    ps["big0"] = nc.alloc_psum_tensor("big0", [P, TT], F32)
    ps["big1"] = nc.alloc_psum_tensor("big1", [P, TT], F32)
    ps["pxa"] = nc.alloc_psum_tensor("pxa", [P, 1024], BF16)
    ps["pxb"] = nc.alloc_psum_tensor("pxb", [P, 1024], BF16)
    assert nc.psum_banks_remaining == 0

    def S_ap(h_, dc):
        sl = h_ * 2 + dc
        return ps[f"S{sl // 4}"][:, (sl % 4) * P:(sl % 4 + 1) * P]

    ctx = ExitStack()
    sems = {}
    for s in ["T", "V", "S", "YN", "GN", "LC", "MA", "MB", "DA", "DB", "LW",
              "LF", "SPA", "SPB", "SG", "SD", "X0", "X1", "H0", "H1",
              "K0", "K1", "U0", "U1", "SC1", "SC2", "SC3", "SC2B", "SC3B", "CC", "ST"]:
        sems[s] = ctx.enter_context(nc.semaphore(s))
    p = Prog()
    p.sems = sems

    xT_v = xT.rearrange("(kc q) n -> q kc n", q=P)
    hbh_v = hb_h.rearrange("(kc q) n -> q kc n", q=P)
    kfm_v = hb_kfm.rearrange("(r q) n -> q r n", q=P)
    ktm_v = hb_ktm.rearrange("(g q) c -> q g c", q=P)
    v_v = hb_v.rearrange("(g q) c -> q g c", q=P)
    x2h_v = x2h.rearrange("(kc q) n -> q kc n", q=P)
    h2h_v = h2h.rearrange("(kc q) n -> q kc n", q=P)
    out_v = out.rearrange("(kc q) n -> q kc n", q=P)

    def program():
        T = {}

        def dma(eng, dst, src, sem, tag=None, deps=()):
            return p.op(eng, lambda e: e.dma_start(out=dst, in_=src),
                        deps=deps, inc=16, tag=tag, sem=sem)

        dma("sync", sb["ident"][:, :], idt.ap(), "LC")
        dma("sync", sb["triu"][:, :], tri.ap(), "LC")
        dma("sync", sb["onescol"][:, :], onc.ap(), "LC")
        dma("sync", sb["onesrow"][:, :], onr.ap(), "LC")
        dma("sync", sb["fmqb"][:, :], fmqb, "LC")
        dma("sync", sb["fmkb"][:, :], fmkb, "LC")
        dma("sync", sb["rmsw"][:, :], rmsw, "LC")
        dma("sync", sb["smrow"][:, :], smask, "LC")
        dma("sync", sb["crow_sb"][:, :], crow, "LC", tag="ld_consts")
        p.op("gpsimd", lambda e: e.dma_start(out=sb["fmqT"][:, :], in_=fmqT),
             inc=16, sem="LW")
        p.op("gpsimd", lambda e: e.dma_start(out=sb["fmkT"][:, :], in_=fmkT),
             inc=16, sem="LW", tag="ld_fm")

        p.op("vector", lambda e: e.memset(sb["dmy"][:, :], 0.0))
        p.op("vector", lambda e: e.memset(sb["epscol"][:, :], 1e-5))
        p.op("vector", lambda e: e.memset(sb["rccol"][:, :], RC), tag="rc_set")
        p.op("vector", lambda e: e.memset(sb["mrccol"][:, :], -RC), tag="mrc_set")
        p.op("vector", lambda e: e.memset(sb["c2ecol"][:, :], 128.0 * 1e-5 * (2.0 ** -48)),
             tag="c2e_set")
        p.op("vector", lambda e: e.tensor_scalar(sb["bqp"][:, :], sb["fmqb"][:, :], -LN16, None, op0=ALU.add),
             deps=["ld_consts"])
        p.op("vector", lambda e: e.tensor_scalar(sb["bqn"][:, :], sb["fmqb"][:, :], -1.0, -LN16, op0=ALU.mult, op1=ALU.add))
        p.op("vector", lambda e: e.tensor_scalar(sb["bkn"][:, :], sb["fmkb"][:, :], -1.0, None, op0=ALU.mult),
             tag="bias_cols")

        p.op("scalar", lambda e: e.activation(sb["crow_sb"][:, :], sb["crow_sb"][:, :], AF.Silu),
             deps=["ld_consts"], tag="silu_row")
        def mk_ccol(e):
            r = None
            for kc in range(KC):
                r = e.matmul(ps["big0"][:, kc:kc + 1],
                             sb["crow_sb"][0:1, kc * P:(kc + 1) * P],
                             sb["onesrow"][0:1, 0:1],
                             start=(kc == 0), stop=(kc == KC - 1))
            return r
        p.op("tensor", mk_ccol, deps=["silu_row", "ld_consts"], tag="ccol_mm")
        p.op("scalar", lambda e: e.activation(sb["ccols"][:, :], ps["big0"][:, 0:8], AF.Identity),
             deps=["ccol_mm"], tag="ccol_ev")

        wmod_sb = [sb["wmod0"], sb["wmod1"]]
        for g in range(6):
            for kc in range(KC):
                slot = (g * KC + kc) % 2
                prev = f"modmm_g{g}_k{kc}_m2" if (g * KC + kc) >= 2 else None
                gp, kp = divmod(g * KC + kc - 2, KC)
                dd = [f"modmm_g{gp}_k{kp}"] if (g * KC + kc) >= 2 else []
                dma("sync", wmod_sb[slot][:, :],
                    wmod[kc * P:(kc + 1) * P, g * D:(g + 1) * D],
                    f"M{'AB'[slot]}", tag=f"modld_g{g}_k{kc}", deps=dd)
            for kc in range(KC):
                slot = (g * KC + kc) % 2
                def mk_mod(g=g, kc=kc, slot=slot):
                    def f(e):
                        e.matmul(ps["big0"][0:1, :],
                                 sb["ccols"][:, kc:kc + 1],
                                 wmod_sb[slot][:, 0:TT],
                                 start=(kc == 0), stop=(kc == KC - 1))
                        return e.matmul(ps["big1"][0:1, :],
                                        sb["ccols"][:, kc:kc + 1],
                                        wmod_sb[slot][:, TT:D],
                                        start=(kc == 0), stop=(kc == KC - 1))
                    return f
                dd = [f"modld_g{g}_k{kc}", "ccol_ev"]
                if g >= 1:
                    dd.append(f"modev_g{g-1}")
                p.op("tensor", mk_mod(), deps=dd, tag=f"modmm_g{g}_k{kc}")
            def mk_ev(g=g):
                def f(e):
                    e.activation(sb["modrow"][:, g * D:g * D + TT], ps["big0"][0:1, :], AF.Identity)
                    return e.activation(sb["modrow"][:, g * D + TT:(g + 1) * D], ps["big1"][0:1, :], AF.Identity)
                return f
            p.op("scalar", mk_ev(), deps=[f"modmm_g{g}_k{KC-1}"], tag=f"modev_g{g}")

        for i, s_ in enumerate([2, 5]):
            sl_ = sb["modrow"][:, s_ * D:(s_ + 1) * D]
            p.op("scalar", lambda e, sl_=sl_: e.activation(sb["clamp_a"][:, :], sl_, AF.Abs),
                 deps=[f"modev_g{s_}", (f"cl_done{i-1}" if i else None)], tag=f"cl_abs{i}")
            p.op("scalar", lambda e, sl_=sl_: e.activation(sb["clamp_b"][:, :], sl_, AF.Sign),
                 deps=[(f"cl_done{i-1}" if i else None)], tag=f"cl_sgn{i}")
            p.op("vector", lambda e: e.tensor_scalar_max(sb["clamp_a"][:, :], sb["clamp_a"][:, :], 0.25),
                 deps=[f"cl_abs{i}"], tag=f"cl_max{i}")
            p.op("vector", lambda e, sl_=sl_: e.tensor_mul(sl_, sb["clamp_a"][:, :], sb["clamp_b"][:, :]),
                 deps=[f"cl_sgn{i}", f"cl_max{i}"], tag=f"cl_done{i}")

        def mk_scat(e):
            r = None
            for s_ in range(6):
                for kc in range(KC):
                    r = e.matmul(ps["big0"][:, s_ * 8 + kc:s_ * 8 + kc + 1],
                                 sb["modrow"][0:1, s_ * D + kc * P:s_ * D + (kc + 1) * P],
                                 sb["onesrow"][0:1, 0:1],
                                 start=(s_ == 0 and kc == 0), stop=(s_ == 5 and kc == KC - 1))
            return r
        p.op("tensor", mk_scat,
             deps=["cl_done0", "cl_done1", "modev_g0", "modev_g1",
                   "modev_g3", "modev_g4", "modev_g5", "ld_consts"],
             tag="mod_scat")
        p.op("scalar", lambda e: e.activation(sb["mcols"][:, :, :], ps["big0"][:, 0:48], AF.Identity),
             deps=["mod_scat"], tag="mcols_ev")
        p.op("vector", lambda e: e.tensor_scalar(sb["ca1"][:, :], sb["mcols"][:, 1, :], 1.0, None, op0=ALU.add),
             deps=["mcols_ev"])
        p.op("vector", lambda e: e.tensor_scalar(sb["cb1"][:, :], sb["mcols"][:, 4, :], 1.0, None, op0=ALU.add),
             tag="modcols_done")
        p.op("tensor", lambda e: e.matmul(ps["big1"][:, 0:8], sb["onesrow"][:, :],
                                          sb["smrow"][:, :], start=True, stop=True),
             deps=["ld_consts", f"modev_g5"], tag="sm_mm")
        p.op("scalar", lambda e: e.activation(sb["sm128"][:, :], ps["big1"][:, 0:8], AF.Identity),
             deps=["sm_mm"], tag="sm128_done")

        p.barrier("p0", sb)

        def finish_early():
            for t_ in range(NT):
                dma("sync", out_v[:, :, t_ * TT:(t_ + 1) * TT],
                    sb["xs0"][:, :, :], "ST", tag=f"fout{t_}")
            p.op("sync", lambda e: e.wait_ge(p.sems["ST"], 16 * NT), inc=0)
            p.op("gpsimd", lambda e: e.wait_ge(p.sems["ST"], 16 * NT), inc=0)

        if PH < 1:
            finish_early()
            return
        PSL = {"big0": "mcols_ev", "big1": "sm128_done"}
        dqi_t1 = {}
        dqi_st = {}
        dqi_q = {}
        dqi_tp = {}
        dqn = [sb["dqn0"], sb["dqn1"]]
        dqi = 0

        DQJ = []

        def emit_dequant(wi, nat, natq, R, C):
            natq_v = natq.rearrange("(cb q) r -> q cb r", q=P)
            NSP = 2 if C > 1536 else 1
            Cs = C // NSP
            for rt in range(R // P):
                for sp in range(NSP):
                    DQJ.append((nat, natq_v, rt, sp * Cs, Cs, Cs // P, Cs // 32))

        def run_dq_jobs(n):
            nonlocal dqi
            for _ in range(n):
                if not DQJ:
                    return
                nat, natq_v, rt, coff, C, CB, CG = DQJ.pop(0)
                i = dqi
                dqi += 1
                slot = i % 2
                dma("sync", dqn[slot][:, 0:C],
                    nat[rt * P:(rt + 1) * P, coff:coff + C],
                    f"D{'AB'[i % 2]}", tag=f"dq_ld_{i}", deps=[dqi_t1.get(i - 2)])
                w3 = dqn[slot][:, 0:C].rearrange("q (g e) -> q g e", e=32)
                p.op("vector", lambda e, w3=w3, CG=CG: e.tensor_reduce(
                        sb["dsc"][:, 0:CG].unsqueeze(2), w3, axis=AX.X, op=ALU.max,
                        apply_absolute_value=True),
                     deps=[f"dq_ld_{i}", dqi_q.get(i - 1), dqi_tp.get(i - 1)],
                     tag=f"dq_sc_{i}")
                p.op("vector", lambda e, CG=CG: e.tensor_scalar(
                        sb["dsc"][:, 0:CG], sb["dsc"][:, 0:CG], 1.0 / 7.0, 1e-12,
                        op0=ALU.mult, op1=ALU.max),
                     deps=[f"dq_sc_{i}"], tag=f"dq_s7_{i}")
                p.op("vector", lambda e, CG=CG: e.reciprocal(sb["drc"][:, 0:CG], sb["dsc"][:, 0:CG]),
                     deps=[f"dq_s7_{i}"], tag=f"dq_rc_{i}")
                rcb = sb["drc"][:, 0:CG].unsqueeze(2).broadcast_to([P, CG, 32])
                scb = sb["dsc"][:, 0:CG].unsqueeze(2).broadcast_to([P, CG, 32])
                p.op("vector", lambda e, w3=w3, rcb=rcb, C=C: e.tensor_tensor(
                        sb["dt1"][:, 0:C].rearrange("q (g e) -> q g e", e=32), w3, rcb, op=ALU.mult),
                     deps=[f"dq_rc_{i}"], tag=f"dq_t1_{i}")
                dqi_t1[i] = f"dq_t1_{i}"
                p.op("scalar", lambda e, C=C: e.activation(
                        sb["dt1"][:, 0:C], sb["dt1"][:, 0:C], AF.Identity, bias=sb["rccol"][:, :]),
                     deps=[f"dq_t1_{i}", "rc_set"], tag=f"dq_t2_{i}")
                p.op("vector", lambda e, scb=scb, C=C: e.scalar_tensor_tensor(
                        sb["dqb"][:, 0:C].rearrange("q (g e) -> q g e", e=32),
                        sb["dt1"][:, 0:C].rearrange("q (g e) -> q g e", e=32),
                        -RC, scb, op0=ALU.add, op1=ALU.mult),
                     deps=[f"dq_t2_{i}"], tag=f"dq_q_{i}")
                dqi_q[i] = f"dq_q_{i}"
                NG = (CB + 3) // 4
                for g in range(NG):
                    gb = min(4, CB - g * 4)
                    def mk_dqtp(e, g=g, gb=gb):
                        r = None
                        for j in range(gb):
                            cb = g * 4 + j
                            r = e.transpose(ps["pxb"][:, j * P:(j + 1) * P],
                                            sb["dqb"][:, cb * P:(cb + 1) * P],
                                            sb["ident"][:, :])
                        return r
                    p.op("tensor", mk_dqtp,
                         deps=[f"dq_q_{i}", PSL.get("pxb"), "ld_consts"],
                         tag=f"dq_tp_{i}_{g}")
                    p.op("scalar", lambda e, g=g, gb=gb: e.activation(
                            sb["dst"][:, g * 4:g * 4 + gb, :],
                            ps["pxb"][:, 0:gb * P].rearrange("q (cb c) -> q cb c", c=P),
                            AF.Identity),
                         deps=[f"dq_tp_{i}_{g}", dqi_st.get(i - 1)],
                         tag=f"dq_ev_{i}_{g}")
                    PSL["pxb"] = f"dq_ev_{i}_{g}"
                dqi_tp[i] = f"dq_tp_{i}_{NG-1}"
                dma("gpsimd", natq_v[:, coff // P:coff // P + CB, rt * P:(rt + 1) * P],
                    sb["dst"][:, 0:CB, :],
                    "SD", tag=f"dq_st_{i}", deps=[f"dq_ev_{i}_{NG-1}"])
                dqi_st[i] = f"dq_st_{i}"

        emit_dequant(0, wup, wupT_q, DFF, D)
        emit_dequant(1, wgate, wgT_q, DFF, D)
        emit_dequant(2, wdown, wdT_q, D, DFF)

        if PH < 2:
            finish_early()
            return
        def emit_ln(pre, src_tag, src_f32, xbf, sqp, tmp1, tmp2, rsb, mrsb,
                    hout, cacol, sacol, m0base, cast_deps, mod_deps):
            for kc in range(KC):
                p.op("scalar", lambda e, kc=kc: e.activation(
                        xbf[:, kc, :], src_f32[:, kc, :], AF.Identity),
                     deps=[src_tag] + (cast_deps if kc == 0 else []),
                     tag=f"{pre}cast{kc}")
                p.op("vector", lambda e, kc=kc: e.tensor_mul(
                        sqp[kc % 2][:, :], xbf[:, kc, :], xbf[:, kc, :]),
                     deps=[f"{pre}cast{kc}", T.get(f"{pre}stats{kc-2}")],
                     tag=f"{pre}sq{kc}")
                def mk_stats(e, kc=kc):
                    e.matmul(ps["big0"][0:1, :], sb["onescol"][:, :], xbf[:, kc, :],
                             start=(kc == 0), stop=(kc == KC - 1))
                    return e.matmul(ps["big1"][0:1, :], sb["onescol"][:, :], sqp[kc % 2][:, :],
                                    start=(kc == 0), stop=(kc == KC - 1))
                dd = [f"{pre}sq{kc}", "ld_consts"]
                if kc == 0:
                    dd += [PSL.get("big0"), PSL.get("big1")]
                p.op("tensor", mk_stats, deps=dd, tag=f"{pre}stats{kc}")
                T[f"{pre}stats{kc}"] = f"{pre}stats{kc}"
            p.settag(f"{pre}stats", f"{pre}stats{KC-1}")
            def mk_fin1(e):
                e.activation(sb["row_a"][:, :], ps["big0"][0:1, :], AF.Identity, scale=1.0 / D)
                return e.activation(sb["row_b"][:, :], ps["big1"][0:1, :], AF.Identity, scale=1.0 / D)
            p.op("scalar", mk_fin1, deps=[f"{pre}stats"], tag=f"{pre}fin1")
            p.op("vector", lambda e: e.tensor_mul(sb["row_m"][:, :], sb["row_a"][:, :], sb["row_a"][:, :]),
                 deps=[f"{pre}fin1"], tag=f"{pre}msq")
            p.op("vector", lambda e: e.tensor_sub(sb["row_b"][:, :], sb["row_b"][:, :], sb["row_m"][:, :]),
                 deps=[f"{pre}msq"], tag=f"{pre}var")
            p.op("scalar", lambda e: e.activation(sb["row_r"][:, :], sb["row_b"][:, :], AF.Ln, bias=sb["epscol"][0:1, :]),
                 deps=[f"{pre}var"], tag=f"{pre}rstd0")
            p.op("scalar", lambda e: e.activation(sb["row_r"][:, :], sb["row_r"][:, :], AF.Exp, scale=-0.5),
                 deps=[f"{pre}rstd0"], tag=f"{pre}rstd")
            p.op("vector", lambda e: e.tensor_mul(sb["row_m"][:, :], sb["row_a"][:, :], sb["row_r"][:, :]),
                 deps=[f"{pre}rstd"], tag=f"{pre}mr")
            def mk_bc(e):
                e.matmul(ps["big0"][:, :], sb["onesrow"][:, :], sb["row_r"][:, :], start=True, stop=True)
                return e.matmul(ps["big1"][:, :], sb["onesrow"][:, :], sb["row_m"][:, :], start=True, stop=True)
            p.op("tensor", mk_bc, deps=[f"{pre}mr", f"{pre}fin1"], tag=f"{pre}bc")
            def mk_bce(e):
                e.activation(rsb[:, :], ps["big0"][:, :], AF.Identity)
                return e.activation(mrsb[:, :], ps["big1"][:, :], AF.Identity)
            p.op("scalar", mk_bce, deps=[f"{pre}bc"], tag=f"{pre}bce")
            PSL["big0"] = f"{pre}bce"
            PSL["big1"] = f"{pre}bce"
            p.op("vector", lambda e: e.tensor_tensor(
                    xbf[:, :, :], xbf[:, :, :],
                    rsb[:, :].unsqueeze(1).broadcast_to([P, KC, TT]), op=ALU.mult),
                 deps=[f"{pre}bce"] + (mod_deps or []), tag=f"{pre}t1")
            p.op("vector", lambda e: e.tensor_tensor(
                    xbf[:, :, :], xbf[:, :, :],
                    mrsb[:, :].unsqueeze(1).broadcast_to([P, KC, TT]), op=ALU.subtract),
                 deps=[f"{pre}t1"], tag=f"{pre}t2")
            for kc in range(KC):
                p.op("scalar", lambda e, kc=kc: e.activation(
                        hout[:, kc, :], xbf[:, kc, :], AF.Identity,
                        scale=cacol[:, kc:kc + 1], bias=sacol[:, kc:kc + 1]),
                     deps=[f"{pre}t2", "modcols_done"], tag=f"{pre}smod{kc}")
                if m0base is not None:
                    p.op("vector", lambda e, kc=kc: e.tensor_reduce(
                            sb["m0slots"][:, m0base + kc:m0base + kc + 1], hout[:, kc, :],
                            axis=AX.X, op=ALU.max, apply_absolute_value=True),
                         deps=[f"{pre}smod{kc}"], tag=f"{pre}m0_{kc}")
            return f"{pre}smod{KC-1}"

        p.op("gpsimd", lambda e: e.dma_start(
                out=sb["wk_sb"][:, :, :], in_=wkT.rearrange("(kc q) f -> q kc f", q=P)),
             inc=16, sem="LW")
        p.op("gpsimd", lambda e: e.dma_start(
                out=sb["wv_sb"][:, :, :], in_=wvT.rearrange("(kc q) f -> q kc f", q=P)),
             inc=16, sem="LW", tag="ld_wkv")
        xs = [sb["xs0"], sb["xs1"]]
        KQLAST = [None, None]
        for t in range(NT):
            sl = t % 2
            dma("sync", xs[sl][:, :, :], xT_v[:, :, t * TT:(t + 1) * TT],
                f"X{sl}", tag=f"axld{t}", deps=[T.get(f"ax_free{t-2}")])
            lnt = emit_ln(
                f"aln{t}", f"axld{t}", xs[sl], sb["xbf"], [sb["sq0"], sb["sq1"]],
                sb["tmp1"], sb["tmp2"],
                sb["rsb"], sb["mrsb"], sb["h_sb"], sb["ca1"], sb["mcols"][:, 0, :],
                None,
                cast_deps=[T.get(f"aln_xbf_free{t-1}")],
                mod_deps=[T.get(f"vspill{t-1}"), T.get(f"h_used{t-1}")])
            T[f"ax_free{t}"] = f"aln{t}cast{KC-1}"
            T[f"aln_xbf_free{t}"] = f"aln{t}smod{KC-1}"
            dma("gpsimd", hbh_v[:, :, t * TT:(t + 1) * TT], sb["h_sb"][:, :, :],
                "SPA", tag=f"hspill{t}", deps=[lnt])
            T[f"hspill{t}"] = f"hspill{t}"
            for tb in range(NCH):
                for vh in range(2):
                    bs = f"big{vh}"
                    def mk_v(e, tb=tb, vh=vh, bs=bs):
                        r = None
                        for kc in range(KC):
                            r = e.matmul(ps[bs][:, :],
                                         sb["h_sb"][:, kc, tb * CHN:(tb + 1) * CHN],
                                         sb["wv_sb"][:, kc, vh * TT:(vh + 1) * TT],
                                         start=(kc == 0), stop=(kc == KC - 1))
                        return r
                    p.op("tensor", mk_v, deps=[lnt, "ld_wkv", PSL.get(bs)],
                         tag=f"vmm{t}_{tb}_{vh}")
                    p.op("scalar", lambda e, tb=tb, vh=vh, bs=bs: e.activation(
                            sb["v_sb"][:, tb, vh * TT:(vh + 1) * TT], ps[bs][:, :], AF.Identity),
                         deps=[f"vmm{t}_{tb}_{vh}", T.get(f"hspill{t}"), T.get(f"v_used{t-1}")],
                         tag=f"vev{t}_{tb}_{vh}")
                    PSL[bs] = f"vev{t}_{tb}_{vh}"
            def emit_kmm(h_):
                ks = h_ % 2
                def mk_k(e, h_=h_, ks=ks):
                    r = None
                    for kc in range(KC):
                        r = e.matmul(ps[f"big{ks}"][:, :],
                                     sb["wk_sb"][:, kc, h_ * P:(h_ + 1) * P],
                                     sb["h_sb"][:, kc, :],
                                     start=(kc == 0), stop=(kc == KC - 1))
                    return r
                p.op("tensor", mk_k, deps=[lnt, "ld_wkv", PSL.get(f"big{ks}")],
                     tag=f"kmm{t}_{h_}")

            def emit_kev_kymm_kexp(h_):
                ks = h_ % 2
                p.op("scalar", lambda e, ks=ks, bs=f"big{ks}": e.activation(
                        sb[f"kq{ks}"][:, :], ps[bs][:, :], AF.Identity),
                     deps=[f"kmm{t}_{h_}", KQLAST[ks]],
                     tag=f"kev{t}_{h_}")
                p.op("tensor", lambda e, ks=ks: e.matmul(
                        ps[f"big{ks}"][:, :], sb["fmkT"][:, :], sb[f"kq{ks}"][:, :],
                        start=True, stop=True),
                     deps=[f"kev{t}_{h_}", "ld_wkv"], tag=f"kymm{t}_{h_}")
                KQLAST[ks] = f"kymm{t}_{h_}"
                def mk_kexp(e, h_=h_, ks=ks):
                    e.activation(sb["kfm_sb"][:, 2 * h_, :], ps[f"big{ks}"][:, :], AF.Exp,
                                 bias=sb["fmkb"][:, :])
                    return e.activation(sb["kfm_sb"][:, 2 * h_ + 1, :], ps[f"big{ks}"][:, :], AF.Exp,
                                        bias=sb["bkn"][:, :], scale=-1.0)
                p.op("scalar", mk_kexp,
                     deps=[f"kymm{t}_{h_}", "bias_cols", T.get(f"hspill{t}"),
                           T.get(f"kfm_used{t-1}")],
                     tag=f"kexp{t}_{h_}")
                PSL[f"big{ks}"] = f"kexp{t}_{h_}"

            def emit_ktp(h_, dc):
                def mk_ktp(e, h_=h_, dc=dc):
                    r = None
                    for tb in range(NCH):
                        r = e.transpose(
                            ps["pxa"][:, tb * P:(tb + 1) * P],
                            sb["kfm_sb"][:, 2 * h_ + dc, tb * CHN:(tb + 1) * CHN],
                            sb["ident"][:, :])
                    return r
                p.op("tensor", mk_ktp,
                     deps=[f"kexp{t}_{h_}", T.get("pxa_ev"), "ld_consts"],
                     tag=f"ktp{t}_{h_}_{dc}")
                p.op("vector", lambda e, h_=h_, dc=dc: e.tensor_copy(
                        sb["ktm_sb"][:, 0:NCH, h_ * DF + dc * P:h_ * DF + (dc + 1) * P],
                        ps["pxa"][:, 0:TT].rearrange("q (tb c) -> q tb c", c=P)),
                     deps=[f"ktp{t}_{h_}_{dc}", T.get(f"hspill{t}"),
                           T.get(f"ktm_used{t-1}")],
                     tag=f"ktev{t}_{h_}_{dc}")
                T["pxa_ev"] = f"ktev{t}_{h_}_{dc}"

            for hh in range(H + 2):
                if hh < H:
                    emit_kmm(hh)
                if hh >= 2:
                    emit_ktp(hh - 2, 0)
                if 1 <= hh <= H:
                    emit_kev_kymm_kexp(hh - 1)
                if hh >= 2:
                    emit_ktp(hh - 2, 1)
                run_dq_jobs(1)
            for h_ in range(H):
                def mk_sacc(e, t=t, h_=h_):
                    r = None
                    for ch in range(NCH):
                        for dc in range(2):
                            st_ = (t == 0 and ch == 0 and dc == 0 and h_ % 2 == 0)
                            r = e.matmul(S_ap(h_, dc),
                                         sb["ktm_sb"][:, ch, h_ * DF + dc * P:h_ * DF + (dc + 1) * P],
                                         sb["v_sb"][:, ch, h_ * P:(h_ + 1) * P],
                                         start=st_, stop=False, skip_group_check=True)
                    return r
                p.op("tensor", mk_sacc,
                     deps=[f"ktev{t}_{h_}_1", f"vev{t}_{NCH-1}_1"],
                     tag=f"sacc{t}_{h_}")
            T[f"h_used{t}"] = f"kmm{t}_{H-1}"
            T[f"v_used{t}"] = f"sacc{t}_{H-1}"
            T[f"ktm_used{t}"] = f"sacc{t}_{H-1}"
            T[f"kfm_used{t}"] = f"ktp{t}_{H-1}_1"
            dma("gpsimd", kfm_v[:, :, t * TT:(t + 1) * TT], sb["kfm_sb"][:, :, :],
                "SPA", tag=f"kfmspill{t}",
                deps=[f"kexp{t}_{H-1}", f"ktp{t}_{H-1}_1"])
            T[f"kfmspill{t}"] = f"kfmspill{t}"
            dma("gpsimd", ktm_v[:, t * NCH:(t + 1) * NCH, :], sb["ktm_sb"][:, :, :],
                "SPA", tag=f"ktmspill{t}", deps=[f"sacc{t}_{H-1}"])
            T[f"ktmspill{t}"] = f"ktmspill{t}"
            dma("gpsimd", v_v[:, t * NCH:(t + 1) * NCH, :], sb["v_sb"][:, :, :],
                "SPA", tag=f"vspill{t}", deps=[f"sacc{t}_{H-1}"])
            T[f"vspill{t}"] = f"vspill{t}"

        run_dq_jobs(10 ** 9)
        T["dq_done"] = f"dq_st_{dqi - 1}"

        def mk_sev(e):
            r = None
            for h_ in range(H):
                for dc in range(2):
                    r = e.activation(sb["Ssb"][:, (h_ * 2 + dc) * P:(h_ * 2 + dc + 1) * P],
                                     S_ap(h_, dc), AF.Identity)
            return r
        p.op("scalar", mk_sev, deps=[f"sacc{NT-1}_{H-1}"], tag="s_ev")
        dma("gpsimd", sc_in, sb["Ssb"][:, :], "SPA", tag="s_stage", deps=["s_ev"])
        p.op("gpsimd", lambda e: e.collective_compute(
                "AllGather", ALU.bypass,
                replica_groups=[[0, 1], [2, 3], [4, 5], [6, 7]],
                ins=[sc_in.opt()], outs=[sc_out.opt()]),
             deps=["s_stage"], sem="CC", tag="cc_sg")

        p.barrier("p2a", sb)

        if PH < 3:
            finish_early()
            return
        gS = sb["gS0"]
        SFIN = sb["Sfin"]
        for r in range(2):
            dma("gpsimd", gS[:, :], sc_out[r * P:(r + 1) * P, :],
                "SG", tag=f"gsld{r}", deps=["cc_sg", T.get(f"sin{r-1}")])
            if r == 0:
                p.op("vector", lambda e: e.tensor_scalar(
                        SFIN[:, :], gS[:, :], sb["sm128"][:, 0:1], None, op0=ALU.mult),
                     deps=[f"gsld{r}", "sm128_done"], tag=f"sin{r}")
            else:
                p.op("vector", lambda e, r=r: e.scalar_tensor_tensor(
                        SFIN[:, :], gS[:, :], sb["sm128"][:, r:r + 1], SFIN[:, :],
                        op0=ALU.mult, op1=ALU.add),
                     deps=[f"gsld{r}", f"sin{r-1}"], tag=f"sin{r}")
            T[f"sin{r}"] = f"sin{r}"
        p.settag("sin7", "sin1")

        p.op("gpsimd", lambda e: e.dma_start(
                out=sb["wq_sb"][:, :, :], in_=wqT.rearrange("(kc q) f -> q kc f", q=P)),
             inc=16, sem="LW")
        p.op("gpsimd", lambda e: e.dma_start(
                out=sb["wo_sb"][:, :, :], in_=woT.rearrange("(kc q) f -> q kc f", q=P)),
             inc=16, sem="LW", tag="ld_wqo")
        def mk_worms(e):
            r = None
            for kc in range(KC):
                r = e.tensor_scalar(sb["wo_sb"][:, kc, :], sb["wo_sb"][:, kc, :],
                                    sb["rmsw"][:, :], None, op0=ALU.mult)
            return r
        p.op("vector", mk_worms, deps=["ld_wqo", "ld_consts"], tag="wo_rms")

        cxs = [sb["cxs0"], sb["cxs0"]]
        QQLAST = [None, None]
        psA_at = ps["big0"][:, 0:P]
        psA_o = ps["big0"][:, P:2 * P]
        for t in range(NT):
            sl = t % 2
            dma("sync", cxs[sl][:, :, :], xT_v[:, :, t * TT:(t + 1) * TT],
                f"X{sl}", tag=f"cxld{t}", deps=[T.get(f"cx_free{t-1}")])
            dma("sync", sb["h_r"][:, :, :], hbh_v[:, :, t * TT:(t + 1) * TT],
                f"H{sl}", tag=f"chld{t}",
                deps=["s_stage", T.get(f"hr_used{t-1}")])
            dma("gpsimd", sb["kfm_r"][:, :, :], kfm_v[:, :, t * TT:(t + 1) * TT],
                f"K{sl}", tag=f"ckfld{t}",
                deps=["s_stage", T.get(f"kfmr_used{t-1}")])
            dma("gpsimd", sb["ktm_r"][:, :, :], ktm_v[:, t * NCH:(t + 1) * NCH, :],
                f"K{sl}", tag=f"cktld{t}",
                deps=[T.get(f"ktmr_used{t-1}")])
            dma("gpsimd", sb["v_r"][:, :, :], v_v[:, t * NCH:(t + 1) * NCH, :],
                f"K{sl}", tag=f"cvld{t}",
                deps=[T.get(f"vr_used{t-1}")])
            for h_ in range(H):
                ks = h_ % 2
                def mk_q(e, h_=h_, ks=ks):
                    r = None
                    for kc in range(KC):
                        r = e.matmul(ps[f"big{ks}"][:, :],
                                     sb["wq_sb"][:, kc, h_ * P:(h_ + 1) * P],
                                     sb["h_r"][:, kc, :],
                                     start=(kc == 0), stop=(kc == KC - 1))
                    return r
                p.op("tensor", mk_q, deps=[f"chld{t}", "ld_wqo", PSL.get(f"big{ks}")],
                     tag=f"qmm{t}_{h_}")
                p.op("scalar", lambda e, ks=ks, bs=f"big{ks}": e.activation(
                        sb[f"qq{ks}"][:, :], ps[bs][:, :], AF.Identity),
                     deps=[f"qmm{t}_{h_}", QQLAST[ks]],
                     tag=f"qev{t}_{h_}")
                p.op("tensor", lambda e, ks=ks: e.matmul(
                        ps[f"big{ks}"][:, :], sb["fmqT"][:, :], sb[f"qq{ks}"][:, :],
                        start=True, stop=True),
                     deps=[f"qev{t}_{h_}", "ld_wqo"], tag=f"qymm{t}_{h_}")
                QQLAST[ks] = f"qymm{t}_{h_}"
                def mk_qexp(e, h_=h_, ks=ks):
                    e.activation(sb["qfm_sb"][:, 2 * h_, :], ps[f"big{ks}"][:, :], AF.Exp,
                                 bias=sb["bqp"][:, :])
                    return e.activation(sb["qfm_sb"][:, 2 * h_ + 1, :], ps[f"big{ks}"][:, :], AF.Exp,
                                        bias=sb["bqn"][:, :], scale=-1.0)
                p.op("scalar", mk_qexp,
                     deps=[f"qymm{t}_{h_}", "bias_cols", T.get(f"qfm_used{t-1}")],
                     tag=f"qexp{t}_{h_}")
                PSL[f"big{ks}"] = f"qexp{t}_{h_}"
            T[f"hr_used{t}"] = f"qmm{t}_{H-1}"
            psA = [ps["big0"], ps["big1"]]
            HORD = [0, 2, 4, 6, 1, 3, 5, 7]
            def S2_ap(h_):
                return ps[f"S{h_ // 2}"][:, (2 * h_ % 4) * P:(2 * h_ % 4) * P + DF]
            for k in range(33):
                if k < 32:
                    ch, pos = divmod(k, H)
                    h_ = HORD[pos]
                    sl2 = k % 2
                    pat = psA[sl2][:, 0:P]
                    cds = f"{t}_{k}"
                    if t == 0 and ch == 0:
                        p.op("vector", lambda e, h_=h_, sl2=sl2: e.tensor_copy(
                                sb["Ssnap"][:, sl2, :], SFIN[:, h_ * DF:(h_ + 1) * DF]),
                             deps=["sin7", T.get(f"snapfree{sl2}")], tag=f"snap{cds}")
                    else:
                        p.op("vector", lambda e, h_=h_, sl2=sl2: e.tensor_add(
                                sb["Ssnap"][:, sl2, :], S2_ap(h_),
                                SFIN[:, h_ * DF:(h_ + 1) * DF]),
                             deps=["sin7", T.get(f"supd_bank{h_ // 2}"),
                                   T.get(f"snapfree{sl2}")], tag=f"snap{cds}")
                    def mk_at(e, h_=h_, ch=ch, pat=pat):
                        r = None
                        for dc in range(2):
                            r = e.matmul(pat,
                                         sb["kfm_r"][:, 2 * h_ + dc, ch * CHN:(ch + 1) * CHN],
                                         sb["qfm_sb"][:, 2 * h_ + dc, ch * CHN:(ch + 1) * CHN],
                                         start=(dc == 0), stop=(dc == 1), skip_group_check=True)
                        return r
                    p.op("tensor", mk_at,
                         deps=[f"qexp{t}_{h_}", f"cvld{t}", T.get(f"park_f{sl2}"),
                               T.get(f"atev_f{sl2}"), PSL.get(f"big{sl2}")],
                         tag=f"at{cds}")
                    PSL[f"big{sl2}"] = f"at{cds}"
                    p.op("vector", lambda e, sl2=sl2, pat=pat: e.tensor_tensor(
                            sb["at_sb"][:, sl2, :], pat, sb["triu"][:, :], op=ALU.mult),
                         deps=[f"at{cds}", T.get(f"atsb_f{sl2}")],
                         tag=f"atev{cds}")
                    T[f"atev_f{sl2}"] = f"atev{cds}"
                if k >= 1:
                    km = k - 1
                    ch, pos = divmod(km, H)
                    h_ = HORD[pos]
                    sl2 = km % 2
                    po = psA[sl2][:, P:2 * P]
                    cds = f"{t}_{km}"
                    def mk_o(e, h_=h_, ch=ch, sl2=sl2, po=po):
                        e.matmul(po, sb["at_sb"][:, sl2, :],
                                 sb["v_r"][:, ch, h_ * P:(h_ + 1) * P],
                                 start=False, stop=False, skip_group_check=True)
                        r = None
                        for dc in range(2):
                            r = e.matmul(po,
                                         sb["qfm_sb"][:, 2 * h_ + dc, ch * CHN:(ch + 1) * CHN],
                                         sb["Ssnap"][:, sl2, dc * P:(dc + 1) * P],
                                         start=False, stop=(dc == 1), skip_group_check=True)
                        return r
                    p.op("tensor", mk_o,
                         deps=[f"atev{cds}", f"snap{cds}", f"cvld{t}",
                               T.get(f"park_f{sl2}")],
                         tag=f"omm{cds}")
                    T[f"atsb_f{sl2}"] = f"omm{cds}"
                    T[f"snapfree{sl2}"] = f"omm{cds}"
                    def mk_su(e, h_=h_, ch=ch, t=t):
                        r = None
                        for dc in range(2):
                            st_ = (t == 0 and ch == 0 and dc == 0 and h_ % 2 == 0)
                            r = e.matmul(S_ap(h_, dc),
                                         sb["ktm_r"][:, ch, h_ * DF + dc * P:h_ * DF + (dc + 1) * P],
                                         sb["v_r"][:, ch, h_ * P:(h_ + 1) * P],
                                         start=st_, stop=False, skip_group_check=True)
                        return r
                    p.op("tensor", mk_su,
                         deps=[f"snap{cds}", f"cvld{t}"],
                         tag=f"supd{cds}")
                    T[f"supd_bank{h_ // 2}"] = f"supd{cds}"
                    p.op("vector", lambda e, km=km, po=po: e.tensor_copy(
                            sb["orow"][:, km, :], po),
                         deps=[f"omm{cds}", T.get(f"orow_used{t-1}_{km}")],
                         tag=f"vpark{cds}")
                    T[f"park_f{sl2}"] = f"vpark{cds}"
            T[f"kfmr_used{t}"] = f"at{t}_31"
            T[f"ktmr_used{t}"] = f"supd{t}_31"
            T[f"vr_used{t}"] = f"supd{t}_31"
            T[f"qfm_used{t}"] = f"omm{t}_31"
            for g in range(NCH):
                dh = g % 2
                p.op("scalar", lambda e, g=g, dh=dh: e.activation(
                        sb["o2w"][:, dh, :].rearrange("q (i c) -> q i c", c=P),
                        sb["orow"][:, g * 8:(g + 1) * 8, :], AF.Square, scale=2.0 ** -24),
                     deps=[f"vpark{t}_{g*8+7}", T.get(f"o2w_rd{dh}")],
                     tag=f"osqa{t}_{g}")
                p.op("vector", lambda e, g=g, dh=dh: e.tensor_reduce(
                        sb["rms32"][:, g * 8:(g + 1) * 8].unsqueeze(2),
                        sb["o2w"][:, dh, :].rearrange("q (i c) -> q i c", c=P),
                        axis=AX.X, op=ALU.add),
                     deps=[f"osqa{t}_{g}"], tag=f"osqr{t}_{g}")
                T[f"o2w_rd{dh}"] = f"osqr{t}_{g}"
            p.op("scalar", lambda e: e.activation(sb["rms32"][:, :], sb["rms32"][:, :], AF.Sqrt,
                                                  bias=sb["c2ecol"][:, :]),
                 deps=[f"osqr{t}_{NCH-1}", "c2e_set", T.get(f"rms32_used{t-1}")],
                 tag=f"rchain1_{t}")
            p.op("scalar", lambda e: e.activation(sb["rms32"][:, :], sb["rms32"][:, :], AF.Sqrt),
                 deps=[f"rchain1_{t}"], tag=f"rchain2_{t}")
            p.op("vector", lambda e: e.reciprocal(sb["rms32"][:, :], sb["rms32"][:, :]),
                 deps=[f"rchain2_{t}"], tag=f"rchain3_{t}")
            p.op("scalar", lambda e: e.activation(sb["rms32"][:, :], sb["rms32"][:, :], AF.Square,
                                                  scale=(128.0 ** 0.25) * (2.0 ** -12)),
                 deps=[f"rchain3_{t}"], tag=f"rchain_{t}")
            for g in range(NCH):
                p.op("vector", lambda e, g=g: e.tensor_tensor(
                        sb["orow"][:, g * 8:(g + 1) * 8, :],
                        sb["orow"][:, g * 8:(g + 1) * 8, :],
                        sb["rms32"][:, g * 8:(g + 1) * 8].unsqueeze(2).broadcast_to([P, 8, P]),
                        op=ALU.mult),
                     deps=[f"rchain_{t}"], tag=f"ong{t}_{g}")
            T[f"rms32_used{t}"] = f"ong{t}_{NCH-1}"
            pxs = [ps["pxa"], ps["pxb"]]
            POS = {h2_: i2_ for i2_, h2_ in enumerate(HORD)}
            for ch in range(NCH):
                for half in range(2):
                    px = pxs[(ch * 2 + half) % 2]
                    ks = [ch * 8 + POS[half * 4 + j] for j in range(4)]
                    def mk_otp(e, ks=ks, px=px):
                        r = None
                        for j, k_ in enumerate(ks):
                            r = e.transpose(px[:, j * P:(j + 1) * P],
                                            sb["orow"][:, k_, :], sb["ident"][:, :])
                        return r
                    p.op("tensor", mk_otp,
                         deps=[f"ong{t}_{ch}", T.get(f"px_ev{(ch*2+half)%2}"), "ld_consts"],
                         tag=f"otpg{t}_{ch}_{half}")
                    for k_ in ks:
                        T[f"orow_used{t}_{k_}"] = f"otpg{t}_{ch}_{half}"
                    p.op("vector", lambda e, ch=ch, half=half, px=px: e.tensor_copy(
                            sb["o_sb"][:, half * 4:(half + 1) * 4, ch * CHN:(ch + 1) * CHN],
                            px[:, 0:TT].rearrange("q (j c) -> q j c", c=P)),
                         deps=[f"otpg{t}_{ch}_{half}", T.get(f"o_used{t-1}")],
                         tag=f"oevg{t}_{ch}_{half}")
                    T[f"px_ev{(ch*2+half)%2}"] = f"oevg{t}_{ch}_{half}"
            for fo in range(KC):
                bs = f"big{fo % 2}"
                def mk_wo(e, fo=fo, bs=bs):
                    r = None
                    for kc in range(KC):
                        r = e.matmul(ps[bs][:, :],
                                     sb["wo_sb"][:, kc, fo * P:(fo + 1) * P],
                                     sb["o_sb"][:, kc, :],
                                     start=(kc == 0), stop=(kc == KC - 1))
                    return r
                p.op("tensor", mk_wo,
                     deps=[f"oevg{t}_{NCH-1}_1", "wo_rms", PSL.get(bs),
                           T.get(f"park_f{fo % 2}"), T.get(f"atev_f{fo % 2}")],
                     tag=f"wo{t}_{fo}")
                p.op("scalar", lambda e, fo=fo, bs=bs: e.activation(
                        sb["ep_sb"][:, :], ps[bs][:, :], AF.Identity,
                        scale=sb["mcols"][:, 2, fo:fo + 1]),
                     deps=[f"wo{t}_{fo}", "modcols_done", T.get("ep_used")],
                     tag=f"ep1{t}_{fo}")
                PSL[bs] = f"ep1{t}_{fo}"
                p.op("vector", lambda e, fo=fo, sl=sl: e.tensor_add(
                        sb["x2_sb"][:, fo, :], cxs[sl][:, fo, :], sb["ep_sb"][:, :]),
                     deps=[f"ep1{t}_{fo}", f"cxld{t}", T.get(f"x2_free{t-1}")],
                     tag=f"x2{t}_{fo}")
                T["ep_used"] = f"x2{t}_{fo}"
            T[f"o_used{t}"] = f"wo{t}_{KC-1}"
            T[f"cx_free{t}"] = f"x2{t}_{KC-1}"
            dma("gpsimd", x2h_v[:, :, t * TT:(t + 1) * TT], sb["x2_sb"][:, :, :],
                "SPB", tag=f"x2sp{t}", deps=[f"x2{t}_{KC-1}"])
            ln2t = emit_ln(
                f"cln{t}", f"x2{t}_{KC-1}", sb["x2_sb"], sb["cxbf"],
                [sb["csq0"], sb["csq1"]], sb["ctmp1"],
                sb["ctmp2"], sb["crsb"], sb["cmrsb"], sb["h2_sb"], sb["cb1"],
                sb["mcols"][:, 3, :], t * KC,
                cast_deps=[T.get(f"cln_xbf_free{t-1}")],
                mod_deps=[f"x2sp{t}"])
            T[f"x2_free{t}"] = f"cln{t}cast{KC-1}"
            T[f"cln_xbf_free{t}"] = f"cln{t}smod{KC-1}"
            dma("gpsimd", h2h_v[:, :, t * TT:(t + 1) * TT], sb["h2_sb"][:, :, :],
                "SPB", tag=f"h2sp{t}", deps=[ln2t])
            T[f"h2sp{t}"] = f"h2sp{t}"

        p.barrier("p2c", sb)

        MB = {"prev_stage": None, "prev_ld": None}

        def mbarrier(name, folds, dep):
            for j, (sn, lo, hi) in enumerate(folds):
                p.op("vector", lambda e, sn=sn, lo=lo, hi=hi, j=j: e.tensor_reduce(
                        sb["mxst"][:, j:j + 1], sb[sn][:, lo:hi], axis=AX.X, op=ALU.max),
                     deps=[dep, MB["prev_stage"]], tag=f"{name}_f{j}")
            if len(folds) == 1:
                p.op("vector", lambda e: e.memset(sb["mxst"][:, 1:2], 0.0),
                     deps=[MB["prev_stage"]], tag=f"{name}_f1")
            dma("gpsimd", mx_in, sb["mxst"][:, :], "SG", tag=f"{name}_stage",
                deps=[f"{name}_f0", f"{name}_f1"])
            MB["prev_stage"] = f"{name}_stage"
            p.op("gpsimd", lambda e: e.collective_compute(
                    "AllReduce", ALU.max, replica_groups=[list(range(8))],
                    ins=[mx_in.opt()], outs=[mx_out.opt()]),
                 deps=[f"{name}_stage"], sem="CC", tag=f"{name}_cc")
            dma("gpsimd", sb["mxrow"][:, :], mx_out, "SG", tag=f"{name}_ld",
                deps=[f"{name}_cc", MB["prev_ld"]])
            p.op("vector", lambda e: e.tensor_reduce(
                    sb["mxfold"][:, :].unsqueeze(2),
                    sb["mxrow"][:, :].rearrange("a (q c) -> a c q", c=2),
                    axis=AX.X, op=ALU.max),
                 deps=[f"{name}_ld"], tag=f"{name}_max")
            MB["prev_ld"] = f"{name}_max"

        def bcast_col(name, src_ap, dep, tg):
            p.op("tensor", lambda e: e.matmul(ps["big0"][:, 0:1], sb["onesrow"][:, :],
                                              src_ap, start=True, stop=True),
                 deps=[dep, PSL.get("big0"), "ld_consts"], tag=f"bcm_{tg}")
            p.op("scalar", lambda e: e.activation(sb[name][:, :], ps["big0"][:, 0:1], AF.Identity),
                 deps=[f"bcm_{tg}"], tag=f"bce_{tg}")
            PSL["big0"] = f"bce_{tg}"
            return f"bce_{tg}"

        if PH < 4:
            finish_early()
            return
        mbarrier("m0", [("m0slots", 0, 64)], f"cln{NT-1}m0_{KC-1}")
        p.op("vector", lambda e: e.tensor_scalar(sb["srow"][:, 0:1], sb["mxfold"][:, 0:1],
                                                 1.0 / QMAX, None, op0=ALU.mult),
             deps=["m0_max"], tag="s0row")
        p.op("vector", lambda e: e.reciprocal(sb["srow"][:, 1:2], sb["srow"][:, 0:1]),
             deps=["s0row"], tag="r0row")
        bcast_col("s0c", sb["srow"][0:1, 0:1], "s0row", "s0c")
        bcast_col("r0c", sb["srow"][0:1, 1:2], "r0row", "r0c")

        if PH < 5:
            finish_early()
            return
        dma("sync", sb["wup_sb"][:, :, :], wupT_q.rearrange("(kc q) f -> q kc f", q=P),
            "LF", tag="ld_wup", deps=[T["dq_done"]])
        dma("sync", sb["wg_sb"][:, :, :], wgT_q.rearrange("(kc q) f -> q kc f", q=P),
            "LF", tag="ld_wg", deps=[T["dq_done"]])
        h2r = [sb["h2r0"], sb["h2r1"]]
        xqt = [sb["xqt0"], sb["xqt1"]]
        for t in range(NT):
            sl = t % 2
            dma("sync", h2r[sl][:, :, :], h2h_v[:, :, t * TT:(t + 1) * TT],
                f"H{sl}", tag=f"bh2ld{t}",
                deps=[f"h2sp{NT-1}", T.get(f"bh2_free{t-2}")])
            for kc in range(KC):
                ts_ = kc % 2
                p.op("scalar", lambda e, sl=sl, kc=kc, ts_=ts_: e.activation(
                        xqt[ts_][:, :], h2r[sl][:, kc, :], AF.Identity,
                        scale=sb["r0c"][:, :], bias=sb["rccol"][:, :]),
                     deps=[f"bh2ld{t}", "bce_r0c",
                           (f"bxq{t}_{kc-2}" if kc >= 2 else
                            (f"bxq{t-1}_{KC-2+kc}" if t >= 1 else None))],
                     tag=f"bxqt{t}_{kc}")
                p.op("vector", lambda e, kc=kc, ts_=ts_: e.tensor_scalar(
                        sb["xq_sb"][:, kc, :], xqt[ts_][:, :], -RC, sb["s0c"][:, :],
                        op0=ALU.add, op1=ALU.mult),
                     deps=[f"bxqt{t}_{kc}", "bce_s0c", T.get(f"xq_used{t-1}")],
                     tag=f"bxq{t}_{kc}")
            T[f"bh2_free{t}"] = f"bxqt{t}_{KC-1}"
            for which, (wname, stname, slots, ldt, dsth) in enumerate(
                    [("wup_sb", "Ust", "m1slots", "ld_wup", Uh),
                     ("wg_sb", "Gst", "m2slots", "ld_wg", Gh)]):
                for ib in range(NI):
                    bs = f"big{ib % 2}"
                    def mk_u(e, wname=wname, ib=ib, bs=bs):
                        r = None
                        for kc in range(KC):
                            r = e.matmul(ps[bs][:, :], sb[wname][:, kc, ib * P:(ib + 1) * P],
                                         sb["xq_sb"][:, kc, :],
                                         start=(kc == 0), stop=(kc == KC - 1))
                        return r
                    p.op("tensor", mk_u,
                         deps=[f"bxq{t}_{KC-1}", "ld_wg", PSL.get(bs),
                               PSL.get(bs + "_mx")],
                         tag=f"bmm{t}_{which}_{ib}")
                    p.op("scalar", lambda e, stname=stname, ib=ib, bs=bs: e.activation(
                            sb[stname][:, ib, :], ps[bs][:, :], AF.Identity),
                         deps=[f"bmm{t}_{which}_{ib}", T.get(f"bst{t-1}_1")],
                         tag=f"bev{t}_{which}_{ib}")
                    p.op("vector", lambda e, slots=slots, stname=stname, ib=ib, t=t: e.tensor_reduce(
                            sb[slots][:, t * NI + ib:t * NI + ib + 1],
                            sb[stname][:, ib, :], axis=AX.X, op=ALU.max,
                            apply_absolute_value=True),
                         deps=[f"bev{t}_{which}_{ib}"],
                         tag=f"bmx{t}_{which}_{ib}")
                    PSL[bs] = f"bev{t}_{which}_{ib}"
                    PSL[bs + "_mx"] = f"bmx{t}_{which}_{ib}"
                dma("gpsimd", dsth[:, t * NI * TT:(t + 1) * NI * TT],
                    sb[stname][:, :, :], "SC1", tag=f"bst{t}_{which}",
                    deps=[f"bev{t}_{which}_{NI-1}", f"bmx{t}_{which}_{NI-1}"])
                T[f"bst{t}_{which}"] = f"bst{t}_{which}"
            T[f"xq_used{t}"] = f"bmm{t}_1_{NI-1}"

        if PH < 6:
            finish_early()
            return
        mbarrier("m12", [("m1slots", 0, NT * NI), ("m2slots", 0, NT * NI)],
                 f"bmx{NT-1}_1_{NI-1}")
        p.op("vector", lambda e: e.tensor_scalar(sb["srow"][:, 2:4], sb["mxfold"][:, 0:2],
                                                 1.0 / QMAX, None, op0=ALU.mult),
             deps=["m12_max"], tag="s12row")
        p.op("vector", lambda e: e.reciprocal(sb["srow"][:, 4:6], sb["srow"][:, 2:4]),
             deps=["s12row"], tag="r12row")
        bcast_col("r1c", sb["srow"][0:1, 4:5], "r12row", "r1c")
        bcast_col("r2c", sb["srow"][0:1, 5:6], "r12row", "r2c")
        bcast_col("s2c", sb["srow"][0:1, 3:4], "s12row", "s2c")
        p.op("vector", lambda e: e.tensor_scalar(sb["brc2c"][:, :], sb["s2c"][:, :],
                                                 -RC, None, op0=ALU.mult),
             deps=["bce_s2c"], tag="brc2_done")

        dma("sync", sb["wd_sb"][:, :, :], wdT_q.rearrange("(ic q) f -> q ic f", q=P),
            "LF", tag="ld_wd", deps=[T["dq_done"], f"bst{NT-1}_1"])
        gld = [sb["bufA0"], sb["bufA1"]]
        t1f = [sb["t1f0"], sb["t1f1"], sb["t1g0"], sb["t1g1"]]
        sgf = [sb["sgf0"], sb["sgf1"], sb["sgf2"], sb["sgf3"]]
        NH = NI // 2
        for t in range(NT):
            sl = t % 2
            dma("gpsimd", gld[sl][:, :, :], Gh[:, t * NI * TT:(t + 1) * NI * TT],
                f"U{sl}", tag=f"c1ld{t}",
                deps=[f"bst{NT-1}_1", T.get(f"c1_free{t-2}")])
            for ib in range(NI):
                ts_ = ib % 4
                pg = (f"c1gp{t}_{ib-4}" if ib >= 4 else
                      (f"c1gp{t-1}_{NI-4+ib}" if t >= 1 else None))
                p.op("scalar", lambda e, sl=sl, ib=ib, ts_=ts_: e.activation(
                        t1f[ts_][:, :], gld[sl][:, ib, :], AF.Identity,
                        scale=sb["r2c"][:, :], bias=sb["rccol"][:, :]),
                     deps=[f"c1ld{t}", "bce_r2c", pg],
                     tag=f"c1t1{t}_{ib}")
                p.op("scalar", lambda e, ts_=ts_: e.activation(
                        sgf[ts_][:, :], t1f[ts_][:, :], AF.Sigmoid,
                        scale=sb["s2c"][:, :], bias=sb["brc2c"][:, :]),
                     deps=["brc2_done", f"c1t1{t}_{ib}"], tag=f"c1sg{t}_{ib}")
                p.op("scalar", lambda e, ts_=ts_: e.activation(
                        sb["tgbf"][:, ts_, :], t1f[ts_][:, :], AF.Identity,
                        bias=sb["mrccol"][:, :]),
                     deps=[f"c1sg{t}_{ib}", "mrc_set", pg], tag=f"c1tg{t}_{ib}")
                p.op("vector", lambda e, ts_=ts_: e.tensor_scalar(
                        sgf[ts_][:, :], sgf[ts_][:, :], 65536.0, RC,
                        op0=ALU.mult, op1=ALU.add),
                     deps=[f"c1sg{t}_{ib}"], tag=f"c1t3{t}_{ib}")
                p.op("vector", lambda e, ts_=ts_: e.tensor_scalar(
                        sb["srbf"][:, :], sgf[ts_][:, :], -RC, 65535.0,
                        op0=ALU.add, op1=ALU.min),
                     deps=[f"c1t3{t}_{ib}"], tag=f"c1sr{t}_{ib}")
                p.op("vector", lambda e, ib=ib, ts_=ts_: e.tensor_mul(
                        sb["g2st"][:, ib, :], sb["tgbf"][:, ts_, :], sb["srbf"][:, :]),
                     deps=[f"c1sr{t}_{ib}", f"c1tg{t}_{ib}",
                           T.get(f"c1st{t-1}_{ib // NH}")],
                     tag=f"c1gp{t}_{ib}")
                p.op("vector", lambda e, ib=ib, t=t: e.tensor_reduce(
                        sb["m3slots"][:, t * NI + ib:t * NI + ib + 1],
                        sb["g2st"][:, ib, :], axis=AX.X, op=ALU.max,
                        apply_absolute_value=True),
                     deps=[f"c1gp{t}_{ib}"], tag=f"c1mx{t}_{ib}")
                if ib == NH - 1:
                    dma("gpsimd", g2h[:, (t * NI) * TT:(t * NI + NH) * TT],
                        sb["g2st"][:, 0:NH, :],
                        "SC2", tag=f"c1st{t}_0", deps=[f"c1gp{t}_{NH-1}", f"c1mx{t}_{NH-1}"])
                    T[f"c1st{t}_0"] = f"c1st{t}_0"
            dma("gpsimd", g2h[:, (t * NI + NH) * TT:(t + 1) * NI * TT],
                sb["g2st"][:, NH:NI, :],
                "SC2B", tag=f"c1st{t}_1", deps=[f"c1gp{t}_{NI-1}", f"c1mx{t}_{NI-1}"])
            T[f"c1st{t}_1"] = f"c1st{t}_1"
            T[f"c1_free{t}"] = f"c1t1{t}_{NI-1}"

        if PH < 7:
            finish_early()
            return
        mbarrier("m3", [("m3slots", 0, NT * NI)], f"c1mx{NT-1}_{NI-1}")
        p.op("vector", lambda e: e.tensor_mul(sb["srow"][:, 6:7], sb["mxfold"][:, 0:1],
                                              sb["srow"][:, 3:4]),
             deps=["m3_max", "s12row"], tag="m3a")
        p.op("vector", lambda e: e.tensor_scalar(sb["srow"][:, 6:7], sb["srow"][:, 6:7],
                                                 1.0 / (65536.0 * QMAX), None, op0=ALU.mult),
             deps=["m3a"], tag="s3row")
        p.op("vector", lambda e: e.reciprocal(sb["srow"][:, 7:8], sb["srow"][:, 6:7]),
             deps=["s3row"], tag="r3row")
        p.op("vector", lambda e: e.tensor_mul(sb["srow"][:, 8:9], sb["srow"][:, 3:4],
                                              sb["srow"][:, 7:8]),
             deps=["r3row"], tag="c32a")
        p.op("vector", lambda e: e.tensor_scalar(sb["srow"][:, 8:9], sb["srow"][:, 8:9],
                                                 1.0 / 65536.0, None, op0=ALU.mult),
             deps=["c32a"], tag="c32row")
        bcast_col("c32c", sb["srow"][0:1, 8:9], "c32row", "c32c")

        uld = [sb["bufB0"], sb["bufB1"]]
        for t in range(NT):
            sl = t % 2
            dma("gpsimd", gld[sl][:, :, :], g2h[:, t * NI * TT:(t + 1) * NI * TT],
                f"U{sl}", tag=f"c2gld{t}",
                deps=[f"c1st{t}_0", f"c1st{t}_1",
                      (T.get(f"c2_free{t-2}") if t >= 2 else T.get(f"c1_free{NT-2+t}"))])
            dma("gpsimd", uld[sl][:, :, :], Uh[:, t * NI * TT:(t + 1) * NI * TT],
                f"K{sl}", tag=f"c2uld{t}",
                deps=[f"bst{NT-1}_1", T.get(f"c2_free{t-2}")])
            for ib in range(NI):
                ts_ = ib % 4
                pg = (f"c2pr{t}_{ib-4}" if ib >= 4 else
                      (f"c2pr{t-1}_{NI-4+ib}" if t >= 1 else None))
                p.op("scalar", lambda e, sl=sl, ib=ib, ts_=ts_: e.activation(
                        t1f[ts_][:, :], gld[sl][:, ib, :], AF.Identity,
                        scale=sb["c32c"][:, :], bias=sb["rccol"][:, :]),
                     deps=[f"c2gld{t}", "bce_c32c", pg], tag=f"c2t4{t}_{ib}")
                p.op("vector", lambda e, ts_=ts_: e.tensor_scalar(
                        sb["g2bf"][:, :], t1f[ts_][:, :], -RC, None, op0=ALU.add),
                     deps=[f"c2t4{t}_{ib}"] +
                          ([f"c1st{NT-1}_0", f"c1st{NT-1}_1"]
                           if (t == 0 and ib == 0) else []),
                     tag=f"c2g2{t}_{ib}")
                p.op("scalar", lambda e, sl=sl, ib=ib, ts_=ts_: e.activation(
                        sgf[ts_][:, :], uld[sl][:, ib, :], AF.Identity,
                        scale=sb["r1c"][:, :], bias=sb["rccol"][:, :]),
                     deps=[f"c2uld{t}", "bce_r1c", pg], tag=f"c2t5{t}_{ib}")
                p.op("vector", lambda e, ts_=ts_: e.tensor_scalar(
                        sb["upbf"][:, :], sgf[ts_][:, :], -RC, None, op0=ALU.add),
                     deps=[f"c2t5{t}_{ib}"], tag=f"c2up{t}_{ib}")
                p.op("vector", lambda e, ib=ib: e.tensor_mul(
                        sb["prst"][:, ib, :], sb["g2bf"][:, :], sb["upbf"][:, :]),
                     deps=[f"c2g2{t}_{ib}", f"c2up{t}_{ib}",
                           T.get(f"c2st{t-1}_{ib // NH}")],
                     tag=f"c2pr{t}_{ib}")
                if ib == NH - 1:
                    dma("gpsimd", prh[:, (t * NI) * TT:(t * NI + NH) * TT],
                        sb["prst"][:, 0:NH, :],
                        "SC3", tag=f"c2st{t}_0", deps=[f"c2pr{t}_{NH-1}", f"c2mx{t}_{NH-1}"])
                    T[f"c2st{t}_0"] = f"c2st{t}_0"
                p.op("vector", lambda e, ib=ib, t=t: e.tensor_reduce(
                        sb["m4slots"][:, t * NI + ib:t * NI + ib + 1],
                        sb["prst"][:, ib, :], axis=AX.X, op=ALU.max,
                        apply_absolute_value=True),
                     deps=[f"c2pr{t}_{ib}"], tag=f"c2mx{t}_{ib}")
            dma("gpsimd", prh[:, (t * NI + NH) * TT:(t + 1) * NI * TT],
                sb["prst"][:, NH:NI, :],
                "SC3B", tag=f"c2st{t}_1", deps=[f"c2pr{t}_{NI-1}", f"c2mx{t}_{NI-1}"])
            T[f"c2st{t}_1"] = f"c2st{t}_1"
            T[f"c2_free{t}"] = f"c2pr{t}_{NI-1}"

        if PH < 8:
            finish_early()
            return
        mbarrier("m4", [("m4slots", 0, NT * NI)], f"c2mx{NT-1}_{NI-1}")
        p.op("vector", lambda e: e.tensor_mul(sb["srow"][:, 9:10], sb["srow"][:, 6:7],
                                              sb["srow"][:, 2:3]),
             deps=["s3row", "s12row", "m4_max"], tag="s31")
        p.op("vector", lambda e: e.tensor_mul(sb["srow"][:, 10:11], sb["mxfold"][:, 0:1],
                                              sb["srow"][:, 9:10]),
             deps=["s31"], tag="m4a")
        p.op("vector", lambda e: e.tensor_scalar(sb["srow"][:, 10:11], sb["srow"][:, 10:11],
                                                 1.0 / QMAX, None, op0=ALU.mult),
             deps=["m4a"], tag="s4row")
        p.op("vector", lambda e: e.reciprocal(sb["srow"][:, 11:12], sb["srow"][:, 10:11]),
             deps=["s4row"], tag="r4row")
        p.op("vector", lambda e: e.tensor_mul(sb["srow"][:, 11:12], sb["srow"][:, 11:12],
                                              sb["srow"][:, 9:10]),
             deps=["r4row"], tag="c4row")
        bcast_col("c4c", sb["srow"][0:1, 11:12], "c4row", "c4c")
        bcast_col("s4c", sb["srow"][0:1, 10:11], "s4row", "s4c")
        p.op("vector", lambda e: e.tensor_scalar(sb["gbs4"][:, :], sb["mcols"][:, 5, :],
                                                 sb["s4c"][:, :], None, op0=ALU.mult),
             deps=["bce_s4c", "modcols_done"], tag="gbs4_done")

        x2r = [sb["x2r0"], sb["x2r1"]]
        t1g = [sb["t1g0"], sb["t1g1"]]
        outs = [sb["outs0"], sb["outs0"]]
        ep2 = [sb["ep20"], sb["ep21"]]
        for t in range(NT):
            sl = t % 2
            dma("gpsimd", gld[sl][:, :, :], prh[:, t * NI * TT:(t + 1) * NI * TT],
                f"U{sl}", tag=f"c3pld{t}",
                deps=[f"c2st{t}_0", f"c2st{t}_1",
                      (T.get(f"c3_free{t-2}") if t >= 2 else T.get(f"c2_free{NT-2+t}"))])
            dma("sync", x2r[sl][:, :, :], x2h_v[:, :, t * TT:(t + 1) * TT],
                f"X{sl}", tag=f"c3xld{t}",
                deps=[f"h2sp{NT-1}", f"c2st{NT-1}_0", f"c2st{NT-1}_1", T.get(f"c3x2_free{t-2}")])
            for ib2 in range(NI // 2):
                ts_ = ib2 % 2
                pg = (f"c3o{t}_{ib2-2}" if ib2 >= 2 else
                      (f"c3o{t-1}_{NI//2-2+ib2}" if t >= 1 else None))
                def mk_c3t(e, sl=sl, ib2=ib2, ts_=ts_):
                    r = None
                    for j in range(2):
                        r = e.activation(t1f[ts_][:, :] if j == 0 else t1g[ts_][:, :],
                                         gld[sl][:, 2 * ib2 + j, :], AF.Identity,
                                         scale=sb["c4c"][:, :], bias=sb["rccol"][:, :])
                    return r
                p.op("scalar", mk_c3t,
                     deps=[f"c3pld{t}", "bce_c4c", pg], tag=f"c3t{t}_{ib2}")
                def mk_c3o(e, ib2=ib2, ts_=ts_):
                    e.tensor_scalar(sb["obf"][:, 2 * ib2, :], t1f[ts_][:, :],
                                    -RC, None, op0=ALU.add)
                    return e.tensor_scalar(sb["obf"][:, 2 * ib2 + 1, :], t1g[ts_][:, :],
                                           -RC, None, op0=ALU.add)
                p.op("vector", mk_c3o,
                     deps=[f"c3t{t}_{ib2}", T.get(f"c3mm_done{t-1}")],
                     tag=f"c3o{t}_{ib2}")
            for fo in range(KC):
                bs = f"big{fo % 2}"
                es = fo % 2
                def mk_dn(e, fo=fo, bs=bs):
                    r = None
                    for ic in range(NI):
                        r = e.matmul(ps[bs][:, :], sb["wd_sb"][:, ic, fo * P:(fo + 1) * P],
                                     sb["obf"][:, ic, :],
                                     start=(ic == 0), stop=(ic == NI - 1))
                    return r
                p.op("tensor", mk_dn,
                     deps=[f"c3o{t}_{NI//2-1}", "ld_wd", PSL.get(bs)],
                     tag=f"c3mm{t}_{fo}")
                p.op("scalar", lambda e, fo=fo, bs=bs, es=es: e.activation(
                        ep2[es][:, :], ps[bs][:, :], AF.Identity,
                        scale=sb["gbs4"][:, fo:fo + 1]),
                     deps=[f"c3mm{t}_{fo}", "gbs4_done", T.get(f"c3ep_free{t}_{fo-2}"),
                           T.get(f"c3ep_freeprev{t}") if fo < 2 else None] +
                          ([f"c2st{NT-1}_0", f"c2st{NT-1}_1"]
                           if (t == 0 and fo == 0) else []),
                     tag=f"c3ep{t}_{fo}")
                PSL[bs] = f"c3ep{t}_{fo}"
                p.op("vector", lambda e, fo=fo, sl=sl, es=es: e.tensor_add(
                        outs[sl][:, fo, :], x2r[sl][:, fo, :], ep2[es][:, :]),
                     deps=[f"c3ep{t}_{fo}", f"c3xld{t}", T.get(f"c3ost{t-1}")],
                     tag=f"c3out{t}_{fo}")
                T[f"c3ep_free{t}_{fo}"] = f"c3out{t}_{fo}"
            T[f"c3ep_freeprev{t+1}"] = f"c3out{t}_{KC-1}"
            T[f"c3mm_done{t}"] = f"c3mm{t}_{KC-1}"
            T[f"c3x2_free{t}"] = f"c3out{t}_{KC-1}"
            T[f"c3_free{t}"] = f"c3t{t}_{NI//2-1}"
            dma("sync", out_v[:, :, t * TT:(t + 1) * TT], outs[sl][:, :, :],
                "ST", tag=f"c3ost{t}", deps=[f"c3out{t}_{KC-1}"])
            T[f"c3ost{t}"] = f"c3ost{t}"

        p.op("sync", lambda e: e.wait_ge(p.sems["ST"], 16 * NT), inc=0)
        p.op("gpsimd", lambda e: e.wait_ge(p.sems["ST"], 16 * NT), inc=0)

    p.reset("count")
    program()
    snap = dict(p.tags)
    p.reset("count")
    program()
    assert p.tags == snap, "walk is not deterministic"

    with ctx:
        block = ctx.enter_context(nc.Block())

        def mk_section(eng):
            def f(engine):
                p.engs[eng] = engine
                p.reset(eng)
                program()
            return f

        block.sync(mk_section("sync"))
        block.gpsimd(mk_section("gpsimd"))
        block.tensor(mk_section("tensor"))
        block.scalar(mk_section("scalar"))
        block.vector(mk_section("vector"))

    return nc


def stage(inputs):
    x = np.asarray(inputs["L0_attn_x"], dtype=np.float32)
    shared = {
        "wqT": np.ascontiguousarray(np.asarray(inputs["L0_attn_Wq"], np.float32).T),
        "wkT": np.ascontiguousarray(np.asarray(inputs["L0_attn_Wk"], np.float32).T),
        "wvT": np.ascontiguousarray(np.asarray(inputs["L0_attn_Wv"], np.float32).T),
        "woT": np.ascontiguousarray(np.asarray(inputs["L0_attn_Wo"], np.float32).T),
        "fmqT": np.ascontiguousarray(np.asarray(inputs["L0_attn_fmq_w"], np.float32).T),
        "fmkT": np.ascontiguousarray(np.asarray(inputs["L0_attn_fmk_w"], np.float32).T),
        "fmqb": np.ascontiguousarray(np.asarray(inputs["L0_attn_fmq_b"], np.float32).reshape(P, 1)),
        "fmkb": np.ascontiguousarray(np.asarray(inputs["L0_attn_fmk_b"], np.float32).reshape(P, 1)),
        "rmsw": np.ascontiguousarray(np.asarray(inputs["L0_attn_rms_w"], np.float32).reshape(P, 1)),
        "wup": np.ascontiguousarray(np.asarray(inputs["L0_ffn_w_up"], np.float32)),
        "wgate": np.ascontiguousarray(np.asarray(inputs["L0_ffn_w_gate"], np.float32)),
        "wdown": np.ascontiguousarray(np.asarray(inputs["L0_ffn_w_down"], np.float32)),
        "wmod": np.ascontiguousarray(np.asarray(inputs["L0_wmod"], np.float32)),
    }
    c = np.asarray(inputs["L0_c"], dtype=np.float32)
    in_maps = []
    for core in range(8):
        b, hf = core // 2, core % 2
        m = dict(shared)
        m["xT"] = np.ascontiguousarray(x[b, hf * TOK:(hf + 1) * TOK, :].T)
        m["crow"] = np.ascontiguousarray(c[b:b + 1, :])
        sm = np.zeros((1, 8), dtype=np.float32)
        if hf == 1:
            sm[0, 0] = 1.0
        m["smask"] = sm
        in_maps.append(m)
    return in_maps


def kernel(**inputs):
    in_maps = stage(inputs)
    nc = build_nc()
    res = run_bass_kernel_spmd(nc, in_maps, core_ids=list(range(8)))
    out = np.empty((4, 2 * TOK, D), dtype=np.float32)
    for core in range(8):
        b, hf = core // 2, core % 2
        out[b, hf * TOK:(hf + 1) * TOK, :] = res.results[core]["out"].T
    return out
